# version 32
# baseline (speedup 1.0000x reference)
import numpy as np
import ml_dtypes
from contextlib import ExitStack

import concourse.bass as bass
import concourse.mybir as mybir
from concourse.bass_utils import run_bass_kernel_spmd

F32 = mybir.dt.float32
BF16 = mybir.dt.bfloat16
FP16 = mybir.dt.float16
AF = mybir.ActivationFunctionType
ALU = mybir.AluOpType
AXX = mybir.AxisListType.X
BF = ml_dtypes.bfloat16

D = 1024; DIN = 2048; NH = 32; PD = 64
Q = 128; QT = 512
EPS = 1e-5; NEG = -1e30

ENGINES = ("sync", "scalar", "vector", "tensor", "gpsimd")

def _esz(dt):
    try:
        return dt.itemsize
    except Exception:
        return {F32: 4, BF16: 2, FP16: 2}.get(dt, 4)

def ap_range(ap):
    th = ap.tensor
    t = th.name
    esz = _esz(ap.dtype)
    pairs = list(ap.ap)
    onchip = "DRAM" not in str(getattr(ap, "space", "")).upper()
    lo = hi = ap.offset * esz
    dims = pairs[1:] if onchip else pairs
    for step, count in dims:
        if count <= 1:
            continue
        d = step * (count - 1) * esz
        if d >= 0:
            hi += d
        else:
            lo += d
    if onchip:
        pitch = int(th.shape[-1]) * _esz(th.dtype)
        base = (lo // pitch) * pitch
        lo -= base
        hi -= base
        if hi >= pitch:
            lo, hi = 0, pitch
    return (t, lo, hi + esz)

class Op:
    __slots__ = ("eng", "fn", "reads", "writes", "xwaits", "dmach", "dmaval",
                 "mark", "waits", "idx", "barrier_inc")
    def __init__(self, eng, fn, reads, writes, xwaits, dmach):
        self.eng = eng; self.fn = fn
        self.reads = reads; self.writes = writes
        self.xwaits = list(xwaits)
        self.dmach = dmach
        self.dmaval = 0
        self.mark = None
        self.waits = []
        self.idx = 0
        self.barrier_inc = None

class Ch:
    __slots__ = ("name", "n", "sem")
    def __init__(self, name):
        self.name = name; self.n = 0; self.sem = None

class Sched:
    def __init__(self):
        self.ops = []
        self.chans = {}
        self.mark_ch = {e: self.chan("mk_" + e) for e in ENGINES}

    def chan(self, name):
        if name not in self.chans:
            self.chans[name] = Ch(name)
        return self.chans[name]

    DMA_POOL = 10

    def op(self, eng, fn, r=(), w=(), waits=(), dma=False, tag=None):
        reads = [ap_range(a) for a in r if a is not None]
        writes = [ap_range(a) for a in w if a is not None]
        ch = None
        o = Op(eng, fn, reads, writes, waits, ch)
        if dma:
            if not hasattr(self, "dmacnt"):
                self.dmacnt = {}
            i = self.dmacnt.get(eng, 0)
            self.dmacnt[eng] = i + 1
            ch = self.chan(f"dq_{eng}_{i % self.DMA_POOL}")
            if ch.n > 0:
                o.xwaits.append((ch, ch.n))
            ch.n += 16
            o.dmach = ch
            o.dmaval = ch.n
        self.ops.append(o)
        return o

    def dma(self, eng, dst, src, waits=(), tag=None, slow=False):
        if slow:
            fn = lambda e, d=dst, s=src: e.dma_start(out=d, in_=s,
                                                     allow_slow_non_contiguous=True)
        else:
            fn = lambda e, d=dst, s=src: e.dma_start(out=d, in_=s)
        return self.op(eng, fn, r=[src], w=[dst], waits=waits, dma=True, tag=tag)

    def barrier(self):
        bc = {e: self.chan("bar_" + e) for e in ENGINES}
        snap = {}
        for e in ENGINES:
            fn = lambda eng: eng.nop()
            o = self.op(e, fn)
            o.barrier_inc = bc[e]
            bc[e].n += 1
            snap[e] = bc[e].n
            o.xwaits = list(o.xwaits)
        for e in ENGINES:
            o = self.op(e, None,
                        waits=[(bc[e2], snap[e2]) for e2 in ENGINES if e2 != e])
        self._bar_pending = getattr(self, "_bar_pending", [])

    def finalize(self):
        writer = {}
        readers = {}
        for o in self.ops:
            deps = set()
            for t, lo, hi in o.reads:
                for (wl, wh), wo in writer.get(t, ()):
                    if lo < wh and hi > wl:
                        deps.add(wo)
            for t, lo, hi in o.writes:
                for (wl, wh), wo in writer.get(t, ()):
                    if lo < wh and hi > wl:
                        deps.add(wo)
                for (rl, rh), ro in readers.get(t, ()):
                    if lo < rh and hi > rl:
                        deps.add(ro)
            for dep in deps:
                if dep is o:
                    continue
                if dep.dmach is not None:
                    o.waits.append((dep.dmach, dep.dmaval))
                elif dep.eng != o.eng:
                    if dep.mark is None:
                        dep.mark = True
                    o.waits.append(("MARK", dep))
                else:
                    if o.eng == "tensor":
                        continue
                    spans = [hi - lo for (_, lo, hi) in dep.writes] or [0]
                    if max(spans) < 2048:
                        if dep.mark is None:
                            dep.mark = True
                        o.waits.append(("MARK", dep))
            for t, lo, hi in o.writes:
                wl_ = [(rg, wo) for (rg, wo) in writer.get(t, ())
                       if not (lo <= rg[0] and hi >= rg[1])]
                wl_.append(((lo, hi), o))
                writer[t] = wl_
                rl_ = [(rg, ro) for (rg, ro) in readers.get(t, ())
                       if not (lo <= rg[0] and hi >= rg[1])]
                readers[t] = rl_
            for t, lo, hi in o.reads:
                lst = readers.setdefault(t, [])
                key = (o.eng, lo, hi)
                for i_, (rg, ro) in enumerate(lst):
                    if (ro.eng, rg[0], rg[1]) == key:
                        lst[i_] = ((lo, hi), o)
                        break
                else:
                    lst.append(((lo, hi), o))
        cnt = {e: 0 for e in ENGINES}
        for o in self.ops:
            if getattr(o, "barrier_inc", None) is not None:
                pass
            if o.mark:
                cnt[o.eng] += 1
                o.mark = cnt[o.eng]
        for o in self.ops:
            ws = {}
            for item in o.waits:
                if isinstance(item, tuple) and item[0] == "MARK":
                    dep = item[1]
                    c = self.mark_ch[dep.eng]
                    v = dep.mark
                else:
                    c, v = item
                if v and v > 0:
                    ws[c] = max(ws.get(c, 0), v)
            for c, v in o.xwaits:
                if v and v > 0:
                    ws[c] = max(ws.get(c, 0), v)
            o.waits = list(ws.items())
        self.mark_totals = cnt

    def check(self):
        vals = {c: 0 for c in self.chans.values()}
        qs = {}
        for i, o in enumerate(self.ops):
            o.idx = i
            qs.setdefault(o.eng, []).append(o)
        heads = {e: 0 for e in qs}
        done, prog = 0, True
        while prog:
            prog = False
            for e, lst in qs.items():
                while heads[e] < len(lst):
                    o = lst[heads[e]]
                    if any(vals[c] < v for c, v in o.waits):
                        break
                    if o.dmach is not None:
                        vals[o.dmach] = max(vals[o.dmach], o.dmaval)
                    if o.mark:
                        vals[self.mark_ch[o.eng]] = max(vals[self.mark_ch[o.eng]], o.mark)
                    bi = getattr(o, "barrier_inc", None)
                    if bi is not None:
                        vals[bi] = vals[bi] + 1
                    heads[e] += 1; done += 1; prog = True
        if done != len(self.ops):
            stuck = []
            for e, lst in qs.items():
                if heads[e] < len(lst):
                    o = lst[heads[e]]
                    bad = [(c.name, v, vals[c]) for c, v in o.waits if vals[c] < v]
                    stuck.append((e, o.idx, bad[:3]))
            raise RuntimeError(f"deadlock, {len(self.ops) - done} stuck: {stuck[:5]}")

    def emit(self, nc, block, es):
        for c in self.chans.values():
            c.sem = es.enter_context(nc.semaphore(c.name))
        for ename in ENGINES:
            ops = [o for o in self.ops if o.eng == ename]
            def body(eng, ops=ops):
                for o in ops:
                    for c, v in o.waits:
                        eng.wait_ge(c.sem, v)
                    if o.fn is None:
                        continue
                    ins = o.fn(eng)
                    if o.dmach is not None:
                        ins.then_inc(o.dmach.sem, 16)
                    if o.mark:
                        ins.then_inc(self.mark_ch[o.eng].sem, 1)
                    bi = getattr(o, "barrier_inc", None)
                    if bi is not None:
                        ins.then_inc(bi.sem, 1)
            getattr(block, ename)(body)

MORDER = [34, 32, 0, 33, 1, 16, 17]
for _i in range(14):
    MORDER += [18 + _i, 2 + _i]
WIN_TILE_OFF = []
_off = 0
for _m in MORDER:
    WIN_TILE_OFF.append(_off)
    _off += 8 * 128 * (128 if _m < 34 else 32)
WIN_TOTAL = _off

def _tile_win(W):
    blobs = []
    for m in MORDER:
        w = 128 if m < 34 else 32
        t = W[:, m * 128:m * 128 + w].reshape(8, 128, w).transpose(1, 0, 2)
        blobs.append(np.ascontiguousarray(t).reshape(-1))
    return np.concatenate(blobs)

def _consts():
    ident = np.eye(128, dtype=np.float32).astype(BF)
    id32x4 = np.tile(np.eye(32, dtype=np.float32), (4, 1)).astype(BF)
    m = np.where(np.arange(Q)[:, None] > np.arange(Q)[None, :],
                 np.float32(NEG), np.float32(0.0))
    mask_rep = np.tile(m, (1, 8)).astype(BF)
    e = np.zeros((32, 32 * Q), np.float32)
    for h in range(32):
        e[h, h * Q:(h + 1) * Q] = 1.0
    e32x2 = np.concatenate([e, e], 0).astype(np.float16)
    ep = np.zeros((32, 32 * PD), np.float32)
    for h in range(32):
        ep[h, h * PD:(h + 1) * PD] = 1.0
    e32p = np.concatenate([ep, ep], 0).astype(np.float16)
    ones_bf = np.ones((1, 128), dtype=BF)
    ones_f16 = np.ones((1, 128), dtype=np.float16)
    identj = np.eye(128, dtype=np.float32)[:, ::-1].copy().astype(BF)
    return dict(ident=ident, identj=identj, id32x4=id32x4, mask_rep=mask_rep,
                e32x2=e32x2, e32p=e32p, ones_bf=ones_bf, ones_f16=ones_f16)

def host_inputs(inputs, b, T):
    im = {}
    im["x"] = np.ascontiguousarray(np.asarray(inputs["L0_attn_hidden_states"])[b])
    for d, pre in (("f", "fwd"), ("b", "bwd")):
        g = lambda n: np.asarray(inputs[f"L0_attn_{pre}_{n}"])
        im[f"win_{d}"] = _tile_win(g("W_in"))
        im[f"wout_{d}"] = np.ascontiguousarray(g("W_out"))
        im[f"convw_{d}"] = np.ascontiguousarray(g("conv_w").T.reshape(18, 128, 4))
        im[f"convb_{d}"] = np.ascontiguousarray(g("conv_b").reshape(18, 128, 1))
        im[f"dtb_{d}"] = np.ascontiguousarray(g("dt_bias").reshape(32, 1))
        im[f"alog_{d}"] = np.ascontiguousarray(g("A_log").reshape(32, 1))
        im[f"drep_{d}"] = np.ascontiguousarray(np.repeat(g("D"), 64).reshape(1, 2048))
        im[f"normw_{d}"] = np.ascontiguousarray(g("norm_w").reshape(16, 128, 1))
    gf = lambda n: np.asarray(inputs[f"L0_ffn_{n}"])
    im["fproj"] = np.ascontiguousarray(gf("proj").reshape(8, 128, 16))
    im["fc1"] = np.ascontiguousarray(gf("C1").reshape(16, 4, 128).transpose(1, 0, 2))
    im["fc2"] = np.ascontiguousarray(gf("C2").reshape(16, 4, 128).transpose(1, 0, 2))
    im["fwin1"] = np.ascontiguousarray(gf("Win1")[:1024].reshape(8, 128, 512))
    im["fb1"] = np.ascontiguousarray(gf("Win1")[1024].reshape(4, 128, 1))
    im["fwout1t"] = np.ascontiguousarray(gf("Wout1").T.reshape(4, 128, 4096))
    im["fwin2"] = np.ascontiguousarray(gf("Win2")[:4096].reshape(32, 128, 512))
    im["fb2"] = np.ascontiguousarray(gf("Win2")[4096].reshape(4, 128, 1))
    im["fwout2t"] = np.ascontiguousarray(gf("Wout2").T.reshape(4, 128, 1024))
    im.update(_consts())
    return im

def build_program(T=2048, taps=(), stage=99, dirs=("f", "b")):
    NT = T // 128
    NQ = T // QT
    taps = set(taps)

    nc = bass.Bass("TRN2", target_bir_lowering=False, detect_race_conditions=False)
    dr = {}
    def din(name, shape, dt=F32):
        dr[name] = nc.dram_tensor(name, list(shape), dt, kind="ExternalInput").ap()
    din("x", (T, D))
    for d in ("f", "b"):
        din(f"win_{d}", (WIN_TOTAL,))
        din(f"wout_{d}", (DIN, D))
        din(f"convw_{d}", (18, 128, 4)); din(f"convb_{d}", (18, 128, 1))
        din(f"dtb_{d}", (32, 1)); din(f"alog_{d}", (32, 1))
        din(f"drep_{d}", (1, 2048)); din(f"normw_{d}", (16, 128, 1))
    din("fproj", (8, 128, 16)); din("fc1", (4, 16, 128)); din("fc2", (4, 16, 128))
    din("fwin1", (8, 128, 512)); din("fb1", (4, 128, 1))
    din("fwout1t", (4, 128, 4096))
    din("fwin2", (32, 128, 512)); din("fb2", (4, 128, 1))
    din("fwout2t", (4, 128, 1024))
    din("ident", (128, 128), BF16); din("identj", (128, 128), BF16)
    din("id32x4", (128, 32), BF16)
    din("mask_rep", (128, 1024), BF16)
    din("e32x2", (64, 4096), FP16); din("e32p", (64, 2048), FP16)
    din("ones_bf", (1, 128), BF16); din("ones_f16", (1, 128), FP16)
    out = nc.dram_tensor("out", [T, D], F32, kind="ExternalOutput").ap()
    wbf = {}
    for d_ in ("f", "b"):
        wbf[f"win_{d_}"] = nc.dram_tensor(f"win_{d_}_bf", [WIN_TOTAL], BF16).ap()
        wbf[f"wout_{d_}"] = nc.dram_tensor(f"wout_{d_}_bf", [DIN, D], BF16).ap()
    scr_cum = nc.dram_tensor("scr_cum", [2, 4, 32, 128], FP16).ap()
    scr_ncl = nc.dram_tensor("scr_ncl", [4, 32, 128], FP16).ap()

    tapd = {}
    def deftap(name, shape, dt=F32):
        if name in taps:
            tapd[name] = nc.dram_tensor("tap_" + name, list(shape), dt,
                                        kind="ExternalOutput").ap()
    deftap("x1q", (128, 8 * QT), BF16)
    deftap("ztm", (128, 4 * 2048), BF16)
    deftap("xbc", (128, 18 * 516), BF16)
    deftap("dtq", (66, QT), FP16)
    deftap("cumhl", (64, QT), FP16)
    deftap("Xtm", (128, 2048), BF16)
    deftap("Mpg", (128, 2048), BF16)
    deftap("ybuf", (128, 2048), BF16)
    deftap("yfm", (128, 16 * 512), BF16)
    deftap("Sst", (128, 2048), BF16)
    deftap("attn_f", (128, NT * 1024), BF16)
    deftap("attn", (128, NT * 1024), BF16)
    deftap("xmid", (128, NT * 1024), BF16)
    deftap("acat", (128, 64), F32)
    deftap("aorig", (128, 64), F32)
    deftap("zs_keep", (128, 64), F32)
    deftap("ascr", (128, 64), F32)
    deftap("astat", (128, 24), F32)
    deftap("hfm", (128, 32 * 512), BF16)

    S = Sched()
    es = ExitStack()
    sb = lambda n, shp, dt=F32: es.enter_context(nc.sbuf_tensor(n, list(shp), dt))

    xres = sb("xres", (128, NT * 1024), BF16)
    psum = es.enter_context(nc.psum_tensor("ps", [128, 4096], F32))
    def bank(b, p0=0, pn=128, w=512):
        return psum[p0:p0 + pn, 512 * b:512 * b + w]
    ident = sb("identS", (128, 128), BF16)
    identj = sb("identJS", (128, 128), BF16)
    epscol = sb("epscol", (128, 1))

    def tapcp(name, src, eng="sync"):
        t = tapd.get(name)
        if t is not None:
            S.dma(eng, t[:, :], src)

    statc = sb("statc", (128, 16 * 2))
    colc = sb("colc", (128, 2 * 2))
    lnslot = [0]
    def ln_cols(src_tile, cv=None):
        sl = lnslot[0] % 2; lnslot[0] += 1
        st = statc[:, 16 * sl:16 * sl + 16]
        if cv is None:
            cv = colc[:, 2 * sl:2 * sl + 2]
        w_ = src_tile.shape[-1]
        half = w_ // 2
        S.op("vector", lambda e: e.bn_stats(st[:, 0:6], src_tile[:, 0:half]),
             r=[src_tile], w=[st[:, 0:6]])
        S.op("vector", lambda e: e.bn_stats(st[:, 6:12], src_tile[:, half:w_]),
             r=[src_tile], w=[st[:, 6:12]])
        S.op("vector", lambda e: e.bn_aggr(st[:, 12:14], st[:, 0:12]),
             r=[st[:, 0:12]], w=[st[:, 12:14]])
        S.op("scalar", lambda e: e.activation(st[:, 14:15], st[:, 13:14], AF.Sqrt,
                                              bias=epscol[:, :], scale=1.0),
             r=[st[:, 12:14], epscol[:, :]], w=[st[:, 14:15]])
        S.op("vector", lambda e: e.reciprocal(cv[:, 0:1], st[:, 14:15]),
             r=[st[:, 14:15]], w=[cv[:, 0:1]])
        S.op("vector", lambda e: e.scalar_tensor_tensor(cv[:, 1:2], st[:, 12:13], -1.0,
                                                        cv[:, 0:1], ALU.mult, ALU.mult),
             r=[st[:, 12:14], cv[:, 0:1]], w=[cv[:, 1:2]])
        return cv[:, 0:1], cv[:, 1:2]

    tslot = [0]
    def tbank(b):
        return psum[:, 512 * b:512 * b + 256].bitcast(BF16)
    def transpose_pack(srcs, evac_eng, evac_fn, extra_w=(), rev=False, idn=None,
                       idn_r=(), base=2):
        b = base + tslot[0] % 2; tslot[0] += 1
        ba = tbank(b)
        if idn is None:
            idn = identj if rev else ident
        for j, s in enumerate(srcs):
            S.op("tensor", lambda e, s=s, j=j, ba=ba, idn=idn: e.transpose(
                ba[:, 128 * j:128 * j + 128], s, idn[:, :]),
                r=[s, idn[:, :]] + list(idn_r), w=[ba[:, 128 * j:128 * j + 128]])
        S.op(evac_eng, lambda e: evac_fn(e, ba), r=[ba], w=list(extra_w))

    def ln_apply_transpose(src_tm, inv, ngm, lntmp, dst_fm_q, tcq, rev=False):
        S.op("vector", lambda e, inv=inv, ngm=ngm, src_tm=src_tm: e.tensor_scalar(
            lntmp[:, 0:1024], src_tm, inv, ngm, ALU.mult, ALU.add),
             r=[src_tm, inv, ngm], w=[lntmp[:, 0:1024]])
        dcq = (3 - tcq) if rev else tcq
        for half in range(2):
            srcs = [lntmp[:, 128 * (4 * half + j):128 * (4 * half + j) + 128] for j in range(4)]
            dsts = [dst_fm_q[:, 512 * (4 * half + j) + 128 * dcq:
                             512 * (4 * half + j) + 128 * dcq + 128] for j in range(4)]
            def ev(e, ba, dsts=dsts):
                for j in range(4):
                    e.tensor_copy(dsts[j], ba[:, 128 * j:128 * j + 128])
                return e.engine_nop()
            transpose_pack(srcs, "vector", ev, extra_w=dsts, rev=rev, base=4)

    def ln_and_transpose(src_tm, lntmp, dst_fm_q, tcq, rev=False):
        inv, ngm = ln_cols(src_tm)
        S.op("vector", lambda e: e.tensor_scalar(lntmp[:, 0:1024], src_tm, inv, ngm,
                                                 ALU.mult, ALU.add),
             r=[src_tm, inv, ngm], w=[lntmp[:, 0:1024]])
        dcq = (3 - tcq) if rev else tcq
        for half in range(2):
            srcs = [lntmp[:, 128 * (4 * half + j):128 * (4 * half + j) + 128] for j in range(4)]
            dsts = [dst_fm_q[:, 512 * (4 * half + j) + 128 * dcq:
                             512 * (4 * half + j) + 128 * dcq + 128] for j in range(4)]
            def ev(e, ba, dsts=dsts):
                for j in range(4):
                    e.tensor_copy(dsts[j], ba[:, 128 * j:128 * j + 128])
                return e.engine_nop()
            transpose_pack(srcs, "vector", ev, extra_w=dsts, rev=rev)

    esA = ExitStack()
    sbA = lambda n, shp, dt=F32: esA.enter_context(nc.sbuf_tensor(n, list(shp), dt))
    attn = sbA("attn", (128, NT * 1024), BF16)
    x1q = sbA("x1q", (128, 8 * QT), BF16)
    lntmp = sbA("lntmp", (128, 1024), BF16)
    ztm = sbA("ztm", (128, 4 * 2048), BF16)
    zscr = sbA("zscr", (128, 2 * 512), BF16)
    xbc = sbA("xbc", (128, 18 * 516), BF16)
    stash = sbA("stash", (128, 18 * 4), BF16)
    dt_fm = sbA("dt_fm", (32, QT))
    lns1 = sbA("lns1", (32, QT))
    aA = sbA("aA", (32, QT)); cumA = sbA("cumA", (32, QT))
    cumB = lns1
    losc = sbA("losc", (32, QT), FP16)
    stack66 = sbA("stack66", (66, QT), FP16)
    cumhl = sbA("cumhl", (64, QT), FP16)
    rhs66 = sbA("rhs66", (66, 4096), FP16)
    Xtm2 = sbA("Xtm2", (128, 2 * 2048), BF16)
    Btm2 = sbA("Btm2", (128, 256), BF16)
    Gp = sbA("Gp", (128, 128), BF16)
    Epg = sbA("Epg", (128, 2 * 1024), BF16)
    Mpg = sbA("Mpg", (128, 2 * 1024), BF16)
    decrow = sbA("decrow", (1, 2048), FP16)
    wcol = sbA("wcol", (128, 32), BF16)
    expc = sbA("expc", (128, 2048), BF16)
    rmsc4 = expc[:, :].bitcast(F32)[:, 0:4]
    ybuf = sbA("ybuf", (128, 2048), BF16)
    rmsc = sbA("rmsc", (128, 2))
    yfm = sbA("yfm", (128, 16 * 512), BF16)
    Sst = sbA("Sst", (128, 2048), BF16)
    Stmp = sbA("Stmp", (128, 2048), BF16)
    yps = Stmp
    drow = Stmp[0:1, :]
    yscr = Stmp[:, :].bitcast(F32)
    winr = sbA("winr", (128, 2 * 2048), BF16)
    wor = sbA("wor", (128, 2 * 1024), BF16)
    dbc = sbA("dbc", (128, 2048), BF16)
    cvacc = sbA("cvacc", (128, 1024), BF16)
    maskS = sbA("maskS", (128, 1024), BF16)
    e32pS = sbA("e32pS", (64, 2048), FP16)
    onesS = sbA("onesS", (1, 128), BF16)
    onesH = sbA("onesH", (1, 128), FP16)

    ccol = sbA("ccol", (128, 18))
    wcol4 = sbA("wcol4", (128, 4 * 18))
    nwcol = sbA("nwcol", (128, 16))
    scol = sbA("scol", (32, 4))

    S.dma("sync", ident[:, :], dr["ident"][:, :])
    S.dma("sync", identj[:, :], dr["identj"][:, :])
    S.dma("sync", maskS[:, :], dr["mask_rep"][:, :])
    S.dma("sync", rhs66[0:64, :], dr["e32x2"][:, :])
    S.dma("sync", e32pS[:, :], dr["e32p"][:, :])
    S.dma("sync", onesS[:, :], dr["ones_bf"][:, :])
    S.dma("sync", onesH[:, :], dr["ones_f16"][:, :])
    S.dma("gpsimd", xres[:, 0:NT * 1024].rearrange("p (j d) -> p j d", d=1024),
          bass.AP(dr["x"].tensor, 0, [[1024, 128], [128 * 1024, NT], [1, 1024]]))
    for d_ in ("f", "b"):
        S.dma("gpsimd", wbf[f"win_{d_}"][:], dr[f"win_{d_}"][:])
        S.dma("gpsimd", wbf[f"wout_{d_}"][:, :], dr[f"wout_{d_}"][:, :])
    S.op("vector", lambda e: e.memset(stack66[64:66, :], 1.0), w=[stack66[64:66, :]])
    S.op("vector", lambda e: e.memset(epscol[:, :], float(EPS)), w=[epscol[:, :]])
    S.op("vector", lambda e: e.memset(xbc[:, :], 0.0), w=[xbc[:, :]])

    def mamba_dir(d, rev):
        S.dma("gpsimd", drow[:, :], dr[f"drep_{d}"][:, :])
        S.dma("sync", wcol4[:, :].rearrange("p (c k) -> p c k", k=4),
              bass.AP(dr[f"convw_{d}"].tensor, 0, [[4, 128], [512, 18], [1, 4]]))
        S.dma("sync", ccol[:, :].unsqueeze(2),
              bass.AP(dr[f"convb_{d}"].tensor, 0, [[1, 128], [128, 18], [1, 1]]), slow=True)
        S.dma("sync", nwcol[:, :].unsqueeze(2),
              bass.AP(dr[f"normw_{d}"].tensor, 0, [[1, 128], [128, 16], [1, 1]]), slow=True)
        S.dma("sync", scol[0:32, 0:1], dr[f"dtb_{d}"][:, :])
        S.dma("sync", scol[0:32, 1:2], dr[f"alog_{d}"][:, :])
        S.op("vector", lambda e: e.tensor_scalar(scol[0:32, 2:3], scol[0:32, 0:1], -1.0,
                                                 None, ALU.mult),
             r=[scol[0:32, 0:1]], w=[scol[0:32, 2:3]])
        S.op("scalar", lambda e: e.activation(scol[0:32, 3:4], scol[0:32, 1:2], AF.Exp),
             r=[scol[0:32, 1:2]], w=[scol[0:32, 3:4]])
        for j in range(4):
            S.op("tensor", lambda e, j=j: e.matmul(bank(j), onesS[:, :],
                                                   drow[:, 512 * j:512 * j + 512],
                                                   start=True, stop=True),
                 r=[onesS[:, :], drow[:, 512 * j:512 * j + 512]], w=[bank(j)])
        S.op("vector", lambda e: e.tensor_copy(dbc[:, 0:1024], psum[:, 0:1024]),
             r=[psum[:, 0:1024]], w=[dbc[:, 0:1024]])
        S.op("vector", lambda e: e.tensor_copy(dbc[:, 1024:2048], psum[:, 1024:2048]),
             r=[psum[:, 1024:2048]], w=[dbc[:, 1024:2048]])
        S.op("vector", lambda e: e.memset(Sst[:, :], 0.0), w=[Sst[:, :]])
        S.op("vector", lambda e: e.memset(stash[:, :], 0.0), w=[stash[:, :]])

        x1cv = Stmp[:, :].bitcast(F32)[:, 0:8]
        def emit_x1_stats(qq, tcq):
            oqq = (NQ - 1 - qq) if rev else qq
            tc = 4 * oqq + tcq
            ln_cols(xres[:, 1024 * tc:1024 * tc + 1024],
                    cv=x1cv[:, 2 * tcq:2 * tcq + 2])
        def emit_x1_apply(qq, tcq):
            oqq = (NQ - 1 - qq) if rev else qq
            tc = 4 * oqq + tcq
            ln_apply_transpose(xres[:, 1024 * tc:1024 * tc + 1024],
                               x1cv[:, 2 * tcq:2 * tcq + 1],
                               x1cv[:, 2 * tcq + 1:2 * tcq + 2],
                               lntmp, x1q, tcq, rev=rev)
        def build_x1(qq):
            for tcq in range(4):
                ln_and_transpose(xres[:, 1024 * (4 * ((NQ - 1 - qq) if rev else qq) + tcq):
                                      1024 * (4 * ((NQ - 1 - qq) if rev else qq) + tcq) + 1024],
                                 lntmp, x1q, tcq, rev=rev)
        carry_steps = []
        build_x1(0)
        for q in range(NQ):
            oq = (NQ - 1 - q) if rev else q
            if q + 1 < NQ:
                for tcq in range(4):
                    carry_steps.append(lambda qq=q + 1, tcq=tcq: emit_x1_stats(qq, tcq))
            def win_dma(s2):
                slot = s2 % 2
                wsrc = wbf[f"win_{d}"]
                if s2 == 0:
                    src = bass.AP(wsrc.tensor, WIN_TILE_OFF[0],
                                  [[8 * 32, 128], [1, 8 * 32]])
                    S.dma("sync", winr[:, 0:256], src)
                else:
                    p0 = 2 * s2 - 1
                    src = bass.AP(wsrc.tensor, WIN_TILE_OFF[p0],
                                  [[8 * 128, 128], [WIN_TILE_OFF[p0 + 1] - WIN_TILE_OFF[p0], 2],
                                   [1, 8 * 128]])
                    S.dma("sync",
                          winr[:, 2048 * slot:2048 * slot + 2048].rearrange(
                              "p (m kw) -> p m kw", m=2), src)

            def emit_conv_tile(ct, pb):
                bk = bank(pb)
                wc = lambda k: wcol4[:, 4 * ct + k:4 * ct + k + 1]
                S.op("vector", lambda e, ct=ct, bk=bk: e.tensor_scalar(
                    cvacc[:, 3:512], bk[:, 0:509], wc(0), None, ALU.mult),
                    r=[bk, wc(0)], w=[cvacc[:, 3:512]])
                S.op("vector", lambda e, ct=ct: e.tensor_scalar(
                    cvacc[:, 0:3], stash[:, 4 * ct:4 * ct + 3], wc(0), None, ALU.mult),
                    r=[stash[:, 4 * ct:4 * ct + 3], wc(0)], w=[cvacc[:, 0:3]])
                for k in (1, 2):
                    d_, s_ = (k % 2, (k + 1) % 2)
                    hw = 3 - k
                    S.op("vector", lambda e, ct=ct, k=k, d_=d_, s_=s_, bk=bk, hw=hw:
                         e.scalar_tensor_tensor(
                             cvacc[:, 512 * d_ + hw:512 * d_ + 512], bk[:, 0:509 + k],
                             wc(k), cvacc[:, 512 * s_ + hw:512 * s_ + 512],
                             ALU.mult, ALU.add),
                         r=[bk, wc(k), cvacc[:, 512 * s_:512 * s_ + 512]],
                         w=[cvacc[:, 512 * d_:512 * d_ + 512]])
                    S.op("vector", lambda e, ct=ct, k=k, d_=d_, s_=s_, hw=hw:
                         e.scalar_tensor_tensor(
                             cvacc[:, 512 * d_:512 * d_ + hw],
                             stash[:, 4 * ct + k:4 * ct + 3],
                             wc(k), cvacc[:, 512 * s_:512 * s_ + hw],
                             ALU.mult, ALU.add),
                         r=[stash[:, 4 * ct + k:4 * ct + 3], wc(k),
                            cvacc[:, 512 * s_:512 * s_ + hw]],
                         w=[cvacc[:, 512 * d_:512 * d_ + hw]])
                S.op("vector", lambda e, ct=ct, bk=bk: e.scalar_tensor_tensor(
                    cvacc[:, 512:1024], bk[:, 0:512], wc(3), cvacc[:, 0:512],
                    ALU.mult, ALU.add),
                    r=[bk, wc(3), cvacc[:, 0:512]], w=[cvacc[:, 512:1024]])
                S.op("vector", lambda e, ct=ct, bk=bk: e.tensor_copy(
                    stash[:, 4 * ct:4 * ct + 3], bk[:, 509:512]),
                    r=[bk], w=[stash[:, 4 * ct:4 * ct + 3]])
                S.op("scalar", lambda e, ct=ct: e.activation(
                    xbc[:, 516 * ct + 3:516 * ct + 515], cvacc[:, 512:1024], AF.Silu,
                    bias=ccol[:, ct:ct + 1], scale=1.0),
                    r=[cvacc[:, 512:1024], ccol[:, ct:ct + 1]],
                    w=[xbc[:, 516 * ct + 3:516 * ct + 515]])
                S.op("vector", lambda e, ct=ct: e.tensor_copy(xbc[:, 516 * ct:516 * ct + 3],
                                                              stash[:, 4 * ct:4 * ct + 3]),
                     r=[stash[:, 4 * ct:4 * ct + 3]], w=[xbc[:, 516 * ct:516 * ct + 3]])

            def dt_steps():
                def s0():
                    S.op("scalar", lambda e: e.activation(lns1[0:32, :], dt_fm[0:32, :], AF.Sigmoid,
                                                          bias=scol[0:32, 2:3], scale=-1.0),
                         r=[dt_fm[0:32, :], scol[0:32, 2:3]], w=[lns1[0:32, :]])
                    S.op("scalar", lambda e: e.activation(lns1[0:32, :], lns1[0:32, :], AF.Ln),
                         r=[lns1[0:32, :]], w=[lns1[0:32, :]])
                    S.op("vector", lambda e: e.tensor_scalar(aA[0:32, :], lns1[0:32, :],
                                                             scol[0:32, 3:4], None, ALU.mult),
                         r=[lns1[0:32, :], scol[0:32, 3:4]], w=[aA[0:32, :]])
                    S.op("scalar", lambda e: e.activation(dt_fm[0:32, :], lns1[0:32, :], AF.Ln, scale=-1.0),
                         r=[lns1[0:32, :]], w=[dt_fm[0:32, :]])
                    S.op("vector", lambda e: e.tensor_scalar(dt_fm[0:32, :], dt_fm[0:32, :], -80.0,
                                                             None, ALU.max),
                         r=[dt_fm[0:32, :]], w=[dt_fm[0:32, :]])
                yield s0
                bufs = [aA, cumA, cumB, cumA, cumB, cumA, cumB, cumA]
                for i, dd in enumerate((1, 2, 4, 8, 16, 32, 64)):
                    def s1(srcb=bufs[i], dstb=bufs[i + 1], dd=dd):
                        def stg(e, s=srcb, t=dstb, dd=dd):
                            s3 = s[0:32, :].rearrange("p (c t) -> p c t", t=Q)
                            d3 = t[0:32, :].rearrange("p (c t) -> p c t", t=Q)
                            e.tensor_copy(d3[:, :, 0:dd], s3[:, :, 0:dd])
                            return e.tensor_tensor(d3[:, :, dd:Q], s3[:, :, dd:Q],
                                                   s3[:, :, 0:Q - dd], ALU.add)
                        S.op("vector", stg, r=[srcb[0:32, :]], w=[dstb[0:32, :]])
                    yield s1
                cum = bufs[7]
                scw0 = bass.AP(scr_cum.tensor, 0, [[128, 32], [4096, 4], [1, 128]])
                scw1 = bass.AP(scr_cum.tensor, 16384, [[128, 32], [4096, 4], [1, 128]])
                r3 = lambda t: t.rearrange("h (c u) -> h c u", u=128)
                def s2():
                    S.op("vector", lambda e: e.tensor_copy(cumhl[0:32, :], cum[0:32, :]),
                         r=[cum[0:32, :]], w=[cumhl[0:32, :]])
                    S.op("vector", lambda e: e.tensor_tensor(losc[0:32, :], cum[0:32, :],
                                                             cumhl[0:32, :], ALU.subtract),
                         r=[cum[0:32, :], cumhl[0:32, :]], w=[losc[0:32, :]])
                    S.dma("sync", scw0, r3(cumhl[0:32, :]))
                    S.dma("sync", scw1, r3(losc[0:32, :]))
                yield s2
                def s3():
                    S.dma("sync", r3(cumhl[32:64, :]), scw1)
                    S.op("vector", lambda e: e.tensor_tensor(lns1[0:32, :], dt_fm[0:32, :],
                                                             cum[0:32, :], ALU.subtract),
                         r=[dt_fm[0:32, :], cum[0:32, :]], w=[lns1[0:32, :]])
                    S.op("vector", lambda e: e.tensor_copy(stack66[0:32, :], lns1[0:32, :]),
                         r=[lns1[0:32, :]], w=[stack66[0:32, :]])
                yield s3
                def s4():
                    S.op("vector", lambda e: e.tensor_tensor(losc[0:32, :], lns1[0:32, :],
                                                             stack66[0:32, :], ALU.subtract),
                         r=[lns1[0:32, :], stack66[0:32, :]], w=[losc[0:32, :]])
                    snw = bass.AP(scr_ncl.tensor, 0, [[128, 32], [4096, 4], [1, 128]])
                    S.dma("sync", snw, r3(losc[0:32, :]))
                    S.dma("sync", r3(stack66[32:64, :]), snw)
                    if "dtq" in taps and d == "f" and q == NQ - 1:
                        tapcp("dtq", stack66[:, :])
                    if "cumhl" in taps and d == "f" and q == NQ - 1:
                        tapcp("cumhl", cumhl[:, :])
                yield s4

            win_dma(0); win_dma(1)
            dt_it = dt_steps()
            pend_conv = []
            pend_ztr = [None]
            def emit_ztr(zm):
                half = 512 * (zm % 2)
                srcs = [zscr[:, half + 128 * j:half + 128 * j + 128] for j in range(4)]
                zdst = ztm[:, :].rearrange("p (c f) -> p c f", f=2048)[:, :, 128 * zm:128 * zm + 128]
                def zev(e, ba, zdst=zdst):
                    return e.activation(zdst, ba.rearrange("p (c f) -> p c f", f=128),
                                        AF.Silu)
                transpose_pack(srcs, "scalar", zev, base=4,
                               extra_w=[ztm[:, :].rearrange("p (c f) -> p c f", f=2048)[:, :, 128 * zm:128 * zm + 128]])
            for pos in range(35):
                m = MORDER[pos]
                w = 128 if m < 34 else 32
                s2 = 0 if pos == 0 else (pos + 1) // 2
                slot = s2 % 2
                sub = 0 if pos == 0 else (pos - 1) % 2
                base = 2048 * slot + 1024 * sub
                pb = pos % 4
                for k in range(8):
                    wt = winr[:, base + w * k:base + w * k + w]
                    rhs = x1q[:, 512 * k:512 * k + 512]
                    S.op("tensor", lambda e, wt=wt, rhs=rhs, k=k, pb=pb, w=w: e.matmul(
                        bank(pb, 0, w), wt, rhs, start=(k == 0), stop=(k == 7)),
                        r=[wt, rhs], w=[bank(pb, 0, w)])
                if m < 16 and pend_ztr[0] is not None:
                    emit_ztr(pend_ztr[0]); pend_ztr[0] = None
                if m == 34:
                    S.op("vector", lambda e, pb=pb: e.tensor_copy(dt_fm[0:32, :], bank(pb, 0, 32)),
                         r=[bank(pb, 0, 32)], w=[dt_fm[0:32, :]])
                elif m >= 16:
                    pass
                else:
                    half = 512 * (m % 2)
                    S.op("vector", lambda e, pb=pb, half=half: e.tensor_copy(
                        zscr[:, half:half + 512], bank(pb)),
                        r=[bank(pb)], w=[zscr[:, half:half + 512]])
                if pend_conv:
                    emit_conv_tile(*pend_conv.pop(0))
                if m >= 16 and m < 34:
                    pend_conv.append((m - 16, pb))
                if m < 16:
                    pend_ztr[0] = m
                step = next(dt_it, None)
                if step is not None:
                    step()
                elif carry_steps:
                    carry_steps.pop(0)()
                if (pos == 0 or sub == 1) and s2 + 2 <= 17:
                    win_dma(s2 + 2)
            if pend_ztr[0] is not None:
                emit_ztr(pend_ztr[0]); pend_ztr[0] = None
            while carry_steps:
                carry_steps.pop(0)()
            if q + 1 < NQ:
                for tcq in range(4):
                    emit_x1_apply(q + 1, tcq)
                    if pend_conv:
                        emit_conv_tile(*pend_conv.pop(0))
            while pend_conv:
                emit_conv_tile(*pend_conv.pop(0))

            if "xbc" in taps and d == "f" and q == NQ - 1:
                tapcp("xbc", xbc[:, :])
            if "ztm" in taps and d == "f" and q == NQ - 1:
                tapcp("ztm", ztm[:, :])
            if "x1q" in taps and d == "f" and q == NQ - 1:
                tapcp("x1q", x1q[:, :])
            if stage < 2:
                continue

            def emit_xb_transp(cq):
                cofs = 128 * cq
                Xtm = Xtm2[:, 2048 * (cq % 2):2048 * (cq % 2) + 2048]
                Bfm = xbc[:, 516 * 16 + 3 + cofs:516 * 16 + 3 + cofs + 128]
                for pk in range(4):
                    srcs = [xbc[:, 516 * (4 * pk + j) + 3 + cofs:516 * (4 * pk + j) + 3 + cofs + 128]
                            for j in range(4)]
                    def xev(e, ba, pk=pk, Xtm=Xtm):
                        return e.activation(Xtm[:, 512 * pk:512 * pk + 512], ba, AF.Copy)
                    transpose_pack(srcs, "scalar", xev, extra_w=[Xtm[:, 512 * pk:512 * pk + 512]])
                def bev(e, ba):
                    return e.activation(Btm2[:, 128 * (cq % 2):128 * (cq % 2) + 128],
                                        ba[:, 0:128], AF.Copy)
                transpose_pack([Bfm], "scalar", bev,
                               extra_w=[Btm2[:, 128 * (cq % 2):128 * (cq % 2) + 128]])
            for cq in range(4):
                cofs = 128 * cq
                csl = slice(cofs, cofs + 128)
                Xtm = Xtm2[:, 2048 * (cq % 2):2048 * (cq % 2) + 2048]
                Btm = Btm2[:, 128 * (cq % 2):128 * (cq % 2) + 128]
                Bfm = xbc[:, 516 * 16 + 3 + cofs:516 * 16 + 3 + cofs + 128]
                Cfm = xbc[:, 516 * 17 + 3 + cofs:516 * 17 + 3 + cofs + 128]
                S.dma("sync", rhs66[64:66, :],
                      bass.AP(scr_cum.tensor, 4096 * cq, [[16384, 2], [1, 4096]]))
                if cq == 0:
                    emit_xb_transp(0)
                S.op("tensor", lambda e, Bfm=Bfm, Cfm=Cfm: e.matmul(
                    bank(6)[:, 0:128], Bfm, Cfm, start=True, stop=True),
                    r=[Bfm, Cfm], w=[bank(6)[:, 0:128]])
                S.op("vector", lambda e: e.tensor_copy(Gp[:, :], bank(6)[:, 0:128]),
                     r=[bank(6)[:, 0:128]], w=[Gp[:, :]])
                def emit_dpage(pg, bs):
                    ps_ = pg % 2
                    S.op("tensor", lambda e, bs=bs, hb=0: e.matmul(
                        bank(bs[0]), ident[:, :], maskS[:, 0:512],
                        start=True, stop=False),
                        r=[ident[:, :], maskS[:, 0:512]], w=[bank(bs[0])])
                    S.op("tensor", lambda e, bs=bs: e.matmul(
                        bank(bs[1]), ident[:, :], maskS[:, 512:1024],
                        start=True, stop=False),
                        r=[ident[:, :], maskS[:, 512:1024]], w=[bank(bs[1])])
                    st66 = stack66[:, csl]
                    for hb in range(2):
                        S.op("tensor", lambda e, bs=bs, hb=hb, pg=pg, st66=st66: e.matmul(
                            bank(bs[hb]), st66,
                            rhs66[:, 1024 * pg + 512 * hb:1024 * pg + 512 * hb + 512],
                            start=False, stop=True),
                            r=[st66, rhs66[:, 1024 * pg + 512 * hb:1024 * pg + 512 * hb + 512]],
                            w=[bank(bs[hb])])
                    S.op("scalar", lambda e, ps_=ps_, bs=bs: e.activation(
                        Epg[:, 1024 * ps_:1024 * ps_ + 1024],
                        psum[:, 512 * bs[0]:512 * bs[0] + 1024], AF.Exp),
                        r=[psum[:, 512 * bs[0]:512 * bs[0] + 1024]],
                        w=[Epg[:, 1024 * ps_:1024 * ps_ + 1024]])
                    S.op("gpsimd", lambda e, ps_=ps_: e.tensor_tensor(
                        Mpg[:, 1024 * ps_:1024 * ps_ + 1024].rearrange("p (h t) -> p h t", t=128),
                        Epg[:, 1024 * ps_:1024 * ps_ + 1024].rearrange("p (h t) -> p h t", t=128),
                        Gp[:, :].unsqueeze(1).broadcast_to([128, 8, 128]), ALU.mult),
                        r=[Epg[:, 1024 * ps_:1024 * ps_ + 1024], Gp[:, :]],
                        w=[Mpg[:, 1024 * ps_:1024 * ps_ + 1024]])
                    S.op("vector", lambda e, ps_=ps_, pg=pg: e.tensor_copy(
                        wcol[:, 8 * pg:8 * pg + 8].unsqueeze(2),
                        Epg[:, 1024 * ps_:1024 * ps_ + 1024].rearrange("p (h t) -> p h t", t=128)[:, :, 127:128]),
                        r=[Epg[:, 1024 * ps_:1024 * ps_ + 1024]], w=[wcol[:, 8 * pg:8 * pg + 8]])
                def emit_intra(pg, ib):
                    ps_ = pg % 2
                    for hh in range(8):
                        h = 8 * pg + hh
                        mp = Mpg[:, 1024 * ps_ + 128 * hh:1024 * ps_ + 128 * hh + 128]
                        S.op("tensor", lambda e, mp=mp, h=h, hh=hh, ib=ib, Xtm=Xtm: e.matmul(
                            bank(ib)[:, 64 * hh:64 * hh + 64], mp, Xtm[:, 64 * h:64 * h + 64],
                            start=True, stop=True),
                            r=[mp, Xtm[:, 64 * h:64 * h + 64]],
                            w=[bank(ib)[:, 64 * hh:64 * hh + 64]])
                chl = cumhl[:, csl]
                for hf in range(2):
                    pga, pgb = 2 * hf, 2 * hf + 1
                    emit_dpage(pga, [0, 1])
                    emit_dpage(pgb, [4, 5])
                    for j_ in range(2):
                        S.op("tensor", lambda e, j_=j_, hf=hf, chl=chl: e.matmul(
                            bank(2 + j_), chl,
                            e32pS[:, 1024 * hf + 512 * j_:1024 * hf + 512 * j_ + 512],
                            start=True, stop=True),
                            r=[chl, e32pS[:, 1024 * hf + 512 * j_:1024 * hf + 512 * j_ + 512]],
                            w=[bank(2 + j_)])
                    S.op("scalar", lambda e, hf=hf: e.activation(
                        expc[:, 1024 * hf:1024 * hf + 1024], psum[:, 1024:2048], AF.Exp),
                        r=[psum[:, 1024:2048]], w=[expc[:, 1024 * hf:1024 * hf + 1024]])
                    emit_intra(pga, 6)
                    emit_intra(pgb, 7)
                    for js, sl in enumerate((pga, pgb)):
                        S.op("tensor", lambda e, js=js, sl=sl, Cfm=Cfm: e.matmul(
                            bank(2 + js), Cfm, Sst[:, 512 * sl:512 * sl + 512],
                            start=True, stop=True),
                            r=[Cfm, Sst[:, 512 * sl:512 * sl + 512]], w=[bank(2 + js)])
                    for js, sl in enumerate((pga, pgb)):
                        ib = 6 + js
                        S.op("vector", lambda e, sl=sl, js=js: e.tensor_tensor(
                            yscr[:, 512 * js:512 * js + 512],
                            expc[:, 512 * sl:512 * sl + 512], bank(2 + js), ALU.mult),
                            r=[expc[:, 512 * sl:512 * sl + 512], bank(2 + js)],
                            w=[yscr[:, 512 * js:512 * js + 512]])
                        S.op("vector", lambda e, js=js, ib=ib, sl=sl: e.tensor_tensor(
                            yps[:, 512 * sl:512 * sl + 512], yscr[:, 512 * js:512 * js + 512],
                            bank(ib), ALU.add),
                            r=[yscr[:, 512 * js:512 * js + 512], bank(ib)],
                            w=[yps[:, 512 * sl:512 * sl + 512]])
                        S.op("gpsimd", lambda e, sl=sl, Xtm=Xtm: e.tensor_tensor(
                            ybuf[:, 512 * sl:512 * sl + 512], Xtm[:, 512 * sl:512 * sl + 512],
                            dbc[:, 512 * sl:512 * sl + 512], ALU.mult),
                            r=[Xtm[:, 512 * sl:512 * sl + 512], dbc[:, 512 * sl:512 * sl + 512]],
                            w=[ybuf[:, 512 * sl:512 * sl + 512]])
                        S.op("vector", lambda e, sl=sl: e.tensor_tensor(
                            yscr[:, 512:1024].bitcast(BF16) if False else yps[:, 512 * sl:512 * sl + 512],
                            ybuf[:, 512 * sl:512 * sl + 512],
                            yps[:, 512 * sl:512 * sl + 512], ALU.add),
                            r=[ybuf[:, 512 * sl:512 * sl + 512], yps[:, 512 * sl:512 * sl + 512]],
                            w=[yps[:, 512 * sl:512 * sl + 512]])
                        S.op("vector", lambda e, sl=sl, cq=cq: e.tensor_tensor(
                            ybuf[:, 512 * sl:512 * sl + 512], yps[:, 512 * sl:512 * sl + 512],
                            ztm[:, 2048 * cq + 512 * sl:2048 * cq + 512 * sl + 512], ALU.mult),
                            r=[yps[:, 512 * sl:512 * sl + 512],
                               ztm[:, 2048 * cq + 512 * sl:2048 * cq + 512 * sl + 512]],
                            w=[ybuf[:, 512 * sl:512 * sl + 512]])
                        dsl_ = 0 if sl == 3 else 512 * sl
                        S.op("vector", lambda e, sl=sl, dsl_=dsl_: e.scalar_tensor_tensor(
                            Mpg[:, dsl_:dsl_ + 512], ybuf[:, 512 * sl:512 * sl + 512], 1.0,
                            ybuf[:, 512 * sl:512 * sl + 512], ALU.bypass, ALU.mult,
                            accum_out=rmsc4[:, sl:sl + 1]),
                            r=[ybuf[:, 512 * sl:512 * sl + 512]],
                            w=[Mpg[:, dsl_:dsl_ + 512], rmsc4[:, sl:sl + 1]])
                if cq + 1 < 4:
                    emit_xb_transp(cq + 1)
                lcol = cumhl[:, cofs + 127:cofs + 128]
                for j in range(2):
                    S.op("tensor", lambda e, j=j, lcol=lcol: e.matmul(
                        bank(j, 0, 1), lcol, e32pS[:, 512 * j:512 * j + 512],
                        start=True, stop=True),
                        r=[lcol, e32pS[:, 512 * j:512 * j + 512]], w=[bank(j, 0, 1)])
                S.op("scalar", lambda e: e.activation(decrow[:, 0:1024], psum[0:1, 0:1024],
                                                      AF.Exp),
                     r=[psum[0:1, 0:1024]], w=[decrow[:, 0:1024]])
                for j in range(2):
                    S.op("tensor", lambda e, j=j, lcol=lcol: e.matmul(
                        bank(4 + j, 0, 1), lcol, e32pS[:, 1024 + 512 * j:1024 + 512 * j + 512],
                        start=True, stop=True),
                        r=[lcol, e32pS[:, 1024 + 512 * j:1024 + 512 * j + 512]],
                        w=[bank(4 + j, 0, 1)])
                S.op("scalar", lambda e: e.activation(decrow[:, 1024:2048], psum[0:1, 2048:3072],
                                                      AF.Exp),
                     r=[psum[0:1, 2048:3072]], w=[decrow[:, 1024:2048]])
                for sh in range(2):
                    for j in range(2):
                        js = 2 * sh + j
                        S.op("tensor", lambda e, js=js, j=j, sh=sh: e.matmul(
                            bank(4 + 2 * sh + j), onesH[:, :], decrow[:, 512 * js:512 * js + 512],
                            start=True, stop=True),
                            r=[onesH[:, :], decrow[:, 512 * js:512 * js + 512]],
                            w=[bank(4 + 2 * sh + j)])
                S.op("scalar", lambda e: e.activation(rmsc[:, 1:2], epscol[:, :], AF.Sqrt),
                     r=[epscol[:, :]], w=[rmsc[:, 1:2]])
                S.op("vector", lambda e: e.tensor_reduce(
                    rmsc[:, 0:1], rmsc4[:, 0:4].unsqueeze(1), AXX, ALU.add),
                    r=[rmsc4[:, 0:4]], w=[rmsc[:, 0:1]])
                S.op("scalar", lambda e: e.activation(rmsc[:, 1:2], rmsc[:, 0:1], AF.Sqrt,
                                                      bias=epscol[:, :], scale=1.0 / DIN),
                     r=[rmsc[:, 0:1], epscol[:, :]], w=[rmsc[:, 1:2]])
                S.op("vector", lambda e: e.reciprocal(rmsc[:, 1:2], rmsc[:, 1:2]),
                     r=[rmsc[:, 1:2]], w=[rmsc[:, 1:2]])
                idn0 = identj if rev else ident
                S.op("vector", lambda e, idn0=idn0: e.tensor_scalar(
                    Gp[:, :], idn0[:, :], rmsc[:, 1:2], None, ALU.mult),
                    r=[idn0[:, :], rmsc[:, 1:2]], w=[Gp[:, :]])
                S.op("gpsimd", lambda e, Xtm=Xtm: e.tensor_tensor(
                    Stmp[:, :].rearrange("p (h w) -> p h w", w=64),
                    Xtm[:, :].rearrange("p (h w) -> p h w", w=64),
                    wcol[:, :].unsqueeze(2).broadcast_to([128, 32, 64]), ALU.mult),
                    r=[Xtm[:, :], wcol[:, :], rmsc[:, 0:1]], w=[Stmp[:, :]])
                ycofs = 128 * (3 - cq) if rev else cofs
                for pk in range(4):
                    b_ = 2 + tslot[0] % 2; tslot[0] += 1
                    bk = bank(b_)
                    for j in range(4):
                        s = ybuf[:, 128 * (4 * pk + j):128 * (4 * pk + j) + 128]
                        S.op("tensor", lambda e, s=s, j=j, bk=bk: e.matmul(
                            bk[:, 128 * j:128 * j + 128], s, Gp[:, :],
                            start=True, stop=True),
                            r=[s, Gp[:, :]], w=[bk[:, 128 * j:128 * j + 128]])
                    def yev2(e, pk=pk, bk=bk, ycofs=ycofs):
                        d3 = yfm[:, :].rearrange("p (i t) -> p i t", t=512)[
                            :, 4 * pk:4 * pk + 4, ycofs:ycofs + 128]
                        s3 = bk.rearrange("p (j t) -> p j t", t=128)
                        nw3 = nwcol[:, 4 * pk:4 * pk + 4].unsqueeze(2).broadcast_to(
                            [128, 4, 128])
                        return e.tensor_tensor(d3, s3, nw3, ALU.mult)
                    dsts = [yfm[:, 512 * (4 * pk + j) + ycofs:512 * (4 * pk + j) + ycofs + 128]
                            for j in range(4)]
                    S.op("vector", yev2, r=[bk, nwcol[:, 4 * pk:4 * pk + 4]], w=dsts)
                for sh in range(2):
                    for j in range(2):
                        js = 2 * sh + j
                        S.op("tensor", lambda e, js=js, j=j, sh=sh, Btm=Btm: e.matmul(
                            bank(2 * sh + j), Btm[:, :], Stmp[:, 512 * js:512 * js + 512],
                            start=True, stop=True),
                            r=[Btm[:, :], Stmp[:, 512 * js:512 * js + 512]],
                            w=[bank(2 * sh + j)])
                    S.op("vector", lambda e, sh=sh: e.tensor_tensor(
                        Sst[:, 1024 * sh:1024 * sh + 1024], Sst[:, 1024 * sh:1024 * sh + 1024],
                        psum[:, 2048 + 1024 * sh:2048 + 1024 * sh + 1024], ALU.mult),
                        r=[Sst[:, 1024 * sh:1024 * sh + 1024],
                           psum[:, 2048 + 1024 * sh:2048 + 1024 * sh + 1024]],
                        w=[Sst[:, 1024 * sh:1024 * sh + 1024]])
                    S.op("vector", lambda e, sh=sh: e.tensor_tensor(
                        Sst[:, 1024 * sh:1024 * sh + 1024], Sst[:, 1024 * sh:1024 * sh + 1024],
                        psum[:, 1024 * sh:1024 * sh + 1024], ALU.add),
                        r=[Sst[:, 1024 * sh:1024 * sh + 1024],
                           psum[:, 1024 * sh:1024 * sh + 1024]],
                        w=[Sst[:, 1024 * sh:1024 * sh + 1024]])
                if "Xtm" in taps and d == "f" and q == 0 and cq == 0:
                    tapcp("Xtm", Xtm[:, :])
                if "Mpg" in taps and d == "f" and q == 0 and cq == 0:
                    tapcp("Mpg", Mpg[:, :])
                if "ybuf" in taps and d == "f" and q == 0 and cq == 0:
                    tapcp("ybuf", ybuf[:, :])
                if "Sst" in taps and d == "f" and q == 0 and cq == 0:
                    tapcp("Sst", Sst[:, :])
            if stage < 3:
                continue
            def wo_dma(k):
                S.dma("sync", wor[:, 1024 * (k % 2):1024 * (k % 2) + 1024],
                      wbf[f"wout_{d}"][128 * k:128 * k + 128, :])
            wo_dma(0); wo_dma(1)
            for k in range(16):
                slot = k % 2
                for tt in range(4):
                    lhs = yfm[:, 512 * k + 128 * tt:512 * k + 128 * tt + 128]
                    for nb in range(2):
                        S.op("tensor", lambda e, tt=tt, nb=nb, slot=slot, lhs=lhs, k=k: e.matmul(
                            bank(2 * tt + nb), lhs,
                            wor[:, 1024 * slot + 512 * nb:1024 * slot + 512 * nb + 512],
                            start=(k == 0), stop=(k == 15)),
                            r=[lhs, wor[:, 1024 * slot + 512 * nb:1024 * slot + 512 * nb + 512]],
                            w=[bank(2 * tt + nb)])
                if k + 2 < 16:
                    wo_dma(k + 2)
            for tt in range(4):
                otile = (NT - 4 * q - 4 + tt) if rev else (4 * q + tt)
                for nb in range(2):
                    dsl = attn[:, 1024 * otile + 512 * nb:1024 * otile + 512 * nb + 512]
                    if rev:
                        S.op("vector", lambda e, dsl=dsl, tt=tt, nb=nb: e.tensor_tensor(
                            dsl, dsl, bank(2 * tt + nb), ALU.add),
                            r=[dsl, bank(2 * tt + nb)], w=[dsl])
                    else:
                        S.op("vector", lambda e, dsl=dsl, tt=tt, nb=nb: e.tensor_copy(
                            dsl, bank(2 * tt + nb)),
                            r=[bank(2 * tt + nb)], w=[dsl])
            if rev and stage >= 4:
                def mk_fold(otile):
                    at = attn[:, 1024 * otile:1024 * otile + 1024]
                    xt = xres[:, 1024 * otile:1024 * otile + 1024]
                    st = {}
                    def c1():
                        st["ln"] = ln_cols(at)
                    def c2():
                        inv, ngm = st["ln"]
                        S.op("vector", lambda e, at=at, inv=inv, ngm=ngm: e.tensor_scalar(
                            lntmp[:, 0:1024], at, inv, ngm, ALU.mult, ALU.add),
                            r=[at, inv, ngm], w=[lntmp[:, 0:1024]])
                        S.op("vector", lambda e, xt=xt: e.tensor_tensor(
                            xt, xt, lntmp[:, 0:1024], ALU.add),
                            r=[xt, lntmp[:, 0:1024]], w=[xt])
                    return [c1, c2]
                for tt in range(4):
                    carry_steps.extend(mk_fold(NT - 4 * q - 4 + tt))
                if q == NQ - 1:
                    while carry_steps:
                        carry_steps.pop(0)()
        if "yfm" in taps and d == "f":
            tapcp("yfm", yfm[:, :])

    mamba_dir("f", False)
    tapcp("attn_f", attn[:, :])
    S.barrier()
    if "b" in dirs:
        mamba_dir("b", True)
        tapcp("attn", attn[:, :])
        S.barrier()

    if stage >= 4:
        tapcp("xmid", xres[:, :])
    S.barrier()

    esA.close()
    if stage >= 5:
        esB = ExitStack()
        sbB = lambda n, shp, dt=F32: esB.enter_context(nc.sbuf_tensor(n, list(shp), dt))
        x2q = sbB("x2q", (128, 8 * QT), BF16)
        lntmp2 = sbB("lntmp2", (128, 1024), BF16)
        wproj = sbB("wproj", (128, 8 * 16), BF16)
        wc1 = sbB("wc1", (16, 4 * 128), BF16)
        wc2 = sbB("wc2", (16, 4 * 128), BF16)
        wwin1 = sbB("wwin1", (128, 8 * 512), BF16)
        wb = sbB("wb", (128, 8))
        wwout1 = sbB("wwout1", (128, 4 * 4096), BF16)
        wwin2 = sbB("wwin2", (128, 32 * 512), BF16)
        wwout2 = sbB("wwout2", (128, 4 * 1024), BF16)
        acat = sbB("acat", (128, 64)); ascr = sbB("ascr", (128, 64))
        asrt = sbB("asrt", (128, 64)); aorig = sbB("aorig", (128, 64))
        zs_keep = sbB("zs_keep", (128, 64))
        astat = sbB("astat", (128, 24))
        invk16 = sbB("invk16", (128, 16))
        pos16 = sbB("pos16", (128, 16))
        afm = sbB("afm", (16, QT), BF16)
        g1b = sbB("g1b", (128, 4 * 512)); g2b = sbB("g2b", (128, 4 * 512))
        z1g = sbB("z1g", (128, 4 * 512), BF16); z2g = sbB("z2g", (128, 4 * 512), BF16)
        hfm = sbB("hfm", (128, 32 * 512), BF16)
        fout = sbB("fout", (128, 1024))

        r2 = lambda t, n: t.rearrange("p (a b) -> p a b", a=n)
        S.dma("gpsimd", r2(wproj[:, :], 8),
              bass.AP(dr["fproj"].tensor, 0, [[16, 128], [2048, 8], [1, 16]]))
        S.dma("gpsimd", r2(wc1[0:16, :], 4),
              bass.AP(dr["fc1"].tensor, 0, [[128, 16], [2048, 4], [1, 128]]))
        S.dma("gpsimd", r2(wc2[0:16, :], 4),
              bass.AP(dr["fc2"].tensor, 0, [[128, 16], [2048, 4], [1, 128]]))
        S.dma("gpsimd", r2(wwin1[:, :], 8),
              bass.AP(dr["fwin1"].tensor, 0, [[512, 128], [65536, 8], [1, 512]]))
        S.dma("sync", wb[:, 0:4].unsqueeze(2),
              bass.AP(dr["fb1"].tensor, 0, [[1, 128], [128, 4], [1, 1]]), slow=True)
        S.dma("gpsimd", r2(wwout1[:, :], 4),
              bass.AP(dr["fwout1t"].tensor, 0, [[4096, 128], [524288, 4], [1, 4096]]))
        S.dma("gpsimd", r2(wwin2[:, :], 32),
              bass.AP(dr["fwin2"].tensor, 0, [[512, 128], [65536, 32], [1, 512]]))
        S.dma("sync", wb[:, 4:8].unsqueeze(2),
              bass.AP(dr["fb2"].tensor, 0, [[1, 128], [128, 4], [1, 1]]), slow=True)
        S.dma("gpsimd", r2(wwout2[:, :], 4),
              bass.AP(dr["fwout2t"].tensor, 0, [[1024, 128], [131072, 4], [1, 1024]]))

        def bld_invk(e):
            for i in range(16):
                e.memset(invk16[:, i:i + 1], 1.0 / (i + 1))
                e.memset(pos16[:, i:i + 1], float(i + 1))
            return e.engine_nop()
        S.op("vector", bld_invk, w=[invk16[:, :], pos16[:, :]])

        a3 = lambda t: t.rearrange("p (c e) -> p c e", e=16)
        bc4 = lambda col: col.unsqueeze(2).broadcast_to([128, 4, 16])
        bck = lambda t: t.unsqueeze(1).broadcast_to([128, 4, 16])

        def emit_entmax():
            ra = lambda t: [t[:, :]]
            S.op("vector", lambda e: e.tensor_reduce(astat[:, 0:4], a3(acat[:, :]), AXX, ALU.add),
                 r=[acat[:, :]], w=[astat[:, 0:4]])
            S.op("vector", lambda e: e.tensor_scalar(astat[:, 0:4], astat[:, 0:4], 1.0 / 16,
                                                     None, ALU.mult),
                 r=[astat[:, 0:4]], w=[astat[:, 0:4]])
            S.op("vector", lambda e: e.tensor_tensor(a3(aorig[:, :]), a3(acat[:, :]),
                                                     bc4(astat[:, 0:4]), ALU.subtract),
                 r=[acat[:, :], astat[:, 0:4]], w=[aorig[:, :]])
            S.op("vector", lambda e: e.tensor_tensor(a3(ascr[:, :]), a3(aorig[:, :]),
                                                     a3(aorig[:, :]), ALU.mult),
                 r=[aorig[:, :]], w=[ascr[:, :]])
            S.op("vector", lambda e: e.tensor_reduce(astat[:, 4:8], a3(ascr[:, :]), AXX, ALU.add),
                 r=[ascr[:, :]], w=[astat[:, 4:8]])
            S.op("scalar", lambda e: e.activation(astat[:, 8:12], astat[:, 4:8], AF.Sqrt,
                                                  bias=epscol[:, :], scale=1.0 / 16),
                 r=[astat[:, 4:8], epscol[:, :]], w=[astat[:, 8:12]])
            S.op("vector", lambda e: e.reciprocal(astat[:, 8:12], astat[:, 8:12]),
                 r=[astat[:, 8:12]], w=[astat[:, 8:12]])
            S.op("vector", lambda e: e.tensor_tensor(a3(aorig[:, :]), a3(aorig[:, :]),
                                                     bc4(astat[:, 8:12]), ALU.mult),
                 r=[aorig[:, :], astat[:, 8:12]], w=[aorig[:, :]])
            S.op("vector", lambda e: e.tensor_scalar(aorig[:, :], aorig[:, :], 0.5, None, ALU.mult),
                 r=[aorig[:, :]], w=[aorig[:, :]])
            S.op("vector", lambda e: e.tensor_reduce(astat[:, 12:16], a3(aorig[:, :]), AXX, ALU.max),
                 r=[aorig[:, :]], w=[astat[:, 12:16]])
            S.op("vector", lambda e: e.tensor_tensor(a3(aorig[:, :]), a3(aorig[:, :]),
                                                     bc4(astat[:, 12:16]), ALU.subtract),
                 r=[aorig[:, :], astat[:, 12:16]], w=[aorig[:, :]])
            S.op("vector", lambda e: e.tensor_copy(asrt[:, :], aorig[:, :]),
                 r=[aorig[:, :]], w=[asrt[:, :]])
            def sap(buf, off, dims):
                pairs = [[64, 128]] + [[s, c] for (s, c) in dims if c > 1]
                return bass.AP(buf.tensor if hasattr(buf, "tensor") else buf, off, pairs)
            stages = []
            kk = 2
            while kk <= 16:
                jj = kk // 2
                while jj >= 1:
                    stages.append((kk, jj)); jj //= 2
                kk *= 2
            for (kk, jj) in stages:
                def mk(kk=kk, jj=jj):
                    def f(e):
                        u = 16 // kk
                        g = max(1, kk // (2 * jj))
                        last = None
                        if u == 1:
                            dims = [(16, 4), (2 * jj, g), (1, jj)]
                            lo = sap(asrt[:, :], 0, dims); hi = sap(asrt[:, :], jj, dims)
                            e.tensor_tensor(sap(ascr[:, :], 0, dims), lo, hi, ALU.max)
                            last = e.tensor_tensor(sap(ascr[:, :], jj, dims), lo, hi, ALU.min)
                        else:
                            for half in range(2):
                                o0 = kk * half
                                dims = [(16, 4), (2 * kk, u // 2), (2 * jj, g), (1, jj)]
                                lo = sap(asrt[:, :], o0, dims)
                                hi = sap(asrt[:, :], o0 + jj, dims)
                                omax = sap(ascr[:, :], o0 if half == 0 else o0 + jj, dims)
                                omin = sap(ascr[:, :], o0 + jj if half == 0 else o0, dims)
                                e.tensor_tensor(omax, lo, hi, ALU.max)
                                last = e.tensor_tensor(omin, lo, hi, ALU.min)
                        return last
                    return f
                S.op("vector", mk(), r=[asrt[:, :]], w=[ascr[:, :]])
                S.op("vector", lambda e: e.tensor_copy(asrt[:, :], ascr[:, :]),
                     r=[ascr[:, :]], w=[asrt[:, :]])
            S.op("vector", lambda e: e.tensor_copy(zs_keep[:, :], asrt[:, :]),
                 r=[asrt[:, :]], w=[zs_keep[:, :]])
            S.op("vector", lambda e: e.tensor_copy(ascr[:, :], asrt[:, :]),
                 r=[asrt[:, :]], w=[ascr[:, :]])
            S.op("vector", lambda e: e.tensor_tensor(acat[:, :], asrt[:, :], asrt[:, :], ALU.mult),
                 r=[asrt[:, :]], w=[acat[:, :]])
            for buf in (ascr, acat):
                for ddd in (1, 2, 4, 8):
                    def f2(e, buf=buf, ddd=ddd):
                        t = a3(buf[:, :]); s = a3(asrt[:, :])
                        e.tensor_copy(s[:, :, 0:ddd], t[:, :, 0:ddd])
                        return e.tensor_tensor(s[:, :, ddd:16], t[:, :, ddd:16],
                                               t[:, :, 0:16 - ddd], ALU.add)
                    S.op("vector", f2, r=[buf[:, :]], w=[asrt[:, :]])
                    S.op("vector", lambda e, buf=buf: e.tensor_copy(buf[:, :], asrt[:, :]),
                         r=[asrt[:, :]], w=[buf[:, :]])
            S.op("vector", lambda e: e.tensor_tensor(a3(ascr[:, :]), a3(ascr[:, :]),
                                                     bck(invk16[:, :]), ALU.mult),
                 r=[ascr[:, :], invk16[:, :]], w=[ascr[:, :]])
            S.op("vector", lambda e: e.tensor_tensor(a3(acat[:, :]), a3(acat[:, :]),
                                                     bck(invk16[:, :]), ALU.mult),
                 r=[acat[:, :], invk16[:, :]], w=[acat[:, :]])
            S.op("vector", lambda e: e.tensor_tensor(a3(asrt[:, :]), a3(ascr[:, :]),
                                                     a3(ascr[:, :]), ALU.mult),
                 r=[ascr[:, :]], w=[asrt[:, :]])
            S.op("vector", lambda e: e.tensor_tensor(a3(acat[:, :]), a3(acat[:, :]),
                                                     a3(asrt[:, :]), ALU.subtract),
                 r=[acat[:, :], asrt[:, :]], w=[acat[:, :]])
            S.op("vector", lambda e: e.tensor_tensor(a3(acat[:, :]), bck(invk16[:, :]),
                                                     a3(acat[:, :]), ALU.subtract),
                 r=[acat[:, :], invk16[:, :]], w=[acat[:, :]])
            S.op("vector", lambda e: e.tensor_scalar(acat[:, :], acat[:, :], 0.0, None, ALU.max),
                 r=[acat[:, :]], w=[acat[:, :]])
            S.op("scalar", lambda e: e.activation(acat[:, :], acat[:, :], AF.Sqrt),
                 r=[acat[:, :]], w=[acat[:, :]])
            S.op("vector", lambda e: e.tensor_tensor(ascr[:, :], ascr[:, :], acat[:, :],
                                                     ALU.subtract),
                 r=[ascr[:, :], acat[:, :]], w=[ascr[:, :]])
            S.op("vector", lambda e: e.tensor_tensor(acat[:, :], ascr[:, :], zs_keep[:, :],
                                                     ALU.is_le),
                 r=[ascr[:, :], zs_keep[:, :]], w=[acat[:, :]])
            S.op("vector", lambda e: e.tensor_reduce(astat[:, 16:20], a3(acat[:, :]), AXX, ALU.add),
                 r=[acat[:, :]], w=[astat[:, 16:20]])
            S.op("vector", lambda e: e.tensor_tensor(a3(acat[:, :]), bck(pos16[:, :]),
                                                     bc4(astat[:, 16:20]), ALU.is_le),
                 r=[pos16[:, :], astat[:, 16:20]], w=[acat[:, :]])
            S.op("vector", lambda e: e.tensor_copy(asrt[:, :], acat[:, :]),
                 r=[acat[:, :]], w=[asrt[:, :]])
            S.op("vector", lambda e: e.tensor_tensor(a3(acat[:, :])[:, :, 0:15],
                                                     a3(asrt[:, :])[:, :, 0:15],
                                                     a3(asrt[:, :])[:, :, 1:16], ALU.subtract),
                 r=[asrt[:, :]], w=[acat[:, :]])
            S.op("vector", lambda e: e.tensor_tensor(ascr[:, :], ascr[:, :], acat[:, :], ALU.mult),
                 r=[ascr[:, :], acat[:, :]], w=[ascr[:, :]])
            S.op("vector", lambda e: e.tensor_reduce(astat[:, 20:24], a3(ascr[:, :]), AXX, ALU.add),
                 r=[ascr[:, :]], w=[astat[:, 20:24]])
            S.op("vector", lambda e: e.tensor_tensor(a3(acat[:, :]), a3(aorig[:, :]),
                                                     bc4(astat[:, 20:24]), ALU.subtract),
                 r=[aorig[:, :], astat[:, 20:24]], w=[acat[:, :]])
            S.op("vector", lambda e: e.tensor_scalar(acat[:, :], acat[:, :], 0.0, None, ALU.max),
                 r=[acat[:, :]], w=[acat[:, :]])
            S.op("vector", lambda e: e.tensor_tensor(acat[:, :], acat[:, :], acat[:, :], ALU.mult),
                 r=[acat[:, :]], w=[acat[:, :]])

        def ffn_prelude_a(q):
            for tcq in range(4):
                tc = 4 * q + tcq
                ln_and_transpose(xres[:, 1024 * tc:1024 * tc + 1024], lntmp2, x2q, tcq)
            for tcq in range(4):
                for k in range(8):
                    xa = x2q[:, 512 * k + 128 * tcq:512 * k + 128 * tcq + 128]
                    S.op("tensor", lambda e, xa=xa, k=k, tcq=tcq: e.matmul(
                        bank(6, 0, 128, 64)[:, 16 * tcq:16 * tcq + 16], xa,
                        wproj[:, 16 * k:16 * k + 16], start=(k == 0), stop=(k == 7)),
                        r=[xa, wproj[:, 16 * k:16 * k + 16]],
                        w=[bank(6, 0, 128, 64)[:, 16 * tcq:16 * tcq + 16]])
            S.op("vector", lambda e: e.tensor_copy(acat[:, :], bank(6, 0, 128, 64)),
                 r=[bank(6, 0, 128, 64)], w=[acat[:, :]])
        def ffn_prelude_b():
            S.op("vector", lambda e: e.tensor_copy(lntmp2[:, 0:64], acat[:, :]),
                 r=[acat[:, :]], w=[lntmp2[:, 0:64]])
            b = 2 + tslot[0] % 2; tslot[0] += 1
            tb = tbank(b)
            for tcq in range(4):
                s_ = lntmp2[:, 16 * tcq:16 * tcq + 16]
                S.op("tensor", lambda e, s_=s_, tcq=tcq, tb=tb: e.transpose(
                    tb[0:16, 128 * tcq:128 * tcq + 128], s_, ident[:, :]),
                    r=[s_, ident[:, :]], w=[tb[:, 128 * tcq:128 * tcq + 128]])
            S.op("vector", lambda e, tb=tb: e.tensor_copy(afm[0:16, :], tb[0:16, :]),
                 r=[tb], w=[afm[0:16, :]])
        def emit_g(wc, gb, cg):
            for r_ in range(4):
                wca = wc[0:16, 128 * r_:128 * r_ + 128]
                S.op("tensor", lambda e, wca=wca, cg=cg: e.matmul(
                    bank(cg), wca, afm[0:16, :], start=True, stop=True),
                    r=[wca, afm[0:16, :]], w=[bank(cg)])
                S.op("vector", lambda e, gb=gb, r_=r_, cg=cg: e.tensor_copy(
                    gb[:, 512 * r_:512 * r_ + 512], bank(cg)),
                    r=[bank(cg)], w=[gb[:, 512 * r_:512 * r_ + 512]])
        z1r = hfm[:, 0:2048]
        ffn_prelude_a(0)
        emit_entmax()
        for q in range(NQ):
            if q == 0:
                for r_ in range(4):
                    for k in range(8):
                        wa = wwin1[:, 512 * k + 128 * r_:512 * k + 128 * r_ + 128]
                        S.op("tensor", lambda e, wa=wa, r_=r_, k=k: e.matmul(
                            bank(r_), wa, x2q[:, 512 * k:512 * k + 512],
                            start=(k == 0), stop=(k == 7)),
                            r=[wa, x2q[:, 512 * k:512 * k + 512]], w=[bank(r_)])
                    S.op("vector", lambda e, r_=r_: e.tensor_scalar(
                        z1r[:, 512 * r_:512 * r_ + 512], bank(r_), wb[:, r_:r_ + 1],
                        None, ALU.add),
                        r=[bank(r_), wb[:, r_:r_ + 1]],
                        w=[z1r[:, 512 * r_:512 * r_ + 512]])
                ffn_prelude_b()
                emit_g(wc1, g1b, 6)
                emit_g(wc2, g2b, 7)
                for r_ in range(4):
                    S.op("vector", lambda e, r_=r_: e.tensor_tensor(
                        z1g[:, 512 * r_:512 * r_ + 512], z1r[:, 512 * r_:512 * r_ + 512],
                        g1b[:, 512 * r_:512 * r_ + 512], ALU.mult),
                        r=[z1r[:, 512 * r_:512 * r_ + 512], g1b[:, 512 * r_:512 * r_ + 512]],
                        w=[z1g[:, 512 * r_:512 * r_ + 512]])
            else:
                emit_g(wc1, g1b, 6)
                emit_g(wc2, g2b, 7)
                for r_ in range(4):
                    for k in range(8):
                        wa = wwin1[:, 512 * k + 128 * r_:512 * k + 128 * r_ + 128]
                        S.op("tensor", lambda e, wa=wa, r_=r_, k=k: e.matmul(
                            bank(r_), wa, x2q[:, 512 * k:512 * k + 512],
                            start=(k == 0), stop=(k == 7)),
                            r=[wa, x2q[:, 512 * k:512 * k + 512]], w=[bank(r_)])
                    S.op("vector", lambda e, r_=r_: e.scalar_tensor_tensor(
                        z1g[:, 512 * r_:512 * r_ + 512], bank(r_), wb[:, r_:r_ + 1],
                        g1b[:, 512 * r_:512 * r_ + 512], ALU.add, ALU.mult),
                        r=[bank(r_), wb[:, r_:r_ + 1], g1b[:, 512 * r_:512 * r_ + 512]],
                        w=[z1g[:, 512 * r_:512 * r_ + 512]])
            if q + 1 < NQ:
                ffn_prelude_a(q + 1)
            for o in range(32):
                pb = 4 + o % 2
                for r_ in range(4):
                    wa = wwout1[:, 4096 * r_ + 128 * o:4096 * r_ + 128 * o + 128]
                    S.op("tensor", lambda e, wa=wa, r_=r_, pb=pb: e.matmul(
                        bank(pb), wa, z1g[:, 512 * r_:512 * r_ + 512],
                        start=(r_ == 0), stop=(r_ == 3)),
                        r=[wa, z1g[:, 512 * r_:512 * r_ + 512]], w=[bank(pb)])
                S.op("scalar", lambda e, o=o, pb=pb: e.activation(
                    hfm[:, 512 * o:512 * o + 512], bank(pb), AF.Gelu),
                    r=[bank(pb)], w=[hfm[:, 512 * o:512 * o + 512]])
            if "hfm" in taps and q == 0:
                tapcp("hfm", hfm[:, :])
            if q + 1 < NQ:
                emit_entmax()
            for r_ in range(4):
                for k in range(32):
                    wa = wwin2[:, 512 * k + 128 * r_:512 * k + 128 * r_ + 128]
                    S.op("tensor", lambda e, wa=wa, r_=r_, k=k: e.matmul(
                        bank(r_), wa, hfm[:, 512 * k:512 * k + 512],
                        start=(k == 0), stop=(k == 31)),
                        r=[wa, hfm[:, 512 * k:512 * k + 512]], w=[bank(r_)])
                S.op("vector", lambda e, r_=r_: e.scalar_tensor_tensor(
                    z2g[:, 512 * r_:512 * r_ + 512], bank(r_), wb[:, 4 + r_:5 + r_],
                    g2b[:, 512 * r_:512 * r_ + 512], ALU.add, ALU.mult),
                    r=[bank(r_), wb[:, 4 + r_:5 + r_], g2b[:, 512 * r_:512 * r_ + 512]],
                    w=[z2g[:, 512 * r_:512 * r_ + 512]])
            if q + 1 < NQ:
                ffn_prelude_b()
            for tt in range(4):
                pb = 4 + 2 * (tt % 2)
                for r_ in range(4):
                    za = z2g[:, 512 * r_ + 128 * tt:512 * r_ + 128 * tt + 128]
                    for nb in range(2):
                        S.op("tensor", lambda e, za=za, r_=r_, nb=nb, pb=pb: e.matmul(
                            bank(pb + nb), za,
                            wwout2[:, 1024 * r_ + 512 * nb:1024 * r_ + 512 * nb + 512],
                            start=(r_ == 0), stop=(r_ == 3)),
                            r=[za, wwout2[:, 1024 * r_ + 512 * nb:1024 * r_ + 512 * nb + 512]],
                            w=[bank(pb + nb)])
                tc = 4 * q + tt
                pspan = psum[:, 512 * pb:512 * pb + 1024]
                inv, ngm = ln_cols(pspan)
                S.op("vector", lambda e, pspan=pspan, inv=inv, ngm=ngm: e.tensor_scalar(
                    fout[:, :], pspan, inv, ngm, ALU.mult, ALU.add),
                     r=[pspan, inv, ngm], w=[fout[:, :]])
                xt = xres[:, 1024 * tc:1024 * tc + 1024]
                S.op("vector", lambda e, xt=xt: e.tensor_tensor(fout[:, :], fout[:, :], xt,
                                                                ALU.add),
                     r=[fout[:, :], xt], w=[fout[:, :]])
                S.dma("sync", out[128 * tc:128 * tc + 128, :], fout[:, :])
    S.finalize()
    alld = [(c, c.n) for name, c in S.chans.items() if name.startswith("dq_") and c.n > 0]
    tail = Op("sync", None, [], [], [], None)
    tail.waits = list(alld)
    S.ops.append(tail)
    tail2 = Op("tensor", None, [], [], [], None)
    tail2.waits = list(alld)
    S.ops.append(tail2)
    S.check()
    with nc.Block() as block:
        S.emit(nc, block, es)
    if stage >= 5:
        esB.close()
    es.close()
    return nc, tapd


def kernel(**inputs):
    x = np.asarray(inputs["L0_attn_hidden_states"])
    B, T, _ = x.shape
    nc, _ = build_program(T=T)
    in_maps = [host_inputs(inputs, b, T) for b in range(B)]
    res = run_bass_kernel_spmd(nc, in_maps, core_ids=list(range(8)))
    return np.stack([r["out"] for r in res.results]).astype(np.float32)


# version 33
# speedup vs baseline: 1.0517x; 1.0517x over previous
import numpy as np
import ml_dtypes
from contextlib import ExitStack

import concourse.bass as bass
import concourse.mybir as mybir
from concourse.bass_utils import run_bass_kernel_spmd

F32 = mybir.dt.float32
BF16 = mybir.dt.bfloat16
FP16 = mybir.dt.float16
AF = mybir.ActivationFunctionType
ALU = mybir.AluOpType
AXX = mybir.AxisListType.X
BF = ml_dtypes.bfloat16

D = 1024; DIN = 2048; NH = 32; PD = 64
Q = 128; QT = 512
EPS = 1e-5; NEG = -1e30

ENGINES = ("sync", "scalar", "vector", "tensor", "gpsimd")

def _esz(dt):
    try:
        return dt.itemsize
    except Exception:
        return {F32: 4, BF16: 2, FP16: 2}.get(dt, 4)

def ap_range(ap):
    th = ap.tensor
    t = th.name
    esz = _esz(ap.dtype)
    pairs = list(ap.ap)
    onchip = "DRAM" not in str(getattr(ap, "space", "")).upper()
    lo = hi = ap.offset * esz
    dims = pairs[1:] if onchip else pairs
    for step, count in dims:
        if count <= 1:
            continue
        d = step * (count - 1) * esz
        if d >= 0:
            hi += d
        else:
            lo += d
    if onchip:
        pitch = int(th.shape[-1]) * _esz(th.dtype)
        base = (lo // pitch) * pitch
        lo -= base
        hi -= base
        if hi >= pitch:
            lo, hi = 0, pitch
    return (t, lo, hi + esz)

class Op:
    __slots__ = ("eng", "fn", "reads", "writes", "xwaits", "dmach", "dmaval",
                 "mark", "waits", "idx", "barrier_inc")
    def __init__(self, eng, fn, reads, writes, xwaits, dmach):
        self.eng = eng; self.fn = fn
        self.reads = reads; self.writes = writes
        self.xwaits = list(xwaits)
        self.dmach = dmach
        self.dmaval = 0
        self.mark = None
        self.waits = []
        self.idx = 0
        self.barrier_inc = None

class Ch:
    __slots__ = ("name", "n", "sem")
    def __init__(self, name):
        self.name = name; self.n = 0; self.sem = None

class Sched:
    def __init__(self):
        self.ops = []
        self.chans = {}
        self.mark_ch = {e: self.chan("mk_" + e) for e in ENGINES}

    def chan(self, name):
        if name not in self.chans:
            self.chans[name] = Ch(name)
        return self.chans[name]

    DMA_POOL = 10

    def op(self, eng, fn, r=(), w=(), waits=(), dma=False, tag=None):
        reads = [ap_range(a) for a in r if a is not None]
        writes = [ap_range(a) for a in w if a is not None]
        ch = None
        o = Op(eng, fn, reads, writes, waits, ch)
        if dma:
            if not hasattr(self, "dmacnt"):
                self.dmacnt = {}
            i = self.dmacnt.get(eng, 0)
            self.dmacnt[eng] = i + 1
            ch = self.chan(f"dq_{eng}_{i % self.DMA_POOL}")
            if ch.n > 0:
                o.xwaits.append((ch, ch.n))
            ch.n += 16
            o.dmach = ch
            o.dmaval = ch.n
        self.ops.append(o)
        return o

    def dma(self, eng, dst, src, waits=(), tag=None, slow=False):
        if slow:
            fn = lambda e, d=dst, s=src: e.dma_start(out=d, in_=s,
                                                     allow_slow_non_contiguous=True)
        else:
            fn = lambda e, d=dst, s=src: e.dma_start(out=d, in_=s)
        return self.op(eng, fn, r=[src], w=[dst], waits=waits, dma=True, tag=tag)

    def barrier(self):
        bc = {e: self.chan("bar_" + e) for e in ENGINES}
        snap = {}
        for e in ENGINES:
            fn = lambda eng: eng.nop()
            o = self.op(e, fn)
            o.barrier_inc = bc[e]
            bc[e].n += 1
            snap[e] = bc[e].n
            o.xwaits = list(o.xwaits)
        for e in ENGINES:
            o = self.op(e, None,
                        waits=[(bc[e2], snap[e2]) for e2 in ENGINES if e2 != e])
        self._bar_pending = getattr(self, "_bar_pending", [])

    def finalize(self):
        writer = {}
        readers = {}
        for o in self.ops:
            deps = set()
            for t, lo, hi in o.reads:
                for (wl, wh), wo in writer.get(t, ()):
                    if lo < wh and hi > wl:
                        deps.add(wo)
            for t, lo, hi in o.writes:
                for (wl, wh), wo in writer.get(t, ()):
                    if lo < wh and hi > wl:
                        deps.add(wo)
                for (rl, rh), ro in readers.get(t, ()):
                    if lo < rh and hi > rl:
                        deps.add(ro)
            for dep in deps:
                if dep is o:
                    continue
                if dep.dmach is not None:
                    o.waits.append((dep.dmach, dep.dmaval))
                elif dep.eng != o.eng:
                    if dep.mark is None:
                        dep.mark = True
                    o.waits.append(("MARK", dep))
                else:
                    if o.eng == "tensor":
                        continue
                    spans = [hi - lo for (_, lo, hi) in dep.writes] or [0]
                    if max(spans) < 2048:
                        if dep.mark is None:
                            dep.mark = True
                        o.waits.append(("MARK", dep))
            for t, lo, hi in o.writes:
                wl_ = [(rg, wo) for (rg, wo) in writer.get(t, ())
                       if not (lo <= rg[0] and hi >= rg[1])]
                wl_.append(((lo, hi), o))
                writer[t] = wl_
                rl_ = [(rg, ro) for (rg, ro) in readers.get(t, ())
                       if not (lo <= rg[0] and hi >= rg[1])]
                readers[t] = rl_
            for t, lo, hi in o.reads:
                lst = readers.setdefault(t, [])
                key = (o.eng, lo, hi)
                for i_, (rg, ro) in enumerate(lst):
                    if (ro.eng, rg[0], rg[1]) == key:
                        lst[i_] = ((lo, hi), o)
                        break
                else:
                    lst.append(((lo, hi), o))
        cnt = {e: 0 for e in ENGINES}
        for o in self.ops:
            if getattr(o, "barrier_inc", None) is not None:
                pass
            if o.mark:
                cnt[o.eng] += 1
                o.mark = cnt[o.eng]
        for o in self.ops:
            ws = {}
            for item in o.waits:
                if isinstance(item, tuple) and item[0] == "MARK":
                    dep = item[1]
                    c = self.mark_ch[dep.eng]
                    v = dep.mark
                else:
                    c, v = item
                if v and v > 0:
                    ws[c] = max(ws.get(c, 0), v)
            for c, v in o.xwaits:
                if v and v > 0:
                    ws[c] = max(ws.get(c, 0), v)
            o.waits = list(ws.items())
        self.mark_totals = cnt

    def check(self):
        vals = {c: 0 for c in self.chans.values()}
        qs = {}
        for i, o in enumerate(self.ops):
            o.idx = i
            qs.setdefault(o.eng, []).append(o)
        heads = {e: 0 for e in qs}
        done, prog = 0, True
        while prog:
            prog = False
            for e, lst in qs.items():
                while heads[e] < len(lst):
                    o = lst[heads[e]]
                    if any(vals[c] < v for c, v in o.waits):
                        break
                    if o.dmach is not None:
                        vals[o.dmach] = max(vals[o.dmach], o.dmaval)
                    if o.mark:
                        vals[self.mark_ch[o.eng]] = max(vals[self.mark_ch[o.eng]], o.mark)
                    bi = getattr(o, "barrier_inc", None)
                    if bi is not None:
                        vals[bi] = vals[bi] + 1
                    heads[e] += 1; done += 1; prog = True
        if done != len(self.ops):
            stuck = []
            for e, lst in qs.items():
                if heads[e] < len(lst):
                    o = lst[heads[e]]
                    bad = [(c.name, v, vals[c]) for c, v in o.waits if vals[c] < v]
                    stuck.append((e, o.idx, bad[:3]))
            raise RuntimeError(f"deadlock, {len(self.ops) - done} stuck: {stuck[:5]}")

    def emit(self, nc, block, es):
        for c in self.chans.values():
            c.sem = es.enter_context(nc.semaphore(c.name))
        for ename in ENGINES:
            ops = [o for o in self.ops if o.eng == ename]
            def body(eng, ops=ops):
                for o in ops:
                    for c, v in o.waits:
                        eng.wait_ge(c.sem, v)
                    if o.fn is None:
                        continue
                    ins = o.fn(eng)
                    if o.dmach is not None:
                        ins.then_inc(o.dmach.sem, 16)
                    if o.mark:
                        ins.then_inc(self.mark_ch[o.eng].sem, 1)
                    bi = getattr(o, "barrier_inc", None)
                    if bi is not None:
                        ins.then_inc(bi.sem, 1)
            getattr(block, ename)(body)

MORDER = [34, 32, 0, 33, 1, 16, 17]
for _i in range(14):
    MORDER += [18 + _i, 2 + _i]
WIN_TILE_OFF = []
_off = 0
for _m in MORDER:
    WIN_TILE_OFF.append(_off)
    _off += 8 * 128 * (128 if _m < 34 else 32)
WIN_TOTAL = _off

def _tile_win(W):
    blobs = []
    for m in MORDER:
        w = 128 if m < 34 else 32
        t = W[:, m * 128:m * 128 + w].reshape(8, 128, w).transpose(1, 0, 2)
        blobs.append(np.ascontiguousarray(t).reshape(-1))
    return np.concatenate(blobs)

def _consts():
    ident = np.eye(128, dtype=np.float32).astype(BF)
    id32x4 = np.tile(np.eye(32, dtype=np.float32), (4, 1)).astype(BF)
    m = np.where(np.arange(Q)[:, None] > np.arange(Q)[None, :],
                 np.float32(NEG), np.float32(0.0))
    mask_rep = np.tile(m, (1, 8)).astype(BF)
    e = np.zeros((32, 32 * Q), np.float32)
    for h in range(32):
        e[h, h * Q:(h + 1) * Q] = 1.0
    e32x2 = np.concatenate([e, e], 0).astype(np.float16)
    ep = np.zeros((32, 32 * PD), np.float32)
    for h in range(32):
        ep[h, h * PD:(h + 1) * PD] = 1.0
    e32p = np.concatenate([ep, ep], 0).astype(np.float16)
    ones_bf = np.ones((1, 128), dtype=BF)
    ones_f16 = np.ones((1, 128), dtype=np.float16)
    identj = np.eye(128, dtype=np.float32)[:, ::-1].copy().astype(BF)
    return dict(ident=ident, identj=identj, id32x4=id32x4, mask_rep=mask_rep,
                e32x2=e32x2, e32p=e32p, ones_bf=ones_bf, ones_f16=ones_f16)

def host_inputs(inputs, b, T):
    im = {}
    im["x"] = np.ascontiguousarray(np.asarray(inputs["L0_attn_hidden_states"])[b])
    for d, pre in (("f", "fwd"), ("b", "bwd")):
        g = lambda n: np.asarray(inputs[f"L0_attn_{pre}_{n}"])
        im[f"win_{d}"] = _tile_win(g("W_in"))
        im[f"wout_{d}"] = np.ascontiguousarray(g("W_out"))
        im[f"convw_{d}"] = np.ascontiguousarray(g("conv_w").T.reshape(18, 128, 4))
        im[f"convb_{d}"] = np.ascontiguousarray(g("conv_b").reshape(18, 128, 1))
        im[f"dtb_{d}"] = np.ascontiguousarray(g("dt_bias").reshape(32, 1))
        im[f"alog_{d}"] = np.ascontiguousarray(g("A_log").reshape(32, 1))
        im[f"drep_{d}"] = np.ascontiguousarray(np.repeat(g("D"), 64).reshape(1, 2048))
        im[f"normw_{d}"] = np.ascontiguousarray(g("norm_w").reshape(16, 128, 1))
    gf = lambda n: np.asarray(inputs[f"L0_ffn_{n}"])
    im["fproj"] = np.ascontiguousarray(gf("proj").reshape(8, 128, 16))
    im["fc1"] = np.ascontiguousarray(gf("C1").reshape(16, 4, 128).transpose(1, 0, 2))
    im["fc2"] = np.ascontiguousarray(gf("C2").reshape(16, 4, 128).transpose(1, 0, 2))
    im["fwin1"] = np.ascontiguousarray(gf("Win1")[:1024].reshape(8, 128, 512))
    im["fb1"] = np.ascontiguousarray(gf("Win1")[1024].reshape(4, 128, 1))
    im["fwout1t"] = np.ascontiguousarray(gf("Wout1").T.reshape(4, 128, 4096))
    im["fwin2"] = np.ascontiguousarray(gf("Win2")[:4096].reshape(32, 128, 512))
    im["fb2"] = np.ascontiguousarray(gf("Win2")[4096].reshape(4, 128, 1))
    im["fwout2t"] = np.ascontiguousarray(gf("Wout2").T.reshape(4, 128, 1024))
    im.update(_consts())
    return im

def build_program(T=2048, taps=(), stage=99, dirs=("f", "b")):
    NT = T // 128
    NQ = T // QT
    taps = set(taps)

    nc = bass.Bass("TRN2", target_bir_lowering=False, detect_race_conditions=False)
    dr = {}
    def din(name, shape, dt=F32):
        dr[name] = nc.dram_tensor(name, list(shape), dt, kind="ExternalInput").ap()
    din("x", (T, D))
    for d in ("f", "b"):
        din(f"win_{d}", (WIN_TOTAL,))
        din(f"wout_{d}", (DIN, D))
        din(f"convw_{d}", (18, 128, 4)); din(f"convb_{d}", (18, 128, 1))
        din(f"dtb_{d}", (32, 1)); din(f"alog_{d}", (32, 1))
        din(f"drep_{d}", (1, 2048)); din(f"normw_{d}", (16, 128, 1))
    din("fproj", (8, 128, 16)); din("fc1", (4, 16, 128)); din("fc2", (4, 16, 128))
    din("fwin1", (8, 128, 512)); din("fb1", (4, 128, 1))
    din("fwout1t", (4, 128, 4096))
    din("fwin2", (32, 128, 512)); din("fb2", (4, 128, 1))
    din("fwout2t", (4, 128, 1024))
    din("ident", (128, 128), BF16); din("identj", (128, 128), BF16)
    din("id32x4", (128, 32), BF16)
    din("mask_rep", (128, 1024), BF16)
    din("e32x2", (64, 4096), FP16); din("e32p", (64, 2048), FP16)
    din("ones_bf", (1, 128), BF16); din("ones_f16", (1, 128), FP16)
    out = nc.dram_tensor("out", [T, D], F32, kind="ExternalOutput").ap()
    wbf = {}
    for d_ in ("f", "b"):
        wbf[f"win_{d_}"] = nc.dram_tensor(f"win_{d_}_bf", [WIN_TOTAL], BF16).ap()
        wbf[f"wout_{d_}"] = nc.dram_tensor(f"wout_{d_}_bf", [DIN, D], BF16).ap()
    scr_cum = nc.dram_tensor("scr_cum", [2, 4, 32, 128], FP16).ap()
    scr_ncl = nc.dram_tensor("scr_ncl", [4, 32, 128], FP16).ap()

    tapd = {}
    def deftap(name, shape, dt=F32):
        if name in taps:
            tapd[name] = nc.dram_tensor("tap_" + name, list(shape), dt,
                                        kind="ExternalOutput").ap()
    deftap("x1q", (128, 8 * QT), BF16)
    deftap("ztm", (128, 4 * 2048), BF16)
    deftap("xbc", (128, 18 * 516), BF16)
    deftap("dtq", (66, QT), FP16)
    deftap("cumhl", (64, QT), FP16)
    deftap("Xtm", (128, 2048), BF16)
    deftap("Mpg", (128, 2048), BF16)
    deftap("ybuf", (128, 2048), BF16)
    deftap("yfm", (128, 16 * 512), BF16)
    deftap("Sst", (128, 2048), BF16)
    deftap("attn_f", (128, NT * 1024), BF16)
    deftap("attn", (128, NT * 1024), BF16)
    deftap("xmid", (128, NT * 1024), BF16)
    deftap("acat", (128, 64), F32)
    deftap("aorig", (128, 64), F32)
    deftap("zs_keep", (128, 64), F32)
    deftap("ascr", (128, 64), F32)
    deftap("astat", (128, 24), F32)
    deftap("hfm", (128, 32 * 512), BF16)

    S = Sched()
    es = ExitStack()
    sb = lambda n, shp, dt=F32: es.enter_context(nc.sbuf_tensor(n, list(shp), dt))

    xres = sb("xres", (128, NT * 1024), BF16)
    psum = es.enter_context(nc.psum_tensor("ps", [128, 4096], F32))
    def bank(b, p0=0, pn=128, w=512):
        return psum[p0:p0 + pn, 512 * b:512 * b + w]
    ident = sb("identS", (128, 128), BF16)
    identj = sb("identJS", (128, 128), BF16)
    epscol = sb("epscol", (128, 1))

    def tapcp(name, src, eng="sync"):
        t = tapd.get(name)
        if t is not None:
            S.dma(eng, t[:, :], src)

    statc = sb("statc", (128, 16 * 2))
    colc = sb("colc", (128, 2 * 2))
    lnslot = [0]
    def ln_cols(src_tile, cv=None):
        sl = lnslot[0] % 2; lnslot[0] += 1
        st = statc[:, 16 * sl:16 * sl + 16]
        if cv is None:
            cv = colc[:, 2 * sl:2 * sl + 2]
        w_ = src_tile.shape[-1]
        half = w_ // 2
        S.op("vector", lambda e: e.bn_stats(st[:, 0:6], src_tile[:, 0:half]),
             r=[src_tile], w=[st[:, 0:6]])
        S.op("vector", lambda e: e.bn_stats(st[:, 6:12], src_tile[:, half:w_]),
             r=[src_tile], w=[st[:, 6:12]])
        S.op("vector", lambda e: e.bn_aggr(st[:, 12:14], st[:, 0:12]),
             r=[st[:, 0:12]], w=[st[:, 12:14]])
        S.op("scalar", lambda e: e.activation(st[:, 14:15], st[:, 13:14], AF.Sqrt,
                                              bias=epscol[:, :], scale=1.0),
             r=[st[:, 12:14], epscol[:, :]], w=[st[:, 14:15]])
        S.op("vector", lambda e: e.reciprocal(cv[:, 0:1], st[:, 14:15]),
             r=[st[:, 14:15]], w=[cv[:, 0:1]])
        S.op("vector", lambda e: e.scalar_tensor_tensor(cv[:, 1:2], st[:, 12:13], -1.0,
                                                        cv[:, 0:1], ALU.mult, ALU.mult),
             r=[st[:, 12:14], cv[:, 0:1]], w=[cv[:, 1:2]])
        return cv[:, 0:1], cv[:, 1:2]

    tslot = [0]
    def tbank(b):
        return psum[:, 512 * b:512 * b + 256].bitcast(BF16)
    def transpose_pack(srcs, evac_eng, evac_fn, extra_w=(), rev=False, idn=None,
                       idn_r=(), base=2):
        b = base + tslot[0] % 2; tslot[0] += 1
        ba = tbank(b)
        if idn is None:
            idn = identj if rev else ident
        for j, s in enumerate(srcs):
            S.op("tensor", lambda e, s=s, j=j, ba=ba, idn=idn: e.transpose(
                ba[:, 128 * j:128 * j + 128], s, idn[:, :]),
                r=[s, idn[:, :]] + list(idn_r), w=[ba[:, 128 * j:128 * j + 128]])
        S.op(evac_eng, lambda e: evac_fn(e, ba), r=[ba], w=list(extra_w))

    def ln_apply_transpose(src_tm, inv, ngm, lntmp, dst_fm_q, tcq, rev=False):
        S.op("vector", lambda e, inv=inv, ngm=ngm, src_tm=src_tm: e.tensor_scalar(
            lntmp[:, 0:1024], src_tm, inv, ngm, ALU.mult, ALU.add),
             r=[src_tm, inv, ngm], w=[lntmp[:, 0:1024]])
        dcq = (3 - tcq) if rev else tcq
        for half in range(2):
            srcs = [lntmp[:, 128 * (4 * half + j):128 * (4 * half + j) + 128] for j in range(4)]
            dsts = [dst_fm_q[:, 512 * (4 * half + j) + 128 * dcq:
                             512 * (4 * half + j) + 128 * dcq + 128] for j in range(4)]
            def ev(e, ba, dsts=dsts):
                for j in range(4):
                    e.tensor_copy(dsts[j], ba[:, 128 * j:128 * j + 128])
                return e.engine_nop()
            transpose_pack(srcs, "vector", ev, extra_w=dsts, rev=rev, base=4)

    def ln_and_transpose(src_tm, lntmp, dst_fm_q, tcq, rev=False):
        inv, ngm = ln_cols(src_tm)
        S.op("vector", lambda e: e.tensor_scalar(lntmp[:, 0:1024], src_tm, inv, ngm,
                                                 ALU.mult, ALU.add),
             r=[src_tm, inv, ngm], w=[lntmp[:, 0:1024]])
        dcq = (3 - tcq) if rev else tcq
        for half in range(2):
            srcs = [lntmp[:, 128 * (4 * half + j):128 * (4 * half + j) + 128] for j in range(4)]
            dsts = [dst_fm_q[:, 512 * (4 * half + j) + 128 * dcq:
                             512 * (4 * half + j) + 128 * dcq + 128] for j in range(4)]
            def ev(e, ba, dsts=dsts):
                for j in range(4):
                    e.tensor_copy(dsts[j], ba[:, 128 * j:128 * j + 128])
                return e.engine_nop()
            transpose_pack(srcs, "vector", ev, extra_w=dsts, rev=rev)

    esA = ExitStack()
    sbA = lambda n, shp, dt=F32: esA.enter_context(nc.sbuf_tensor(n, list(shp), dt))
    attn = sbA("attn", (128, NT * 1024), BF16)
    x1q = sbA("x1q", (128, 8 * QT), BF16)
    lntmp = sbA("lntmp", (128, 1024), BF16)
    ztm = sbA("ztm", (128, 4 * 2048), BF16)
    zscr = sbA("zscr", (128, 2 * 512), BF16)
    xbc = sbA("xbc", (128, 18 * 516), BF16)
    stash = sbA("stash", (128, 18 * 4), BF16)
    dt_fm = sbA("dt_fm", (32, QT))
    lns1 = sbA("lns1", (32, QT))
    aA = sbA("aA", (32, QT)); cumA = sbA("cumA", (32, QT))
    cumB = lns1
    losc = sbA("losc", (32, QT), FP16)
    stack66 = sbA("stack66", (66, QT), FP16)
    cumhl = sbA("cumhl", (64, QT), FP16)
    rhs66 = sbA("rhs66", (66, 4096), FP16)
    Xtm2 = sbA("Xtm2", (128, 2 * 2048), BF16)
    Btm2 = sbA("Btm2", (128, 256), BF16)
    Gp = sbA("Gp", (128, 128), BF16)
    Epg = sbA("Epg", (128, 2 * 1024), BF16)
    Mpg = sbA("Mpg", (128, 2 * 1024), BF16)
    decrow = sbA("decrow", (1, 2048), FP16)
    wcol = sbA("wcol", (128, 32), BF16)
    expc = sbA("expc", (128, 2048), BF16)
    rmsc4 = expc[:, :].bitcast(F32)[:, 0:4]
    ybuf = sbA("ybuf", (128, 2048), BF16)
    rmsc = sbA("rmsc", (128, 2))
    yfm = sbA("yfm", (128, 16 * 512), BF16)
    Sst = sbA("Sst", (128, 2048), BF16)
    Stmp = sbA("Stmp", (128, 2048), BF16)
    yps = Stmp
    drow = Stmp[0:1, :]
    yscr = Stmp[:, :].bitcast(F32)
    winr = sbA("winr", (128, 2 * 2048), BF16)
    wor = sbA("wor", (128, 2 * 1024), BF16)
    dbc = sbA("dbc", (128, 2048), BF16)
    cvacc = sbA("cvacc", (128, 1024), BF16)
    maskS = sbA("maskS", (128, 1024), BF16)
    e32pS = sbA("e32pS", (64, 2048), FP16)
    onesS = sbA("onesS", (1, 128), BF16)
    onesH = sbA("onesH", (1, 128), FP16)

    ccol = sbA("ccol", (128, 18))
    wcol4 = sbA("wcol4", (128, 4 * 18))
    nwcol = sbA("nwcol", (128, 16))
    scol = sbA("scol", (32, 4))

    S.dma("sync", ident[:, :], dr["ident"][:, :])
    S.dma("sync", identj[:, :], dr["identj"][:, :])
    S.dma("sync", maskS[:, :], dr["mask_rep"][:, :])
    S.dma("sync", rhs66[0:64, :], dr["e32x2"][:, :])
    S.dma("sync", e32pS[:, :], dr["e32p"][:, :])
    S.dma("sync", onesS[:, :], dr["ones_bf"][:, :])
    S.dma("sync", onesH[:, :], dr["ones_f16"][:, :])
    S.dma("gpsimd", xres[:, 0:NT * 1024].rearrange("p (j d) -> p j d", d=1024),
          bass.AP(dr["x"].tensor, 0, [[1024, 128], [128 * 1024, NT], [1, 1024]]))
    for d_ in ("f", "b"):
        S.dma("gpsimd", wbf[f"win_{d_}"][:], dr[f"win_{d_}"][:])
        S.dma("gpsimd", wbf[f"wout_{d_}"][:, :], dr[f"wout_{d_}"][:, :])
    S.op("vector", lambda e: e.memset(stack66[64:66, :], 1.0), w=[stack66[64:66, :]])
    S.op("vector", lambda e: e.memset(epscol[:, :], float(EPS)), w=[epscol[:, :]])
    S.op("vector", lambda e: e.memset(xbc[:, :], 0.0), w=[xbc[:, :]])

    def mamba_dir(d, rev):
        S.dma("gpsimd", drow[:, :], dr[f"drep_{d}"][:, :])
        S.dma("sync", wcol4[:, :].rearrange("p (c k) -> p c k", k=4),
              bass.AP(dr[f"convw_{d}"].tensor, 0, [[4, 128], [512, 18], [1, 4]]))
        S.dma("sync", ccol[:, :].unsqueeze(2),
              bass.AP(dr[f"convb_{d}"].tensor, 0, [[1, 128], [128, 18], [1, 1]]), slow=True)
        S.dma("sync", nwcol[:, :].unsqueeze(2),
              bass.AP(dr[f"normw_{d}"].tensor, 0, [[1, 128], [128, 16], [1, 1]]), slow=True)
        S.dma("sync", scol[0:32, 0:1], dr[f"dtb_{d}"][:, :])
        S.dma("sync", scol[0:32, 1:2], dr[f"alog_{d}"][:, :])
        S.op("vector", lambda e: e.tensor_scalar(scol[0:32, 2:3], scol[0:32, 0:1], -1.0,
                                                 None, ALU.mult),
             r=[scol[0:32, 0:1]], w=[scol[0:32, 2:3]])
        S.op("scalar", lambda e: e.activation(scol[0:32, 3:4], scol[0:32, 1:2], AF.Exp),
             r=[scol[0:32, 1:2]], w=[scol[0:32, 3:4]])
        for j in range(4):
            S.op("tensor", lambda e, j=j: e.matmul(bank(j), onesS[:, :],
                                                   drow[:, 512 * j:512 * j + 512],
                                                   start=True, stop=True),
                 r=[onesS[:, :], drow[:, 512 * j:512 * j + 512]], w=[bank(j)])
        S.op("vector", lambda e: e.tensor_copy(dbc[:, 0:1024], psum[:, 0:1024]),
             r=[psum[:, 0:1024]], w=[dbc[:, 0:1024]])
        S.op("vector", lambda e: e.tensor_copy(dbc[:, 1024:2048], psum[:, 1024:2048]),
             r=[psum[:, 1024:2048]], w=[dbc[:, 1024:2048]])
        S.op("vector", lambda e: e.memset(Sst[:, :], 0.0), w=[Sst[:, :]])
        S.op("vector", lambda e: e.memset(stash[:, :], 0.0), w=[stash[:, :]])

        x1cv = Stmp[:, :].bitcast(F32)[:, 0:8]
        def emit_x1_stats(qq, tcq):
            oqq = (NQ - 1 - qq) if rev else qq
            tc = 4 * oqq + tcq
            ln_cols(xres[:, 1024 * tc:1024 * tc + 1024],
                    cv=x1cv[:, 2 * tcq:2 * tcq + 2])
        def emit_x1_apply(qq, tcq):
            oqq = (NQ - 1 - qq) if rev else qq
            tc = 4 * oqq + tcq
            ln_apply_transpose(xres[:, 1024 * tc:1024 * tc + 1024],
                               x1cv[:, 2 * tcq:2 * tcq + 1],
                               x1cv[:, 2 * tcq + 1:2 * tcq + 2],
                               lntmp, x1q, tcq, rev=rev)
        def build_x1(qq):
            for tcq in range(4):
                ln_and_transpose(xres[:, 1024 * (4 * ((NQ - 1 - qq) if rev else qq) + tcq):
                                      1024 * (4 * ((NQ - 1 - qq) if rev else qq) + tcq) + 1024],
                                 lntmp, x1q, tcq, rev=rev)
        carry_steps = []
        build_x1(0)
        for q in range(NQ):
            oq = (NQ - 1 - q) if rev else q
            if q + 1 < NQ:
                for tcq in range(4):
                    carry_steps.append(lambda qq=q + 1, tcq=tcq: emit_x1_stats(qq, tcq))
            def win_dma(s2):
                slot = s2 % 2
                wsrc = wbf[f"win_{d}"]
                if s2 == 0:
                    src = bass.AP(wsrc.tensor, WIN_TILE_OFF[0],
                                  [[8 * 32, 128], [1, 8 * 32]])
                    S.dma("sync", winr[:, 0:256], src)
                else:
                    p0 = 2 * s2 - 1
                    src = bass.AP(wsrc.tensor, WIN_TILE_OFF[p0],
                                  [[8 * 128, 128], [WIN_TILE_OFF[p0 + 1] - WIN_TILE_OFF[p0], 2],
                                   [1, 8 * 128]])
                    S.dma("sync",
                          winr[:, 2048 * slot:2048 * slot + 2048].rearrange(
                              "p (m kw) -> p m kw", m=2), src)

            def emit_conv_tile(ct, pb):
                bk = bank(pb)
                wc = lambda k: wcol4[:, 4 * ct + k:4 * ct + k + 1]
                S.op("vector", lambda e, ct=ct, bk=bk: e.tensor_scalar(
                    cvacc[:, 3:512], bk[:, 0:509], wc(0), None, ALU.mult),
                    r=[bk, wc(0)], w=[cvacc[:, 3:512]])
                S.op("vector", lambda e, ct=ct: e.tensor_scalar(
                    cvacc[:, 0:3], stash[:, 4 * ct:4 * ct + 3], wc(0), None, ALU.mult),
                    r=[stash[:, 4 * ct:4 * ct + 3], wc(0)], w=[cvacc[:, 0:3]])
                for k in (1, 2):
                    d_, s_ = (k % 2, (k + 1) % 2)
                    hw = 3 - k
                    S.op("vector", lambda e, ct=ct, k=k, d_=d_, s_=s_, bk=bk, hw=hw:
                         e.scalar_tensor_tensor(
                             cvacc[:, 512 * d_ + hw:512 * d_ + 512], bk[:, 0:509 + k],
                             wc(k), cvacc[:, 512 * s_ + hw:512 * s_ + 512],
                             ALU.mult, ALU.add),
                         r=[bk, wc(k), cvacc[:, 512 * s_:512 * s_ + 512]],
                         w=[cvacc[:, 512 * d_:512 * d_ + 512]])
                    S.op("vector", lambda e, ct=ct, k=k, d_=d_, s_=s_, hw=hw:
                         e.scalar_tensor_tensor(
                             cvacc[:, 512 * d_:512 * d_ + hw],
                             stash[:, 4 * ct + k:4 * ct + 3],
                             wc(k), cvacc[:, 512 * s_:512 * s_ + hw],
                             ALU.mult, ALU.add),
                         r=[stash[:, 4 * ct + k:4 * ct + 3], wc(k),
                            cvacc[:, 512 * s_:512 * s_ + hw]],
                         w=[cvacc[:, 512 * d_:512 * d_ + hw]])
                S.op("vector", lambda e, ct=ct, bk=bk: e.scalar_tensor_tensor(
                    cvacc[:, 512:1024], bk[:, 0:512], wc(3), cvacc[:, 0:512],
                    ALU.mult, ALU.add),
                    r=[bk, wc(3), cvacc[:, 0:512]], w=[cvacc[:, 512:1024]])
                S.op("vector", lambda e, ct=ct, bk=bk: e.tensor_copy(
                    stash[:, 4 * ct:4 * ct + 3], bk[:, 509:512]),
                    r=[bk], w=[stash[:, 4 * ct:4 * ct + 3]])
                S.op("scalar", lambda e, ct=ct: e.activation(
                    xbc[:, 516 * ct + 3:516 * ct + 515], cvacc[:, 512:1024], AF.Silu,
                    bias=ccol[:, ct:ct + 1], scale=1.0),
                    r=[cvacc[:, 512:1024], ccol[:, ct:ct + 1]],
                    w=[xbc[:, 516 * ct + 3:516 * ct + 515]])
                S.op("vector", lambda e, ct=ct: e.tensor_copy(xbc[:, 516 * ct:516 * ct + 3],
                                                              stash[:, 4 * ct:4 * ct + 3]),
                     r=[stash[:, 4 * ct:4 * ct + 3]], w=[xbc[:, 516 * ct:516 * ct + 3]])

            def dt_steps():
                def s0():
                    S.op("scalar", lambda e: e.activation(lns1[0:32, :], dt_fm[0:32, :], AF.Sigmoid,
                                                          bias=scol[0:32, 2:3], scale=-1.0),
                         r=[dt_fm[0:32, :], scol[0:32, 2:3]], w=[lns1[0:32, :]])
                    S.op("scalar", lambda e: e.activation(lns1[0:32, :], lns1[0:32, :], AF.Ln),
                         r=[lns1[0:32, :]], w=[lns1[0:32, :]])
                    S.op("vector", lambda e: e.tensor_scalar(aA[0:32, :], lns1[0:32, :],
                                                             scol[0:32, 3:4], None, ALU.mult),
                         r=[lns1[0:32, :], scol[0:32, 3:4]], w=[aA[0:32, :]])
                    S.op("scalar", lambda e: e.activation(dt_fm[0:32, :], lns1[0:32, :], AF.Ln, scale=-1.0),
                         r=[lns1[0:32, :]], w=[dt_fm[0:32, :]])
                    S.op("vector", lambda e: e.tensor_scalar(dt_fm[0:32, :], dt_fm[0:32, :], -80.0,
                                                             None, ALU.max),
                         r=[dt_fm[0:32, :]], w=[dt_fm[0:32, :]])
                yield s0
                bufs = [aA, cumA, cumB, cumA, cumB, cumA, cumB, cumA]
                for i, dd in enumerate((1, 2, 4, 8, 16, 32, 64)):
                    def s1(srcb=bufs[i], dstb=bufs[i + 1], dd=dd):
                        def stg(e, s=srcb, t=dstb, dd=dd):
                            s3 = s[0:32, :].rearrange("p (c t) -> p c t", t=Q)
                            d3 = t[0:32, :].rearrange("p (c t) -> p c t", t=Q)
                            e.tensor_copy(d3[:, :, 0:dd], s3[:, :, 0:dd])
                            return e.tensor_tensor(d3[:, :, dd:Q], s3[:, :, dd:Q],
                                                   s3[:, :, 0:Q - dd], ALU.add)
                        S.op("vector", stg, r=[srcb[0:32, :]], w=[dstb[0:32, :]])
                    yield s1
                cum = bufs[7]
                scw0 = bass.AP(scr_cum.tensor, 0, [[128, 32], [4096, 4], [1, 128]])
                scw1 = bass.AP(scr_cum.tensor, 16384, [[128, 32], [4096, 4], [1, 128]])
                r3 = lambda t: t.rearrange("h (c u) -> h c u", u=128)
                def s2():
                    S.op("vector", lambda e: e.tensor_copy(cumhl[0:32, :], cum[0:32, :]),
                         r=[cum[0:32, :]], w=[cumhl[0:32, :]])
                    S.op("vector", lambda e: e.tensor_tensor(losc[0:32, :], cum[0:32, :],
                                                             cumhl[0:32, :], ALU.subtract),
                         r=[cum[0:32, :], cumhl[0:32, :]], w=[losc[0:32, :]])
                    S.dma("sync", scw0, r3(cumhl[0:32, :]))
                    S.dma("sync", scw1, r3(losc[0:32, :]))
                yield s2
                def s3():
                    S.dma("sync", r3(cumhl[32:64, :]), scw1)
                    S.op("vector", lambda e: e.tensor_tensor(lns1[0:32, :], dt_fm[0:32, :],
                                                             cum[0:32, :], ALU.subtract),
                         r=[dt_fm[0:32, :], cum[0:32, :]], w=[lns1[0:32, :]])
                    S.op("vector", lambda e: e.tensor_copy(stack66[0:32, :], lns1[0:32, :]),
                         r=[lns1[0:32, :]], w=[stack66[0:32, :]])
                yield s3
                def s4():
                    S.op("vector", lambda e: e.tensor_tensor(losc[0:32, :], lns1[0:32, :],
                                                             stack66[0:32, :], ALU.subtract),
                         r=[lns1[0:32, :], stack66[0:32, :]], w=[losc[0:32, :]])
                    snw = bass.AP(scr_ncl.tensor, 0, [[128, 32], [4096, 4], [1, 128]])
                    S.dma("sync", snw, r3(losc[0:32, :]))
                    S.dma("sync", r3(stack66[32:64, :]), snw)
                    if "dtq" in taps and d == "f" and q == NQ - 1:
                        tapcp("dtq", stack66[:, :])
                    if "cumhl" in taps and d == "f" and q == NQ - 1:
                        tapcp("cumhl", cumhl[:, :])
                yield s4

            win_dma(0); win_dma(1)
            dt_it = dt_steps()
            pend_conv = []
            pend_ztr = [None]
            def emit_ztr(zm):
                half = 512 * (zm % 2)
                srcs = [zscr[:, half + 128 * j:half + 128 * j + 128] for j in range(4)]
                zdst = ztm[:, :].rearrange("p (c f) -> p c f", f=2048)[:, :, 128 * zm:128 * zm + 128]
                def zev(e, ba, zdst=zdst):
                    return e.activation(zdst, ba.rearrange("p (c f) -> p c f", f=128),
                                        AF.Silu)
                transpose_pack(srcs, "scalar", zev, base=4,
                               extra_w=[ztm[:, :].rearrange("p (c f) -> p c f", f=2048)[:, :, 128 * zm:128 * zm + 128]])
            for pos in range(35):
                m = MORDER[pos]
                w = 128 if m < 34 else 32
                s2 = 0 if pos == 0 else (pos + 1) // 2
                slot = s2 % 2
                sub = 0 if pos == 0 else (pos - 1) % 2
                base = 2048 * slot + 1024 * sub
                pb = pos % 4
                for k in range(8):
                    wt = winr[:, base + w * k:base + w * k + w]
                    rhs = x1q[:, 512 * k:512 * k + 512]
                    S.op("tensor", lambda e, wt=wt, rhs=rhs, k=k, pb=pb, w=w: e.matmul(
                        bank(pb, 0, w), wt, rhs, start=(k == 0), stop=(k == 7)),
                        r=[wt, rhs], w=[bank(pb, 0, w)])
                if m < 16 and pend_ztr[0] is not None:
                    emit_ztr(pend_ztr[0]); pend_ztr[0] = None
                if m == 34:
                    S.op("vector", lambda e, pb=pb: e.tensor_copy(dt_fm[0:32, :], bank(pb, 0, 32)),
                         r=[bank(pb, 0, 32)], w=[dt_fm[0:32, :]])
                elif m >= 16:
                    pass
                else:
                    half = 512 * (m % 2)
                    S.op("vector", lambda e, pb=pb, half=half: e.tensor_copy(
                        zscr[:, half:half + 512], bank(pb)),
                        r=[bank(pb)], w=[zscr[:, half:half + 512]])
                if pend_conv:
                    emit_conv_tile(*pend_conv.pop(0))
                if m >= 16 and m < 34:
                    pend_conv.append((m - 16, pb))
                if m < 16:
                    pend_ztr[0] = m
                step = next(dt_it, None)
                if step is not None:
                    step()
                elif carry_steps:
                    carry_steps.pop(0)()
                if (pos == 0 or sub == 1) and s2 + 2 <= 17:
                    win_dma(s2 + 2)
            if pend_ztr[0] is not None:
                emit_ztr(pend_ztr[0]); pend_ztr[0] = None
            while carry_steps:
                carry_steps.pop(0)()
            if q + 1 < NQ:
                for tcq in range(4):
                    emit_x1_apply(q + 1, tcq)
                    if pend_conv:
                        emit_conv_tile(*pend_conv.pop(0))
            while pend_conv:
                emit_conv_tile(*pend_conv.pop(0))

            if "xbc" in taps and d == "f" and q == NQ - 1:
                tapcp("xbc", xbc[:, :])
            if "ztm" in taps and d == "f" and q == NQ - 1:
                tapcp("ztm", ztm[:, :])
            if "x1q" in taps and d == "f" and q == NQ - 1:
                tapcp("x1q", x1q[:, :])
            if stage < 2:
                continue

            def emit_xb_transp(cq):
                cofs = 128 * cq
                Xtm = Xtm2[:, 2048 * (cq % 2):2048 * (cq % 2) + 2048]
                Bfm = xbc[:, 516 * 16 + 3 + cofs:516 * 16 + 3 + cofs + 128]
                for pk in range(4):
                    srcs = [xbc[:, 516 * (4 * pk + j) + 3 + cofs:516 * (4 * pk + j) + 3 + cofs + 128]
                            for j in range(4)]
                    def xev(e, ba, pk=pk, Xtm=Xtm):
                        return e.activation(Xtm[:, 512 * pk:512 * pk + 512], ba, AF.Copy)
                    transpose_pack(srcs, "scalar", xev, extra_w=[Xtm[:, 512 * pk:512 * pk + 512]])
                def bev(e, ba):
                    return e.activation(Btm2[:, 128 * (cq % 2):128 * (cq % 2) + 128],
                                        ba[:, 0:128], AF.Copy)
                transpose_pack([Bfm], "scalar", bev,
                               extra_w=[Btm2[:, 128 * (cq % 2):128 * (cq % 2) + 128]])
            for cq in range(4):
                cofs = 128 * cq
                csl = slice(cofs, cofs + 128)
                Xtm = Xtm2[:, 2048 * (cq % 2):2048 * (cq % 2) + 2048]
                Btm = Btm2[:, 128 * (cq % 2):128 * (cq % 2) + 128]
                Bfm = xbc[:, 516 * 16 + 3 + cofs:516 * 16 + 3 + cofs + 128]
                Cfm = xbc[:, 516 * 17 + 3 + cofs:516 * 17 + 3 + cofs + 128]
                S.dma("sync", rhs66[64:66, :],
                      bass.AP(scr_cum.tensor, 4096 * cq, [[16384, 2], [1, 4096]]))
                if cq == 0:
                    emit_xb_transp(0)
                S.op("tensor", lambda e, Bfm=Bfm, Cfm=Cfm: e.matmul(
                    bank(6)[:, 0:128], Bfm, Cfm, start=True, stop=True),
                    r=[Bfm, Cfm], w=[bank(6)[:, 0:128]])
                S.op("vector", lambda e: e.tensor_copy(Gp[:, :], bank(6)[:, 0:128]),
                     r=[bank(6)[:, 0:128]], w=[Gp[:, :]])
                def emit_dpage(pg, bs):
                    ps_ = pg % 2
                    S.op("tensor", lambda e, bs=bs, hb=0: e.matmul(
                        bank(bs[0]), ident[:, :], maskS[:, 0:512],
                        start=True, stop=False),
                        r=[ident[:, :], maskS[:, 0:512]], w=[bank(bs[0])])
                    S.op("tensor", lambda e, bs=bs: e.matmul(
                        bank(bs[1]), ident[:, :], maskS[:, 512:1024],
                        start=True, stop=False),
                        r=[ident[:, :], maskS[:, 512:1024]], w=[bank(bs[1])])
                    st66 = stack66[:, csl]
                    for hb in range(2):
                        S.op("tensor", lambda e, bs=bs, hb=hb, pg=pg, st66=st66: e.matmul(
                            bank(bs[hb]), st66,
                            rhs66[:, 1024 * pg + 512 * hb:1024 * pg + 512 * hb + 512],
                            start=False, stop=True),
                            r=[st66, rhs66[:, 1024 * pg + 512 * hb:1024 * pg + 512 * hb + 512]],
                            w=[bank(bs[hb])])
                    S.op("scalar", lambda e, ps_=ps_, bs=bs: e.activation(
                        Epg[:, 1024 * ps_:1024 * ps_ + 1024],
                        psum[:, 512 * bs[0]:512 * bs[0] + 1024], AF.Exp),
                        r=[psum[:, 512 * bs[0]:512 * bs[0] + 1024]],
                        w=[Epg[:, 1024 * ps_:1024 * ps_ + 1024]])
                    S.op("vector", lambda e, ps_=ps_: e.tensor_tensor(
                        Mpg[:, 1024 * ps_:1024 * ps_ + 1024].rearrange("p (h t) -> p h t", t=128),
                        Epg[:, 1024 * ps_:1024 * ps_ + 1024].rearrange("p (h t) -> p h t", t=128),
                        Gp[:, :].unsqueeze(1).broadcast_to([128, 8, 128]), ALU.mult),
                        r=[Epg[:, 1024 * ps_:1024 * ps_ + 1024], Gp[:, :]],
                        w=[Mpg[:, 1024 * ps_:1024 * ps_ + 1024]])
                    S.op("vector", lambda e, ps_=ps_, pg=pg: e.tensor_copy(
                        wcol[:, 8 * pg:8 * pg + 8].unsqueeze(2),
                        Epg[:, 1024 * ps_:1024 * ps_ + 1024].rearrange("p (h t) -> p h t", t=128)[:, :, 127:128]),
                        r=[Epg[:, 1024 * ps_:1024 * ps_ + 1024]], w=[wcol[:, 8 * pg:8 * pg + 8]])
                def emit_intra(pg, ib):
                    ps_ = pg % 2
                    for hh in range(8):
                        h = 8 * pg + hh
                        mp = Mpg[:, 1024 * ps_ + 128 * hh:1024 * ps_ + 128 * hh + 128]
                        S.op("tensor", lambda e, mp=mp, h=h, hh=hh, ib=ib, Xtm=Xtm: e.matmul(
                            bank(ib)[:, 64 * hh:64 * hh + 64], mp, Xtm[:, 64 * h:64 * h + 64],
                            start=True, stop=True),
                            r=[mp, Xtm[:, 64 * h:64 * h + 64]],
                            w=[bank(ib)[:, 64 * hh:64 * hh + 64]])
                chl = cumhl[:, csl]
                for hf in range(2):
                    pga, pgb = 2 * hf, 2 * hf + 1
                    emit_dpage(pga, [0, 1])
                    emit_dpage(pgb, [4, 5])
                    for j_ in range(2):
                        S.op("tensor", lambda e, j_=j_, hf=hf, chl=chl: e.matmul(
                            bank(2 + j_), chl,
                            e32pS[:, 1024 * hf + 512 * j_:1024 * hf + 512 * j_ + 512],
                            start=True, stop=True),
                            r=[chl, e32pS[:, 1024 * hf + 512 * j_:1024 * hf + 512 * j_ + 512]],
                            w=[bank(2 + j_)])
                    S.op("scalar", lambda e, hf=hf: e.activation(
                        expc[:, 1024 * hf:1024 * hf + 1024], psum[:, 1024:2048], AF.Exp),
                        r=[psum[:, 1024:2048]], w=[expc[:, 1024 * hf:1024 * hf + 1024]])
                    emit_intra(pga, 6)
                    emit_intra(pgb, 7)
                    for js, sl in enumerate((pga, pgb)):
                        S.op("tensor", lambda e, js=js, sl=sl, Cfm=Cfm: e.matmul(
                            bank(2 + js), Cfm, Sst[:, 512 * sl:512 * sl + 512],
                            start=True, stop=True),
                            r=[Cfm, Sst[:, 512 * sl:512 * sl + 512]], w=[bank(2 + js)])
                    for js, sl in enumerate((pga, pgb)):
                        ib = 6 + js
                        S.op("vector", lambda e, sl=sl, js=js: e.tensor_tensor(
                            yscr[:, 512 * js:512 * js + 512],
                            expc[:, 512 * sl:512 * sl + 512], bank(2 + js), ALU.mult),
                            r=[expc[:, 512 * sl:512 * sl + 512], bank(2 + js)],
                            w=[yscr[:, 512 * js:512 * js + 512]])
                        S.op("vector", lambda e, js=js, ib=ib, sl=sl: e.tensor_tensor(
                            yps[:, 512 * sl:512 * sl + 512], yscr[:, 512 * js:512 * js + 512],
                            bank(ib), ALU.add),
                            r=[yscr[:, 512 * js:512 * js + 512], bank(ib)],
                            w=[yps[:, 512 * sl:512 * sl + 512]])
                        S.op("gpsimd", lambda e, sl=sl, Xtm=Xtm: e.tensor_tensor(
                            ybuf[:, 512 * sl:512 * sl + 512], Xtm[:, 512 * sl:512 * sl + 512],
                            dbc[:, 512 * sl:512 * sl + 512], ALU.mult),
                            r=[Xtm[:, 512 * sl:512 * sl + 512], dbc[:, 512 * sl:512 * sl + 512]],
                            w=[ybuf[:, 512 * sl:512 * sl + 512]])
                        S.op("vector", lambda e, sl=sl: e.tensor_tensor(
                            yscr[:, 512:1024].bitcast(BF16) if False else yps[:, 512 * sl:512 * sl + 512],
                            ybuf[:, 512 * sl:512 * sl + 512],
                            yps[:, 512 * sl:512 * sl + 512], ALU.add),
                            r=[ybuf[:, 512 * sl:512 * sl + 512], yps[:, 512 * sl:512 * sl + 512]],
                            w=[yps[:, 512 * sl:512 * sl + 512]])
                        S.op("vector", lambda e, sl=sl, cq=cq: e.tensor_tensor(
                            ybuf[:, 512 * sl:512 * sl + 512], yps[:, 512 * sl:512 * sl + 512],
                            ztm[:, 2048 * cq + 512 * sl:2048 * cq + 512 * sl + 512], ALU.mult),
                            r=[yps[:, 512 * sl:512 * sl + 512],
                               ztm[:, 2048 * cq + 512 * sl:2048 * cq + 512 * sl + 512]],
                            w=[ybuf[:, 512 * sl:512 * sl + 512]])
                        dsl_ = 0 if sl == 3 else 512 * sl
                        S.op("vector", lambda e, sl=sl, dsl_=dsl_: e.scalar_tensor_tensor(
                            Mpg[:, dsl_:dsl_ + 512], ybuf[:, 512 * sl:512 * sl + 512], 1.0,
                            ybuf[:, 512 * sl:512 * sl + 512], ALU.bypass, ALU.mult,
                            accum_out=rmsc4[:, sl:sl + 1]),
                            r=[ybuf[:, 512 * sl:512 * sl + 512]],
                            w=[Mpg[:, dsl_:dsl_ + 512], rmsc4[:, sl:sl + 1]])
                if cq + 1 < 4:
                    emit_xb_transp(cq + 1)
                lcol = cumhl[:, cofs + 127:cofs + 128]
                for j in range(2):
                    S.op("tensor", lambda e, j=j, lcol=lcol: e.matmul(
                        bank(j, 0, 1), lcol, e32pS[:, 512 * j:512 * j + 512],
                        start=True, stop=True),
                        r=[lcol, e32pS[:, 512 * j:512 * j + 512]], w=[bank(j, 0, 1)])
                S.op("scalar", lambda e: e.activation(decrow[:, 0:1024], psum[0:1, 0:1024],
                                                      AF.Exp),
                     r=[psum[0:1, 0:1024]], w=[decrow[:, 0:1024]])
                for j in range(2):
                    S.op("tensor", lambda e, j=j, lcol=lcol: e.matmul(
                        bank(4 + j, 0, 1), lcol, e32pS[:, 1024 + 512 * j:1024 + 512 * j + 512],
                        start=True, stop=True),
                        r=[lcol, e32pS[:, 1024 + 512 * j:1024 + 512 * j + 512]],
                        w=[bank(4 + j, 0, 1)])
                S.op("scalar", lambda e: e.activation(decrow[:, 1024:2048], psum[0:1, 2048:3072],
                                                      AF.Exp),
                     r=[psum[0:1, 2048:3072]], w=[decrow[:, 1024:2048]])
                for sh in range(2):
                    for j in range(2):
                        js = 2 * sh + j
                        S.op("tensor", lambda e, js=js, j=j, sh=sh: e.matmul(
                            bank(4 + 2 * sh + j), onesH[:, :], decrow[:, 512 * js:512 * js + 512],
                            start=True, stop=True),
                            r=[onesH[:, :], decrow[:, 512 * js:512 * js + 512]],
                            w=[bank(4 + 2 * sh + j)])
                S.op("scalar", lambda e: e.activation(rmsc[:, 1:2], epscol[:, :], AF.Sqrt),
                     r=[epscol[:, :]], w=[rmsc[:, 1:2]])
                S.op("vector", lambda e: e.tensor_reduce(
                    rmsc[:, 0:1], rmsc4[:, 0:4].unsqueeze(1), AXX, ALU.add),
                    r=[rmsc4[:, 0:4]], w=[rmsc[:, 0:1]])
                S.op("scalar", lambda e: e.activation(rmsc[:, 1:2], rmsc[:, 0:1], AF.Sqrt,
                                                      bias=epscol[:, :], scale=1.0 / DIN),
                     r=[rmsc[:, 0:1], epscol[:, :]], w=[rmsc[:, 1:2]])
                S.op("vector", lambda e: e.reciprocal(rmsc[:, 1:2], rmsc[:, 1:2]),
                     r=[rmsc[:, 1:2]], w=[rmsc[:, 1:2]])
                idn0 = identj if rev else ident
                S.op("vector", lambda e, idn0=idn0: e.tensor_scalar(
                    Gp[:, :], idn0[:, :], rmsc[:, 1:2], None, ALU.mult),
                    r=[idn0[:, :], rmsc[:, 1:2]], w=[Gp[:, :]])
                S.op("gpsimd", lambda e, Xtm=Xtm: e.tensor_tensor(
                    Stmp[:, :].rearrange("p (h w) -> p h w", w=64),
                    Xtm[:, :].rearrange("p (h w) -> p h w", w=64),
                    wcol[:, :].unsqueeze(2).broadcast_to([128, 32, 64]), ALU.mult),
                    r=[Xtm[:, :], wcol[:, :], rmsc[:, 0:1]], w=[Stmp[:, :]])
                ycofs = 128 * (3 - cq) if rev else cofs
                for pk in range(4):
                    b_ = 2 + tslot[0] % 2; tslot[0] += 1
                    bk = bank(b_)
                    for j in range(4):
                        s = ybuf[:, 128 * (4 * pk + j):128 * (4 * pk + j) + 128]
                        S.op("tensor", lambda e, s=s, j=j, bk=bk: e.matmul(
                            bk[:, 128 * j:128 * j + 128], s, Gp[:, :],
                            start=True, stop=True),
                            r=[s, Gp[:, :]], w=[bk[:, 128 * j:128 * j + 128]])
                    def yev2(e, pk=pk, bk=bk, ycofs=ycofs):
                        d3 = yfm[:, :].rearrange("p (i t) -> p i t", t=512)[
                            :, 4 * pk:4 * pk + 4, ycofs:ycofs + 128]
                        s3 = bk.rearrange("p (j t) -> p j t", t=128)
                        nw3 = nwcol[:, 4 * pk:4 * pk + 4].unsqueeze(2).broadcast_to(
                            [128, 4, 128])
                        return e.tensor_tensor(d3, s3, nw3, ALU.mult)
                    dsts = [yfm[:, 512 * (4 * pk + j) + ycofs:512 * (4 * pk + j) + ycofs + 128]
                            for j in range(4)]
                    S.op("vector", yev2, r=[bk, nwcol[:, 4 * pk:4 * pk + 4]], w=dsts)
                for sh in range(2):
                    for j in range(2):
                        js = 2 * sh + j
                        S.op("tensor", lambda e, js=js, j=j, sh=sh, Btm=Btm: e.matmul(
                            bank(2 * sh + j), Btm[:, :], Stmp[:, 512 * js:512 * js + 512],
                            start=True, stop=True),
                            r=[Btm[:, :], Stmp[:, 512 * js:512 * js + 512]],
                            w=[bank(2 * sh + j)])
                    S.op("vector", lambda e, sh=sh: e.tensor_tensor(
                        Sst[:, 1024 * sh:1024 * sh + 1024], Sst[:, 1024 * sh:1024 * sh + 1024],
                        psum[:, 2048 + 1024 * sh:2048 + 1024 * sh + 1024], ALU.mult),
                        r=[Sst[:, 1024 * sh:1024 * sh + 1024],
                           psum[:, 2048 + 1024 * sh:2048 + 1024 * sh + 1024]],
                        w=[Sst[:, 1024 * sh:1024 * sh + 1024]])
                    S.op("vector", lambda e, sh=sh: e.tensor_tensor(
                        Sst[:, 1024 * sh:1024 * sh + 1024], Sst[:, 1024 * sh:1024 * sh + 1024],
                        psum[:, 1024 * sh:1024 * sh + 1024], ALU.add),
                        r=[Sst[:, 1024 * sh:1024 * sh + 1024],
                           psum[:, 1024 * sh:1024 * sh + 1024]],
                        w=[Sst[:, 1024 * sh:1024 * sh + 1024]])
                if "Xtm" in taps and d == "f" and q == 0 and cq == 0:
                    tapcp("Xtm", Xtm[:, :])
                if "Mpg" in taps and d == "f" and q == 0 and cq == 0:
                    tapcp("Mpg", Mpg[:, :])
                if "ybuf" in taps and d == "f" and q == 0 and cq == 0:
                    tapcp("ybuf", ybuf[:, :])
                if "Sst" in taps and d == "f" and q == 0 and cq == 0:
                    tapcp("Sst", Sst[:, :])
            if stage < 3:
                continue
            def wo_dma(k):
                S.dma("sync", wor[:, 1024 * (k % 2):1024 * (k % 2) + 1024],
                      wbf[f"wout_{d}"][128 * k:128 * k + 128, :])
            wo_dma(0); wo_dma(1)
            for k in range(16):
                slot = k % 2
                for tt in range(4):
                    lhs = yfm[:, 512 * k + 128 * tt:512 * k + 128 * tt + 128]
                    for nb in range(2):
                        S.op("tensor", lambda e, tt=tt, nb=nb, slot=slot, lhs=lhs, k=k: e.matmul(
                            bank(2 * tt + nb), lhs,
                            wor[:, 1024 * slot + 512 * nb:1024 * slot + 512 * nb + 512],
                            start=(k == 0), stop=(k == 15)),
                            r=[lhs, wor[:, 1024 * slot + 512 * nb:1024 * slot + 512 * nb + 512]],
                            w=[bank(2 * tt + nb)])
                if k + 2 < 16:
                    wo_dma(k + 2)
            for tt in range(4):
                otile = (NT - 4 * q - 4 + tt) if rev else (4 * q + tt)
                for nb in range(2):
                    dsl = attn[:, 1024 * otile + 512 * nb:1024 * otile + 512 * nb + 512]
                    if rev:
                        S.op("vector", lambda e, dsl=dsl, tt=tt, nb=nb: e.tensor_tensor(
                            dsl, dsl, bank(2 * tt + nb), ALU.add),
                            r=[dsl, bank(2 * tt + nb)], w=[dsl])
                    else:
                        S.op("vector", lambda e, dsl=dsl, tt=tt, nb=nb: e.tensor_copy(
                            dsl, bank(2 * tt + nb)),
                            r=[bank(2 * tt + nb)], w=[dsl])
            if rev and stage >= 4:
                def mk_fold(otile):
                    at = attn[:, 1024 * otile:1024 * otile + 1024]
                    xt = xres[:, 1024 * otile:1024 * otile + 1024]
                    st = {}
                    def c1():
                        st["ln"] = ln_cols(at)
                    def c2():
                        inv, ngm = st["ln"]
                        S.op("vector", lambda e, at=at, inv=inv, ngm=ngm: e.tensor_scalar(
                            lntmp[:, 0:1024], at, inv, ngm, ALU.mult, ALU.add),
                            r=[at, inv, ngm], w=[lntmp[:, 0:1024]])
                        S.op("vector", lambda e, xt=xt: e.tensor_tensor(
                            xt, xt, lntmp[:, 0:1024], ALU.add),
                            r=[xt, lntmp[:, 0:1024]], w=[xt])
                    return [c1, c2]
                for tt in range(4):
                    carry_steps.extend(mk_fold(NT - 4 * q - 4 + tt))
                if q == NQ - 1:
                    while carry_steps:
                        carry_steps.pop(0)()
        if "yfm" in taps and d == "f":
            tapcp("yfm", yfm[:, :])

    mamba_dir("f", False)
    tapcp("attn_f", attn[:, :])
    if "b" in dirs:
        mamba_dir("b", True)
        tapcp("attn", attn[:, :])

    if stage >= 4:
        tapcp("xmid", xres[:, :])
    S.barrier()

    esA.close()
    if stage >= 5:
        esB = ExitStack()
        sbB = lambda n, shp, dt=F32: esB.enter_context(nc.sbuf_tensor(n, list(shp), dt))
        x2q = sbB("x2q", (128, 8 * QT), BF16)
        lntmp2 = sbB("lntmp2", (128, 1024), BF16)
        wproj = sbB("wproj", (128, 8 * 16), BF16)
        wc1 = sbB("wc1", (16, 4 * 128), BF16)
        wc2 = sbB("wc2", (16, 4 * 128), BF16)
        wwin1 = sbB("wwin1", (128, 8 * 512), BF16)
        wb = sbB("wb", (128, 8))
        wwout1 = sbB("wwout1", (128, 4 * 4096), BF16)
        wwin2 = sbB("wwin2", (128, 32 * 512), BF16)
        wwout2 = sbB("wwout2", (128, 4 * 1024), BF16)
        acat = sbB("acat", (128, 64)); ascr = sbB("ascr", (128, 64))
        asrt = sbB("asrt", (128, 64)); aorig = sbB("aorig", (128, 64))
        zs_keep = sbB("zs_keep", (128, 64))
        astat = sbB("astat", (128, 24))
        invk16 = sbB("invk16", (128, 16))
        pos16 = sbB("pos16", (128, 16))
        afm = sbB("afm", (16, QT), BF16)
        g1b = sbB("g1b", (128, 4 * 512)); g2b = sbB("g2b", (128, 4 * 512))
        z1g = sbB("z1g", (128, 4 * 512), BF16); z2g = sbB("z2g", (128, 4 * 512), BF16)
        hfm = sbB("hfm", (128, 32 * 512), BF16)
        fout = sbB("fout", (128, 1024))

        r2 = lambda t, n: t.rearrange("p (a b) -> p a b", a=n)
        S.dma("gpsimd", r2(wproj[:, :], 8),
              bass.AP(dr["fproj"].tensor, 0, [[16, 128], [2048, 8], [1, 16]]))
        S.dma("gpsimd", r2(wc1[0:16, :], 4),
              bass.AP(dr["fc1"].tensor, 0, [[128, 16], [2048, 4], [1, 128]]))
        S.dma("gpsimd", r2(wc2[0:16, :], 4),
              bass.AP(dr["fc2"].tensor, 0, [[128, 16], [2048, 4], [1, 128]]))
        S.dma("gpsimd", r2(wwin1[:, :], 8),
              bass.AP(dr["fwin1"].tensor, 0, [[512, 128], [65536, 8], [1, 512]]))
        S.dma("sync", wb[:, 0:4].unsqueeze(2),
              bass.AP(dr["fb1"].tensor, 0, [[1, 128], [128, 4], [1, 1]]), slow=True)
        S.dma("gpsimd", r2(wwout1[:, :], 4),
              bass.AP(dr["fwout1t"].tensor, 0, [[4096, 128], [524288, 4], [1, 4096]]))
        S.dma("gpsimd", r2(wwin2[:, :], 32),
              bass.AP(dr["fwin2"].tensor, 0, [[512, 128], [65536, 32], [1, 512]]))
        S.dma("sync", wb[:, 4:8].unsqueeze(2),
              bass.AP(dr["fb2"].tensor, 0, [[1, 128], [128, 4], [1, 1]]), slow=True)
        S.dma("gpsimd", r2(wwout2[:, :], 4),
              bass.AP(dr["fwout2t"].tensor, 0, [[1024, 128], [131072, 4], [1, 1024]]))

        def bld_invk(e):
            for i in range(16):
                e.memset(invk16[:, i:i + 1], 1.0 / (i + 1))
                e.memset(pos16[:, i:i + 1], float(i + 1))
            return e.engine_nop()
        S.op("vector", bld_invk, w=[invk16[:, :], pos16[:, :]])

        a3 = lambda t: t.rearrange("p (c e) -> p c e", e=16)
        bc4 = lambda col: col.unsqueeze(2).broadcast_to([128, 4, 16])
        bck = lambda t: t.unsqueeze(1).broadcast_to([128, 4, 16])

        def emit_entmax():
            ra = lambda t: [t[:, :]]
            S.op("vector", lambda e: e.tensor_reduce(astat[:, 0:4], a3(acat[:, :]), AXX, ALU.add),
                 r=[acat[:, :]], w=[astat[:, 0:4]])
            S.op("vector", lambda e: e.tensor_scalar(astat[:, 0:4], astat[:, 0:4], 1.0 / 16,
                                                     None, ALU.mult),
                 r=[astat[:, 0:4]], w=[astat[:, 0:4]])
            S.op("vector", lambda e: e.tensor_tensor(a3(aorig[:, :]), a3(acat[:, :]),
                                                     bc4(astat[:, 0:4]), ALU.subtract),
                 r=[acat[:, :], astat[:, 0:4]], w=[aorig[:, :]])
            S.op("vector", lambda e: e.tensor_tensor(a3(ascr[:, :]), a3(aorig[:, :]),
                                                     a3(aorig[:, :]), ALU.mult),
                 r=[aorig[:, :]], w=[ascr[:, :]])
            S.op("vector", lambda e: e.tensor_reduce(astat[:, 4:8], a3(ascr[:, :]), AXX, ALU.add),
                 r=[ascr[:, :]], w=[astat[:, 4:8]])
            S.op("scalar", lambda e: e.activation(astat[:, 8:12], astat[:, 4:8], AF.Sqrt,
                                                  bias=epscol[:, :], scale=1.0 / 16),
                 r=[astat[:, 4:8], epscol[:, :]], w=[astat[:, 8:12]])
            S.op("vector", lambda e: e.reciprocal(astat[:, 8:12], astat[:, 8:12]),
                 r=[astat[:, 8:12]], w=[astat[:, 8:12]])
            S.op("vector", lambda e: e.tensor_tensor(a3(aorig[:, :]), a3(aorig[:, :]),
                                                     bc4(astat[:, 8:12]), ALU.mult),
                 r=[aorig[:, :], astat[:, 8:12]], w=[aorig[:, :]])
            S.op("vector", lambda e: e.tensor_scalar(aorig[:, :], aorig[:, :], 0.5, None, ALU.mult),
                 r=[aorig[:, :]], w=[aorig[:, :]])
            S.op("vector", lambda e: e.tensor_reduce(astat[:, 12:16], a3(aorig[:, :]), AXX, ALU.max),
                 r=[aorig[:, :]], w=[astat[:, 12:16]])
            S.op("vector", lambda e: e.tensor_tensor(a3(aorig[:, :]), a3(aorig[:, :]),
                                                     bc4(astat[:, 12:16]), ALU.subtract),
                 r=[aorig[:, :], astat[:, 12:16]], w=[aorig[:, :]])
            S.op("vector", lambda e: e.tensor_copy(asrt[:, :], aorig[:, :]),
                 r=[aorig[:, :]], w=[asrt[:, :]])
            def sap(buf, off, dims):
                pairs = [[64, 128]] + [[s, c] for (s, c) in dims if c > 1]
                return bass.AP(buf.tensor if hasattr(buf, "tensor") else buf, off, pairs)
            stages = []
            kk = 2
            while kk <= 16:
                jj = kk // 2
                while jj >= 1:
                    stages.append((kk, jj)); jj //= 2
                kk *= 2
            for (kk, jj) in stages:
                def mk(kk=kk, jj=jj):
                    def f(e):
                        u = 16 // kk
                        g = max(1, kk // (2 * jj))
                        last = None
                        if u == 1:
                            dims = [(16, 4), (2 * jj, g), (1, jj)]
                            lo = sap(asrt[:, :], 0, dims); hi = sap(asrt[:, :], jj, dims)
                            e.tensor_tensor(sap(ascr[:, :], 0, dims), lo, hi, ALU.max)
                            last = e.tensor_tensor(sap(ascr[:, :], jj, dims), lo, hi, ALU.min)
                        else:
                            for half in range(2):
                                o0 = kk * half
                                dims = [(16, 4), (2 * kk, u // 2), (2 * jj, g), (1, jj)]
                                lo = sap(asrt[:, :], o0, dims)
                                hi = sap(asrt[:, :], o0 + jj, dims)
                                omax = sap(ascr[:, :], o0 if half == 0 else o0 + jj, dims)
                                omin = sap(ascr[:, :], o0 + jj if half == 0 else o0, dims)
                                e.tensor_tensor(omax, lo, hi, ALU.max)
                                last = e.tensor_tensor(omin, lo, hi, ALU.min)
                        return last
                    return f
                S.op("vector", mk(), r=[asrt[:, :]], w=[ascr[:, :]])
                S.op("vector", lambda e: e.tensor_copy(asrt[:, :], ascr[:, :]),
                     r=[ascr[:, :]], w=[asrt[:, :]])
            S.op("vector", lambda e: e.tensor_copy(zs_keep[:, :], asrt[:, :]),
                 r=[asrt[:, :]], w=[zs_keep[:, :]])
            S.op("vector", lambda e: e.tensor_copy(ascr[:, :], asrt[:, :]),
                 r=[asrt[:, :]], w=[ascr[:, :]])
            S.op("vector", lambda e: e.tensor_tensor(acat[:, :], asrt[:, :], asrt[:, :], ALU.mult),
                 r=[asrt[:, :]], w=[acat[:, :]])
            for buf in (ascr, acat):
                for ddd in (1, 2, 4, 8):
                    def f2(e, buf=buf, ddd=ddd):
                        t = a3(buf[:, :]); s = a3(asrt[:, :])
                        e.tensor_copy(s[:, :, 0:ddd], t[:, :, 0:ddd])
                        return e.tensor_tensor(s[:, :, ddd:16], t[:, :, ddd:16],
                                               t[:, :, 0:16 - ddd], ALU.add)
                    S.op("vector", f2, r=[buf[:, :]], w=[asrt[:, :]])
                    S.op("vector", lambda e, buf=buf: e.tensor_copy(buf[:, :], asrt[:, :]),
                         r=[asrt[:, :]], w=[buf[:, :]])
            S.op("vector", lambda e: e.tensor_tensor(a3(ascr[:, :]), a3(ascr[:, :]),
                                                     bck(invk16[:, :]), ALU.mult),
                 r=[ascr[:, :], invk16[:, :]], w=[ascr[:, :]])
            S.op("vector", lambda e: e.tensor_tensor(a3(acat[:, :]), a3(acat[:, :]),
                                                     bck(invk16[:, :]), ALU.mult),
                 r=[acat[:, :], invk16[:, :]], w=[acat[:, :]])
            S.op("vector", lambda e: e.tensor_tensor(a3(asrt[:, :]), a3(ascr[:, :]),
                                                     a3(ascr[:, :]), ALU.mult),
                 r=[ascr[:, :]], w=[asrt[:, :]])
            S.op("vector", lambda e: e.tensor_tensor(a3(acat[:, :]), a3(acat[:, :]),
                                                     a3(asrt[:, :]), ALU.subtract),
                 r=[acat[:, :], asrt[:, :]], w=[acat[:, :]])
            S.op("vector", lambda e: e.tensor_tensor(a3(acat[:, :]), bck(invk16[:, :]),
                                                     a3(acat[:, :]), ALU.subtract),
                 r=[acat[:, :], invk16[:, :]], w=[acat[:, :]])
            S.op("vector", lambda e: e.tensor_scalar(acat[:, :], acat[:, :], 0.0, None, ALU.max),
                 r=[acat[:, :]], w=[acat[:, :]])
            S.op("scalar", lambda e: e.activation(acat[:, :], acat[:, :], AF.Sqrt),
                 r=[acat[:, :]], w=[acat[:, :]])
            S.op("vector", lambda e: e.tensor_tensor(ascr[:, :], ascr[:, :], acat[:, :],
                                                     ALU.subtract),
                 r=[ascr[:, :], acat[:, :]], w=[ascr[:, :]])
            S.op("vector", lambda e: e.tensor_tensor(acat[:, :], ascr[:, :], zs_keep[:, :],
                                                     ALU.is_le),
                 r=[ascr[:, :], zs_keep[:, :]], w=[acat[:, :]])
            S.op("vector", lambda e: e.tensor_reduce(astat[:, 16:20], a3(acat[:, :]), AXX, ALU.add),
                 r=[acat[:, :]], w=[astat[:, 16:20]])
            S.op("vector", lambda e: e.tensor_tensor(a3(acat[:, :]), bck(pos16[:, :]),
                                                     bc4(astat[:, 16:20]), ALU.is_le),
                 r=[pos16[:, :], astat[:, 16:20]], w=[acat[:, :]])
            S.op("vector", lambda e: e.tensor_copy(asrt[:, :], acat[:, :]),
                 r=[acat[:, :]], w=[asrt[:, :]])
            S.op("vector", lambda e: e.tensor_tensor(a3(acat[:, :])[:, :, 0:15],
                                                     a3(asrt[:, :])[:, :, 0:15],
                                                     a3(asrt[:, :])[:, :, 1:16], ALU.subtract),
                 r=[asrt[:, :]], w=[acat[:, :]])
            S.op("vector", lambda e: e.tensor_tensor(ascr[:, :], ascr[:, :], acat[:, :], ALU.mult),
                 r=[ascr[:, :], acat[:, :]], w=[ascr[:, :]])
            S.op("vector", lambda e: e.tensor_reduce(astat[:, 20:24], a3(ascr[:, :]), AXX, ALU.add),
                 r=[ascr[:, :]], w=[astat[:, 20:24]])
            S.op("vector", lambda e: e.tensor_tensor(a3(acat[:, :]), a3(aorig[:, :]),
                                                     bc4(astat[:, 20:24]), ALU.subtract),
                 r=[aorig[:, :], astat[:, 20:24]], w=[acat[:, :]])
            S.op("vector", lambda e: e.tensor_scalar(acat[:, :], acat[:, :], 0.0, None, ALU.max),
                 r=[acat[:, :]], w=[acat[:, :]])
            S.op("vector", lambda e: e.tensor_tensor(acat[:, :], acat[:, :], acat[:, :], ALU.mult),
                 r=[acat[:, :]], w=[acat[:, :]])

        def ffn_prelude_a(q):
            for tcq in range(4):
                tc = 4 * q + tcq
                ln_and_transpose(xres[:, 1024 * tc:1024 * tc + 1024], lntmp2, x2q, tcq)
            for tcq in range(4):
                for k in range(8):
                    xa = x2q[:, 512 * k + 128 * tcq:512 * k + 128 * tcq + 128]
                    S.op("tensor", lambda e, xa=xa, k=k, tcq=tcq: e.matmul(
                        bank(6, 0, 128, 64)[:, 16 * tcq:16 * tcq + 16], xa,
                        wproj[:, 16 * k:16 * k + 16], start=(k == 0), stop=(k == 7)),
                        r=[xa, wproj[:, 16 * k:16 * k + 16]],
                        w=[bank(6, 0, 128, 64)[:, 16 * tcq:16 * tcq + 16]])
            S.op("vector", lambda e: e.tensor_copy(acat[:, :], bank(6, 0, 128, 64)),
                 r=[bank(6, 0, 128, 64)], w=[acat[:, :]])
        def ffn_prelude_b():
            S.op("vector", lambda e: e.tensor_copy(lntmp2[:, 0:64], acat[:, :]),
                 r=[acat[:, :]], w=[lntmp2[:, 0:64]])
            b = 2 + tslot[0] % 2; tslot[0] += 1
            tb = tbank(b)
            for tcq in range(4):
                s_ = lntmp2[:, 16 * tcq:16 * tcq + 16]
                S.op("tensor", lambda e, s_=s_, tcq=tcq, tb=tb: e.transpose(
                    tb[0:16, 128 * tcq:128 * tcq + 128], s_, ident[:, :]),
                    r=[s_, ident[:, :]], w=[tb[:, 128 * tcq:128 * tcq + 128]])
            S.op("vector", lambda e, tb=tb: e.tensor_copy(afm[0:16, :], tb[0:16, :]),
                 r=[tb], w=[afm[0:16, :]])
        def emit_g(wc, gb, cg):
            for r_ in range(4):
                wca = wc[0:16, 128 * r_:128 * r_ + 128]
                S.op("tensor", lambda e, wca=wca, cg=cg: e.matmul(
                    bank(cg), wca, afm[0:16, :], start=True, stop=True),
                    r=[wca, afm[0:16, :]], w=[bank(cg)])
                S.op("vector", lambda e, gb=gb, r_=r_, cg=cg: e.tensor_copy(
                    gb[:, 512 * r_:512 * r_ + 512], bank(cg)),
                    r=[bank(cg)], w=[gb[:, 512 * r_:512 * r_ + 512]])
        z1r = hfm[:, 0:2048]
        ffn_prelude_a(0)
        emit_entmax()
        for q in range(NQ):
            if q == 0:
                for r_ in range(4):
                    for k in range(8):
                        wa = wwin1[:, 512 * k + 128 * r_:512 * k + 128 * r_ + 128]
                        S.op("tensor", lambda e, wa=wa, r_=r_, k=k: e.matmul(
                            bank(r_), wa, x2q[:, 512 * k:512 * k + 512],
                            start=(k == 0), stop=(k == 7)),
                            r=[wa, x2q[:, 512 * k:512 * k + 512]], w=[bank(r_)])
                    S.op("vector", lambda e, r_=r_: e.tensor_scalar(
                        z1r[:, 512 * r_:512 * r_ + 512], bank(r_), wb[:, r_:r_ + 1],
                        None, ALU.add),
                        r=[bank(r_), wb[:, r_:r_ + 1]],
                        w=[z1r[:, 512 * r_:512 * r_ + 512]])
                ffn_prelude_b()
                emit_g(wc1, g1b, 6)
                emit_g(wc2, g2b, 7)
                for r_ in range(4):
                    S.op("vector", lambda e, r_=r_: e.tensor_tensor(
                        z1g[:, 512 * r_:512 * r_ + 512], z1r[:, 512 * r_:512 * r_ + 512],
                        g1b[:, 512 * r_:512 * r_ + 512], ALU.mult),
                        r=[z1r[:, 512 * r_:512 * r_ + 512], g1b[:, 512 * r_:512 * r_ + 512]],
                        w=[z1g[:, 512 * r_:512 * r_ + 512]])
            else:
                emit_g(wc1, g1b, 6)
                emit_g(wc2, g2b, 7)
                for r_ in range(4):
                    for k in range(8):
                        wa = wwin1[:, 512 * k + 128 * r_:512 * k + 128 * r_ + 128]
                        S.op("tensor", lambda e, wa=wa, r_=r_, k=k: e.matmul(
                            bank(r_), wa, x2q[:, 512 * k:512 * k + 512],
                            start=(k == 0), stop=(k == 7)),
                            r=[wa, x2q[:, 512 * k:512 * k + 512]], w=[bank(r_)])
                    S.op("vector", lambda e, r_=r_: e.scalar_tensor_tensor(
                        z1g[:, 512 * r_:512 * r_ + 512], bank(r_), wb[:, r_:r_ + 1],
                        g1b[:, 512 * r_:512 * r_ + 512], ALU.add, ALU.mult),
                        r=[bank(r_), wb[:, r_:r_ + 1], g1b[:, 512 * r_:512 * r_ + 512]],
                        w=[z1g[:, 512 * r_:512 * r_ + 512]])
            if q + 1 < NQ:
                ffn_prelude_a(q + 1)
            for o in range(32):
                pb = 4 + o % 2
                for r_ in range(4):
                    wa = wwout1[:, 4096 * r_ + 128 * o:4096 * r_ + 128 * o + 128]
                    S.op("tensor", lambda e, wa=wa, r_=r_, pb=pb: e.matmul(
                        bank(pb), wa, z1g[:, 512 * r_:512 * r_ + 512],
                        start=(r_ == 0), stop=(r_ == 3)),
                        r=[wa, z1g[:, 512 * r_:512 * r_ + 512]], w=[bank(pb)])
                S.op("scalar", lambda e, o=o, pb=pb: e.activation(
                    hfm[:, 512 * o:512 * o + 512], bank(pb), AF.Gelu),
                    r=[bank(pb)], w=[hfm[:, 512 * o:512 * o + 512]])
            if "hfm" in taps and q == 0:
                tapcp("hfm", hfm[:, :])
            if q + 1 < NQ:
                emit_entmax()
            for r_ in range(4):
                for k in range(32):
                    wa = wwin2[:, 512 * k + 128 * r_:512 * k + 128 * r_ + 128]
                    S.op("tensor", lambda e, wa=wa, r_=r_, k=k: e.matmul(
                        bank(r_), wa, hfm[:, 512 * k:512 * k + 512],
                        start=(k == 0), stop=(k == 31)),
                        r=[wa, hfm[:, 512 * k:512 * k + 512]], w=[bank(r_)])
                S.op("vector", lambda e, r_=r_: e.scalar_tensor_tensor(
                    z2g[:, 512 * r_:512 * r_ + 512], bank(r_), wb[:, 4 + r_:5 + r_],
                    g2b[:, 512 * r_:512 * r_ + 512], ALU.add, ALU.mult),
                    r=[bank(r_), wb[:, 4 + r_:5 + r_], g2b[:, 512 * r_:512 * r_ + 512]],
                    w=[z2g[:, 512 * r_:512 * r_ + 512]])
            if q + 1 < NQ:
                ffn_prelude_b()
            for tt in range(4):
                pb = 4 + 2 * (tt % 2)
                for r_ in range(4):
                    za = z2g[:, 512 * r_ + 128 * tt:512 * r_ + 128 * tt + 128]
                    for nb in range(2):
                        S.op("tensor", lambda e, za=za, r_=r_, nb=nb, pb=pb: e.matmul(
                            bank(pb + nb), za,
                            wwout2[:, 1024 * r_ + 512 * nb:1024 * r_ + 512 * nb + 512],
                            start=(r_ == 0), stop=(r_ == 3)),
                            r=[za, wwout2[:, 1024 * r_ + 512 * nb:1024 * r_ + 512 * nb + 512]],
                            w=[bank(pb + nb)])
                tc = 4 * q + tt
                pspan = psum[:, 512 * pb:512 * pb + 1024]
                inv, ngm = ln_cols(pspan)
                S.op("vector", lambda e, pspan=pspan, inv=inv, ngm=ngm: e.tensor_scalar(
                    fout[:, :], pspan, inv, ngm, ALU.mult, ALU.add),
                     r=[pspan, inv, ngm], w=[fout[:, :]])
                xt = xres[:, 1024 * tc:1024 * tc + 1024]
                S.op("vector", lambda e, xt=xt: e.tensor_tensor(fout[:, :], fout[:, :], xt,
                                                                ALU.add),
                     r=[fout[:, :], xt], w=[fout[:, :]])
                S.dma("sync", out[128 * tc:128 * tc + 128, :], fout[:, :])
    S.finalize()
    alld = [(c, c.n) for name, c in S.chans.items() if name.startswith("dq_") and c.n > 0]
    tail = Op("sync", None, [], [], [], None)
    tail.waits = list(alld)
    S.ops.append(tail)
    tail2 = Op("tensor", None, [], [], [], None)
    tail2.waits = list(alld)
    S.ops.append(tail2)
    S.check()
    with nc.Block() as block:
        S.emit(nc, block, es)
    if stage >= 5:
        esB.close()
    es.close()
    return nc, tapd


def kernel(**inputs):
    x = np.asarray(inputs["L0_attn_hidden_states"])
    B, T, _ = x.shape
    nc, _ = build_program(T=T)
    in_maps = [host_inputs(inputs, b, T) for b in range(B)]
    res = run_bass_kernel_spmd(nc, in_maps, core_ids=list(range(8)))
    return np.stack([r["out"] for r in res.results]).astype(np.float32)


# version 34
# speedup vs baseline: 1.0658x; 1.0134x over previous
import numpy as np
import ml_dtypes
from contextlib import ExitStack

import concourse.bass as bass
import concourse.mybir as mybir
from concourse.bass_utils import run_bass_kernel_spmd

F32 = mybir.dt.float32
BF16 = mybir.dt.bfloat16
FP16 = mybir.dt.float16
AF = mybir.ActivationFunctionType
ALU = mybir.AluOpType
AXX = mybir.AxisListType.X
BF = ml_dtypes.bfloat16

D = 1024; DIN = 2048; NH = 32; PD = 64
Q = 128; QT = 512
EPS = 1e-5; NEG = -1e30

ENGINES = ("sync", "scalar", "vector", "tensor", "gpsimd")

def _esz(dt):
    try:
        return dt.itemsize
    except Exception:
        return {F32: 4, BF16: 2, FP16: 2}.get(dt, 4)

def ap_range(ap):
    th = ap.tensor
    t = th.name
    esz = _esz(ap.dtype)
    pairs = list(ap.ap)
    onchip = "DRAM" not in str(getattr(ap, "space", "")).upper()
    lo = hi = ap.offset * esz
    dims = pairs[1:] if onchip else pairs
    for step, count in dims:
        if count <= 1:
            continue
        d = step * (count - 1) * esz
        if d >= 0:
            hi += d
        else:
            lo += d
    if onchip:
        pitch = int(th.shape[-1]) * _esz(th.dtype)
        base = (lo // pitch) * pitch
        lo -= base
        hi -= base
        if hi >= pitch:
            lo, hi = 0, pitch
    return (t, lo, hi + esz)

class Op:
    __slots__ = ("eng", "fn", "reads", "writes", "xwaits", "dmach", "dmaval",
                 "mark", "waits", "idx", "barrier_inc")
    def __init__(self, eng, fn, reads, writes, xwaits, dmach):
        self.eng = eng; self.fn = fn
        self.reads = reads; self.writes = writes
        self.xwaits = list(xwaits)
        self.dmach = dmach
        self.dmaval = 0
        self.mark = None
        self.waits = []
        self.idx = 0
        self.barrier_inc = None

class Ch:
    __slots__ = ("name", "n", "sem")
    def __init__(self, name):
        self.name = name; self.n = 0; self.sem = None

class Sched:
    def __init__(self):
        self.ops = []
        self.chans = {}
        self.mark_ch = {e: self.chan("mk_" + e) for e in ENGINES}

    def chan(self, name):
        if name not in self.chans:
            self.chans[name] = Ch(name)
        return self.chans[name]

    DMA_POOL = 10

    def op(self, eng, fn, r=(), w=(), waits=(), dma=False, tag=None):
        reads = [ap_range(a) for a in r if a is not None]
        writes = [ap_range(a) for a in w if a is not None]
        ch = None
        o = Op(eng, fn, reads, writes, waits, ch)
        if dma:
            if not hasattr(self, "dmacnt"):
                self.dmacnt = {}
            i = self.dmacnt.get(eng, 0)
            self.dmacnt[eng] = i + 1
            ch = self.chan(f"dq_{eng}_{i % self.DMA_POOL}")
            if ch.n > 0:
                o.xwaits.append((ch, ch.n))
            ch.n += 16
            o.dmach = ch
            o.dmaval = ch.n
        self.ops.append(o)
        return o

    def dma(self, eng, dst, src, waits=(), tag=None, slow=False):
        if slow:
            fn = lambda e, d=dst, s=src: e.dma_start(out=d, in_=s,
                                                     allow_slow_non_contiguous=True)
        else:
            fn = lambda e, d=dst, s=src: e.dma_start(out=d, in_=s)
        return self.op(eng, fn, r=[src], w=[dst], waits=waits, dma=True, tag=tag)

    def barrier(self):
        bc = {e: self.chan("bar_" + e) for e in ENGINES}
        snap = {}
        for e in ENGINES:
            fn = lambda eng: eng.nop()
            o = self.op(e, fn)
            o.barrier_inc = bc[e]
            bc[e].n += 1
            snap[e] = bc[e].n
            o.xwaits = list(o.xwaits)
        for e in ENGINES:
            o = self.op(e, None,
                        waits=[(bc[e2], snap[e2]) for e2 in ENGINES if e2 != e])
        self._bar_pending = getattr(self, "_bar_pending", [])

    def finalize(self):
        writer = {}
        readers = {}
        for o in self.ops:
            deps = set()
            for t, lo, hi in o.reads:
                for (wl, wh), wo in writer.get(t, ()):
                    if lo < wh and hi > wl:
                        deps.add(wo)
            for t, lo, hi in o.writes:
                for (wl, wh), wo in writer.get(t, ()):
                    if lo < wh and hi > wl:
                        deps.add(wo)
                for (rl, rh), ro in readers.get(t, ()):
                    if lo < rh and hi > rl:
                        deps.add(ro)
            for dep in deps:
                if dep is o:
                    continue
                if dep.dmach is not None:
                    o.waits.append((dep.dmach, dep.dmaval))
                elif dep.eng != o.eng:
                    if dep.mark is None:
                        dep.mark = True
                    o.waits.append(("MARK", dep))
                else:
                    if o.eng == "tensor":
                        continue
                    spans = [hi - lo for (_, lo, hi) in dep.writes] or [0]
                    if max(spans) < 2048:
                        if dep.mark is None:
                            dep.mark = True
                        o.waits.append(("MARK", dep))
            for t, lo, hi in o.writes:
                wl_ = [(rg, wo) for (rg, wo) in writer.get(t, ())
                       if not (lo <= rg[0] and hi >= rg[1])]
                wl_.append(((lo, hi), o))
                writer[t] = wl_
                rl_ = [(rg, ro) for (rg, ro) in readers.get(t, ())
                       if not (lo <= rg[0] and hi >= rg[1])]
                readers[t] = rl_
            for t, lo, hi in o.reads:
                lst = readers.setdefault(t, [])
                key = (o.eng, lo, hi)
                for i_, (rg, ro) in enumerate(lst):
                    if (ro.eng, rg[0], rg[1]) == key:
                        lst[i_] = ((lo, hi), o)
                        break
                else:
                    lst.append(((lo, hi), o))
        cnt = {e: 0 for e in ENGINES}
        for o in self.ops:
            if getattr(o, "barrier_inc", None) is not None:
                pass
            if o.mark:
                cnt[o.eng] += 1
                o.mark = cnt[o.eng]
        for o in self.ops:
            ws = {}
            for item in o.waits:
                if isinstance(item, tuple) and item[0] == "MARK":
                    dep = item[1]
                    c = self.mark_ch[dep.eng]
                    v = dep.mark
                else:
                    c, v = item
                if v and v > 0:
                    ws[c] = max(ws.get(c, 0), v)
            for c, v in o.xwaits:
                if v and v > 0:
                    ws[c] = max(ws.get(c, 0), v)
            o.waits = list(ws.items())
        self.mark_totals = cnt

    def check(self):
        vals = {c: 0 for c in self.chans.values()}
        qs = {}
        for i, o in enumerate(self.ops):
            o.idx = i
            qs.setdefault(o.eng, []).append(o)
        heads = {e: 0 for e in qs}
        done, prog = 0, True
        while prog:
            prog = False
            for e, lst in qs.items():
                while heads[e] < len(lst):
                    o = lst[heads[e]]
                    if any(vals[c] < v for c, v in o.waits):
                        break
                    if o.dmach is not None:
                        vals[o.dmach] = max(vals[o.dmach], o.dmaval)
                    if o.mark:
                        vals[self.mark_ch[o.eng]] = max(vals[self.mark_ch[o.eng]], o.mark)
                    bi = getattr(o, "barrier_inc", None)
                    if bi is not None:
                        vals[bi] = vals[bi] + 1
                    heads[e] += 1; done += 1; prog = True
        if done != len(self.ops):
            stuck = []
            for e, lst in qs.items():
                if heads[e] < len(lst):
                    o = lst[heads[e]]
                    bad = [(c.name, v, vals[c]) for c, v in o.waits if vals[c] < v]
                    stuck.append((e, o.idx, bad[:3]))
            raise RuntimeError(f"deadlock, {len(self.ops) - done} stuck: {stuck[:5]}")

    def emit(self, nc, block, es):
        for c in self.chans.values():
            c.sem = es.enter_context(nc.semaphore(c.name))
        for ename in ENGINES:
            ops = [o for o in self.ops if o.eng == ename]
            def body(eng, ops=ops):
                for o in ops:
                    for c, v in o.waits:
                        eng.wait_ge(c.sem, v)
                    if o.fn is None:
                        continue
                    ins = o.fn(eng)
                    if o.dmach is not None:
                        ins.then_inc(o.dmach.sem, 16)
                    if o.mark:
                        ins.then_inc(self.mark_ch[o.eng].sem, 1)
                    bi = getattr(o, "barrier_inc", None)
                    if bi is not None:
                        ins.then_inc(bi.sem, 1)
            getattr(block, ename)(body)

MORDER = [34, 32, 0, 33, 1, 16, 17]
for _i in range(14):
    MORDER += [18 + _i, 2 + _i]
WIN_TILE_OFF = []
_off = 0
for _m in MORDER:
    WIN_TILE_OFF.append(_off)
    _off += 8 * 128 * (128 if _m < 34 else 32)
WIN_TOTAL = _off

def _tile_win(W):
    blobs = []
    for m in MORDER:
        w = 128 if m < 34 else 32
        t = W[:, m * 128:m * 128 + w].reshape(8, 128, w).transpose(1, 0, 2)
        blobs.append(np.ascontiguousarray(t).reshape(-1))
    return np.concatenate(blobs)

def _consts():
    ident = np.eye(128, dtype=np.float32).astype(BF)
    id32x4 = np.tile(np.eye(32, dtype=np.float32), (4, 1)).astype(BF)
    m = np.where(np.arange(Q)[:, None] > np.arange(Q)[None, :],
                 np.float32(NEG), np.float32(0.0))
    mask_rep = np.tile(m, (1, 8)).astype(BF)
    e = np.zeros((32, 32 * Q), np.float32)
    for h in range(32):
        e[h, h * Q:(h + 1) * Q] = 1.0
    e32x2 = np.concatenate([e, e], 0).astype(np.float16)
    ep = np.zeros((32, 32 * PD), np.float32)
    for h in range(32):
        ep[h, h * PD:(h + 1) * PD] = 1.0
    e32p = np.concatenate([ep, ep], 0).astype(np.float16)
    ones_bf = np.ones((1, 128), dtype=BF)
    ones_f16 = np.ones((1, 128), dtype=np.float16)
    identj = np.eye(128, dtype=np.float32)[:, ::-1].copy().astype(BF)
    return dict(ident=ident, identj=identj, id32x4=id32x4, mask_rep=mask_rep,
                e32x2=e32x2, e32p=e32p, ones_bf=ones_bf, ones_f16=ones_f16)

def host_inputs(inputs, b, T):
    im = {}
    im["x"] = np.ascontiguousarray(np.asarray(inputs["L0_attn_hidden_states"])[b])
    for d, pre in (("f", "fwd"), ("b", "bwd")):
        g = lambda n: np.asarray(inputs[f"L0_attn_{pre}_{n}"])
        im[f"win_{d}"] = _tile_win(g("W_in"))
        im[f"wout_{d}"] = np.ascontiguousarray(g("W_out"))
        im[f"convw_{d}"] = np.ascontiguousarray(g("conv_w").T.reshape(18, 128, 4))
        im[f"convb_{d}"] = np.ascontiguousarray(g("conv_b").reshape(18, 128, 1))
        im[f"dtb_{d}"] = np.ascontiguousarray(g("dt_bias").reshape(32, 1))
        im[f"alog_{d}"] = np.ascontiguousarray(g("A_log").reshape(32, 1))
        im[f"drep_{d}"] = np.ascontiguousarray(np.repeat(g("D"), 64).reshape(1, 2048))
        im[f"normw_{d}"] = np.ascontiguousarray(g("norm_w").reshape(16, 128, 1))
    gf = lambda n: np.asarray(inputs[f"L0_ffn_{n}"])
    im["fproj"] = np.ascontiguousarray(gf("proj").reshape(8, 128, 16))
    im["fc1"] = np.ascontiguousarray(gf("C1").reshape(16, 4, 128).transpose(1, 0, 2))
    im["fc2"] = np.ascontiguousarray(gf("C2").reshape(16, 4, 128).transpose(1, 0, 2))
    im["fwin1"] = np.ascontiguousarray(gf("Win1")[:1024].reshape(8, 128, 512))
    im["fb1"] = np.ascontiguousarray(gf("Win1")[1024].reshape(4, 128, 1))
    im["fwout1t"] = np.ascontiguousarray(gf("Wout1").T.reshape(4, 128, 4096))
    im["fwin2"] = np.ascontiguousarray(gf("Win2")[:4096].reshape(32, 128, 512))
    im["fb2"] = np.ascontiguousarray(gf("Win2")[4096].reshape(4, 128, 1))
    im["fwout2t"] = np.ascontiguousarray(gf("Wout2").T.reshape(4, 128, 1024))
    im.update(_consts())
    return im

def build_program(T=2048, taps=(), stage=99, dirs=("f", "b")):
    NT = T // 128
    NQ = T // QT
    taps = set(taps)

    nc = bass.Bass("TRN2", target_bir_lowering=False, detect_race_conditions=False)
    dr = {}
    def din(name, shape, dt=F32):
        dr[name] = nc.dram_tensor(name, list(shape), dt, kind="ExternalInput").ap()
    din("x", (T, D))
    for d in ("f", "b"):
        din(f"win_{d}", (WIN_TOTAL,))
        din(f"wout_{d}", (DIN, D))
        din(f"convw_{d}", (18, 128, 4)); din(f"convb_{d}", (18, 128, 1))
        din(f"dtb_{d}", (32, 1)); din(f"alog_{d}", (32, 1))
        din(f"drep_{d}", (1, 2048)); din(f"normw_{d}", (16, 128, 1))
    din("fproj", (8, 128, 16)); din("fc1", (4, 16, 128)); din("fc2", (4, 16, 128))
    din("fwin1", (8, 128, 512)); din("fb1", (4, 128, 1))
    din("fwout1t", (4, 128, 4096))
    din("fwin2", (32, 128, 512)); din("fb2", (4, 128, 1))
    din("fwout2t", (4, 128, 1024))
    din("ident", (128, 128), BF16); din("identj", (128, 128), BF16)
    din("id32x4", (128, 32), BF16)
    din("mask_rep", (128, 1024), BF16)
    din("e32x2", (64, 4096), FP16); din("e32p", (64, 2048), FP16)
    din("ones_bf", (1, 128), BF16); din("ones_f16", (1, 128), FP16)
    out = nc.dram_tensor("out", [T, D], F32, kind="ExternalOutput").ap()
    wbf = {}
    for d_ in ("f", "b"):
        wbf[f"win_{d_}"] = nc.dram_tensor(f"win_{d_}_bf", [WIN_TOTAL], BF16).ap()
        wbf[f"wout_{d_}"] = nc.dram_tensor(f"wout_{d_}_bf", [DIN, D], BF16).ap()
    scr_cum = nc.dram_tensor("scr_cum", [2, 4, 32, 128], FP16).ap()
    scr_ncl = nc.dram_tensor("scr_ncl", [4, 32, 128], FP16).ap()

    tapd = {}
    def deftap(name, shape, dt=F32):
        if name in taps:
            tapd[name] = nc.dram_tensor("tap_" + name, list(shape), dt,
                                        kind="ExternalOutput").ap()
    deftap("x1q", (128, 8 * QT), BF16)
    deftap("ztm", (128, 4 * 2048), BF16)
    deftap("xbc", (128, 18 * 516), BF16)
    deftap("dtq", (66, QT), FP16)
    deftap("cumhl", (64, QT), FP16)
    deftap("Xtm", (128, 2048), BF16)
    deftap("Mpg", (128, 2048), BF16)
    deftap("ybuf", (128, 2048), BF16)
    deftap("yfm", (128, 16 * 512), BF16)
    deftap("Sst", (128, 2048), BF16)
    deftap("attn_f", (128, NT * 1024), BF16)
    deftap("attn", (128, NT * 1024), BF16)
    deftap("xmid", (128, NT * 1024), BF16)
    deftap("acat", (128, 64), F32)
    deftap("aorig", (128, 64), F32)
    deftap("zs_keep", (128, 64), F32)
    deftap("ascr", (128, 64), F32)
    deftap("astat", (128, 24), F32)
    deftap("hfm", (128, 32 * 512), BF16)

    S = Sched()
    es = ExitStack()
    sb = lambda n, shp, dt=F32: es.enter_context(nc.sbuf_tensor(n, list(shp), dt))

    xres = sb("xres", (128, NT * 1024), BF16)
    psum = es.enter_context(nc.psum_tensor("ps", [128, 4096], F32))
    def bank(b, p0=0, pn=128, w=512):
        return psum[p0:p0 + pn, 512 * b:512 * b + w]
    ident = sb("identS", (128, 128), BF16)
    identj = sb("identJS", (128, 128), BF16)
    epscol = sb("epscol", (128, 1))

    def tapcp(name, src, eng="sync"):
        t = tapd.get(name)
        if t is not None:
            S.dma(eng, t[:, :], src)

    statc = sb("statc", (128, 16 * 2))
    colc = sb("colc", (128, 2 * 2))
    lnslot = [0]
    def ln_cols(src_tile, cv=None):
        sl = lnslot[0] % 2; lnslot[0] += 1
        st = statc[:, 16 * sl:16 * sl + 16]
        if cv is None:
            cv = colc[:, 2 * sl:2 * sl + 2]
        w_ = src_tile.shape[-1]
        half = w_ // 2
        S.op("vector", lambda e: e.bn_stats(st[:, 0:6], src_tile[:, 0:half]),
             r=[src_tile], w=[st[:, 0:6]])
        S.op("vector", lambda e: e.bn_stats(st[:, 6:12], src_tile[:, half:w_]),
             r=[src_tile], w=[st[:, 6:12]])
        S.op("vector", lambda e: e.bn_aggr(st[:, 12:14], st[:, 0:12]),
             r=[st[:, 0:12]], w=[st[:, 12:14]])
        S.op("scalar", lambda e: e.activation(st[:, 14:15], st[:, 13:14], AF.Sqrt,
                                              bias=epscol[:, :], scale=1.0),
             r=[st[:, 12:14], epscol[:, :]], w=[st[:, 14:15]])
        S.op("vector", lambda e: e.reciprocal(cv[:, 0:1], st[:, 14:15]),
             r=[st[:, 14:15]], w=[cv[:, 0:1]])
        S.op("vector", lambda e: e.scalar_tensor_tensor(cv[:, 1:2], st[:, 12:13], -1.0,
                                                        cv[:, 0:1], ALU.mult, ALU.mult),
             r=[st[:, 12:14], cv[:, 0:1]], w=[cv[:, 1:2]])
        return cv[:, 0:1], cv[:, 1:2]

    tslot = [0]
    def tbank(b):
        return psum[:, 512 * b:512 * b + 256].bitcast(BF16)
    def transpose_pack(srcs, evac_eng, evac_fn, extra_w=(), rev=False, idn=None,
                       idn_r=(), base=2):
        b = base + tslot[0] % 2; tslot[0] += 1
        ba = tbank(b)
        if idn is None:
            idn = identj if rev else ident
        for j, s in enumerate(srcs):
            S.op("tensor", lambda e, s=s, j=j, ba=ba, idn=idn: e.transpose(
                ba[:, 128 * j:128 * j + 128], s, idn[:, :]),
                r=[s, idn[:, :]] + list(idn_r), w=[ba[:, 128 * j:128 * j + 128]])
        S.op(evac_eng, lambda e: evac_fn(e, ba), r=[ba], w=list(extra_w))

    def ln_apply_transpose(src_tm, inv, ngm, lntmp, dst_fm_q, tcq, rev=False):
        S.op("vector", lambda e, inv=inv, ngm=ngm, src_tm=src_tm: e.tensor_scalar(
            lntmp[:, 0:1024], src_tm, inv, ngm, ALU.mult, ALU.add),
             r=[src_tm, inv, ngm], w=[lntmp[:, 0:1024]])
        dcq = (3 - tcq) if rev else tcq
        for half in range(2):
            srcs = [lntmp[:, 128 * (4 * half + j):128 * (4 * half + j) + 128] for j in range(4)]
            dsts = [dst_fm_q[:, 512 * (4 * half + j) + 128 * dcq:
                             512 * (4 * half + j) + 128 * dcq + 128] for j in range(4)]
            def ev(e, ba, dsts=dsts):
                for j in range(4):
                    e.tensor_copy(dsts[j], ba[:, 128 * j:128 * j + 128])
                return e.engine_nop()
            transpose_pack(srcs, "vector", ev, extra_w=dsts, rev=rev, base=4)

    def ln_and_transpose(src_tm, lntmp, dst_fm_q, tcq, rev=False):
        inv, ngm = ln_cols(src_tm)
        S.op("vector", lambda e: e.tensor_scalar(lntmp[:, 0:1024], src_tm, inv, ngm,
                                                 ALU.mult, ALU.add),
             r=[src_tm, inv, ngm], w=[lntmp[:, 0:1024]])
        dcq = (3 - tcq) if rev else tcq
        for half in range(2):
            srcs = [lntmp[:, 128 * (4 * half + j):128 * (4 * half + j) + 128] for j in range(4)]
            dsts = [dst_fm_q[:, 512 * (4 * half + j) + 128 * dcq:
                             512 * (4 * half + j) + 128 * dcq + 128] for j in range(4)]
            def ev(e, ba, dsts=dsts):
                for j in range(4):
                    e.tensor_copy(dsts[j], ba[:, 128 * j:128 * j + 128])
                return e.engine_nop()
            transpose_pack(srcs, "vector", ev, extra_w=dsts, rev=rev)

    esA = ExitStack()
    sbA = lambda n, shp, dt=F32: esA.enter_context(nc.sbuf_tensor(n, list(shp), dt))
    attn = sbA("attn", (128, NT * 1024), BF16)
    x1q = sbA("x1q", (128, 8 * QT), BF16)
    lntmp = sbA("lntmp", (128, 1024), BF16)
    ztm = sbA("ztm", (128, 4 * 2048), BF16)
    zscr = sbA("zscr", (128, 2 * 512), BF16)
    xbc = sbA("xbc", (128, 18 * 516), BF16)
    stash = sbA("stash", (128, 18 * 4), BF16)
    dt_fm = sbA("dt_fm", (32, QT))
    lns1 = sbA("lns1", (32, QT))
    aA = sbA("aA", (32, QT)); cumA = sbA("cumA", (32, QT))
    cumB = lns1
    losc = sbA("losc", (32, QT), FP16)
    stack66 = sbA("stack66", (66, QT), FP16)
    cumhl = sbA("cumhl", (64, QT), FP16)
    rhs66 = sbA("rhs66", (66, 4096), FP16)
    Xtm2 = sbA("Xtm2", (128, 2 * 2048), BF16)
    Btm2 = sbA("Btm2", (128, 256), BF16)
    Gp = sbA("Gp", (128, 128), BF16)
    Epg = sbA("Epg", (128, 2 * 1024), BF16)
    Mpg = sbA("Mpg", (128, 2 * 1024), BF16)
    decrow = sbA("decrow", (1, 2048), FP16)
    wcol = sbA("wcol", (128, 32), BF16)
    expc = sbA("expc", (128, 2048), BF16)
    rmsc4 = expc[:, :].bitcast(F32)[:, 0:4]
    ybuf = sbA("ybuf", (128, 2048), BF16)
    rmsc = sbA("rmsc", (128, 2))
    yfm = sbA("yfm", (128, 16 * 512), BF16)
    Sst = sbA("Sst", (128, 2048), BF16)
    Stmp = sbA("Stmp", (128, 2048), BF16)
    yps = Stmp
    drow = Stmp[0:1, :]
    yscr = Stmp[:, :].bitcast(F32)
    winr = sbA("winr", (128, 2 * 2048), BF16)
    wor = sbA("wor", (128, 2 * 1024), BF16)
    dbc = sbA("dbc", (128, 2048), BF16)
    cvacc = sbA("cvacc", (128, 1024), BF16)
    maskS = sbA("maskS", (128, 1024), BF16)
    e32pS = sbA("e32pS", (64, 2048), FP16)
    onesS = sbA("onesS", (1, 128), BF16)
    onesH = sbA("onesH", (1, 128), FP16)

    ccol = sbA("ccol", (128, 18))
    wcol4 = sbA("wcol4", (128, 4 * 18))
    nwcol = sbA("nwcol", (128, 16))
    scol = sbA("scol", (32, 4))

    S.dma("sync", ident[:, :], dr["ident"][:, :])
    S.dma("sync", identj[:, :], dr["identj"][:, :])
    S.dma("sync", maskS[:, :], dr["mask_rep"][:, :])
    S.dma("sync", rhs66[0:64, :], dr["e32x2"][:, :])
    S.dma("sync", e32pS[:, :], dr["e32p"][:, :])
    S.dma("sync", onesS[:, :], dr["ones_bf"][:, :])
    S.dma("sync", onesH[:, :], dr["ones_f16"][:, :])
    S.dma("gpsimd", xres[:, 0:NT * 1024].rearrange("p (j d) -> p j d", d=1024),
          bass.AP(dr["x"].tensor, 0, [[1024, 128], [128 * 1024, NT], [1, 1024]]))
    for d_ in ("f", "b"):
        S.dma("gpsimd", wbf[f"win_{d_}"][:], dr[f"win_{d_}"][:])
        S.dma("gpsimd", wbf[f"wout_{d_}"][:, :], dr[f"wout_{d_}"][:, :])
    S.op("vector", lambda e: e.memset(stack66[64:66, :], 1.0), w=[stack66[64:66, :]])
    S.op("vector", lambda e: e.memset(epscol[:, :], float(EPS)), w=[epscol[:, :]])
    S.op("vector", lambda e: e.memset(xbc[:, :], 0.0), w=[xbc[:, :]])

    def mamba_dir(d, rev):
        S.dma("gpsimd", drow[:, :], dr[f"drep_{d}"][:, :])
        S.dma("sync", wcol4[:, :].rearrange("p (c k) -> p c k", k=4),
              bass.AP(dr[f"convw_{d}"].tensor, 0, [[4, 128], [512, 18], [1, 4]]))
        S.dma("sync", ccol[:, :].unsqueeze(2),
              bass.AP(dr[f"convb_{d}"].tensor, 0, [[1, 128], [128, 18], [1, 1]]), slow=True)
        S.dma("sync", nwcol[:, :].unsqueeze(2),
              bass.AP(dr[f"normw_{d}"].tensor, 0, [[1, 128], [128, 16], [1, 1]]), slow=True)
        S.dma("sync", scol[0:32, 0:1], dr[f"dtb_{d}"][:, :])
        S.dma("sync", scol[0:32, 1:2], dr[f"alog_{d}"][:, :])
        S.op("vector", lambda e: e.tensor_scalar(scol[0:32, 2:3], scol[0:32, 0:1], -1.0,
                                                 None, ALU.mult),
             r=[scol[0:32, 0:1]], w=[scol[0:32, 2:3]])
        S.op("scalar", lambda e: e.activation(scol[0:32, 3:4], scol[0:32, 1:2], AF.Exp),
             r=[scol[0:32, 1:2]], w=[scol[0:32, 3:4]])
        for j in range(4):
            S.op("tensor", lambda e, j=j: e.matmul(bank(j), onesS[:, :],
                                                   drow[:, 512 * j:512 * j + 512],
                                                   start=True, stop=True),
                 r=[onesS[:, :], drow[:, 512 * j:512 * j + 512]], w=[bank(j)])
        S.op("vector", lambda e: e.tensor_copy(dbc[:, 0:1024], psum[:, 0:1024]),
             r=[psum[:, 0:1024]], w=[dbc[:, 0:1024]])
        S.op("vector", lambda e: e.tensor_copy(dbc[:, 1024:2048], psum[:, 1024:2048]),
             r=[psum[:, 1024:2048]], w=[dbc[:, 1024:2048]])
        S.op("vector", lambda e: e.memset(Sst[:, :], 0.0), w=[Sst[:, :]])
        S.op("vector", lambda e: e.memset(stash[:, :], 0.0), w=[stash[:, :]])

        x1cv = Stmp[:, :].bitcast(F32)[:, 0:8]
        def emit_x1_stats(qq, tcq):
            oqq = (NQ - 1 - qq) if rev else qq
            tc = 4 * oqq + tcq
            ln_cols(xres[:, 1024 * tc:1024 * tc + 1024],
                    cv=x1cv[:, 2 * tcq:2 * tcq + 2])
        def emit_x1_apply(qq, tcq):
            oqq = (NQ - 1 - qq) if rev else qq
            tc = 4 * oqq + tcq
            ln_apply_transpose(xres[:, 1024 * tc:1024 * tc + 1024],
                               x1cv[:, 2 * tcq:2 * tcq + 1],
                               x1cv[:, 2 * tcq + 1:2 * tcq + 2],
                               lntmp, x1q, tcq, rev=rev)
        def build_x1(qq):
            for tcq in range(4):
                ln_and_transpose(xres[:, 1024 * (4 * ((NQ - 1 - qq) if rev else qq) + tcq):
                                      1024 * (4 * ((NQ - 1 - qq) if rev else qq) + tcq) + 1024],
                                 lntmp, x1q, tcq, rev=rev)
        carry_steps = []
        build_x1(0)
        for q in range(NQ):
            oq = (NQ - 1 - q) if rev else q
            if q + 1 < NQ:
                for tcq in range(4):
                    carry_steps.append(lambda qq=q + 1, tcq=tcq: emit_x1_stats(qq, tcq))
            def win_dma(s2):
                slot = s2 % 2
                wsrc = wbf[f"win_{d}"]
                if s2 == 0:
                    src = bass.AP(wsrc.tensor, WIN_TILE_OFF[0],
                                  [[8 * 32, 128], [1, 8 * 32]])
                    S.dma("sync", winr[:, 0:256], src)
                else:
                    p0 = 2 * s2 - 1
                    src = bass.AP(wsrc.tensor, WIN_TILE_OFF[p0],
                                  [[8 * 128, 128], [WIN_TILE_OFF[p0 + 1] - WIN_TILE_OFF[p0], 2],
                                   [1, 8 * 128]])
                    S.dma("sync",
                          winr[:, 2048 * slot:2048 * slot + 2048].rearrange(
                              "p (m kw) -> p m kw", m=2), src)

            def emit_conv_tile(ct, pb):
                bk = bank(pb)
                wc = lambda k: wcol4[:, 4 * ct + k:4 * ct + k + 1]
                S.op("vector", lambda e, ct=ct, bk=bk: e.tensor_scalar(
                    cvacc[:, 3:512], bk[:, 0:509], wc(0), None, ALU.mult),
                    r=[bk, wc(0)], w=[cvacc[:, 3:512]])
                S.op("vector", lambda e, ct=ct: e.tensor_scalar(
                    cvacc[:, 0:3], stash[:, 4 * ct:4 * ct + 3], wc(0), None, ALU.mult),
                    r=[stash[:, 4 * ct:4 * ct + 3], wc(0)], w=[cvacc[:, 0:3]])
                for k in (1, 2):
                    d_, s_ = (k % 2, (k + 1) % 2)
                    hw = 3 - k
                    S.op("vector", lambda e, ct=ct, k=k, d_=d_, s_=s_, bk=bk, hw=hw:
                         e.scalar_tensor_tensor(
                             cvacc[:, 512 * d_ + hw:512 * d_ + 512], bk[:, 0:509 + k],
                             wc(k), cvacc[:, 512 * s_ + hw:512 * s_ + 512],
                             ALU.mult, ALU.add),
                         r=[bk, wc(k), cvacc[:, 512 * s_:512 * s_ + 512]],
                         w=[cvacc[:, 512 * d_:512 * d_ + 512]])
                    S.op("vector", lambda e, ct=ct, k=k, d_=d_, s_=s_, hw=hw:
                         e.scalar_tensor_tensor(
                             cvacc[:, 512 * d_:512 * d_ + hw],
                             stash[:, 4 * ct + k:4 * ct + 3],
                             wc(k), cvacc[:, 512 * s_:512 * s_ + hw],
                             ALU.mult, ALU.add),
                         r=[stash[:, 4 * ct + k:4 * ct + 3], wc(k),
                            cvacc[:, 512 * s_:512 * s_ + hw]],
                         w=[cvacc[:, 512 * d_:512 * d_ + hw]])
                S.op("vector", lambda e, ct=ct, bk=bk: e.scalar_tensor_tensor(
                    cvacc[:, 512:1024], bk[:, 0:512], wc(3), cvacc[:, 0:512],
                    ALU.mult, ALU.add),
                    r=[bk, wc(3), cvacc[:, 0:512]], w=[cvacc[:, 512:1024]])
                S.op("vector", lambda e, ct=ct, bk=bk: e.tensor_copy(
                    stash[:, 4 * ct:4 * ct + 3], bk[:, 509:512]),
                    r=[bk], w=[stash[:, 4 * ct:4 * ct + 3]])
                S.op("scalar", lambda e, ct=ct: e.activation(
                    xbc[:, 516 * ct + 3:516 * ct + 515], cvacc[:, 512:1024], AF.Silu,
                    bias=ccol[:, ct:ct + 1], scale=1.0),
                    r=[cvacc[:, 512:1024], ccol[:, ct:ct + 1]],
                    w=[xbc[:, 516 * ct + 3:516 * ct + 515]])
                S.op("vector", lambda e, ct=ct: e.tensor_copy(xbc[:, 516 * ct:516 * ct + 3],
                                                              stash[:, 4 * ct:4 * ct + 3]),
                     r=[stash[:, 4 * ct:4 * ct + 3]], w=[xbc[:, 516 * ct:516 * ct + 3]])

            def dt_steps():
                def s0():
                    S.op("scalar", lambda e: e.activation(lns1[0:32, :], dt_fm[0:32, :], AF.Sigmoid,
                                                          bias=scol[0:32, 2:3], scale=-1.0),
                         r=[dt_fm[0:32, :], scol[0:32, 2:3]], w=[lns1[0:32, :]])
                    S.op("scalar", lambda e: e.activation(lns1[0:32, :], lns1[0:32, :], AF.Ln),
                         r=[lns1[0:32, :]], w=[lns1[0:32, :]])
                    S.op("vector", lambda e: e.tensor_scalar(aA[0:32, :], lns1[0:32, :],
                                                             scol[0:32, 3:4], None, ALU.mult),
                         r=[lns1[0:32, :], scol[0:32, 3:4]], w=[aA[0:32, :]])
                    S.op("scalar", lambda e: e.activation(dt_fm[0:32, :], lns1[0:32, :], AF.Ln, scale=-1.0),
                         r=[lns1[0:32, :]], w=[dt_fm[0:32, :]])
                    S.op("vector", lambda e: e.tensor_scalar(dt_fm[0:32, :], dt_fm[0:32, :], -80.0,
                                                             None, ALU.max),
                         r=[dt_fm[0:32, :]], w=[dt_fm[0:32, :]])
                yield s0
                bufs = [aA, cumA, cumB, cumA, cumB, cumA, cumB, cumA]
                for i, dd in enumerate((1, 2, 4, 8, 16, 32, 64)):
                    def s1(srcb=bufs[i], dstb=bufs[i + 1], dd=dd):
                        def stg(e, s=srcb, t=dstb, dd=dd):
                            s3 = s[0:32, :].rearrange("p (c t) -> p c t", t=Q)
                            d3 = t[0:32, :].rearrange("p (c t) -> p c t", t=Q)
                            e.tensor_copy(d3[:, :, 0:dd], s3[:, :, 0:dd])
                            return e.tensor_tensor(d3[:, :, dd:Q], s3[:, :, dd:Q],
                                                   s3[:, :, 0:Q - dd], ALU.add)
                        S.op("vector", stg, r=[srcb[0:32, :]], w=[dstb[0:32, :]])
                    yield s1
                cum = bufs[7]
                scw0 = bass.AP(scr_cum.tensor, 0, [[128, 32], [4096, 4], [1, 128]])
                scw1 = bass.AP(scr_cum.tensor, 16384, [[128, 32], [4096, 4], [1, 128]])
                r3 = lambda t: t.rearrange("h (c u) -> h c u", u=128)
                def s2():
                    S.op("vector", lambda e: e.tensor_copy(cumhl[0:32, :], cum[0:32, :]),
                         r=[cum[0:32, :]], w=[cumhl[0:32, :]])
                    S.op("vector", lambda e: e.tensor_tensor(losc[0:32, :], cum[0:32, :],
                                                             cumhl[0:32, :], ALU.subtract),
                         r=[cum[0:32, :], cumhl[0:32, :]], w=[losc[0:32, :]])
                    S.dma("sync", scw0, r3(cumhl[0:32, :]))
                    S.dma("sync", scw1, r3(losc[0:32, :]))
                yield s2
                def s3():
                    S.dma("sync", r3(cumhl[32:64, :]), scw1)
                    S.op("vector", lambda e: e.tensor_tensor(lns1[0:32, :], dt_fm[0:32, :],
                                                             cum[0:32, :], ALU.subtract),
                         r=[dt_fm[0:32, :], cum[0:32, :]], w=[lns1[0:32, :]])
                    S.op("vector", lambda e: e.tensor_copy(stack66[0:32, :], lns1[0:32, :]),
                         r=[lns1[0:32, :]], w=[stack66[0:32, :]])
                yield s3
                def s4():
                    S.op("vector", lambda e: e.tensor_tensor(losc[0:32, :], lns1[0:32, :],
                                                             stack66[0:32, :], ALU.subtract),
                         r=[lns1[0:32, :], stack66[0:32, :]], w=[losc[0:32, :]])
                    snw = bass.AP(scr_ncl.tensor, 0, [[128, 32], [4096, 4], [1, 128]])
                    S.dma("sync", snw, r3(losc[0:32, :]))
                    S.dma("sync", r3(stack66[32:64, :]), snw)
                    if "dtq" in taps and d == "f" and q == NQ - 1:
                        tapcp("dtq", stack66[:, :])
                    if "cumhl" in taps and d == "f" and q == NQ - 1:
                        tapcp("cumhl", cumhl[:, :])
                yield s4

            win_dma(0); win_dma(1)
            dt_it = dt_steps()
            pend_conv = []
            pend_ztr = [None]
            def emit_ztr(zm):
                half = 512 * (zm % 2)
                srcs = [zscr[:, half + 128 * j:half + 128 * j + 128] for j in range(4)]
                zdst = ztm[:, :].rearrange("p (c f) -> p c f", f=2048)[:, :, 128 * zm:128 * zm + 128]
                def zev(e, ba, zdst=zdst):
                    return e.activation(zdst, ba.rearrange("p (c f) -> p c f", f=128),
                                        AF.Silu)
                transpose_pack(srcs, "scalar", zev, base=4,
                               extra_w=[ztm[:, :].rearrange("p (c f) -> p c f", f=2048)[:, :, 128 * zm:128 * zm + 128]])
            for pos in range(35):
                m = MORDER[pos]
                w = 128 if m < 34 else 32
                s2 = 0 if pos == 0 else (pos + 1) // 2
                slot = s2 % 2
                sub = 0 if pos == 0 else (pos - 1) % 2
                base = 2048 * slot + 1024 * sub
                pb = pos % 4
                for k in range(8):
                    wt = winr[:, base + w * k:base + w * k + w]
                    rhs = x1q[:, 512 * k:512 * k + 512]
                    S.op("tensor", lambda e, wt=wt, rhs=rhs, k=k, pb=pb, w=w: e.matmul(
                        bank(pb, 0, w), wt, rhs, start=(k == 0), stop=(k == 7)),
                        r=[wt, rhs], w=[bank(pb, 0, w)])
                if m < 16 and pend_ztr[0] is not None:
                    emit_ztr(pend_ztr[0]); pend_ztr[0] = None
                if m == 34:
                    S.op("vector", lambda e, pb=pb: e.tensor_copy(dt_fm[0:32, :], bank(pb, 0, 32)),
                         r=[bank(pb, 0, 32)], w=[dt_fm[0:32, :]])
                elif m >= 16:
                    pass
                else:
                    half = 512 * (m % 2)
                    S.op("vector", lambda e, pb=pb, half=half: e.tensor_copy(
                        zscr[:, half:half + 512], bank(pb)),
                        r=[bank(pb)], w=[zscr[:, half:half + 512]])
                if pend_conv:
                    emit_conv_tile(*pend_conv.pop(0))
                if m >= 16 and m < 34:
                    pend_conv.append((m - 16, pb))
                if m < 16:
                    pend_ztr[0] = m
                step = next(dt_it, None)
                if step is not None:
                    step()
                elif carry_steps:
                    carry_steps.pop(0)()
                if (pos == 0 or sub == 1) and s2 + 2 <= 17:
                    win_dma(s2 + 2)
            if pend_ztr[0] is not None:
                emit_ztr(pend_ztr[0]); pend_ztr[0] = None
            while carry_steps:
                carry_steps.pop(0)()
            if q + 1 < NQ:
                for tcq in range(4):
                    emit_x1_apply(q + 1, tcq)
                    if pend_conv:
                        emit_conv_tile(*pend_conv.pop(0))
            while pend_conv:
                emit_conv_tile(*pend_conv.pop(0))

            if "xbc" in taps and d == "f" and q == NQ - 1:
                tapcp("xbc", xbc[:, :])
            if "ztm" in taps and d == "f" and q == NQ - 1:
                tapcp("ztm", ztm[:, :])
            if "x1q" in taps and d == "f" and q == NQ - 1:
                tapcp("x1q", x1q[:, :])
            if stage < 2:
                continue

            def emit_xb_transp(cq):
                cofs = 128 * cq
                Xtm = Xtm2[:, 2048 * (cq % 2):2048 * (cq % 2) + 2048]
                Bfm = xbc[:, 516 * 16 + 3 + cofs:516 * 16 + 3 + cofs + 128]
                for pk in range(4):
                    srcs = [xbc[:, 516 * (4 * pk + j) + 3 + cofs:516 * (4 * pk + j) + 3 + cofs + 128]
                            for j in range(4)]
                    def xev(e, ba, pk=pk, Xtm=Xtm):
                        return e.activation(Xtm[:, 512 * pk:512 * pk + 512], ba, AF.Copy)
                    transpose_pack(srcs, "scalar", xev, extra_w=[Xtm[:, 512 * pk:512 * pk + 512]])
                def bev(e, ba):
                    return e.activation(Btm2[:, 128 * (cq % 2):128 * (cq % 2) + 128],
                                        ba[:, 0:128], AF.Copy)
                transpose_pack([Bfm], "scalar", bev,
                               extra_w=[Btm2[:, 128 * (cq % 2):128 * (cq % 2) + 128]])
            for cq in range(4):
                cofs = 128 * cq
                csl = slice(cofs, cofs + 128)
                Xtm = Xtm2[:, 2048 * (cq % 2):2048 * (cq % 2) + 2048]
                Btm = Btm2[:, 128 * (cq % 2):128 * (cq % 2) + 128]
                Bfm = xbc[:, 516 * 16 + 3 + cofs:516 * 16 + 3 + cofs + 128]
                Cfm = xbc[:, 516 * 17 + 3 + cofs:516 * 17 + 3 + cofs + 128]
                S.dma("sync", rhs66[64:66, :],
                      bass.AP(scr_cum.tensor, 4096 * cq, [[16384, 2], [1, 4096]]))
                if cq == 0:
                    emit_xb_transp(0)
                S.op("tensor", lambda e, Bfm=Bfm, Cfm=Cfm: e.matmul(
                    bank(6)[:, 0:128], Bfm, Cfm, start=True, stop=True),
                    r=[Bfm, Cfm], w=[bank(6)[:, 0:128]])
                S.op("vector", lambda e: e.tensor_copy(Gp[:, :], bank(6)[:, 0:128]),
                     r=[bank(6)[:, 0:128]], w=[Gp[:, :]])
                def emit_dpage_mm(pg, bs):
                    S.op("tensor", lambda e, bs=bs, hb=0: e.matmul(
                        bank(bs[0]), ident[:, :], maskS[:, 0:512],
                        start=True, stop=False),
                        r=[ident[:, :], maskS[:, 0:512]], w=[bank(bs[0])])
                    S.op("tensor", lambda e, bs=bs: e.matmul(
                        bank(bs[1]), ident[:, :], maskS[:, 512:1024],
                        start=True, stop=False),
                        r=[ident[:, :], maskS[:, 512:1024]], w=[bank(bs[1])])
                    st66 = stack66[:, csl]
                    for hb in range(2):
                        S.op("tensor", lambda e, bs=bs, hb=hb, pg=pg, st66=st66: e.matmul(
                            bank(bs[hb]), st66,
                            rhs66[:, 1024 * pg + 512 * hb:1024 * pg + 512 * hb + 512],
                            start=False, stop=True),
                            r=[st66, rhs66[:, 1024 * pg + 512 * hb:1024 * pg + 512 * hb + 512]],
                            w=[bank(bs[hb])])
                def emit_dpage_evac(pg, bs):
                    ps_ = pg % 2
                    S.op("scalar", lambda e, ps_=ps_, bs=bs: e.activation(
                        Epg[:, 1024 * ps_:1024 * ps_ + 1024],
                        psum[:, 512 * bs[0]:512 * bs[0] + 1024], AF.Exp),
                        r=[psum[:, 512 * bs[0]:512 * bs[0] + 1024]],
                        w=[Epg[:, 1024 * ps_:1024 * ps_ + 1024]])
                    S.op("vector", lambda e, ps_=ps_: e.tensor_tensor(
                        Mpg[:, 1024 * ps_:1024 * ps_ + 1024].rearrange("p (h t) -> p h t", t=128),
                        Epg[:, 1024 * ps_:1024 * ps_ + 1024].rearrange("p (h t) -> p h t", t=128),
                        Gp[:, :].unsqueeze(1).broadcast_to([128, 8, 128]), ALU.mult),
                        r=[Epg[:, 1024 * ps_:1024 * ps_ + 1024], Gp[:, :]],
                        w=[Mpg[:, 1024 * ps_:1024 * ps_ + 1024]])
                    S.op("vector", lambda e, ps_=ps_, pg=pg: e.tensor_copy(
                        wcol[:, 8 * pg:8 * pg + 8].unsqueeze(2),
                        Epg[:, 1024 * ps_:1024 * ps_ + 1024].rearrange("p (h t) -> p h t", t=128)[:, :, 127:128]),
                        r=[Epg[:, 1024 * ps_:1024 * ps_ + 1024]], w=[wcol[:, 8 * pg:8 * pg + 8]])
                def emit_dpage(pg, bs):
                    emit_dpage_mm(pg, bs)
                    emit_dpage_evac(pg, bs)
                def emit_intra(pg, ib):
                    ps_ = pg % 2
                    for hh in range(8):
                        h = 8 * pg + hh
                        mp = Mpg[:, 1024 * ps_ + 128 * hh:1024 * ps_ + 128 * hh + 128]
                        S.op("tensor", lambda e, mp=mp, h=h, hh=hh, ib=ib, Xtm=Xtm: e.matmul(
                            bank(ib)[:, 64 * hh:64 * hh + 64], mp, Xtm[:, 64 * h:64 * h + 64],
                            start=True, stop=True),
                            r=[mp, Xtm[:, 64 * h:64 * h + 64]],
                            w=[bank(ib)[:, 64 * hh:64 * hh + 64]])
                chl = cumhl[:, csl]
                def emit_expcum(hf):
                    for j_ in range(2):
                        S.op("tensor", lambda e, j_=j_, hf=hf, chl=chl: e.matmul(
                            bank(2 + j_), chl,
                            e32pS[:, 1024 * hf + 512 * j_:1024 * hf + 512 * j_ + 512],
                            start=True, stop=True),
                            r=[chl, e32pS[:, 1024 * hf + 512 * j_:1024 * hf + 512 * j_ + 512]],
                            w=[bank(2 + j_)])
                    S.op("scalar", lambda e, hf=hf: e.activation(
                        expc[:, 1024 * hf:1024 * hf + 1024], psum[:, 1024:2048], AF.Exp),
                        r=[psum[:, 1024:2048]], w=[expc[:, 1024 * hf:1024 * hf + 1024]])
                for hf in range(2):
                    pga, pgb = 2 * hf, 2 * hf + 1
                    if hf == 0:
                        emit_dpage(pga, [0, 1])
                        emit_dpage(pgb, [4, 5])
                        emit_expcum(0)
                        emit_dpage_mm(2, [0, 1])
                        emit_dpage_mm(3, [4, 5])
                    else:
                        emit_expcum(1)
                    emit_intra(pga, 6)
                    emit_intra(pgb, 7)
                    if hf == 0:
                        emit_dpage_evac(2, [0, 1])
                        emit_dpage_evac(3, [4, 5])
                    for js, sl in enumerate((pga, pgb)):
                        S.op("tensor", lambda e, js=js, sl=sl, Cfm=Cfm: e.matmul(
                            bank(2 + js), Cfm, Sst[:, 512 * sl:512 * sl + 512],
                            start=True, stop=True),
                            r=[Cfm, Sst[:, 512 * sl:512 * sl + 512]], w=[bank(2 + js)])
                    for js, sl in enumerate((pga, pgb)):
                        ib = 6 + js
                        S.op("vector", lambda e, sl=sl, js=js: e.tensor_tensor(
                            yscr[:, 512 * js:512 * js + 512],
                            expc[:, 512 * sl:512 * sl + 512], bank(2 + js), ALU.mult),
                            r=[expc[:, 512 * sl:512 * sl + 512], bank(2 + js)],
                            w=[yscr[:, 512 * js:512 * js + 512]])
                        S.op("vector", lambda e, js=js, ib=ib, sl=sl: e.tensor_tensor(
                            yps[:, 512 * sl:512 * sl + 512], yscr[:, 512 * js:512 * js + 512],
                            bank(ib), ALU.add),
                            r=[yscr[:, 512 * js:512 * js + 512], bank(ib)],
                            w=[yps[:, 512 * sl:512 * sl + 512]])
                        S.op("gpsimd", lambda e, sl=sl, Xtm=Xtm: e.tensor_tensor(
                            ybuf[:, 512 * sl:512 * sl + 512], Xtm[:, 512 * sl:512 * sl + 512],
                            dbc[:, 512 * sl:512 * sl + 512], ALU.mult),
                            r=[Xtm[:, 512 * sl:512 * sl + 512], dbc[:, 512 * sl:512 * sl + 512]],
                            w=[ybuf[:, 512 * sl:512 * sl + 512]])
                        S.op("vector", lambda e, sl=sl: e.tensor_tensor(
                            yscr[:, 512:1024].bitcast(BF16) if False else yps[:, 512 * sl:512 * sl + 512],
                            ybuf[:, 512 * sl:512 * sl + 512],
                            yps[:, 512 * sl:512 * sl + 512], ALU.add),
                            r=[ybuf[:, 512 * sl:512 * sl + 512], yps[:, 512 * sl:512 * sl + 512]],
                            w=[yps[:, 512 * sl:512 * sl + 512]])
                        S.op("vector", lambda e, sl=sl, cq=cq: e.tensor_tensor(
                            ybuf[:, 512 * sl:512 * sl + 512], yps[:, 512 * sl:512 * sl + 512],
                            ztm[:, 2048 * cq + 512 * sl:2048 * cq + 512 * sl + 512], ALU.mult),
                            r=[yps[:, 512 * sl:512 * sl + 512],
                               ztm[:, 2048 * cq + 512 * sl:2048 * cq + 512 * sl + 512]],
                            w=[ybuf[:, 512 * sl:512 * sl + 512]])
                        dbuf = Epg if sl < 2 else Mpg
                        dsl_ = 0 if sl == 3 else 512 * (sl % 2) + 1024 * (sl // 2)
                        S.op("vector", lambda e, sl=sl, dsl_=dsl_, dbuf=dbuf:
                             e.scalar_tensor_tensor(
                                 dbuf[:, dsl_:dsl_ + 512], ybuf[:, 512 * sl:512 * sl + 512],
                                 1.0, ybuf[:, 512 * sl:512 * sl + 512], ALU.bypass, ALU.mult,
                                 accum_out=rmsc4[:, sl:sl + 1]),
                             r=[ybuf[:, 512 * sl:512 * sl + 512]],
                             w=[dbuf[:, dsl_:dsl_ + 512], rmsc4[:, sl:sl + 1]])
                if cq + 1 < 4:
                    emit_xb_transp(cq + 1)
                lcol = cumhl[:, cofs + 127:cofs + 128]
                for j in range(2):
                    S.op("tensor", lambda e, j=j, lcol=lcol: e.matmul(
                        bank(j, 0, 1), lcol, e32pS[:, 512 * j:512 * j + 512],
                        start=True, stop=True),
                        r=[lcol, e32pS[:, 512 * j:512 * j + 512]], w=[bank(j, 0, 1)])
                for j in range(2):
                    S.op("tensor", lambda e, j=j, lcol=lcol: e.matmul(
                        bank(4 + j, 0, 1), lcol, e32pS[:, 1024 + 512 * j:1024 + 512 * j + 512],
                        start=True, stop=True),
                        r=[lcol, e32pS[:, 1024 + 512 * j:1024 + 512 * j + 512]],
                        w=[bank(4 + j, 0, 1)])
                S.op("scalar", lambda e: e.activation(decrow[:, 0:1024], psum[0:1, 0:1024],
                                                      AF.Exp),
                     r=[psum[0:1, 0:1024]], w=[decrow[:, 0:1024]])
                S.op("scalar", lambda e: e.activation(decrow[:, 1024:2048], psum[0:1, 2048:3072],
                                                      AF.Exp),
                     r=[psum[0:1, 2048:3072]], w=[decrow[:, 1024:2048]])
                for sh in range(2):
                    for j in range(2):
                        js = 2 * sh + j
                        S.op("tensor", lambda e, js=js, j=j, sh=sh: e.matmul(
                            bank(4 + 2 * sh + j), onesH[:, :], decrow[:, 512 * js:512 * js + 512],
                            start=True, stop=True),
                            r=[onesH[:, :], decrow[:, 512 * js:512 * js + 512]],
                            w=[bank(4 + 2 * sh + j)])
                S.op("scalar", lambda e: e.activation(rmsc[:, 1:2], epscol[:, :], AF.Sqrt),
                     r=[epscol[:, :]], w=[rmsc[:, 1:2]])
                S.op("vector", lambda e: e.tensor_reduce(
                    rmsc[:, 0:1], rmsc4[:, 0:4].unsqueeze(1), AXX, ALU.add),
                    r=[rmsc4[:, 0:4]], w=[rmsc[:, 0:1]])
                S.op("scalar", lambda e: e.activation(rmsc[:, 1:2], rmsc[:, 0:1], AF.Sqrt,
                                                      bias=epscol[:, :], scale=1.0 / DIN),
                     r=[rmsc[:, 0:1], epscol[:, :]], w=[rmsc[:, 1:2]])
                S.op("vector", lambda e: e.reciprocal(rmsc[:, 1:2], rmsc[:, 1:2]),
                     r=[rmsc[:, 1:2]], w=[rmsc[:, 1:2]])
                idn0 = identj if rev else ident
                S.op("vector", lambda e, idn0=idn0: e.tensor_scalar(
                    Gp[:, :], idn0[:, :], rmsc[:, 1:2], None, ALU.mult),
                    r=[idn0[:, :], rmsc[:, 1:2]], w=[Gp[:, :]])
                S.op("gpsimd", lambda e, Xtm=Xtm: e.tensor_tensor(
                    Stmp[:, :].rearrange("p (h w) -> p h w", w=64),
                    Xtm[:, :].rearrange("p (h w) -> p h w", w=64),
                    wcol[:, :].unsqueeze(2).broadcast_to([128, 32, 64]), ALU.mult),
                    r=[Xtm[:, :], wcol[:, :], rmsc[:, 0:1]], w=[Stmp[:, :]])
                ycofs = 128 * (3 - cq) if rev else cofs
                for pk in range(4):
                    b_ = 2 + tslot[0] % 2; tslot[0] += 1
                    bk = bank(b_)
                    for j in range(4):
                        s = ybuf[:, 128 * (4 * pk + j):128 * (4 * pk + j) + 128]
                        S.op("tensor", lambda e, s=s, j=j, bk=bk: e.matmul(
                            bk[:, 128 * j:128 * j + 128], s, Gp[:, :],
                            start=True, stop=True),
                            r=[s, Gp[:, :]], w=[bk[:, 128 * j:128 * j + 128]])
                    def yev2(e, pk=pk, bk=bk, ycofs=ycofs):
                        d3 = yfm[:, :].rearrange("p (i t) -> p i t", t=512)[
                            :, 4 * pk:4 * pk + 4, ycofs:ycofs + 128]
                        s3 = bk.rearrange("p (j t) -> p j t", t=128)
                        nw3 = nwcol[:, 4 * pk:4 * pk + 4].unsqueeze(2).broadcast_to(
                            [128, 4, 128])
                        return e.tensor_tensor(d3, s3, nw3, ALU.mult)
                    dsts = [yfm[:, 512 * (4 * pk + j) + ycofs:512 * (4 * pk + j) + ycofs + 128]
                            for j in range(4)]
                    S.op("vector", yev2, r=[bk, nwcol[:, 4 * pk:4 * pk + 4]], w=dsts)
                for sh in range(2):
                    for j in range(2):
                        js = 2 * sh + j
                        S.op("tensor", lambda e, js=js, j=j, sh=sh, Btm=Btm: e.matmul(
                            bank(2 * sh + j), Btm[:, :], Stmp[:, 512 * js:512 * js + 512],
                            start=True, stop=True),
                            r=[Btm[:, :], Stmp[:, 512 * js:512 * js + 512]],
                            w=[bank(2 * sh + j)])
                    S.op("vector", lambda e, sh=sh: e.tensor_tensor(
                        Sst[:, 1024 * sh:1024 * sh + 1024], Sst[:, 1024 * sh:1024 * sh + 1024],
                        psum[:, 2048 + 1024 * sh:2048 + 1024 * sh + 1024], ALU.mult),
                        r=[Sst[:, 1024 * sh:1024 * sh + 1024],
                           psum[:, 2048 + 1024 * sh:2048 + 1024 * sh + 1024]],
                        w=[Sst[:, 1024 * sh:1024 * sh + 1024]])
                    S.op("vector", lambda e, sh=sh: e.tensor_tensor(
                        Sst[:, 1024 * sh:1024 * sh + 1024], Sst[:, 1024 * sh:1024 * sh + 1024],
                        psum[:, 1024 * sh:1024 * sh + 1024], ALU.add),
                        r=[Sst[:, 1024 * sh:1024 * sh + 1024],
                           psum[:, 1024 * sh:1024 * sh + 1024]],
                        w=[Sst[:, 1024 * sh:1024 * sh + 1024]])
                if "Xtm" in taps and d == "f" and q == 0 and cq == 0:
                    tapcp("Xtm", Xtm[:, :])
                if "Mpg" in taps and d == "f" and q == 0 and cq == 0:
                    tapcp("Mpg", Mpg[:, :])
                if "ybuf" in taps and d == "f" and q == 0 and cq == 0:
                    tapcp("ybuf", ybuf[:, :])
                if "Sst" in taps and d == "f" and q == 0 and cq == 0:
                    tapcp("Sst", Sst[:, :])
            if stage < 3:
                continue
            def wo_dma(k):
                S.dma("sync", wor[:, 1024 * (k % 2):1024 * (k % 2) + 1024],
                      wbf[f"wout_{d}"][128 * k:128 * k + 128, :])
            wo_dma(0); wo_dma(1)
            for k in range(16):
                slot = k % 2
                for tt in range(4):
                    lhs = yfm[:, 512 * k + 128 * tt:512 * k + 128 * tt + 128]
                    for nb in range(2):
                        S.op("tensor", lambda e, tt=tt, nb=nb, slot=slot, lhs=lhs, k=k: e.matmul(
                            bank(2 * tt + nb), lhs,
                            wor[:, 1024 * slot + 512 * nb:1024 * slot + 512 * nb + 512],
                            start=(k == 0), stop=(k == 15)),
                            r=[lhs, wor[:, 1024 * slot + 512 * nb:1024 * slot + 512 * nb + 512]],
                            w=[bank(2 * tt + nb)])
                if k + 2 < 16:
                    wo_dma(k + 2)
            for tt in range(4):
                otile = (NT - 4 * q - 4 + tt) if rev else (4 * q + tt)
                for nb in range(2):
                    dsl = attn[:, 1024 * otile + 512 * nb:1024 * otile + 512 * nb + 512]
                    if rev:
                        S.op("vector", lambda e, dsl=dsl, tt=tt, nb=nb: e.tensor_tensor(
                            dsl, dsl, bank(2 * tt + nb), ALU.add),
                            r=[dsl, bank(2 * tt + nb)], w=[dsl])
                    else:
                        S.op("vector", lambda e, dsl=dsl, tt=tt, nb=nb: e.tensor_copy(
                            dsl, bank(2 * tt + nb)),
                            r=[bank(2 * tt + nb)], w=[dsl])
            if rev and stage >= 4:
                def mk_fold(otile):
                    at = attn[:, 1024 * otile:1024 * otile + 1024]
                    xt = xres[:, 1024 * otile:1024 * otile + 1024]
                    st = {}
                    def c1():
                        st["ln"] = ln_cols(at)
                    def c2():
                        inv, ngm = st["ln"]
                        S.op("vector", lambda e, at=at, inv=inv, ngm=ngm: e.tensor_scalar(
                            lntmp[:, 0:1024], at, inv, ngm, ALU.mult, ALU.add),
                            r=[at, inv, ngm], w=[lntmp[:, 0:1024]])
                        S.op("vector", lambda e, xt=xt: e.tensor_tensor(
                            xt, xt, lntmp[:, 0:1024], ALU.add),
                            r=[xt, lntmp[:, 0:1024]], w=[xt])
                    return [c1, c2]
                for tt in range(4):
                    carry_steps.extend(mk_fold(NT - 4 * q - 4 + tt))
                if q == NQ - 1:
                    while carry_steps:
                        carry_steps.pop(0)()
        if "yfm" in taps and d == "f":
            tapcp("yfm", yfm[:, :])

    mamba_dir("f", False)
    tapcp("attn_f", attn[:, :])
    if "b" in dirs:
        mamba_dir("b", True)
        tapcp("attn", attn[:, :])

    if stage >= 4:
        tapcp("xmid", xres[:, :])
    S.barrier()

    esA.close()
    if stage >= 5:
        esB = ExitStack()
        sbB = lambda n, shp, dt=F32: esB.enter_context(nc.sbuf_tensor(n, list(shp), dt))
        x2q = sbB("x2q", (128, 8 * QT), BF16)
        lntmp2 = sbB("lntmp2", (128, 1024), BF16)
        wproj = sbB("wproj", (128, 8 * 16), BF16)
        wc1 = sbB("wc1", (16, 4 * 128), BF16)
        wc2 = sbB("wc2", (16, 4 * 128), BF16)
        wwin1 = sbB("wwin1", (128, 8 * 512), BF16)
        wb = sbB("wb", (128, 8))
        wwout1 = sbB("wwout1", (128, 4 * 4096), BF16)
        wwin2 = sbB("wwin2", (128, 32 * 512), BF16)
        wwout2 = sbB("wwout2", (128, 4 * 1024), BF16)
        acat = sbB("acat", (128, 64)); ascr = sbB("ascr", (128, 64))
        asrt = sbB("asrt", (128, 64)); aorig = sbB("aorig", (128, 64))
        zs_keep = sbB("zs_keep", (128, 64))
        astat = sbB("astat", (128, 24))
        invk16 = sbB("invk16", (128, 16))
        pos16 = sbB("pos16", (128, 16))
        afm = sbB("afm", (16, QT), BF16)
        g1b = sbB("g1b", (128, 4 * 512)); g2b = sbB("g2b", (128, 4 * 512))
        z1g = sbB("z1g", (128, 4 * 512), BF16); z2g = sbB("z2g", (128, 4 * 512), BF16)
        hfm = sbB("hfm", (128, 32 * 512), BF16)
        fout = sbB("fout", (128, 1024))

        r2 = lambda t, n: t.rearrange("p (a b) -> p a b", a=n)
        S.dma("gpsimd", r2(wproj[:, :], 8),
              bass.AP(dr["fproj"].tensor, 0, [[16, 128], [2048, 8], [1, 16]]))
        S.dma("gpsimd", r2(wc1[0:16, :], 4),
              bass.AP(dr["fc1"].tensor, 0, [[128, 16], [2048, 4], [1, 128]]))
        S.dma("gpsimd", r2(wc2[0:16, :], 4),
              bass.AP(dr["fc2"].tensor, 0, [[128, 16], [2048, 4], [1, 128]]))
        S.dma("gpsimd", r2(wwin1[:, :], 8),
              bass.AP(dr["fwin1"].tensor, 0, [[512, 128], [65536, 8], [1, 512]]))
        S.dma("sync", wb[:, 0:4].unsqueeze(2),
              bass.AP(dr["fb1"].tensor, 0, [[1, 128], [128, 4], [1, 1]]), slow=True)
        S.dma("gpsimd", r2(wwout1[:, :], 4),
              bass.AP(dr["fwout1t"].tensor, 0, [[4096, 128], [524288, 4], [1, 4096]]))
        S.dma("gpsimd", r2(wwin2[:, :], 32),
              bass.AP(dr["fwin2"].tensor, 0, [[512, 128], [65536, 32], [1, 512]]))
        S.dma("sync", wb[:, 4:8].unsqueeze(2),
              bass.AP(dr["fb2"].tensor, 0, [[1, 128], [128, 4], [1, 1]]), slow=True)
        S.dma("gpsimd", r2(wwout2[:, :], 4),
              bass.AP(dr["fwout2t"].tensor, 0, [[1024, 128], [131072, 4], [1, 1024]]))

        def bld_invk(e):
            for i in range(16):
                e.memset(invk16[:, i:i + 1], 1.0 / (i + 1))
                e.memset(pos16[:, i:i + 1], float(i + 1))
            return e.engine_nop()
        S.op("vector", bld_invk, w=[invk16[:, :], pos16[:, :]])

        a3 = lambda t: t.rearrange("p (c e) -> p c e", e=16)
        bc4 = lambda col: col.unsqueeze(2).broadcast_to([128, 4, 16])
        bck = lambda t: t.unsqueeze(1).broadcast_to([128, 4, 16])

        def emit_entmax():
            ra = lambda t: [t[:, :]]
            S.op("vector", lambda e: e.tensor_reduce(astat[:, 0:4], a3(acat[:, :]), AXX, ALU.add),
                 r=[acat[:, :]], w=[astat[:, 0:4]])
            S.op("vector", lambda e: e.tensor_scalar(astat[:, 0:4], astat[:, 0:4], 1.0 / 16,
                                                     None, ALU.mult),
                 r=[astat[:, 0:4]], w=[astat[:, 0:4]])
            S.op("vector", lambda e: e.tensor_tensor(a3(aorig[:, :]), a3(acat[:, :]),
                                                     bc4(astat[:, 0:4]), ALU.subtract),
                 r=[acat[:, :], astat[:, 0:4]], w=[aorig[:, :]])
            S.op("vector", lambda e: e.tensor_tensor(a3(ascr[:, :]), a3(aorig[:, :]),
                                                     a3(aorig[:, :]), ALU.mult),
                 r=[aorig[:, :]], w=[ascr[:, :]])
            S.op("vector", lambda e: e.tensor_reduce(astat[:, 4:8], a3(ascr[:, :]), AXX, ALU.add),
                 r=[ascr[:, :]], w=[astat[:, 4:8]])
            S.op("scalar", lambda e: e.activation(astat[:, 8:12], astat[:, 4:8], AF.Sqrt,
                                                  bias=epscol[:, :], scale=1.0 / 16),
                 r=[astat[:, 4:8], epscol[:, :]], w=[astat[:, 8:12]])
            S.op("vector", lambda e: e.reciprocal(astat[:, 8:12], astat[:, 8:12]),
                 r=[astat[:, 8:12]], w=[astat[:, 8:12]])
            S.op("vector", lambda e: e.tensor_tensor(a3(aorig[:, :]), a3(aorig[:, :]),
                                                     bc4(astat[:, 8:12]), ALU.mult),
                 r=[aorig[:, :], astat[:, 8:12]], w=[aorig[:, :]])
            S.op("vector", lambda e: e.tensor_scalar(aorig[:, :], aorig[:, :], 0.5, None, ALU.mult),
                 r=[aorig[:, :]], w=[aorig[:, :]])
            S.op("vector", lambda e: e.tensor_reduce(astat[:, 12:16], a3(aorig[:, :]), AXX, ALU.max),
                 r=[aorig[:, :]], w=[astat[:, 12:16]])
            S.op("vector", lambda e: e.tensor_tensor(a3(aorig[:, :]), a3(aorig[:, :]),
                                                     bc4(astat[:, 12:16]), ALU.subtract),
                 r=[aorig[:, :], astat[:, 12:16]], w=[aorig[:, :]])
            S.op("vector", lambda e: e.tensor_copy(asrt[:, :], aorig[:, :]),
                 r=[aorig[:, :]], w=[asrt[:, :]])
            def sap(buf, off, dims):
                pairs = [[64, 128]] + [[s, c] for (s, c) in dims if c > 1]
                return bass.AP(buf.tensor if hasattr(buf, "tensor") else buf, off, pairs)
            stages = []
            kk = 2
            while kk <= 16:
                jj = kk // 2
                while jj >= 1:
                    stages.append((kk, jj)); jj //= 2
                kk *= 2
            for (kk, jj) in stages:
                def mk(kk=kk, jj=jj):
                    def f(e):
                        u = 16 // kk
                        g = max(1, kk // (2 * jj))
                        last = None
                        if u == 1:
                            dims = [(16, 4), (2 * jj, g), (1, jj)]
                            lo = sap(asrt[:, :], 0, dims); hi = sap(asrt[:, :], jj, dims)
                            e.tensor_tensor(sap(ascr[:, :], 0, dims), lo, hi, ALU.max)
                            last = e.tensor_tensor(sap(ascr[:, :], jj, dims), lo, hi, ALU.min)
                        else:
                            for half in range(2):
                                o0 = kk * half
                                dims = [(16, 4), (2 * kk, u // 2), (2 * jj, g), (1, jj)]
                                lo = sap(asrt[:, :], o0, dims)
                                hi = sap(asrt[:, :], o0 + jj, dims)
                                omax = sap(ascr[:, :], o0 if half == 0 else o0 + jj, dims)
                                omin = sap(ascr[:, :], o0 + jj if half == 0 else o0, dims)
                                e.tensor_tensor(omax, lo, hi, ALU.max)
                                last = e.tensor_tensor(omin, lo, hi, ALU.min)
                        return last
                    return f
                S.op("vector", mk(), r=[asrt[:, :]], w=[ascr[:, :]])
                S.op("vector", lambda e: e.tensor_copy(asrt[:, :], ascr[:, :]),
                     r=[ascr[:, :]], w=[asrt[:, :]])
            S.op("vector", lambda e: e.tensor_copy(zs_keep[:, :], asrt[:, :]),
                 r=[asrt[:, :]], w=[zs_keep[:, :]])
            S.op("vector", lambda e: e.tensor_copy(ascr[:, :], asrt[:, :]),
                 r=[asrt[:, :]], w=[ascr[:, :]])
            S.op("vector", lambda e: e.tensor_tensor(acat[:, :], asrt[:, :], asrt[:, :], ALU.mult),
                 r=[asrt[:, :]], w=[acat[:, :]])
            for buf in (ascr, acat):
                for ddd in (1, 2, 4, 8):
                    def f2(e, buf=buf, ddd=ddd):
                        t = a3(buf[:, :]); s = a3(asrt[:, :])
                        e.tensor_copy(s[:, :, 0:ddd], t[:, :, 0:ddd])
                        return e.tensor_tensor(s[:, :, ddd:16], t[:, :, ddd:16],
                                               t[:, :, 0:16 - ddd], ALU.add)
                    S.op("vector", f2, r=[buf[:, :]], w=[asrt[:, :]])
                    S.op("vector", lambda e, buf=buf: e.tensor_copy(buf[:, :], asrt[:, :]),
                         r=[asrt[:, :]], w=[buf[:, :]])
            S.op("vector", lambda e: e.tensor_tensor(a3(ascr[:, :]), a3(ascr[:, :]),
                                                     bck(invk16[:, :]), ALU.mult),
                 r=[ascr[:, :], invk16[:, :]], w=[ascr[:, :]])
            S.op("vector", lambda e: e.tensor_tensor(a3(acat[:, :]), a3(acat[:, :]),
                                                     bck(invk16[:, :]), ALU.mult),
                 r=[acat[:, :], invk16[:, :]], w=[acat[:, :]])
            S.op("vector", lambda e: e.tensor_tensor(a3(asrt[:, :]), a3(ascr[:, :]),
                                                     a3(ascr[:, :]), ALU.mult),
                 r=[ascr[:, :]], w=[asrt[:, :]])
            S.op("vector", lambda e: e.tensor_tensor(a3(acat[:, :]), a3(acat[:, :]),
                                                     a3(asrt[:, :]), ALU.subtract),
                 r=[acat[:, :], asrt[:, :]], w=[acat[:, :]])
            S.op("vector", lambda e: e.tensor_tensor(a3(acat[:, :]), bck(invk16[:, :]),
                                                     a3(acat[:, :]), ALU.subtract),
                 r=[acat[:, :], invk16[:, :]], w=[acat[:, :]])
            S.op("vector", lambda e: e.tensor_scalar(acat[:, :], acat[:, :], 0.0, None, ALU.max),
                 r=[acat[:, :]], w=[acat[:, :]])
            S.op("scalar", lambda e: e.activation(acat[:, :], acat[:, :], AF.Sqrt),
                 r=[acat[:, :]], w=[acat[:, :]])
            S.op("vector", lambda e: e.tensor_tensor(ascr[:, :], ascr[:, :], acat[:, :],
                                                     ALU.subtract),
                 r=[ascr[:, :], acat[:, :]], w=[ascr[:, :]])
            S.op("vector", lambda e: e.tensor_tensor(acat[:, :], ascr[:, :], zs_keep[:, :],
                                                     ALU.is_le),
                 r=[ascr[:, :], zs_keep[:, :]], w=[acat[:, :]])
            S.op("vector", lambda e: e.tensor_reduce(astat[:, 16:20], a3(acat[:, :]), AXX, ALU.add),
                 r=[acat[:, :]], w=[astat[:, 16:20]])
            S.op("vector", lambda e: e.tensor_tensor(a3(acat[:, :]), bck(pos16[:, :]),
                                                     bc4(astat[:, 16:20]), ALU.is_le),
                 r=[pos16[:, :], astat[:, 16:20]], w=[acat[:, :]])
            S.op("vector", lambda e: e.tensor_copy(asrt[:, :], acat[:, :]),
                 r=[acat[:, :]], w=[asrt[:, :]])
            S.op("vector", lambda e: e.tensor_tensor(a3(acat[:, :])[:, :, 0:15],
                                                     a3(asrt[:, :])[:, :, 0:15],
                                                     a3(asrt[:, :])[:, :, 1:16], ALU.subtract),
                 r=[asrt[:, :]], w=[acat[:, :]])
            S.op("vector", lambda e: e.tensor_tensor(ascr[:, :], ascr[:, :], acat[:, :], ALU.mult),
                 r=[ascr[:, :], acat[:, :]], w=[ascr[:, :]])
            S.op("vector", lambda e: e.tensor_reduce(astat[:, 20:24], a3(ascr[:, :]), AXX, ALU.add),
                 r=[ascr[:, :]], w=[astat[:, 20:24]])
            S.op("vector", lambda e: e.tensor_tensor(a3(acat[:, :]), a3(aorig[:, :]),
                                                     bc4(astat[:, 20:24]), ALU.subtract),
                 r=[aorig[:, :], astat[:, 20:24]], w=[acat[:, :]])
            S.op("vector", lambda e: e.tensor_scalar(acat[:, :], acat[:, :], 0.0, None, ALU.max),
                 r=[acat[:, :]], w=[acat[:, :]])
            S.op("vector", lambda e: e.tensor_tensor(acat[:, :], acat[:, :], acat[:, :], ALU.mult),
                 r=[acat[:, :]], w=[acat[:, :]])

        def ffn_prelude_a(q):
            for tcq in range(4):
                tc = 4 * q + tcq
                ln_and_transpose(xres[:, 1024 * tc:1024 * tc + 1024], lntmp2, x2q, tcq)
            for tcq in range(4):
                for k in range(8):
                    xa = x2q[:, 512 * k + 128 * tcq:512 * k + 128 * tcq + 128]
                    S.op("tensor", lambda e, xa=xa, k=k, tcq=tcq: e.matmul(
                        bank(6, 0, 128, 64)[:, 16 * tcq:16 * tcq + 16], xa,
                        wproj[:, 16 * k:16 * k + 16], start=(k == 0), stop=(k == 7)),
                        r=[xa, wproj[:, 16 * k:16 * k + 16]],
                        w=[bank(6, 0, 128, 64)[:, 16 * tcq:16 * tcq + 16]])
            S.op("vector", lambda e: e.tensor_copy(acat[:, :], bank(6, 0, 128, 64)),
                 r=[bank(6, 0, 128, 64)], w=[acat[:, :]])
        def ffn_prelude_b():
            S.op("vector", lambda e: e.tensor_copy(lntmp2[:, 0:64], acat[:, :]),
                 r=[acat[:, :]], w=[lntmp2[:, 0:64]])
            b = 2 + tslot[0] % 2; tslot[0] += 1
            tb = tbank(b)
            for tcq in range(4):
                s_ = lntmp2[:, 16 * tcq:16 * tcq + 16]
                S.op("tensor", lambda e, s_=s_, tcq=tcq, tb=tb: e.transpose(
                    tb[0:16, 128 * tcq:128 * tcq + 128], s_, ident[:, :]),
                    r=[s_, ident[:, :]], w=[tb[:, 128 * tcq:128 * tcq + 128]])
            S.op("vector", lambda e, tb=tb: e.tensor_copy(afm[0:16, :], tb[0:16, :]),
                 r=[tb], w=[afm[0:16, :]])
        def emit_g(wc, gb, cg0):
            for r_ in range(4):
                cg = 6 + (r_ % 2)
                wca = wc[0:16, 128 * r_:128 * r_ + 128]
                S.op("tensor", lambda e, wca=wca, cg=cg: e.matmul(
                    bank(cg), wca, afm[0:16, :], start=True, stop=True),
                    r=[wca, afm[0:16, :]], w=[bank(cg)])
                S.op("vector", lambda e, gb=gb, r_=r_, cg=cg: e.tensor_copy(
                    gb[:, 512 * r_:512 * r_ + 512], bank(cg)),
                    r=[bank(cg)], w=[gb[:, 512 * r_:512 * r_ + 512]])
        z1r = hfm[:, 0:2048]
        ffn_prelude_a(0)
        emit_entmax()
        for q in range(NQ):
            if q == 0:
                for r_ in range(4):
                    for k in range(8):
                        wa = wwin1[:, 512 * k + 128 * r_:512 * k + 128 * r_ + 128]
                        S.op("tensor", lambda e, wa=wa, r_=r_, k=k: e.matmul(
                            bank(r_), wa, x2q[:, 512 * k:512 * k + 512],
                            start=(k == 0), stop=(k == 7)),
                            r=[wa, x2q[:, 512 * k:512 * k + 512]], w=[bank(r_)])
                    S.op("vector", lambda e, r_=r_: e.tensor_scalar(
                        z1r[:, 512 * r_:512 * r_ + 512], bank(r_), wb[:, r_:r_ + 1],
                        None, ALU.add),
                        r=[bank(r_), wb[:, r_:r_ + 1]],
                        w=[z1r[:, 512 * r_:512 * r_ + 512]])
                ffn_prelude_b()
                emit_g(wc1, g1b, 6)
                emit_g(wc2, g2b, 7)
                for r_ in range(4):
                    S.op("vector", lambda e, r_=r_: e.tensor_tensor(
                        z1g[:, 512 * r_:512 * r_ + 512], z1r[:, 512 * r_:512 * r_ + 512],
                        g1b[:, 512 * r_:512 * r_ + 512], ALU.mult),
                        r=[z1r[:, 512 * r_:512 * r_ + 512], g1b[:, 512 * r_:512 * r_ + 512]],
                        w=[z1g[:, 512 * r_:512 * r_ + 512]])
            else:
                emit_g(wc1, g1b, 6)
                emit_g(wc2, g2b, 7)
                for r_ in range(4):
                    for k in range(8):
                        wa = wwin1[:, 512 * k + 128 * r_:512 * k + 128 * r_ + 128]
                        S.op("tensor", lambda e, wa=wa, r_=r_, k=k: e.matmul(
                            bank(r_), wa, x2q[:, 512 * k:512 * k + 512],
                            start=(k == 0), stop=(k == 7)),
                            r=[wa, x2q[:, 512 * k:512 * k + 512]], w=[bank(r_)])
                    S.op("vector", lambda e, r_=r_: e.scalar_tensor_tensor(
                        z1g[:, 512 * r_:512 * r_ + 512], bank(r_), wb[:, r_:r_ + 1],
                        g1b[:, 512 * r_:512 * r_ + 512], ALU.add, ALU.mult),
                        r=[bank(r_), wb[:, r_:r_ + 1], g1b[:, 512 * r_:512 * r_ + 512]],
                        w=[z1g[:, 512 * r_:512 * r_ + 512]])
            if q + 1 < NQ:
                ffn_prelude_a(q + 1)
            for o in range(32):
                pb = 4 + o % 2
                for r_ in range(4):
                    wa = wwout1[:, 4096 * r_ + 128 * o:4096 * r_ + 128 * o + 128]
                    S.op("tensor", lambda e, wa=wa, r_=r_, pb=pb: e.matmul(
                        bank(pb), wa, z1g[:, 512 * r_:512 * r_ + 512],
                        start=(r_ == 0), stop=(r_ == 3)),
                        r=[wa, z1g[:, 512 * r_:512 * r_ + 512]], w=[bank(pb)])
                S.op("scalar", lambda e, o=o, pb=pb: e.activation(
                    hfm[:, 512 * o:512 * o + 512], bank(pb), AF.Gelu),
                    r=[bank(pb)], w=[hfm[:, 512 * o:512 * o + 512]])
            if "hfm" in taps and q == 0:
                tapcp("hfm", hfm[:, :])
            if q + 1 < NQ:
                emit_entmax()
            for r_ in range(4):
                for k in range(32):
                    wa = wwin2[:, 512 * k + 128 * r_:512 * k + 128 * r_ + 128]
                    S.op("tensor", lambda e, wa=wa, r_=r_, k=k: e.matmul(
                        bank(r_), wa, hfm[:, 512 * k:512 * k + 512],
                        start=(k == 0), stop=(k == 31)),
                        r=[wa, hfm[:, 512 * k:512 * k + 512]], w=[bank(r_)])
                S.op("vector", lambda e, r_=r_: e.scalar_tensor_tensor(
                    z2g[:, 512 * r_:512 * r_ + 512], bank(r_), wb[:, 4 + r_:5 + r_],
                    g2b[:, 512 * r_:512 * r_ + 512], ALU.add, ALU.mult),
                    r=[bank(r_), wb[:, 4 + r_:5 + r_], g2b[:, 512 * r_:512 * r_ + 512]],
                    w=[z2g[:, 512 * r_:512 * r_ + 512]])
            if q + 1 < NQ:
                ffn_prelude_b()
            for tt in range(4):
                pb = 4 + 2 * (tt % 2)
                for r_ in range(4):
                    za = z2g[:, 512 * r_ + 128 * tt:512 * r_ + 128 * tt + 128]
                    for nb in range(2):
                        S.op("tensor", lambda e, za=za, r_=r_, nb=nb, pb=pb: e.matmul(
                            bank(pb + nb), za,
                            wwout2[:, 1024 * r_ + 512 * nb:1024 * r_ + 512 * nb + 512],
                            start=(r_ == 0), stop=(r_ == 3)),
                            r=[za, wwout2[:, 1024 * r_ + 512 * nb:1024 * r_ + 512 * nb + 512]],
                            w=[bank(pb + nb)])
                tc = 4 * q + tt
                pspan = psum[:, 512 * pb:512 * pb + 1024]
                inv, ngm = ln_cols(pspan)
                S.op("vector", lambda e, pspan=pspan, inv=inv, ngm=ngm: e.tensor_scalar(
                    fout[:, :], pspan, inv, ngm, ALU.mult, ALU.add),
                     r=[pspan, inv, ngm], w=[fout[:, :]])
                xt = xres[:, 1024 * tc:1024 * tc + 1024]
                S.op("vector", lambda e, xt=xt: e.tensor_tensor(fout[:, :], fout[:, :], xt,
                                                                ALU.add),
                     r=[fout[:, :], xt], w=[fout[:, :]])
                S.dma("sync", out[128 * tc:128 * tc + 128, :], fout[:, :])
    S.finalize()
    alld = [(c, c.n) for name, c in S.chans.items() if name.startswith("dq_") and c.n > 0]
    tail = Op("sync", None, [], [], [], None)
    tail.waits = list(alld)
    S.ops.append(tail)
    tail2 = Op("tensor", None, [], [], [], None)
    tail2.waits = list(alld)
    S.ops.append(tail2)
    S.check()
    with nc.Block() as block:
        S.emit(nc, block, es)
    if stage >= 5:
        esB.close()
    es.close()
    return nc, tapd


def kernel(**inputs):
    x = np.asarray(inputs["L0_attn_hidden_states"])
    B, T, _ = x.shape
    nc, _ = build_program(T=T)
    in_maps = [host_inputs(inputs, b, T) for b in range(B)]
    res = run_bass_kernel_spmd(nc, in_maps, core_ids=list(range(8)))
    return np.stack([r["out"] for r in res.results]).astype(np.float32)


# version 35
# speedup vs baseline: 1.0786x; 1.0120x over previous
import numpy as np
import ml_dtypes
from contextlib import ExitStack

import concourse.bass as bass
import concourse.mybir as mybir
from concourse.bass_utils import run_bass_kernel_spmd

F32 = mybir.dt.float32
BF16 = mybir.dt.bfloat16
FP16 = mybir.dt.float16
AF = mybir.ActivationFunctionType
ALU = mybir.AluOpType
AXX = mybir.AxisListType.X
BF = ml_dtypes.bfloat16

D = 1024; DIN = 2048; NH = 32; PD = 64
Q = 128; QT = 512
EPS = 1e-5; NEG = -1e30

ENGINES = ("sync", "scalar", "vector", "tensor", "gpsimd")

def _esz(dt):
    try:
        return dt.itemsize
    except Exception:
        return {F32: 4, BF16: 2, FP16: 2}.get(dt, 4)

def ap_range(ap):
    th = ap.tensor
    t = th.name
    esz = _esz(ap.dtype)
    pairs = list(ap.ap)
    onchip = "DRAM" not in str(getattr(ap, "space", "")).upper()
    lo = hi = ap.offset * esz
    dims = pairs[1:] if onchip else pairs
    for step, count in dims:
        if count <= 1:
            continue
        d = step * (count - 1) * esz
        if d >= 0:
            hi += d
        else:
            lo += d
    if onchip:
        pitch = int(th.shape[-1]) * _esz(th.dtype)
        base = (lo // pitch) * pitch
        lo -= base
        hi -= base
        if hi >= pitch:
            lo, hi = 0, pitch
    return (t, lo, hi + esz)

class Op:
    __slots__ = ("eng", "fn", "reads", "writes", "xwaits", "dmach", "dmaval",
                 "mark", "waits", "idx", "barrier_inc")
    def __init__(self, eng, fn, reads, writes, xwaits, dmach):
        self.eng = eng; self.fn = fn
        self.reads = reads; self.writes = writes
        self.xwaits = list(xwaits)
        self.dmach = dmach
        self.dmaval = 0
        self.mark = None
        self.waits = []
        self.idx = 0
        self.barrier_inc = None

class Ch:
    __slots__ = ("name", "n", "sem")
    def __init__(self, name):
        self.name = name; self.n = 0; self.sem = None

class Sched:
    def __init__(self):
        self.ops = []
        self.chans = {}
        self.mark_ch = {e: self.chan("mk_" + e) for e in ENGINES}

    def chan(self, name):
        if name not in self.chans:
            self.chans[name] = Ch(name)
        return self.chans[name]

    DMA_POOL = 10

    def op(self, eng, fn, r=(), w=(), waits=(), dma=False, tag=None):
        reads = [ap_range(a) for a in r if a is not None]
        writes = [ap_range(a) for a in w if a is not None]
        ch = None
        o = Op(eng, fn, reads, writes, waits, ch)
        if dma:
            if not hasattr(self, "dmacnt"):
                self.dmacnt = {}
            i = self.dmacnt.get(eng, 0)
            self.dmacnt[eng] = i + 1
            ch = self.chan(f"dq_{eng}_{i % self.DMA_POOL}")
            if ch.n > 0:
                o.xwaits.append((ch, ch.n))
            ch.n += 16
            o.dmach = ch
            o.dmaval = ch.n
        self.ops.append(o)
        return o

    def dma(self, eng, dst, src, waits=(), tag=None, slow=False):
        if slow:
            fn = lambda e, d=dst, s=src: e.dma_start(out=d, in_=s,
                                                     allow_slow_non_contiguous=True)
        else:
            fn = lambda e, d=dst, s=src: e.dma_start(out=d, in_=s)
        return self.op(eng, fn, r=[src], w=[dst], waits=waits, dma=True, tag=tag)

    def barrier(self):
        bc = {e: self.chan("bar_" + e) for e in ENGINES}
        snap = {}
        for e in ENGINES:
            fn = lambda eng: eng.nop()
            o = self.op(e, fn)
            o.barrier_inc = bc[e]
            bc[e].n += 1
            snap[e] = bc[e].n
            o.xwaits = list(o.xwaits)
        for e in ENGINES:
            o = self.op(e, None,
                        waits=[(bc[e2], snap[e2]) for e2 in ENGINES if e2 != e])
        self._bar_pending = getattr(self, "_bar_pending", [])

    def finalize(self):
        writer = {}
        readers = {}
        for o in self.ops:
            deps = set()
            for t, lo, hi in o.reads:
                for (wl, wh), wo in writer.get(t, ()):
                    if lo < wh and hi > wl:
                        deps.add(wo)
            for t, lo, hi in o.writes:
                for (wl, wh), wo in writer.get(t, ()):
                    if lo < wh and hi > wl:
                        deps.add(wo)
                for (rl, rh), ro in readers.get(t, ()):
                    if lo < rh and hi > rl:
                        deps.add(ro)
            for dep in deps:
                if dep is o:
                    continue
                if dep.dmach is not None:
                    o.waits.append((dep.dmach, dep.dmaval))
                elif dep.eng != o.eng:
                    if dep.mark is None:
                        dep.mark = True
                    o.waits.append(("MARK", dep))
                else:
                    if o.eng == "tensor":
                        continue
                    spans = [hi - lo for (_, lo, hi) in dep.writes] or [0]
                    if max(spans) < 2048:
                        if dep.mark is None:
                            dep.mark = True
                        o.waits.append(("MARK", dep))
            for t, lo, hi in o.writes:
                wl_ = [(rg, wo) for (rg, wo) in writer.get(t, ())
                       if not (lo <= rg[0] and hi >= rg[1])]
                wl_.append(((lo, hi), o))
                writer[t] = wl_
                rl_ = [(rg, ro) for (rg, ro) in readers.get(t, ())
                       if not (lo <= rg[0] and hi >= rg[1])]
                readers[t] = rl_
            for t, lo, hi in o.reads:
                lst = readers.setdefault(t, [])
                key = (o.eng, lo, hi)
                for i_, (rg, ro) in enumerate(lst):
                    if (ro.eng, rg[0], rg[1]) == key:
                        lst[i_] = ((lo, hi), o)
                        break
                else:
                    lst.append(((lo, hi), o))
        cnt = {e: 0 for e in ENGINES}
        for o in self.ops:
            if getattr(o, "barrier_inc", None) is not None:
                pass
            if o.mark:
                cnt[o.eng] += 1
                o.mark = cnt[o.eng]
        for o in self.ops:
            ws = {}
            for item in o.waits:
                if isinstance(item, tuple) and item[0] == "MARK":
                    dep = item[1]
                    c = self.mark_ch[dep.eng]
                    v = dep.mark
                else:
                    c, v = item
                if v and v > 0:
                    ws[c] = max(ws.get(c, 0), v)
            for c, v in o.xwaits:
                if v and v > 0:
                    ws[c] = max(ws.get(c, 0), v)
            o.waits = list(ws.items())
        self.mark_totals = cnt

    def check(self):
        vals = {c: 0 for c in self.chans.values()}
        qs = {}
        for i, o in enumerate(self.ops):
            o.idx = i
            qs.setdefault(o.eng, []).append(o)
        heads = {e: 0 for e in qs}
        done, prog = 0, True
        while prog:
            prog = False
            for e, lst in qs.items():
                while heads[e] < len(lst):
                    o = lst[heads[e]]
                    if any(vals[c] < v for c, v in o.waits):
                        break
                    if o.dmach is not None:
                        vals[o.dmach] = max(vals[o.dmach], o.dmaval)
                    if o.mark:
                        vals[self.mark_ch[o.eng]] = max(vals[self.mark_ch[o.eng]], o.mark)
                    bi = getattr(o, "barrier_inc", None)
                    if bi is not None:
                        vals[bi] = vals[bi] + 1
                    heads[e] += 1; done += 1; prog = True
        if done != len(self.ops):
            stuck = []
            for e, lst in qs.items():
                if heads[e] < len(lst):
                    o = lst[heads[e]]
                    bad = [(c.name, v, vals[c]) for c, v in o.waits if vals[c] < v]
                    stuck.append((e, o.idx, bad[:3]))
            raise RuntimeError(f"deadlock, {len(self.ops) - done} stuck: {stuck[:5]}")

    def emit(self, nc, block, es):
        for c in self.chans.values():
            c.sem = es.enter_context(nc.semaphore(c.name))
        for ename in ENGINES:
            ops = [o for o in self.ops if o.eng == ename]
            def body(eng, ops=ops):
                for o in ops:
                    for c, v in o.waits:
                        eng.wait_ge(c.sem, v)
                    if o.fn is None:
                        continue
                    ins = o.fn(eng)
                    if o.dmach is not None:
                        ins.then_inc(o.dmach.sem, 16)
                    if o.mark:
                        ins.then_inc(self.mark_ch[o.eng].sem, 1)
                    bi = getattr(o, "barrier_inc", None)
                    if bi is not None:
                        ins.then_inc(bi.sem, 1)
            getattr(block, ename)(body)

MORDER = [34, 32, 0, 33, 1, 16, 17]
for _i in range(14):
    MORDER += [18 + _i, 2 + _i]
WIN_TILE_OFF = []
_off = 0
for _m in MORDER:
    WIN_TILE_OFF.append(_off)
    _off += 8 * 128 * (128 if _m < 34 else 32)
WIN_TOTAL = _off

def _tile_win(W):
    blobs = []
    for m in MORDER:
        w = 128 if m < 34 else 32
        t = W[:, m * 128:m * 128 + w].reshape(8, 128, w).transpose(1, 0, 2)
        blobs.append(np.ascontiguousarray(t).reshape(-1))
    return np.concatenate(blobs)

def _consts():
    ident = np.eye(128, dtype=np.float32).astype(BF)
    id32x4 = np.tile(np.eye(32, dtype=np.float32), (4, 1)).astype(BF)
    m = np.where(np.arange(Q)[:, None] > np.arange(Q)[None, :],
                 np.float32(NEG), np.float32(0.0))
    mask_rep = np.tile(m, (1, 8)).astype(BF)
    e = np.zeros((32, 32 * Q), np.float32)
    for h in range(32):
        e[h, h * Q:(h + 1) * Q] = 1.0
    e32x2 = np.concatenate([e, e], 0).astype(np.float16)
    ep = np.zeros((32, 32 * PD), np.float32)
    for h in range(32):
        ep[h, h * PD:(h + 1) * PD] = 1.0
    e32p = np.concatenate([ep, ep], 0).astype(np.float16)
    ones_bf = np.ones((1, 128), dtype=BF)
    ones_f16 = np.ones((1, 128), dtype=np.float16)
    identj = np.eye(128, dtype=np.float32)[:, ::-1].copy().astype(BF)
    return dict(ident=ident, identj=identj, id32x4=id32x4, mask_rep=mask_rep,
                e32x2=e32x2, e32p=e32p, ones_bf=ones_bf, ones_f16=ones_f16)

def host_inputs(inputs, b, T):
    im = {}
    im["x"] = np.ascontiguousarray(np.asarray(inputs["L0_attn_hidden_states"])[b])
    for d, pre in (("f", "fwd"), ("b", "bwd")):
        g = lambda n: np.asarray(inputs[f"L0_attn_{pre}_{n}"])
        im[f"win_{d}"] = _tile_win(g("W_in"))
        im[f"wout_{d}"] = np.ascontiguousarray(g("W_out"))
        im[f"convw_{d}"] = np.ascontiguousarray(g("conv_w").T.reshape(18, 128, 4))
        im[f"convb_{d}"] = np.ascontiguousarray(g("conv_b").reshape(18, 128, 1))
        im[f"dtb_{d}"] = np.ascontiguousarray(g("dt_bias").reshape(32, 1))
        im[f"alog_{d}"] = np.ascontiguousarray(g("A_log").reshape(32, 1))
        im[f"drep_{d}"] = np.ascontiguousarray(np.repeat(g("D"), 64).reshape(1, 2048))
        im[f"normw_{d}"] = np.ascontiguousarray(g("norm_w").reshape(16, 128, 1))
    gf = lambda n: np.asarray(inputs[f"L0_ffn_{n}"])
    im["fproj"] = np.ascontiguousarray(gf("proj").reshape(8, 128, 16))
    im["fc1"] = np.ascontiguousarray(gf("C1").reshape(16, 4, 128).transpose(1, 0, 2))
    im["fc2"] = np.ascontiguousarray(gf("C2").reshape(16, 4, 128).transpose(1, 0, 2))
    im["fwin1"] = np.ascontiguousarray(gf("Win1")[:1024].reshape(8, 128, 512))
    im["fb1"] = np.ascontiguousarray(gf("Win1")[1024].reshape(4, 128, 1))
    im["fwout1t"] = np.ascontiguousarray(gf("Wout1").T.reshape(4, 128, 4096))
    im["fwin2"] = np.ascontiguousarray(gf("Win2")[:4096].reshape(32, 128, 512))
    im["fb2"] = np.ascontiguousarray(gf("Win2")[4096].reshape(4, 128, 1))
    im["fwout2t"] = np.ascontiguousarray(gf("Wout2").T.reshape(4, 128, 1024))
    im.update(_consts())
    return im

def build_program(T=2048, taps=(), stage=99, dirs=("f", "b")):
    NT = T // 128
    NQ = T // QT
    taps = set(taps)

    nc = bass.Bass("TRN2", target_bir_lowering=False, detect_race_conditions=False)
    dr = {}
    def din(name, shape, dt=F32):
        dr[name] = nc.dram_tensor(name, list(shape), dt, kind="ExternalInput").ap()
    din("x", (T, D))
    for d in ("f", "b"):
        din(f"win_{d}", (WIN_TOTAL,))
        din(f"wout_{d}", (DIN, D))
        din(f"convw_{d}", (18, 128, 4)); din(f"convb_{d}", (18, 128, 1))
        din(f"dtb_{d}", (32, 1)); din(f"alog_{d}", (32, 1))
        din(f"drep_{d}", (1, 2048)); din(f"normw_{d}", (16, 128, 1))
    din("fproj", (8, 128, 16)); din("fc1", (4, 16, 128)); din("fc2", (4, 16, 128))
    din("fwin1", (8, 128, 512)); din("fb1", (4, 128, 1))
    din("fwout1t", (4, 128, 4096))
    din("fwin2", (32, 128, 512)); din("fb2", (4, 128, 1))
    din("fwout2t", (4, 128, 1024))
    din("ident", (128, 128), BF16); din("identj", (128, 128), BF16)
    din("id32x4", (128, 32), BF16)
    din("mask_rep", (128, 1024), BF16)
    din("e32x2", (64, 4096), FP16); din("e32p", (64, 2048), FP16)
    din("ones_bf", (1, 128), BF16); din("ones_f16", (1, 128), FP16)
    out = nc.dram_tensor("out", [T, D], F32, kind="ExternalOutput").ap()
    wbf = {}
    for d_ in ("f", "b"):
        wbf[f"win_{d_}"] = nc.dram_tensor(f"win_{d_}_bf", [WIN_TOTAL], BF16).ap()
        wbf[f"wout_{d_}"] = nc.dram_tensor(f"wout_{d_}_bf", [DIN, D], BF16).ap()
    scr_cum = nc.dram_tensor("scr_cum", [2, 4, 32, 128], FP16).ap()
    scr_ncl = nc.dram_tensor("scr_ncl", [4, 32, 128], FP16).ap()

    tapd = {}
    def deftap(name, shape, dt=F32):
        if name in taps:
            tapd[name] = nc.dram_tensor("tap_" + name, list(shape), dt,
                                        kind="ExternalOutput").ap()
    deftap("x1q", (128, 8 * QT), BF16)
    deftap("ztm", (128, 4 * 2048), BF16)
    deftap("xbc", (128, 18 * 516), BF16)
    deftap("dtq", (66, QT), FP16)
    deftap("cumhl", (64, QT), FP16)
    deftap("Xtm", (128, 2048), BF16)
    deftap("Mpg", (128, 2048), BF16)
    deftap("ybuf", (128, 2048), BF16)
    deftap("yfm", (128, 16 * 512), BF16)
    deftap("Sst", (128, 2048), BF16)
    deftap("attn_f", (128, NT * 1024), BF16)
    deftap("attn", (128, NT * 1024), BF16)
    deftap("xmid", (128, NT * 1024), BF16)
    deftap("acat", (128, 64), F32)
    deftap("aorig", (128, 64), F32)
    deftap("zs_keep", (128, 64), F32)
    deftap("ascr", (128, 64), F32)
    deftap("astat", (128, 24), F32)
    deftap("hfm", (128, 32 * 512), BF16)

    S = Sched()
    es = ExitStack()
    sb = lambda n, shp, dt=F32: es.enter_context(nc.sbuf_tensor(n, list(shp), dt))

    xres = sb("xres", (128, NT * 1024), BF16)
    psum = es.enter_context(nc.psum_tensor("ps", [128, 4096], F32))
    def bank(b, p0=0, pn=128, w=512):
        return psum[p0:p0 + pn, 512 * b:512 * b + w]
    ident = sb("identS", (128, 128), BF16)
    identj = sb("identJS", (128, 128), BF16)
    epscol = sb("epscol", (128, 1))

    def tapcp(name, src, eng="sync"):
        t = tapd.get(name)
        if t is not None:
            S.dma(eng, t[:, :], src)

    statc = sb("statc", (128, 16 * 2))
    colc = sb("colc", (128, 2 * 2))
    lnslot = [0]
    def ln_cols(src_tile, cv=None):
        sl = lnslot[0] % 2; lnslot[0] += 1
        st = statc[:, 16 * sl:16 * sl + 16]
        if cv is None:
            cv = colc[:, 2 * sl:2 * sl + 2]
        w_ = src_tile.shape[-1]
        half = w_ // 2
        S.op("vector", lambda e: e.bn_stats(st[:, 0:6], src_tile[:, 0:half]),
             r=[src_tile], w=[st[:, 0:6]])
        S.op("vector", lambda e: e.bn_stats(st[:, 6:12], src_tile[:, half:w_]),
             r=[src_tile], w=[st[:, 6:12]])
        S.op("vector", lambda e: e.bn_aggr(st[:, 12:14], st[:, 0:12]),
             r=[st[:, 0:12]], w=[st[:, 12:14]])
        S.op("scalar", lambda e: e.activation(st[:, 14:15], st[:, 13:14], AF.Sqrt,
                                              bias=epscol[:, :], scale=1.0),
             r=[st[:, 12:14], epscol[:, :]], w=[st[:, 14:15]])
        S.op("vector", lambda e: e.reciprocal(cv[:, 0:1], st[:, 14:15]),
             r=[st[:, 14:15]], w=[cv[:, 0:1]])
        S.op("vector", lambda e: e.scalar_tensor_tensor(cv[:, 1:2], st[:, 12:13], -1.0,
                                                        cv[:, 0:1], ALU.mult, ALU.mult),
             r=[st[:, 12:14], cv[:, 0:1]], w=[cv[:, 1:2]])
        return cv[:, 0:1], cv[:, 1:2]

    tslot = [0]
    def tbank(b):
        return psum[:, 512 * b:512 * b + 256].bitcast(BF16)
    def transpose_pack(srcs, evac_eng, evac_fn, extra_w=(), rev=False, idn=None,
                       idn_r=(), base=2):
        b = base + tslot[0] % 2; tslot[0] += 1
        ba = tbank(b)
        if idn is None:
            idn = identj if rev else ident
        for j, s in enumerate(srcs):
            S.op("tensor", lambda e, s=s, j=j, ba=ba, idn=idn: e.transpose(
                ba[:, 128 * j:128 * j + 128], s, idn[:, :]),
                r=[s, idn[:, :]] + list(idn_r), w=[ba[:, 128 * j:128 * j + 128]])
        S.op(evac_eng, lambda e: evac_fn(e, ba), r=[ba], w=list(extra_w))

    def ln_apply_transpose(src_tm, inv, ngm, lntmp, dst_fm_q, tcq, rev=False):
        S.op("vector", lambda e, inv=inv, ngm=ngm, src_tm=src_tm: e.tensor_scalar(
            lntmp[:, 0:1024], src_tm, inv, ngm, ALU.mult, ALU.add),
             r=[src_tm, inv, ngm], w=[lntmp[:, 0:1024]])
        dcq = (3 - tcq) if rev else tcq
        for half in range(2):
            srcs = [lntmp[:, 128 * (4 * half + j):128 * (4 * half + j) + 128] for j in range(4)]
            dsts = [dst_fm_q[:, 512 * (4 * half + j) + 128 * dcq:
                             512 * (4 * half + j) + 128 * dcq + 128] for j in range(4)]
            def ev(e, ba, dsts=dsts):
                for j in range(4):
                    e.tensor_copy(dsts[j], ba[:, 128 * j:128 * j + 128])
                return e.engine_nop()
            transpose_pack(srcs, "vector", ev, extra_w=dsts, rev=rev, base=4)

    def ln_and_transpose(src_tm, lntmp, dst_fm_q, tcq, rev=False):
        inv, ngm = ln_cols(src_tm)
        S.op("vector", lambda e: e.tensor_scalar(lntmp[:, 0:1024], src_tm, inv, ngm,
                                                 ALU.mult, ALU.add),
             r=[src_tm, inv, ngm], w=[lntmp[:, 0:1024]])
        dcq = (3 - tcq) if rev else tcq
        for half in range(2):
            srcs = [lntmp[:, 128 * (4 * half + j):128 * (4 * half + j) + 128] for j in range(4)]
            dsts = [dst_fm_q[:, 512 * (4 * half + j) + 128 * dcq:
                             512 * (4 * half + j) + 128 * dcq + 128] for j in range(4)]
            def ev(e, ba, dsts=dsts):
                for j in range(4):
                    e.tensor_copy(dsts[j], ba[:, 128 * j:128 * j + 128])
                return e.engine_nop()
            transpose_pack(srcs, "vector", ev, extra_w=dsts, rev=rev)

    esA = ExitStack()
    sbA = lambda n, shp, dt=F32: esA.enter_context(nc.sbuf_tensor(n, list(shp), dt))
    attn = sbA("attn", (128, NT * 1024), BF16)
    x1q = sbA("x1q", (128, 8 * QT), BF16)
    lntmp = sbA("lntmp", (128, 1024), BF16)
    ztm = sbA("ztm", (128, 4 * 2048), BF16)
    zscr = sbA("zscr", (128, 2 * 512), BF16)
    xbc = sbA("xbc", (128, 18 * 516), BF16)
    stash = sbA("stash", (128, 18 * 4), BF16)
    dt_fm = sbA("dt_fm", (32, QT))
    lns1 = sbA("lns1", (32, QT))
    aA = sbA("aA", (32, QT)); cumA = sbA("cumA", (32, QT))
    cumB = lns1
    losc = sbA("losc", (32, QT), FP16)
    stack66 = sbA("stack66", (66, QT), FP16)
    cumhl = sbA("cumhl", (64, QT), FP16)
    rhs66 = sbA("rhs66", (66, 4096), FP16)
    Xtm2 = sbA("Xtm2", (128, 2 * 2048), BF16)
    Btm2 = sbA("Btm2", (128, 256), BF16)
    Gp = sbA("Gp", (128, 128), BF16)
    Epg = sbA("Epg", (128, 2 * 1024), BF16)
    Mpg = sbA("Mpg", (128, 2 * 1024), BF16)
    decrow = sbA("decrow", (1, 2048), FP16)
    wcol = sbA("wcol", (128, 32), BF16)
    expc = sbA("expc", (128, 2048), BF16)
    rmsc4 = expc[:, :].bitcast(F32)[:, 0:4]
    ybuf = sbA("ybuf", (128, 2048), BF16)
    rmsc = sbA("rmsc", (128, 2))
    yfm = sbA("yfm", (128, 16 * 512), BF16)
    Sst = sbA("Sst", (128, 2048), BF16)
    Stmp = sbA("Stmp", (128, 2048), BF16)
    yps = Stmp
    drow = Stmp[0:1, :]
    yscr = Stmp[:, :].bitcast(F32)
    winr = sbA("winr", (128, 2 * 2048), BF16)
    wor = sbA("wor", (128, 2 * 1024), BF16)
    dbc = sbA("dbc", (128, 2048), BF16)
    cvacc = sbA("cvacc", (128, 1024), BF16)
    maskS = sbA("maskS", (128, 1024), BF16)
    e32pS = sbA("e32pS", (64, 2048), FP16)
    onesS = sbA("onesS", (1, 128), BF16)
    onesH = sbA("onesH", (1, 128), FP16)

    ccol = sbA("ccol", (128, 18))
    wcol4 = sbA("wcol4", (128, 4 * 18))
    nwcol = sbA("nwcol", (128, 16))
    scol = sbA("scol", (32, 4))

    S.dma("sync", ident[:, :], dr["ident"][:, :])
    S.dma("sync", identj[:, :], dr["identj"][:, :])
    S.dma("sync", maskS[:, :], dr["mask_rep"][:, :])
    S.dma("sync", rhs66[0:64, :], dr["e32x2"][:, :])
    S.dma("sync", e32pS[:, :], dr["e32p"][:, :])
    S.dma("sync", onesS[:, :], dr["ones_bf"][:, :])
    S.dma("sync", onesH[:, :], dr["ones_f16"][:, :])
    S.dma("gpsimd", xres[:, 0:NT * 1024].rearrange("p (j d) -> p j d", d=1024),
          bass.AP(dr["x"].tensor, 0, [[1024, 128], [128 * 1024, NT], [1, 1024]]))
    for d_ in ("f", "b"):
        S.dma("gpsimd", wbf[f"win_{d_}"][:], dr[f"win_{d_}"][:])
        S.dma("gpsimd", wbf[f"wout_{d_}"][:, :], dr[f"wout_{d_}"][:, :])
    S.op("vector", lambda e: e.memset(stack66[64:66, :], 1.0), w=[stack66[64:66, :]])
    S.op("vector", lambda e: e.memset(epscol[:, :], float(EPS)), w=[epscol[:, :]])
    S.op("vector", lambda e: e.memset(xbc[:, :], 0.0), w=[xbc[:, :]])

    def mamba_dir(d, rev):
        S.dma("gpsimd", drow[:, :], dr[f"drep_{d}"][:, :])
        S.dma("sync", wcol4[:, :].rearrange("p (c k) -> p c k", k=4),
              bass.AP(dr[f"convw_{d}"].tensor, 0, [[4, 128], [512, 18], [1, 4]]))
        S.dma("sync", ccol[:, :].unsqueeze(2),
              bass.AP(dr[f"convb_{d}"].tensor, 0, [[1, 128], [128, 18], [1, 1]]), slow=True)
        S.dma("sync", nwcol[:, :].unsqueeze(2),
              bass.AP(dr[f"normw_{d}"].tensor, 0, [[1, 128], [128, 16], [1, 1]]), slow=True)
        S.dma("sync", scol[0:32, 0:1], dr[f"dtb_{d}"][:, :])
        S.dma("sync", scol[0:32, 1:2], dr[f"alog_{d}"][:, :])
        S.op("vector", lambda e: e.tensor_scalar(scol[0:32, 2:3], scol[0:32, 0:1], -1.0,
                                                 None, ALU.mult),
             r=[scol[0:32, 0:1]], w=[scol[0:32, 2:3]])
        S.op("scalar", lambda e: e.activation(scol[0:32, 3:4], scol[0:32, 1:2], AF.Exp),
             r=[scol[0:32, 1:2]], w=[scol[0:32, 3:4]])
        for j in range(4):
            S.op("tensor", lambda e, j=j: e.matmul(bank(j), onesS[:, :],
                                                   drow[:, 512 * j:512 * j + 512],
                                                   start=True, stop=True),
                 r=[onesS[:, :], drow[:, 512 * j:512 * j + 512]], w=[bank(j)])
        S.op("vector", lambda e: e.tensor_copy(dbc[:, 0:1024], psum[:, 0:1024]),
             r=[psum[:, 0:1024]], w=[dbc[:, 0:1024]])
        S.op("vector", lambda e: e.tensor_copy(dbc[:, 1024:2048], psum[:, 1024:2048]),
             r=[psum[:, 1024:2048]], w=[dbc[:, 1024:2048]])
        S.op("vector", lambda e: e.memset(Sst[:, :], 0.0), w=[Sst[:, :]])
        S.op("vector", lambda e: e.memset(stash[:, :], 0.0), w=[stash[:, :]])

        x1cv = Stmp[:, :].bitcast(F32)[:, 0:8]
        def emit_x1_stats(qq, tcq):
            oqq = (NQ - 1 - qq) if rev else qq
            tc = 4 * oqq + tcq
            ln_cols(xres[:, 1024 * tc:1024 * tc + 1024],
                    cv=x1cv[:, 2 * tcq:2 * tcq + 2])
        def emit_x1_apply(qq, tcq):
            oqq = (NQ - 1 - qq) if rev else qq
            tc = 4 * oqq + tcq
            ln_apply_transpose(xres[:, 1024 * tc:1024 * tc + 1024],
                               x1cv[:, 2 * tcq:2 * tcq + 1],
                               x1cv[:, 2 * tcq + 1:2 * tcq + 2],
                               lntmp, x1q, tcq, rev=rev)
        def build_x1(qq):
            for tcq in range(4):
                ln_and_transpose(xres[:, 1024 * (4 * ((NQ - 1 - qq) if rev else qq) + tcq):
                                      1024 * (4 * ((NQ - 1 - qq) if rev else qq) + tcq) + 1024],
                                 lntmp, x1q, tcq, rev=rev)
        carry_steps = []
        build_x1(0)
        for q in range(NQ):
            oq = (NQ - 1 - q) if rev else q
            if q + 1 < NQ:
                for tcq in range(4):
                    carry_steps.append(lambda qq=q + 1, tcq=tcq: emit_x1_stats(qq, tcq))
            def win_dma(s2):
                slot = s2 % 2
                wsrc = wbf[f"win_{d}"]
                if s2 == 0:
                    src = bass.AP(wsrc.tensor, WIN_TILE_OFF[0],
                                  [[8 * 32, 128], [1, 8 * 32]])
                    S.dma("sync", winr[:, 0:256], src)
                else:
                    p0 = 2 * s2 - 1
                    src = bass.AP(wsrc.tensor, WIN_TILE_OFF[p0],
                                  [[8 * 128, 128], [WIN_TILE_OFF[p0 + 1] - WIN_TILE_OFF[p0], 2],
                                   [1, 8 * 128]])
                    S.dma("sync",
                          winr[:, 2048 * slot:2048 * slot + 2048].rearrange(
                              "p (m kw) -> p m kw", m=2), src)

            def emit_conv_tile(ct, pb):
                bk = bank(pb)
                wc = lambda k: wcol4[:, 4 * ct + k:4 * ct + k + 1]
                S.op("vector", lambda e, ct=ct, bk=bk: e.tensor_scalar(
                    cvacc[:, 3:512], bk[:, 0:509], wc(0), None, ALU.mult),
                    r=[bk, wc(0)], w=[cvacc[:, 3:512]])
                S.op("vector", lambda e, ct=ct: e.tensor_scalar(
                    cvacc[:, 0:3], stash[:, 4 * ct:4 * ct + 3], wc(0), None, ALU.mult),
                    r=[stash[:, 4 * ct:4 * ct + 3], wc(0)], w=[cvacc[:, 0:3]])
                for k in (1, 2):
                    d_, s_ = (k % 2, (k + 1) % 2)
                    hw = 3 - k
                    S.op("vector", lambda e, ct=ct, k=k, d_=d_, s_=s_, bk=bk, hw=hw:
                         e.scalar_tensor_tensor(
                             cvacc[:, 512 * d_ + hw:512 * d_ + 512], bk[:, 0:509 + k],
                             wc(k), cvacc[:, 512 * s_ + hw:512 * s_ + 512],
                             ALU.mult, ALU.add),
                         r=[bk, wc(k), cvacc[:, 512 * s_:512 * s_ + 512]],
                         w=[cvacc[:, 512 * d_:512 * d_ + 512]])
                    S.op("vector", lambda e, ct=ct, k=k, d_=d_, s_=s_, hw=hw:
                         e.scalar_tensor_tensor(
                             cvacc[:, 512 * d_:512 * d_ + hw],
                             stash[:, 4 * ct + k:4 * ct + 3],
                             wc(k), cvacc[:, 512 * s_:512 * s_ + hw],
                             ALU.mult, ALU.add),
                         r=[stash[:, 4 * ct + k:4 * ct + 3], wc(k),
                            cvacc[:, 512 * s_:512 * s_ + hw]],
                         w=[cvacc[:, 512 * d_:512 * d_ + hw]])
                S.op("vector", lambda e, ct=ct, bk=bk: e.scalar_tensor_tensor(
                    cvacc[:, 512:1024], bk[:, 0:512], wc(3), cvacc[:, 0:512],
                    ALU.mult, ALU.add),
                    r=[bk, wc(3), cvacc[:, 0:512]], w=[cvacc[:, 512:1024]])
                S.op("vector", lambda e, ct=ct, bk=bk: e.tensor_copy(
                    stash[:, 4 * ct:4 * ct + 3], bk[:, 509:512]),
                    r=[bk], w=[stash[:, 4 * ct:4 * ct + 3]])
                S.op("scalar", lambda e, ct=ct: e.activation(
                    xbc[:, 516 * ct + 3:516 * ct + 515], cvacc[:, 512:1024], AF.Silu,
                    bias=ccol[:, ct:ct + 1], scale=1.0),
                    r=[cvacc[:, 512:1024], ccol[:, ct:ct + 1]],
                    w=[xbc[:, 516 * ct + 3:516 * ct + 515]])
                S.op("vector", lambda e, ct=ct: e.tensor_copy(xbc[:, 516 * ct:516 * ct + 3],
                                                              stash[:, 4 * ct:4 * ct + 3]),
                     r=[stash[:, 4 * ct:4 * ct + 3]], w=[xbc[:, 516 * ct:516 * ct + 3]])

            def dt_steps():
                def s0():
                    S.op("scalar", lambda e: e.activation(lns1[0:32, :], dt_fm[0:32, :], AF.Sigmoid,
                                                          bias=scol[0:32, 2:3], scale=-1.0),
                         r=[dt_fm[0:32, :], scol[0:32, 2:3]], w=[lns1[0:32, :]])
                    S.op("scalar", lambda e: e.activation(lns1[0:32, :], lns1[0:32, :], AF.Ln),
                         r=[lns1[0:32, :]], w=[lns1[0:32, :]])
                    S.op("vector", lambda e: e.tensor_scalar(aA[0:32, :], lns1[0:32, :],
                                                             scol[0:32, 3:4], None, ALU.mult),
                         r=[lns1[0:32, :], scol[0:32, 3:4]], w=[aA[0:32, :]])
                    S.op("scalar", lambda e: e.activation(dt_fm[0:32, :], lns1[0:32, :], AF.Ln, scale=-1.0),
                         r=[lns1[0:32, :]], w=[dt_fm[0:32, :]])
                    S.op("vector", lambda e: e.tensor_scalar(dt_fm[0:32, :], dt_fm[0:32, :], -80.0,
                                                             None, ALU.max),
                         r=[dt_fm[0:32, :]], w=[dt_fm[0:32, :]])
                yield s0
                bufs = [aA, cumA, cumB, cumA, cumB, cumA, cumB, cumA]
                for i, dd in enumerate((1, 2, 4, 8, 16, 32, 64)):
                    def s1(srcb=bufs[i], dstb=bufs[i + 1], dd=dd):
                        def stg(e, s=srcb, t=dstb, dd=dd):
                            s3 = s[0:32, :].rearrange("p (c t) -> p c t", t=Q)
                            d3 = t[0:32, :].rearrange("p (c t) -> p c t", t=Q)
                            e.tensor_copy(d3[:, :, 0:dd], s3[:, :, 0:dd])
                            return e.tensor_tensor(d3[:, :, dd:Q], s3[:, :, dd:Q],
                                                   s3[:, :, 0:Q - dd], ALU.add)
                        S.op("vector", stg, r=[srcb[0:32, :]], w=[dstb[0:32, :]])
                    yield s1
                cum = bufs[7]
                scw0 = bass.AP(scr_cum.tensor, 0, [[128, 32], [4096, 4], [1, 128]])
                scw1 = bass.AP(scr_cum.tensor, 16384, [[128, 32], [4096, 4], [1, 128]])
                r3 = lambda t: t.rearrange("h (c u) -> h c u", u=128)
                def s2():
                    S.op("vector", lambda e: e.tensor_copy(cumhl[0:32, :], cum[0:32, :]),
                         r=[cum[0:32, :]], w=[cumhl[0:32, :]])
                    S.op("vector", lambda e: e.tensor_tensor(losc[0:32, :], cum[0:32, :],
                                                             cumhl[0:32, :], ALU.subtract),
                         r=[cum[0:32, :], cumhl[0:32, :]], w=[losc[0:32, :]])
                    S.dma("sync", scw0, r3(cumhl[0:32, :]))
                    S.dma("sync", scw1, r3(losc[0:32, :]))
                yield s2
                def s3():
                    S.dma("sync", r3(cumhl[32:64, :]), scw1)
                    S.op("vector", lambda e: e.tensor_tensor(lns1[0:32, :], dt_fm[0:32, :],
                                                             cum[0:32, :], ALU.subtract),
                         r=[dt_fm[0:32, :], cum[0:32, :]], w=[lns1[0:32, :]])
                    S.op("vector", lambda e: e.tensor_copy(stack66[0:32, :], lns1[0:32, :]),
                         r=[lns1[0:32, :]], w=[stack66[0:32, :]])
                yield s3
                def s4():
                    S.op("vector", lambda e: e.tensor_tensor(losc[0:32, :], lns1[0:32, :],
                                                             stack66[0:32, :], ALU.subtract),
                         r=[lns1[0:32, :], stack66[0:32, :]], w=[losc[0:32, :]])
                    snw = bass.AP(scr_ncl.tensor, 0, [[128, 32], [4096, 4], [1, 128]])
                    S.dma("sync", snw, r3(losc[0:32, :]))
                    S.dma("sync", r3(stack66[32:64, :]), snw)
                    if "dtq" in taps and d == "f" and q == NQ - 1:
                        tapcp("dtq", stack66[:, :])
                    if "cumhl" in taps and d == "f" and q == NQ - 1:
                        tapcp("cumhl", cumhl[:, :])
                yield s4

            win_dma(0); win_dma(1)
            dt_it = dt_steps()
            pend_conv = []
            pend_ztr = [None]
            def emit_ztr(zm):
                half = 512 * (zm % 2)
                srcs = [zscr[:, half + 128 * j:half + 128 * j + 128] for j in range(4)]
                zdst = ztm[:, :].rearrange("p (c f) -> p c f", f=2048)[:, :, 128 * zm:128 * zm + 128]
                def zev(e, ba, zdst=zdst):
                    return e.activation(zdst, ba.rearrange("p (c f) -> p c f", f=128),
                                        AF.Silu)
                transpose_pack(srcs, "scalar", zev, base=4,
                               extra_w=[ztm[:, :].rearrange("p (c f) -> p c f", f=2048)[:, :, 128 * zm:128 * zm + 128]])
            for pos in range(35):
                m = MORDER[pos]
                w = 128 if m < 34 else 32
                s2 = 0 if pos == 0 else (pos + 1) // 2
                slot = s2 % 2
                sub = 0 if pos == 0 else (pos - 1) % 2
                base = 2048 * slot + 1024 * sub
                pb = pos % 4
                for k in range(8):
                    wt = winr[:, base + w * k:base + w * k + w]
                    rhs = x1q[:, 512 * k:512 * k + 512]
                    S.op("tensor", lambda e, wt=wt, rhs=rhs, k=k, pb=pb, w=w: e.matmul(
                        bank(pb, 0, w), wt, rhs, start=(k == 0), stop=(k == 7)),
                        r=[wt, rhs], w=[bank(pb, 0, w)])
                if m < 16 and pend_ztr[0] is not None:
                    emit_ztr(pend_ztr[0]); pend_ztr[0] = None
                if m == 34:
                    S.op("vector", lambda e, pb=pb: e.tensor_copy(dt_fm[0:32, :], bank(pb, 0, 32)),
                         r=[bank(pb, 0, 32)], w=[dt_fm[0:32, :]])
                elif m >= 16:
                    pass
                else:
                    half = 512 * (m % 2)
                    S.op("vector", lambda e, pb=pb, half=half: e.tensor_copy(
                        zscr[:, half:half + 512], bank(pb)),
                        r=[bank(pb)], w=[zscr[:, half:half + 512]])
                if pend_conv:
                    emit_conv_tile(*pend_conv.pop(0))
                if m >= 16 and m < 34:
                    pend_conv.append((m - 16, pb))
                if m < 16:
                    pend_ztr[0] = m
                step = next(dt_it, None)
                if step is not None:
                    step()
                elif carry_steps:
                    carry_steps.pop(0)()
                if (pos == 0 or sub == 1) and s2 + 2 <= 17:
                    win_dma(s2 + 2)
            if pend_ztr[0] is not None:
                emit_ztr(pend_ztr[0]); pend_ztr[0] = None
            while carry_steps:
                carry_steps.pop(0)()
            if q + 1 < NQ:
                for tcq in range(4):
                    emit_x1_apply(q + 1, tcq)
                    if pend_conv:
                        emit_conv_tile(*pend_conv.pop(0))
            while pend_conv:
                emit_conv_tile(*pend_conv.pop(0))

            if "xbc" in taps and d == "f" and q == NQ - 1:
                tapcp("xbc", xbc[:, :])
            if "ztm" in taps and d == "f" and q == NQ - 1:
                tapcp("ztm", ztm[:, :])
            if "x1q" in taps and d == "f" and q == NQ - 1:
                tapcp("x1q", x1q[:, :])
            if stage < 2:
                continue

            def emit_xb_transp(cq):
                cofs = 128 * cq
                Xtm = Xtm2[:, 2048 * (cq % 2):2048 * (cq % 2) + 2048]
                Bfm = xbc[:, 516 * 16 + 3 + cofs:516 * 16 + 3 + cofs + 128]
                for pk in range(4):
                    srcs = [xbc[:, 516 * (4 * pk + j) + 3 + cofs:516 * (4 * pk + j) + 3 + cofs + 128]
                            for j in range(4)]
                    def xev(e, ba, pk=pk, Xtm=Xtm):
                        return e.activation(Xtm[:, 512 * pk:512 * pk + 512], ba, AF.Copy)
                    transpose_pack(srcs, "scalar", xev, base=0,
                                   extra_w=[Xtm[:, 512 * pk:512 * pk + 512]])
                def bev(e, ba):
                    return e.activation(Btm2[:, 128 * (cq % 2):128 * (cq % 2) + 128],
                                        ba[:, 0:128], AF.Copy)
                transpose_pack([Bfm], "scalar", bev, base=0,
                               extra_w=[Btm2[:, 128 * (cq % 2):128 * (cq % 2) + 128]])
            for cq in range(4):
                cofs = 128 * cq
                csl = slice(cofs, cofs + 128)
                Xtm = Xtm2[:, 2048 * (cq % 2):2048 * (cq % 2) + 2048]
                Btm = Btm2[:, 128 * (cq % 2):128 * (cq % 2) + 128]
                Bfm = xbc[:, 516 * 16 + 3 + cofs:516 * 16 + 3 + cofs + 128]
                Cfm = xbc[:, 516 * 17 + 3 + cofs:516 * 17 + 3 + cofs + 128]
                S.dma("sync", rhs66[64:66, :],
                      bass.AP(scr_cum.tensor, 4096 * cq, [[16384, 2], [1, 4096]]))
                if cq == 0:
                    emit_xb_transp(0)
                S.op("tensor", lambda e, Bfm=Bfm, Cfm=Cfm: e.matmul(
                    bank(6)[:, 0:128], Bfm, Cfm, start=True, stop=True),
                    r=[Bfm, Cfm], w=[bank(6)[:, 0:128]])
                S.op("vector", lambda e: e.tensor_copy(Gp[:, :], bank(6)[:, 0:128]),
                     r=[bank(6)[:, 0:128]], w=[Gp[:, :]])
                def emit_dpage_mm(pg, bs):
                    S.op("tensor", lambda e, bs=bs, hb=0: e.matmul(
                        bank(bs[0]), ident[:, :], maskS[:, 0:512],
                        start=True, stop=False),
                        r=[ident[:, :], maskS[:, 0:512]], w=[bank(bs[0])])
                    S.op("tensor", lambda e, bs=bs: e.matmul(
                        bank(bs[1]), ident[:, :], maskS[:, 512:1024],
                        start=True, stop=False),
                        r=[ident[:, :], maskS[:, 512:1024]], w=[bank(bs[1])])
                    st66 = stack66[:, csl]
                    for hb in range(2):
                        S.op("tensor", lambda e, bs=bs, hb=hb, pg=pg, st66=st66: e.matmul(
                            bank(bs[hb]), st66,
                            rhs66[:, 1024 * pg + 512 * hb:1024 * pg + 512 * hb + 512],
                            start=False, stop=True),
                            r=[st66, rhs66[:, 1024 * pg + 512 * hb:1024 * pg + 512 * hb + 512]],
                            w=[bank(bs[hb])])
                def emit_dpage_evac(pg, bs):
                    ps_ = pg % 2
                    S.op("scalar", lambda e, ps_=ps_, bs=bs: e.activation(
                        Epg[:, 1024 * ps_:1024 * ps_ + 1024],
                        psum[:, 512 * bs[0]:512 * bs[0] + 1024], AF.Exp),
                        r=[psum[:, 512 * bs[0]:512 * bs[0] + 1024]],
                        w=[Epg[:, 1024 * ps_:1024 * ps_ + 1024]])
                    S.op("vector", lambda e, ps_=ps_: e.tensor_tensor(
                        Mpg[:, 1024 * ps_:1024 * ps_ + 1024].rearrange("p (h t) -> p h t", t=128),
                        Epg[:, 1024 * ps_:1024 * ps_ + 1024].rearrange("p (h t) -> p h t", t=128),
                        Gp[:, :].unsqueeze(1).broadcast_to([128, 8, 128]), ALU.mult),
                        r=[Epg[:, 1024 * ps_:1024 * ps_ + 1024], Gp[:, :]],
                        w=[Mpg[:, 1024 * ps_:1024 * ps_ + 1024]])
                    S.op("vector", lambda e, ps_=ps_, pg=pg: e.tensor_copy(
                        wcol[:, 8 * pg:8 * pg + 8].unsqueeze(2),
                        Epg[:, 1024 * ps_:1024 * ps_ + 1024].rearrange("p (h t) -> p h t", t=128)[:, :, 127:128]),
                        r=[Epg[:, 1024 * ps_:1024 * ps_ + 1024]], w=[wcol[:, 8 * pg:8 * pg + 8]])
                def emit_dpage(pg, bs):
                    emit_dpage_mm(pg, bs)
                    emit_dpage_evac(pg, bs)
                def emit_intra(pg, ib):
                    ps_ = pg % 2
                    for hh in range(8):
                        h = 8 * pg + hh
                        mp = Mpg[:, 1024 * ps_ + 128 * hh:1024 * ps_ + 128 * hh + 128]
                        S.op("tensor", lambda e, mp=mp, h=h, hh=hh, ib=ib, Xtm=Xtm: e.matmul(
                            bank(ib)[:, 64 * hh:64 * hh + 64], mp, Xtm[:, 64 * h:64 * h + 64],
                            start=True, stop=True),
                            r=[mp, Xtm[:, 64 * h:64 * h + 64]],
                            w=[bank(ib)[:, 64 * hh:64 * hh + 64]])
                chl = cumhl[:, csl]
                def emit_expcum(hf):
                    for j_ in range(2):
                        S.op("tensor", lambda e, j_=j_, hf=hf, chl=chl: e.matmul(
                            bank(2 + j_), chl,
                            e32pS[:, 1024 * hf + 512 * j_:1024 * hf + 512 * j_ + 512],
                            start=True, stop=True),
                            r=[chl, e32pS[:, 1024 * hf + 512 * j_:1024 * hf + 512 * j_ + 512]],
                            w=[bank(2 + j_)])
                    S.op("scalar", lambda e, hf=hf: e.activation(
                        expc[:, 1024 * hf:1024 * hf + 1024], psum[:, 1024:2048], AF.Exp),
                        r=[psum[:, 1024:2048]], w=[expc[:, 1024 * hf:1024 * hf + 1024]])
                for hf in range(2):
                    pga, pgb = 2 * hf, 2 * hf + 1
                    if hf == 0:
                        emit_dpage(pga, [0, 1])
                        emit_dpage(pgb, [4, 5])
                        emit_expcum(0)
                        emit_dpage_mm(2, [0, 1])
                        emit_dpage_mm(3, [4, 5])
                    else:
                        emit_expcum(1)
                    emit_intra(pga, 6)
                    emit_intra(pgb, 7)
                    if hf == 0:
                        emit_dpage_evac(2, [0, 1])
                        emit_dpage_evac(3, [4, 5])
                    for js, sl in enumerate((pga, pgb)):
                        S.op("tensor", lambda e, js=js, sl=sl, Cfm=Cfm: e.matmul(
                            bank(2 + js), Cfm, Sst[:, 512 * sl:512 * sl + 512],
                            start=True, stop=True),
                            r=[Cfm, Sst[:, 512 * sl:512 * sl + 512]], w=[bank(2 + js)])
                    for js, sl in enumerate((pga, pgb)):
                        ib = 6 + js
                        S.op("vector", lambda e, sl=sl, js=js: e.tensor_tensor(
                            yscr[:, 512 * js:512 * js + 512],
                            expc[:, 512 * sl:512 * sl + 512], bank(2 + js), ALU.mult),
                            r=[expc[:, 512 * sl:512 * sl + 512], bank(2 + js)],
                            w=[yscr[:, 512 * js:512 * js + 512]])
                        S.op("vector", lambda e, js=js, ib=ib, sl=sl: e.tensor_tensor(
                            yps[:, 512 * sl:512 * sl + 512], yscr[:, 512 * js:512 * js + 512],
                            bank(ib), ALU.add),
                            r=[yscr[:, 512 * js:512 * js + 512], bank(ib)],
                            w=[yps[:, 512 * sl:512 * sl + 512]])
                        S.op("gpsimd", lambda e, sl=sl, Xtm=Xtm: e.tensor_tensor(
                            ybuf[:, 512 * sl:512 * sl + 512], Xtm[:, 512 * sl:512 * sl + 512],
                            dbc[:, 512 * sl:512 * sl + 512], ALU.mult),
                            r=[Xtm[:, 512 * sl:512 * sl + 512], dbc[:, 512 * sl:512 * sl + 512]],
                            w=[ybuf[:, 512 * sl:512 * sl + 512]])
                        S.op("vector", lambda e, sl=sl: e.tensor_tensor(
                            yscr[:, 512:1024].bitcast(BF16) if False else yps[:, 512 * sl:512 * sl + 512],
                            ybuf[:, 512 * sl:512 * sl + 512],
                            yps[:, 512 * sl:512 * sl + 512], ALU.add),
                            r=[ybuf[:, 512 * sl:512 * sl + 512], yps[:, 512 * sl:512 * sl + 512]],
                            w=[yps[:, 512 * sl:512 * sl + 512]])
                        S.op("vector", lambda e, sl=sl, cq=cq: e.tensor_tensor(
                            ybuf[:, 512 * sl:512 * sl + 512], yps[:, 512 * sl:512 * sl + 512],
                            ztm[:, 2048 * cq + 512 * sl:2048 * cq + 512 * sl + 512], ALU.mult),
                            r=[yps[:, 512 * sl:512 * sl + 512],
                               ztm[:, 2048 * cq + 512 * sl:2048 * cq + 512 * sl + 512]],
                            w=[ybuf[:, 512 * sl:512 * sl + 512]])
                        dbuf = Epg if sl < 2 else Mpg
                        dsl_ = 0 if sl == 3 else 512 * (sl % 2) + 1024 * (sl // 2)
                        S.op("vector", lambda e, sl=sl, dsl_=dsl_, dbuf=dbuf:
                             e.scalar_tensor_tensor(
                                 dbuf[:, dsl_:dsl_ + 512], ybuf[:, 512 * sl:512 * sl + 512],
                                 1.0, ybuf[:, 512 * sl:512 * sl + 512], ALU.bypass, ALU.mult,
                                 accum_out=rmsc4[:, sl:sl + 1]),
                             r=[ybuf[:, 512 * sl:512 * sl + 512]],
                             w=[dbuf[:, dsl_:dsl_ + 512], rmsc4[:, sl:sl + 1]])
                if cq + 1 < 4:
                    emit_xb_transp(cq + 1)
                lcol = cumhl[:, cofs + 127:cofs + 128]
                for j in range(2):
                    S.op("tensor", lambda e, j=j, lcol=lcol: e.matmul(
                        bank(j, 0, 1), lcol, e32pS[:, 512 * j:512 * j + 512],
                        start=True, stop=True),
                        r=[lcol, e32pS[:, 512 * j:512 * j + 512]], w=[bank(j, 0, 1)])
                for j in range(2):
                    S.op("tensor", lambda e, j=j, lcol=lcol: e.matmul(
                        bank(4 + j, 0, 1), lcol, e32pS[:, 1024 + 512 * j:1024 + 512 * j + 512],
                        start=True, stop=True),
                        r=[lcol, e32pS[:, 1024 + 512 * j:1024 + 512 * j + 512]],
                        w=[bank(4 + j, 0, 1)])
                S.op("scalar", lambda e: e.activation(decrow[:, 0:1024], psum[0:1, 0:1024],
                                                      AF.Exp),
                     r=[psum[0:1, 0:1024]], w=[decrow[:, 0:1024]])
                S.op("scalar", lambda e: e.activation(decrow[:, 1024:2048], psum[0:1, 2048:3072],
                                                      AF.Exp),
                     r=[psum[0:1, 2048:3072]], w=[decrow[:, 1024:2048]])
                for sh in range(2):
                    for j in range(2):
                        js = 2 * sh + j
                        S.op("tensor", lambda e, js=js, j=j, sh=sh: e.matmul(
                            bank(4 + 2 * sh + j), onesH[:, :], decrow[:, 512 * js:512 * js + 512],
                            start=True, stop=True),
                            r=[onesH[:, :], decrow[:, 512 * js:512 * js + 512]],
                            w=[bank(4 + 2 * sh + j)])
                S.op("scalar", lambda e: e.activation(rmsc[:, 1:2], epscol[:, :], AF.Sqrt),
                     r=[epscol[:, :]], w=[rmsc[:, 1:2]])
                S.op("vector", lambda e: e.tensor_reduce(
                    rmsc[:, 0:1], rmsc4[:, 0:4].unsqueeze(1), AXX, ALU.add),
                    r=[rmsc4[:, 0:4]], w=[rmsc[:, 0:1]])
                S.op("scalar", lambda e: e.activation(rmsc[:, 1:2], rmsc[:, 0:1], AF.Sqrt,
                                                      bias=epscol[:, :], scale=1.0 / DIN),
                     r=[rmsc[:, 0:1], epscol[:, :]], w=[rmsc[:, 1:2]])
                S.op("vector", lambda e: e.reciprocal(rmsc[:, 1:2], rmsc[:, 1:2]),
                     r=[rmsc[:, 1:2]], w=[rmsc[:, 1:2]])
                idn0 = identj if rev else ident
                S.op("vector", lambda e, idn0=idn0: e.tensor_scalar(
                    Gp[:, :], idn0[:, :], rmsc[:, 1:2], None, ALU.mult),
                    r=[idn0[:, :], rmsc[:, 1:2]], w=[Gp[:, :]])
                S.op("gpsimd", lambda e, Xtm=Xtm: e.tensor_tensor(
                    Stmp[:, :].rearrange("p (h w) -> p h w", w=64),
                    Xtm[:, :].rearrange("p (h w) -> p h w", w=64),
                    wcol[:, :].unsqueeze(2).broadcast_to([128, 32, 64]), ALU.mult),
                    r=[Xtm[:, :], wcol[:, :], rmsc[:, 0:1]], w=[Stmp[:, :]])
                ycofs = 128 * (3 - cq) if rev else cofs
                for pk in range(4):
                    b_ = 2 + tslot[0] % 2; tslot[0] += 1
                    bk = bank(b_)
                    for j in range(4):
                        s = ybuf[:, 128 * (4 * pk + j):128 * (4 * pk + j) + 128]
                        S.op("tensor", lambda e, s=s, j=j, bk=bk: e.matmul(
                            bk[:, 128 * j:128 * j + 128], s, Gp[:, :],
                            start=True, stop=True),
                            r=[s, Gp[:, :]], w=[bk[:, 128 * j:128 * j + 128]])
                    def yev2(e, pk=pk, bk=bk, ycofs=ycofs):
                        d3 = yfm[:, :].rearrange("p (i t) -> p i t", t=512)[
                            :, 4 * pk:4 * pk + 4, ycofs:ycofs + 128]
                        s3 = bk.rearrange("p (j t) -> p j t", t=128)
                        nw3 = nwcol[:, 4 * pk:4 * pk + 4].unsqueeze(2).broadcast_to(
                            [128, 4, 128])
                        return e.tensor_tensor(d3, s3, nw3, ALU.mult)
                    dsts = [yfm[:, 512 * (4 * pk + j) + ycofs:512 * (4 * pk + j) + ycofs + 128]
                            for j in range(4)]
                    S.op("vector", yev2, r=[bk, nwcol[:, 4 * pk:4 * pk + 4]], w=dsts)
                for sh in range(2):
                    for j in range(2):
                        js = 2 * sh + j
                        S.op("tensor", lambda e, js=js, j=j, sh=sh, Btm=Btm: e.matmul(
                            bank(2 * sh + j), Btm[:, :], Stmp[:, 512 * js:512 * js + 512],
                            start=True, stop=True),
                            r=[Btm[:, :], Stmp[:, 512 * js:512 * js + 512]],
                            w=[bank(2 * sh + j)])
                    S.op("vector", lambda e, sh=sh: e.tensor_tensor(
                        Sst[:, 1024 * sh:1024 * sh + 1024], Sst[:, 1024 * sh:1024 * sh + 1024],
                        psum[:, 2048 + 1024 * sh:2048 + 1024 * sh + 1024], ALU.mult),
                        r=[Sst[:, 1024 * sh:1024 * sh + 1024],
                           psum[:, 2048 + 1024 * sh:2048 + 1024 * sh + 1024]],
                        w=[Sst[:, 1024 * sh:1024 * sh + 1024]])
                    S.op("vector", lambda e, sh=sh: e.tensor_tensor(
                        Sst[:, 1024 * sh:1024 * sh + 1024], Sst[:, 1024 * sh:1024 * sh + 1024],
                        psum[:, 1024 * sh:1024 * sh + 1024], ALU.add),
                        r=[Sst[:, 1024 * sh:1024 * sh + 1024],
                           psum[:, 1024 * sh:1024 * sh + 1024]],
                        w=[Sst[:, 1024 * sh:1024 * sh + 1024]])
                if "Xtm" in taps and d == "f" and q == 0 and cq == 0:
                    tapcp("Xtm", Xtm[:, :])
                if "Mpg" in taps and d == "f" and q == 0 and cq == 0:
                    tapcp("Mpg", Mpg[:, :])
                if "ybuf" in taps and d == "f" and q == 0 and cq == 0:
                    tapcp("ybuf", ybuf[:, :])
                if "Sst" in taps and d == "f" and q == 0 and cq == 0:
                    tapcp("Sst", Sst[:, :])
            if stage < 3:
                continue
            def wo_dma(k):
                S.dma("sync", wor[:, 1024 * (k % 2):1024 * (k % 2) + 1024],
                      wbf[f"wout_{d}"][128 * k:128 * k + 128, :])
            wo_dma(0); wo_dma(1)
            for k in range(16):
                slot = k % 2
                for tt in range(4):
                    lhs = yfm[:, 512 * k + 128 * tt:512 * k + 128 * tt + 128]
                    for nb in range(2):
                        S.op("tensor", lambda e, tt=tt, nb=nb, slot=slot, lhs=lhs, k=k: e.matmul(
                            bank(2 * tt + nb), lhs,
                            wor[:, 1024 * slot + 512 * nb:1024 * slot + 512 * nb + 512],
                            start=(k == 0), stop=(k == 15)),
                            r=[lhs, wor[:, 1024 * slot + 512 * nb:1024 * slot + 512 * nb + 512]],
                            w=[bank(2 * tt + nb)])
                if k + 2 < 16:
                    wo_dma(k + 2)
            for tt in range(4):
                otile = (NT - 4 * q - 4 + tt) if rev else (4 * q + tt)
                for nb in range(2):
                    dsl = attn[:, 1024 * otile + 512 * nb:1024 * otile + 512 * nb + 512]
                    if rev:
                        S.op("vector", lambda e, dsl=dsl, tt=tt, nb=nb: e.tensor_tensor(
                            dsl, dsl, bank(2 * tt + nb), ALU.add),
                            r=[dsl, bank(2 * tt + nb)], w=[dsl])
                    else:
                        S.op("vector", lambda e, dsl=dsl, tt=tt, nb=nb: e.tensor_copy(
                            dsl, bank(2 * tt + nb)),
                            r=[bank(2 * tt + nb)], w=[dsl])
            if rev and stage >= 4:
                def mk_fold(otile):
                    at = attn[:, 1024 * otile:1024 * otile + 1024]
                    xt = xres[:, 1024 * otile:1024 * otile + 1024]
                    st = {}
                    def c1():
                        st["ln"] = ln_cols(at)
                    def c2():
                        inv, ngm = st["ln"]
                        S.op("vector", lambda e, at=at, inv=inv, ngm=ngm: e.tensor_scalar(
                            lntmp[:, 0:1024], at, inv, ngm, ALU.mult, ALU.add),
                            r=[at, inv, ngm], w=[lntmp[:, 0:1024]])
                        S.op("vector", lambda e, xt=xt: e.tensor_tensor(
                            xt, xt, lntmp[:, 0:1024], ALU.add),
                            r=[xt, lntmp[:, 0:1024]], w=[xt])
                    return [c1, c2]
                for tt in range(4):
                    carry_steps.extend(mk_fold(NT - 4 * q - 4 + tt))
                if q == NQ - 1:
                    while carry_steps:
                        carry_steps.pop(0)()
        if "yfm" in taps and d == "f":
            tapcp("yfm", yfm[:, :])

    mamba_dir("f", False)
    tapcp("attn_f", attn[:, :])
    if "b" in dirs:
        mamba_dir("b", True)
        tapcp("attn", attn[:, :])

    if stage >= 4:
        tapcp("xmid", xres[:, :])
    S.barrier()

    esA.close()
    if stage >= 5:
        esB = ExitStack()
        sbB = lambda n, shp, dt=F32: esB.enter_context(nc.sbuf_tensor(n, list(shp), dt))
        x2q = sbB("x2q", (128, 8 * QT), BF16)
        lntmp2 = sbB("lntmp2", (128, 1024), BF16)
        wproj = sbB("wproj", (128, 8 * 16), BF16)
        wc1 = sbB("wc1", (16, 4 * 128), BF16)
        wc2 = sbB("wc2", (16, 4 * 128), BF16)
        wwin1 = sbB("wwin1", (128, 8 * 512), BF16)
        wb = sbB("wb", (128, 8))
        wwout1 = sbB("wwout1", (128, 4 * 4096), BF16)
        wwin2 = sbB("wwin2", (128, 32 * 512), BF16)
        wwout2 = sbB("wwout2", (128, 4 * 1024), BF16)
        acat = sbB("acat", (128, 64)); ascr = sbB("ascr", (128, 64))
        asrt = sbB("asrt", (128, 64)); aorig = sbB("aorig", (128, 64))
        zs_keep = sbB("zs_keep", (128, 64))
        astat = sbB("astat", (128, 24))
        invk16 = sbB("invk16", (128, 16))
        pos16 = sbB("pos16", (128, 16))
        afm = sbB("afm", (16, QT), BF16)
        g1b = sbB("g1b", (128, 4 * 512)); g2b = sbB("g2b", (128, 4 * 512))
        z1g = sbB("z1g", (128, 4 * 512), BF16); z2g = sbB("z2g", (128, 4 * 512), BF16)
        hfm = sbB("hfm", (128, 32 * 512), BF16)
        fout = sbB("fout", (128, 1024))

        r2 = lambda t, n: t.rearrange("p (a b) -> p a b", a=n)
        S.dma("gpsimd", r2(wproj[:, :], 8),
              bass.AP(dr["fproj"].tensor, 0, [[16, 128], [2048, 8], [1, 16]]))
        S.dma("gpsimd", r2(wc1[0:16, :], 4),
              bass.AP(dr["fc1"].tensor, 0, [[128, 16], [2048, 4], [1, 128]]))
        S.dma("gpsimd", r2(wc2[0:16, :], 4),
              bass.AP(dr["fc2"].tensor, 0, [[128, 16], [2048, 4], [1, 128]]))
        S.dma("gpsimd", r2(wwin1[:, :], 8),
              bass.AP(dr["fwin1"].tensor, 0, [[512, 128], [65536, 8], [1, 512]]))
        S.dma("sync", wb[:, 0:4].unsqueeze(2),
              bass.AP(dr["fb1"].tensor, 0, [[1, 128], [128, 4], [1, 1]]), slow=True)
        S.dma("gpsimd", r2(wwout1[:, :], 4),
              bass.AP(dr["fwout1t"].tensor, 0, [[4096, 128], [524288, 4], [1, 4096]]))
        S.dma("gpsimd", r2(wwin2[:, :], 32),
              bass.AP(dr["fwin2"].tensor, 0, [[512, 128], [65536, 32], [1, 512]]))
        S.dma("sync", wb[:, 4:8].unsqueeze(2),
              bass.AP(dr["fb2"].tensor, 0, [[1, 128], [128, 4], [1, 1]]), slow=True)
        S.dma("gpsimd", r2(wwout2[:, :], 4),
              bass.AP(dr["fwout2t"].tensor, 0, [[1024, 128], [131072, 4], [1, 1024]]))

        def bld_invk(e):
            for i in range(16):
                e.memset(invk16[:, i:i + 1], 1.0 / (i + 1))
                e.memset(pos16[:, i:i + 1], float(i + 1))
            return e.engine_nop()
        S.op("vector", bld_invk, w=[invk16[:, :], pos16[:, :]])

        a3 = lambda t: t.rearrange("p (c e) -> p c e", e=16)
        bc4 = lambda col: col.unsqueeze(2).broadcast_to([128, 4, 16])
        bck = lambda t: t.unsqueeze(1).broadcast_to([128, 4, 16])

        def emit_entmax():
            ra = lambda t: [t[:, :]]
            S.op("vector", lambda e: e.tensor_reduce(astat[:, 0:4], a3(acat[:, :]), AXX, ALU.add),
                 r=[acat[:, :]], w=[astat[:, 0:4]])
            S.op("vector", lambda e: e.tensor_scalar(astat[:, 0:4], astat[:, 0:4], 1.0 / 16,
                                                     None, ALU.mult),
                 r=[astat[:, 0:4]], w=[astat[:, 0:4]])
            S.op("vector", lambda e: e.tensor_tensor(a3(aorig[:, :]), a3(acat[:, :]),
                                                     bc4(astat[:, 0:4]), ALU.subtract),
                 r=[acat[:, :], astat[:, 0:4]], w=[aorig[:, :]])
            S.op("vector", lambda e: e.tensor_tensor(a3(ascr[:, :]), a3(aorig[:, :]),
                                                     a3(aorig[:, :]), ALU.mult),
                 r=[aorig[:, :]], w=[ascr[:, :]])
            S.op("vector", lambda e: e.tensor_reduce(astat[:, 4:8], a3(ascr[:, :]), AXX, ALU.add),
                 r=[ascr[:, :]], w=[astat[:, 4:8]])
            S.op("scalar", lambda e: e.activation(astat[:, 8:12], astat[:, 4:8], AF.Sqrt,
                                                  bias=epscol[:, :], scale=1.0 / 16),
                 r=[astat[:, 4:8], epscol[:, :]], w=[astat[:, 8:12]])
            S.op("vector", lambda e: e.reciprocal(astat[:, 8:12], astat[:, 8:12]),
                 r=[astat[:, 8:12]], w=[astat[:, 8:12]])
            S.op("vector", lambda e: e.tensor_tensor(a3(aorig[:, :]), a3(aorig[:, :]),
                                                     bc4(astat[:, 8:12]), ALU.mult),
                 r=[aorig[:, :], astat[:, 8:12]], w=[aorig[:, :]])
            S.op("vector", lambda e: e.tensor_scalar(aorig[:, :], aorig[:, :], 0.5, None, ALU.mult),
                 r=[aorig[:, :]], w=[aorig[:, :]])
            S.op("vector", lambda e: e.tensor_reduce(astat[:, 12:16], a3(aorig[:, :]), AXX, ALU.max),
                 r=[aorig[:, :]], w=[astat[:, 12:16]])
            S.op("vector", lambda e: e.tensor_tensor(a3(aorig[:, :]), a3(aorig[:, :]),
                                                     bc4(astat[:, 12:16]), ALU.subtract),
                 r=[aorig[:, :], astat[:, 12:16]], w=[aorig[:, :]])
            S.op("vector", lambda e: e.tensor_copy(asrt[:, :], aorig[:, :]),
                 r=[aorig[:, :]], w=[asrt[:, :]])
            def sap(buf, off, dims):
                pairs = [[64, 128]] + [[s, c] for (s, c) in dims if c > 1]
                return bass.AP(buf.tensor if hasattr(buf, "tensor") else buf, off, pairs)
            stages = []
            kk = 2
            while kk <= 16:
                jj = kk // 2
                while jj >= 1:
                    stages.append((kk, jj)); jj //= 2
                kk *= 2
            for (kk, jj) in stages:
                def mk(kk=kk, jj=jj):
                    def f(e):
                        u = 16 // kk
                        g = max(1, kk // (2 * jj))
                        last = None
                        if u == 1:
                            dims = [(16, 4), (2 * jj, g), (1, jj)]
                            lo = sap(asrt[:, :], 0, dims); hi = sap(asrt[:, :], jj, dims)
                            e.tensor_tensor(sap(ascr[:, :], 0, dims), lo, hi, ALU.max)
                            last = e.tensor_tensor(sap(ascr[:, :], jj, dims), lo, hi, ALU.min)
                        else:
                            for half in range(2):
                                o0 = kk * half
                                dims = [(16, 4), (2 * kk, u // 2), (2 * jj, g), (1, jj)]
                                lo = sap(asrt[:, :], o0, dims)
                                hi = sap(asrt[:, :], o0 + jj, dims)
                                omax = sap(ascr[:, :], o0 if half == 0 else o0 + jj, dims)
                                omin = sap(ascr[:, :], o0 + jj if half == 0 else o0, dims)
                                e.tensor_tensor(omax, lo, hi, ALU.max)
                                last = e.tensor_tensor(omin, lo, hi, ALU.min)
                        return last
                    return f
                S.op("vector", mk(), r=[asrt[:, :]], w=[ascr[:, :]])
                S.op("vector", lambda e: e.tensor_copy(asrt[:, :], ascr[:, :]),
                     r=[ascr[:, :]], w=[asrt[:, :]])
            S.op("vector", lambda e: e.tensor_copy(zs_keep[:, :], asrt[:, :]),
                 r=[asrt[:, :]], w=[zs_keep[:, :]])
            S.op("vector", lambda e: e.tensor_copy(ascr[:, :], asrt[:, :]),
                 r=[asrt[:, :]], w=[ascr[:, :]])
            S.op("vector", lambda e: e.tensor_tensor(acat[:, :], asrt[:, :], asrt[:, :], ALU.mult),
                 r=[asrt[:, :]], w=[acat[:, :]])
            for buf in (ascr, acat):
                for ddd in (1, 2, 4, 8):
                    def f2(e, buf=buf, ddd=ddd):
                        t = a3(buf[:, :]); s = a3(asrt[:, :])
                        e.tensor_copy(s[:, :, 0:ddd], t[:, :, 0:ddd])
                        return e.tensor_tensor(s[:, :, ddd:16], t[:, :, ddd:16],
                                               t[:, :, 0:16 - ddd], ALU.add)
                    S.op("vector", f2, r=[buf[:, :]], w=[asrt[:, :]])
                    S.op("vector", lambda e, buf=buf: e.tensor_copy(buf[:, :], asrt[:, :]),
                         r=[asrt[:, :]], w=[buf[:, :]])
            S.op("vector", lambda e: e.tensor_tensor(a3(ascr[:, :]), a3(ascr[:, :]),
                                                     bck(invk16[:, :]), ALU.mult),
                 r=[ascr[:, :], invk16[:, :]], w=[ascr[:, :]])
            S.op("vector", lambda e: e.tensor_tensor(a3(acat[:, :]), a3(acat[:, :]),
                                                     bck(invk16[:, :]), ALU.mult),
                 r=[acat[:, :], invk16[:, :]], w=[acat[:, :]])
            S.op("vector", lambda e: e.tensor_tensor(a3(asrt[:, :]), a3(ascr[:, :]),
                                                     a3(ascr[:, :]), ALU.mult),
                 r=[ascr[:, :]], w=[asrt[:, :]])
            S.op("vector", lambda e: e.tensor_tensor(a3(acat[:, :]), a3(acat[:, :]),
                                                     a3(asrt[:, :]), ALU.subtract),
                 r=[acat[:, :], asrt[:, :]], w=[acat[:, :]])
            S.op("vector", lambda e: e.tensor_tensor(a3(acat[:, :]), bck(invk16[:, :]),
                                                     a3(acat[:, :]), ALU.subtract),
                 r=[acat[:, :], invk16[:, :]], w=[acat[:, :]])
            S.op("vector", lambda e: e.tensor_scalar(acat[:, :], acat[:, :], 0.0, None, ALU.max),
                 r=[acat[:, :]], w=[acat[:, :]])
            S.op("scalar", lambda e: e.activation(acat[:, :], acat[:, :], AF.Sqrt),
                 r=[acat[:, :]], w=[acat[:, :]])
            S.op("vector", lambda e: e.tensor_tensor(ascr[:, :], ascr[:, :], acat[:, :],
                                                     ALU.subtract),
                 r=[ascr[:, :], acat[:, :]], w=[ascr[:, :]])
            S.op("vector", lambda e: e.tensor_tensor(acat[:, :], ascr[:, :], zs_keep[:, :],
                                                     ALU.is_le),
                 r=[ascr[:, :], zs_keep[:, :]], w=[acat[:, :]])
            S.op("vector", lambda e: e.tensor_reduce(astat[:, 16:20], a3(acat[:, :]), AXX, ALU.add),
                 r=[acat[:, :]], w=[astat[:, 16:20]])
            S.op("vector", lambda e: e.tensor_tensor(a3(acat[:, :]), bck(pos16[:, :]),
                                                     bc4(astat[:, 16:20]), ALU.is_le),
                 r=[pos16[:, :], astat[:, 16:20]], w=[acat[:, :]])
            S.op("vector", lambda e: e.tensor_copy(asrt[:, :], acat[:, :]),
                 r=[acat[:, :]], w=[asrt[:, :]])
            S.op("vector", lambda e: e.tensor_tensor(a3(acat[:, :])[:, :, 0:15],
                                                     a3(asrt[:, :])[:, :, 0:15],
                                                     a3(asrt[:, :])[:, :, 1:16], ALU.subtract),
                 r=[asrt[:, :]], w=[acat[:, :]])
            S.op("vector", lambda e: e.tensor_tensor(ascr[:, :], ascr[:, :], acat[:, :], ALU.mult),
                 r=[ascr[:, :], acat[:, :]], w=[ascr[:, :]])
            S.op("vector", lambda e: e.tensor_reduce(astat[:, 20:24], a3(ascr[:, :]), AXX, ALU.add),
                 r=[ascr[:, :]], w=[astat[:, 20:24]])
            S.op("vector", lambda e: e.tensor_tensor(a3(acat[:, :]), a3(aorig[:, :]),
                                                     bc4(astat[:, 20:24]), ALU.subtract),
                 r=[aorig[:, :], astat[:, 20:24]], w=[acat[:, :]])
            S.op("vector", lambda e: e.tensor_scalar(acat[:, :], acat[:, :], 0.0, None, ALU.max),
                 r=[acat[:, :]], w=[acat[:, :]])
            S.op("vector", lambda e: e.tensor_tensor(acat[:, :], acat[:, :], acat[:, :], ALU.mult),
                 r=[acat[:, :]], w=[acat[:, :]])

        def ffn_prelude_a(q):
            for tcq in range(4):
                tc = 4 * q + tcq
                ln_and_transpose(xres[:, 1024 * tc:1024 * tc + 1024], lntmp2, x2q, tcq)
            for tcq in range(4):
                for k in range(8):
                    xa = x2q[:, 512 * k + 128 * tcq:512 * k + 128 * tcq + 128]
                    S.op("tensor", lambda e, xa=xa, k=k, tcq=tcq: e.matmul(
                        bank(6, 0, 128, 64)[:, 16 * tcq:16 * tcq + 16], xa,
                        wproj[:, 16 * k:16 * k + 16], start=(k == 0), stop=(k == 7)),
                        r=[xa, wproj[:, 16 * k:16 * k + 16]],
                        w=[bank(6, 0, 128, 64)[:, 16 * tcq:16 * tcq + 16]])
            S.op("vector", lambda e: e.tensor_copy(acat[:, :], bank(6, 0, 128, 64)),
                 r=[bank(6, 0, 128, 64)], w=[acat[:, :]])
        def ffn_prelude_b():
            S.op("vector", lambda e: e.tensor_copy(lntmp2[:, 0:64], acat[:, :]),
                 r=[acat[:, :]], w=[lntmp2[:, 0:64]])
            b = 2 + tslot[0] % 2; tslot[0] += 1
            tb = tbank(b)
            for tcq in range(4):
                s_ = lntmp2[:, 16 * tcq:16 * tcq + 16]
                S.op("tensor", lambda e, s_=s_, tcq=tcq, tb=tb: e.transpose(
                    tb[0:16, 128 * tcq:128 * tcq + 128], s_, ident[:, :]),
                    r=[s_, ident[:, :]], w=[tb[:, 128 * tcq:128 * tcq + 128]])
            S.op("vector", lambda e, tb=tb: e.tensor_copy(afm[0:16, :], tb[0:16, :]),
                 r=[tb], w=[afm[0:16, :]])
        def emit_g(wc, gb, cg0):
            for r_ in range(4):
                cg = 6 + (r_ % 2)
                wca = wc[0:16, 128 * r_:128 * r_ + 128]
                S.op("tensor", lambda e, wca=wca, cg=cg: e.matmul(
                    bank(cg), wca, afm[0:16, :], start=True, stop=True),
                    r=[wca, afm[0:16, :]], w=[bank(cg)])
                S.op("vector", lambda e, gb=gb, r_=r_, cg=cg: e.tensor_copy(
                    gb[:, 512 * r_:512 * r_ + 512], bank(cg)),
                    r=[bank(cg)], w=[gb[:, 512 * r_:512 * r_ + 512]])
        z1r = hfm[:, 0:2048]
        ffn_prelude_a(0)
        emit_entmax()
        for q in range(NQ):
            if q == 0:
                for r_ in range(4):
                    for k in range(8):
                        wa = wwin1[:, 512 * k + 128 * r_:512 * k + 128 * r_ + 128]
                        S.op("tensor", lambda e, wa=wa, r_=r_, k=k: e.matmul(
                            bank(r_), wa, x2q[:, 512 * k:512 * k + 512],
                            start=(k == 0), stop=(k == 7)),
                            r=[wa, x2q[:, 512 * k:512 * k + 512]], w=[bank(r_)])
                    S.op("vector", lambda e, r_=r_: e.tensor_scalar(
                        z1r[:, 512 * r_:512 * r_ + 512], bank(r_), wb[:, r_:r_ + 1],
                        None, ALU.add),
                        r=[bank(r_), wb[:, r_:r_ + 1]],
                        w=[z1r[:, 512 * r_:512 * r_ + 512]])
                ffn_prelude_b()
                emit_g(wc1, g1b, 6)
                emit_g(wc2, g2b, 7)
                for r_ in range(4):
                    S.op("vector", lambda e, r_=r_: e.tensor_tensor(
                        z1g[:, 512 * r_:512 * r_ + 512], z1r[:, 512 * r_:512 * r_ + 512],
                        g1b[:, 512 * r_:512 * r_ + 512], ALU.mult),
                        r=[z1r[:, 512 * r_:512 * r_ + 512], g1b[:, 512 * r_:512 * r_ + 512]],
                        w=[z1g[:, 512 * r_:512 * r_ + 512]])
            else:
                emit_g(wc1, g1b, 6)
                emit_g(wc2, g2b, 7)
                for r_ in range(4):
                    for k in range(8):
                        wa = wwin1[:, 512 * k + 128 * r_:512 * k + 128 * r_ + 128]
                        S.op("tensor", lambda e, wa=wa, r_=r_, k=k: e.matmul(
                            bank(r_), wa, x2q[:, 512 * k:512 * k + 512],
                            start=(k == 0), stop=(k == 7)),
                            r=[wa, x2q[:, 512 * k:512 * k + 512]], w=[bank(r_)])
                    S.op("vector", lambda e, r_=r_: e.scalar_tensor_tensor(
                        z1g[:, 512 * r_:512 * r_ + 512], bank(r_), wb[:, r_:r_ + 1],
                        g1b[:, 512 * r_:512 * r_ + 512], ALU.add, ALU.mult),
                        r=[bank(r_), wb[:, r_:r_ + 1], g1b[:, 512 * r_:512 * r_ + 512]],
                        w=[z1g[:, 512 * r_:512 * r_ + 512]])
            if q + 1 < NQ:
                ffn_prelude_a(q + 1)
            for o in range(32):
                pb = 4 + o % 2
                for r_ in range(4):
                    wa = wwout1[:, 4096 * r_ + 128 * o:4096 * r_ + 128 * o + 128]
                    S.op("tensor", lambda e, wa=wa, r_=r_, pb=pb: e.matmul(
                        bank(pb), wa, z1g[:, 512 * r_:512 * r_ + 512],
                        start=(r_ == 0), stop=(r_ == 3)),
                        r=[wa, z1g[:, 512 * r_:512 * r_ + 512]], w=[bank(pb)])
                S.op("scalar", lambda e, o=o, pb=pb: e.activation(
                    hfm[:, 512 * o:512 * o + 512], bank(pb), AF.Gelu),
                    r=[bank(pb)], w=[hfm[:, 512 * o:512 * o + 512]])
            if "hfm" in taps and q == 0:
                tapcp("hfm", hfm[:, :])
            if q + 1 < NQ:
                emit_entmax()
            for r_ in range(4):
                for k in range(32):
                    wa = wwin2[:, 512 * k + 128 * r_:512 * k + 128 * r_ + 128]
                    S.op("tensor", lambda e, wa=wa, r_=r_, k=k: e.matmul(
                        bank(r_), wa, hfm[:, 512 * k:512 * k + 512],
                        start=(k == 0), stop=(k == 31)),
                        r=[wa, hfm[:, 512 * k:512 * k + 512]], w=[bank(r_)])
                S.op("vector", lambda e, r_=r_: e.scalar_tensor_tensor(
                    z2g[:, 512 * r_:512 * r_ + 512], bank(r_), wb[:, 4 + r_:5 + r_],
                    g2b[:, 512 * r_:512 * r_ + 512], ALU.add, ALU.mult),
                    r=[bank(r_), wb[:, 4 + r_:5 + r_], g2b[:, 512 * r_:512 * r_ + 512]],
                    w=[z2g[:, 512 * r_:512 * r_ + 512]])
            if q + 1 < NQ:
                ffn_prelude_b()
            for tt in range(4):
                pb = 4 + 2 * (tt % 2)
                for r_ in range(4):
                    za = z2g[:, 512 * r_ + 128 * tt:512 * r_ + 128 * tt + 128]
                    for nb in range(2):
                        S.op("tensor", lambda e, za=za, r_=r_, nb=nb, pb=pb: e.matmul(
                            bank(pb + nb), za,
                            wwout2[:, 1024 * r_ + 512 * nb:1024 * r_ + 512 * nb + 512],
                            start=(r_ == 0), stop=(r_ == 3)),
                            r=[za, wwout2[:, 1024 * r_ + 512 * nb:1024 * r_ + 512 * nb + 512]],
                            w=[bank(pb + nb)])
                tc = 4 * q + tt
                pspan = psum[:, 512 * pb:512 * pb + 1024]
                inv, ngm = ln_cols(pspan)
                S.op("vector", lambda e, pspan=pspan, inv=inv, ngm=ngm: e.tensor_scalar(
                    fout[:, :], pspan, inv, ngm, ALU.mult, ALU.add),
                     r=[pspan, inv, ngm], w=[fout[:, :]])
                xt = xres[:, 1024 * tc:1024 * tc + 1024]
                S.op("vector", lambda e, xt=xt: e.tensor_tensor(fout[:, :], fout[:, :], xt,
                                                                ALU.add),
                     r=[fout[:, :], xt], w=[fout[:, :]])
                S.dma("sync", out[128 * tc:128 * tc + 128, :], fout[:, :])
    S.finalize()
    alld = [(c, c.n) for name, c in S.chans.items() if name.startswith("dq_") and c.n > 0]
    tail = Op("sync", None, [], [], [], None)
    tail.waits = list(alld)
    S.ops.append(tail)
    tail2 = Op("tensor", None, [], [], [], None)
    tail2.waits = list(alld)
    S.ops.append(tail2)
    S.check()
    with nc.Block() as block:
        S.emit(nc, block, es)
    if stage >= 5:
        esB.close()
    es.close()
    return nc, tapd


def kernel(**inputs):
    x = np.asarray(inputs["L0_attn_hidden_states"])
    B, T, _ = x.shape
    nc, _ = build_program(T=T)
    in_maps = [host_inputs(inputs, b, T) for b in range(B)]
    res = run_bass_kernel_spmd(nc, in_maps, core_ids=list(range(8)))
    return np.stack([r["out"] for r in res.results]).astype(np.float32)


# version 36 (final)
# speedup vs baseline: 1.0934x; 1.0137x over previous
import numpy as np
import ml_dtypes
from contextlib import ExitStack

import concourse.bass as bass
import concourse.mybir as mybir
from concourse.bass_utils import run_bass_kernel_spmd

F32 = mybir.dt.float32
BF16 = mybir.dt.bfloat16
FP16 = mybir.dt.float16
AF = mybir.ActivationFunctionType
ALU = mybir.AluOpType
AXX = mybir.AxisListType.X
BF = ml_dtypes.bfloat16

D = 1024; DIN = 2048; NH = 32; PD = 64
Q = 128; QT = 512
EPS = 1e-5; NEG = -1e30

ENGINES = ("sync", "scalar", "vector", "tensor", "gpsimd")

def _esz(dt):
    try:
        return dt.itemsize
    except Exception:
        return {F32: 4, BF16: 2, FP16: 2}.get(dt, 4)

def ap_range(ap):
    th = ap.tensor
    t = th.name
    esz = _esz(ap.dtype)
    pairs = list(ap.ap)
    onchip = "DRAM" not in str(getattr(ap, "space", "")).upper()
    lo = hi = ap.offset * esz
    dims = pairs[1:] if onchip else pairs
    for step, count in dims:
        if count <= 1:
            continue
        d = step * (count - 1) * esz
        if d >= 0:
            hi += d
        else:
            lo += d
    if onchip:
        pitch = int(th.shape[-1]) * _esz(th.dtype)
        base = (lo // pitch) * pitch
        lo -= base
        hi -= base
        if hi >= pitch:
            lo, hi = 0, pitch
    return (t, lo, hi + esz)

class Op:
    __slots__ = ("eng", "fn", "reads", "writes", "xwaits", "dmach", "dmaval",
                 "mark", "waits", "idx", "barrier_inc")
    def __init__(self, eng, fn, reads, writes, xwaits, dmach):
        self.eng = eng; self.fn = fn
        self.reads = reads; self.writes = writes
        self.xwaits = list(xwaits)
        self.dmach = dmach
        self.dmaval = 0
        self.mark = None
        self.waits = []
        self.idx = 0
        self.barrier_inc = None

class Ch:
    __slots__ = ("name", "n", "sem")
    def __init__(self, name):
        self.name = name; self.n = 0; self.sem = None

class Sched:
    def __init__(self):
        self.ops = []
        self.chans = {}
        self.mark_ch = {e: self.chan("mk_" + e) for e in ENGINES}

    def chan(self, name):
        if name not in self.chans:
            self.chans[name] = Ch(name)
        return self.chans[name]

    DMA_POOL = 10

    def op(self, eng, fn, r=(), w=(), waits=(), dma=False, tag=None):
        reads = [ap_range(a) for a in r if a is not None]
        writes = [ap_range(a) for a in w if a is not None]
        ch = None
        o = Op(eng, fn, reads, writes, waits, ch)
        if dma:
            if not hasattr(self, "dmacnt"):
                self.dmacnt = {}
            i = self.dmacnt.get(eng, 0)
            self.dmacnt[eng] = i + 1
            ch = self.chan(f"dq_{eng}_{i % self.DMA_POOL}")
            if ch.n > 0:
                o.xwaits.append((ch, ch.n))
            ch.n += 16
            o.dmach = ch
            o.dmaval = ch.n
        self.ops.append(o)
        return o

    def dma(self, eng, dst, src, waits=(), tag=None, slow=False):
        if slow:
            fn = lambda e, d=dst, s=src: e.dma_start(out=d, in_=s,
                                                     allow_slow_non_contiguous=True)
        else:
            fn = lambda e, d=dst, s=src: e.dma_start(out=d, in_=s)
        return self.op(eng, fn, r=[src], w=[dst], waits=waits, dma=True, tag=tag)

    def barrier(self):
        bc = {e: self.chan("bar_" + e) for e in ENGINES}
        snap = {}
        for e in ENGINES:
            fn = lambda eng: eng.nop()
            o = self.op(e, fn)
            o.barrier_inc = bc[e]
            bc[e].n += 1
            snap[e] = bc[e].n
            o.xwaits = list(o.xwaits)
        for e in ENGINES:
            o = self.op(e, None,
                        waits=[(bc[e2], snap[e2]) for e2 in ENGINES if e2 != e])
        self._bar_pending = getattr(self, "_bar_pending", [])

    def finalize(self):
        writer = {}
        readers = {}
        for o in self.ops:
            deps = set()
            for t, lo, hi in o.reads:
                for (wl, wh), wo in writer.get(t, ()):
                    if lo < wh and hi > wl:
                        deps.add(wo)
            for t, lo, hi in o.writes:
                for (wl, wh), wo in writer.get(t, ()):
                    if lo < wh and hi > wl:
                        deps.add(wo)
                for (rl, rh), ro in readers.get(t, ()):
                    if lo < rh and hi > rl:
                        deps.add(ro)
            for dep in deps:
                if dep is o:
                    continue
                if dep.dmach is not None:
                    o.waits.append((dep.dmach, dep.dmaval))
                elif dep.eng != o.eng:
                    if dep.mark is None:
                        dep.mark = True
                    o.waits.append(("MARK", dep))
                else:
                    if o.eng == "tensor":
                        continue
                    spans = [hi - lo for (_, lo, hi) in dep.writes] or [0]
                    if max(spans) < 2048:
                        if dep.mark is None:
                            dep.mark = True
                        o.waits.append(("MARK", dep))
            for t, lo, hi in o.writes:
                wl_ = [(rg, wo) for (rg, wo) in writer.get(t, ())
                       if not (lo <= rg[0] and hi >= rg[1])]
                wl_.append(((lo, hi), o))
                writer[t] = wl_
                rl_ = [(rg, ro) for (rg, ro) in readers.get(t, ())
                       if not (lo <= rg[0] and hi >= rg[1])]
                readers[t] = rl_
            for t, lo, hi in o.reads:
                lst = readers.setdefault(t, [])
                key = (o.eng, lo, hi)
                for i_, (rg, ro) in enumerate(lst):
                    if (ro.eng, rg[0], rg[1]) == key:
                        lst[i_] = ((lo, hi), o)
                        break
                else:
                    lst.append(((lo, hi), o))
        cnt = {e: 0 for e in ENGINES}
        for o in self.ops:
            if getattr(o, "barrier_inc", None) is not None:
                pass
            if o.mark:
                cnt[o.eng] += 1
                o.mark = cnt[o.eng]
        for o in self.ops:
            ws = {}
            for item in o.waits:
                if isinstance(item, tuple) and item[0] == "MARK":
                    dep = item[1]
                    c = self.mark_ch[dep.eng]
                    v = dep.mark
                else:
                    c, v = item
                if v and v > 0:
                    ws[c] = max(ws.get(c, 0), v)
            for c, v in o.xwaits:
                if v and v > 0:
                    ws[c] = max(ws.get(c, 0), v)
            o.waits = list(ws.items())
        self.mark_totals = cnt

    def check(self):
        vals = {c: 0 for c in self.chans.values()}
        qs = {}
        for i, o in enumerate(self.ops):
            o.idx = i
            qs.setdefault(o.eng, []).append(o)
        heads = {e: 0 for e in qs}
        done, prog = 0, True
        while prog:
            prog = False
            for e, lst in qs.items():
                while heads[e] < len(lst):
                    o = lst[heads[e]]
                    if any(vals[c] < v for c, v in o.waits):
                        break
                    if o.dmach is not None:
                        vals[o.dmach] = max(vals[o.dmach], o.dmaval)
                    if o.mark:
                        vals[self.mark_ch[o.eng]] = max(vals[self.mark_ch[o.eng]], o.mark)
                    bi = getattr(o, "barrier_inc", None)
                    if bi is not None:
                        vals[bi] = vals[bi] + 1
                    heads[e] += 1; done += 1; prog = True
        if done != len(self.ops):
            stuck = []
            for e, lst in qs.items():
                if heads[e] < len(lst):
                    o = lst[heads[e]]
                    bad = [(c.name, v, vals[c]) for c, v in o.waits if vals[c] < v]
                    stuck.append((e, o.idx, bad[:3]))
            raise RuntimeError(f"deadlock, {len(self.ops) - done} stuck: {stuck[:5]}")

    def emit(self, nc, block, es):
        for c in self.chans.values():
            c.sem = es.enter_context(nc.semaphore(c.name))
        for ename in ENGINES:
            ops = [o for o in self.ops if o.eng == ename]
            def body(eng, ops=ops):
                for o in ops:
                    for c, v in o.waits:
                        eng.wait_ge(c.sem, v)
                    if o.fn is None:
                        continue
                    ins = o.fn(eng)
                    if o.dmach is not None:
                        ins.then_inc(o.dmach.sem, 16)
                    if o.mark:
                        ins.then_inc(self.mark_ch[o.eng].sem, 1)
                    bi = getattr(o, "barrier_inc", None)
                    if bi is not None:
                        ins.then_inc(bi.sem, 1)
            getattr(block, ename)(body)

MORDER = [34, 32, 0, 33, 1, 16, 17]
for _i in range(14):
    MORDER += [18 + _i, 2 + _i]
WIN_TILE_OFF = []
_off = 0
for _m in MORDER:
    WIN_TILE_OFF.append(_off)
    _off += 8 * 128 * (128 if _m < 34 else 32)
WIN_TOTAL = _off

def _tile_win(W):
    blobs = []
    for m in MORDER:
        w = 128 if m < 34 else 32
        t = W[:, m * 128:m * 128 + w].reshape(8, 128, w).transpose(1, 0, 2)
        blobs.append(np.ascontiguousarray(t).reshape(-1))
    return np.concatenate(blobs)

def _consts():
    ident = np.eye(128, dtype=np.float32).astype(BF)
    id32x4 = np.tile(np.eye(32, dtype=np.float32), (4, 1)).astype(BF)
    m = np.where(np.arange(Q)[:, None] > np.arange(Q)[None, :],
                 np.float32(NEG), np.float32(0.0))
    mask_rep = np.tile(m, (1, 8)).astype(BF)
    e = np.zeros((32, 32 * Q), np.float32)
    for h in range(32):
        e[h, h * Q:(h + 1) * Q] = 1.0
    e32x2 = np.concatenate([e, e], 0).astype(np.float16)
    ep = np.zeros((32, 32 * PD), np.float32)
    for h in range(32):
        ep[h, h * PD:(h + 1) * PD] = 1.0
    e32p = np.concatenate([ep, ep], 0).astype(np.float16)
    ones_bf = np.ones((1, 128), dtype=BF)
    ones_f16 = np.ones((1, 128), dtype=np.float16)
    identj = np.eye(128, dtype=np.float32)[:, ::-1].copy().astype(BF)
    return dict(ident=ident, identj=identj, id32x4=id32x4, mask_rep=mask_rep,
                e32x2=e32x2, e32p=e32p, ones_bf=ones_bf, ones_f16=ones_f16)

def host_inputs(inputs, b, T):
    im = {}
    im["x"] = np.ascontiguousarray(np.asarray(inputs["L0_attn_hidden_states"])[b])
    for d, pre in (("f", "fwd"), ("b", "bwd")):
        g = lambda n: np.asarray(inputs[f"L0_attn_{pre}_{n}"])
        im[f"win_{d}"] = _tile_win(g("W_in"))
        im[f"wout_{d}"] = np.ascontiguousarray(g("W_out"))
        im[f"convw_{d}"] = np.ascontiguousarray(g("conv_w").T.reshape(18, 128, 4))
        im[f"convb_{d}"] = np.ascontiguousarray(g("conv_b").reshape(18, 128, 1))
        im[f"dtb_{d}"] = np.ascontiguousarray(g("dt_bias").reshape(32, 1))
        im[f"alog_{d}"] = np.ascontiguousarray(g("A_log").reshape(32, 1))
        im[f"drep_{d}"] = np.ascontiguousarray(np.repeat(g("D"), 64).reshape(1, 2048))
        im[f"normw_{d}"] = np.ascontiguousarray(g("norm_w").reshape(16, 128, 1))
    gf = lambda n: np.asarray(inputs[f"L0_ffn_{n}"])
    im["fproj"] = np.ascontiguousarray(gf("proj").reshape(8, 128, 16))
    im["fc1"] = np.ascontiguousarray(gf("C1").reshape(16, 4, 128).transpose(1, 0, 2))
    im["fc2"] = np.ascontiguousarray(gf("C2").reshape(16, 4, 128).transpose(1, 0, 2))
    im["fwin1"] = np.ascontiguousarray(gf("Win1")[:1024].reshape(8, 128, 512))
    im["fb1"] = np.ascontiguousarray(gf("Win1")[1024].reshape(4, 128, 1))
    im["fwout1t"] = np.ascontiguousarray(gf("Wout1").T.reshape(4, 128, 4096))
    im["fwin2"] = np.ascontiguousarray(gf("Win2")[:4096].reshape(32, 128, 512))
    im["fb2"] = np.ascontiguousarray(gf("Win2")[4096].reshape(4, 128, 1))
    im["fwout2t"] = np.ascontiguousarray(gf("Wout2").T.reshape(4, 128, 1024))
    im.update(_consts())
    return im

def build_program(T=2048, taps=(), stage=99, dirs=("f", "b")):
    NT = T // 128
    NQ = T // QT
    taps = set(taps)

    nc = bass.Bass("TRN2", target_bir_lowering=False, detect_race_conditions=False)
    dr = {}
    def din(name, shape, dt=F32):
        dr[name] = nc.dram_tensor(name, list(shape), dt, kind="ExternalInput").ap()
    din("x", (T, D))
    for d in ("f", "b"):
        din(f"win_{d}", (WIN_TOTAL,))
        din(f"wout_{d}", (DIN, D))
        din(f"convw_{d}", (18, 128, 4)); din(f"convb_{d}", (18, 128, 1))
        din(f"dtb_{d}", (32, 1)); din(f"alog_{d}", (32, 1))
        din(f"drep_{d}", (1, 2048)); din(f"normw_{d}", (16, 128, 1))
    din("fproj", (8, 128, 16)); din("fc1", (4, 16, 128)); din("fc2", (4, 16, 128))
    din("fwin1", (8, 128, 512)); din("fb1", (4, 128, 1))
    din("fwout1t", (4, 128, 4096))
    din("fwin2", (32, 128, 512)); din("fb2", (4, 128, 1))
    din("fwout2t", (4, 128, 1024))
    din("ident", (128, 128), BF16); din("identj", (128, 128), BF16)
    din("id32x4", (128, 32), BF16)
    din("mask_rep", (128, 1024), BF16)
    din("e32x2", (64, 4096), FP16); din("e32p", (64, 2048), FP16)
    din("ones_bf", (1, 128), BF16); din("ones_f16", (1, 128), FP16)
    out = nc.dram_tensor("out", [T, D], F32, kind="ExternalOutput").ap()
    wbf = {}
    for d_ in ("f", "b"):
        wbf[f"win_{d_}"] = nc.dram_tensor(f"win_{d_}_bf", [WIN_TOTAL], BF16).ap()
        wbf[f"wout_{d_}"] = nc.dram_tensor(f"wout_{d_}_bf", [DIN, D], BF16).ap()
    scr_cum = nc.dram_tensor("scr_cum", [2, 4, 32, 128], FP16).ap()
    scr_ncl = nc.dram_tensor("scr_ncl", [4, 32, 128], FP16).ap()

    tapd = {}
    def deftap(name, shape, dt=F32):
        if name in taps:
            tapd[name] = nc.dram_tensor("tap_" + name, list(shape), dt,
                                        kind="ExternalOutput").ap()
    deftap("x1q", (128, 8 * QT), BF16)
    deftap("ztm", (128, 4 * 2048), BF16)
    deftap("xbc", (128, 18 * 516), BF16)
    deftap("dtq", (66, QT), FP16)
    deftap("cumhl", (64, QT), FP16)
    deftap("Xtm", (128, 2048), BF16)
    deftap("Mpg", (128, 2048), BF16)
    deftap("ybuf", (128, 2048), BF16)
    deftap("yfm", (128, 16 * 512), BF16)
    deftap("Sst", (128, 2048), BF16)
    deftap("attn_f", (128, NT * 1024), BF16)
    deftap("attn", (128, NT * 1024), BF16)
    deftap("xmid", (128, NT * 1024), BF16)
    deftap("acat", (128, 64), F32)
    deftap("aorig", (128, 64), F32)
    deftap("zs_keep", (128, 64), F32)
    deftap("ascr", (128, 64), F32)
    deftap("astat", (128, 24), F32)
    deftap("hfm", (128, 32 * 512), BF16)

    S = Sched()
    es = ExitStack()
    sb = lambda n, shp, dt=F32: es.enter_context(nc.sbuf_tensor(n, list(shp), dt))

    xres = sb("xres", (128, NT * 1024), BF16)
    psum = es.enter_context(nc.psum_tensor("ps", [128, 4096], F32))
    def bank(b, p0=0, pn=128, w=512):
        return psum[p0:p0 + pn, 512 * b:512 * b + w]
    ident = sb("identS", (128, 128), BF16)
    identj = sb("identJS", (128, 128), BF16)
    epscol = sb("epscol", (128, 1))

    def tapcp(name, src, eng="sync"):
        t = tapd.get(name)
        if t is not None:
            S.dma(eng, t[:, :], src)

    statc = sb("statc", (128, 16 * 2))
    colc = sb("colc", (128, 2 * 2))
    lnslot = [0]
    def ln_cols(src_tile, cv=None):
        sl = lnslot[0] % 2; lnslot[0] += 1
        st = statc[:, 16 * sl:16 * sl + 16]
        if cv is None:
            cv = colc[:, 2 * sl:2 * sl + 2]
        w_ = src_tile.shape[-1]
        half = w_ // 2
        S.op("vector", lambda e: e.bn_stats(st[:, 0:6], src_tile[:, 0:half]),
             r=[src_tile], w=[st[:, 0:6]])
        S.op("vector", lambda e: e.bn_stats(st[:, 6:12], src_tile[:, half:w_]),
             r=[src_tile], w=[st[:, 6:12]])
        S.op("vector", lambda e: e.bn_aggr(st[:, 12:14], st[:, 0:12]),
             r=[st[:, 0:12]], w=[st[:, 12:14]])
        S.op("scalar", lambda e: e.activation(st[:, 14:15], st[:, 13:14], AF.Sqrt,
                                              bias=epscol[:, :], scale=1.0),
             r=[st[:, 12:14], epscol[:, :]], w=[st[:, 14:15]])
        S.op("vector", lambda e: e.reciprocal(cv[:, 0:1], st[:, 14:15]),
             r=[st[:, 14:15]], w=[cv[:, 0:1]])
        S.op("vector", lambda e: e.scalar_tensor_tensor(cv[:, 1:2], st[:, 12:13], -1.0,
                                                        cv[:, 0:1], ALU.mult, ALU.mult),
             r=[st[:, 12:14], cv[:, 0:1]], w=[cv[:, 1:2]])
        return cv[:, 0:1], cv[:, 1:2]

    tslot = [0]
    def tbank(b):
        return psum[:, 512 * b:512 * b + 256].bitcast(BF16)
    def transpose_pack(srcs, evac_eng, evac_fn, extra_w=(), rev=False, idn=None,
                       idn_r=(), base=2):
        b = base + tslot[0] % 2; tslot[0] += 1
        ba = tbank(b)
        if idn is None:
            idn = identj if rev else ident
        for j, s in enumerate(srcs):
            S.op("tensor", lambda e, s=s, j=j, ba=ba, idn=idn: e.transpose(
                ba[:, 128 * j:128 * j + 128], s, idn[:, :]),
                r=[s, idn[:, :]] + list(idn_r), w=[ba[:, 128 * j:128 * j + 128]])
        S.op(evac_eng, lambda e: evac_fn(e, ba), r=[ba], w=list(extra_w))

    def ln_apply_transpose(src_tm, inv, ngm, lntmp, dst_fm_q, tcq, rev=False):
        S.op("vector", lambda e, inv=inv, ngm=ngm, src_tm=src_tm: e.tensor_scalar(
            lntmp[:, 0:1024], src_tm, inv, ngm, ALU.mult, ALU.add),
             r=[src_tm, inv, ngm], w=[lntmp[:, 0:1024]])
        dcq = (3 - tcq) if rev else tcq
        for half in range(2):
            srcs = [lntmp[:, 128 * (4 * half + j):128 * (4 * half + j) + 128] for j in range(4)]
            dsts = [dst_fm_q[:, 512 * (4 * half + j) + 128 * dcq:
                             512 * (4 * half + j) + 128 * dcq + 128] for j in range(4)]
            def ev(e, ba, dsts=dsts):
                for j in range(4):
                    e.tensor_copy(dsts[j], ba[:, 128 * j:128 * j + 128])
                return e.engine_nop()
            transpose_pack(srcs, "vector", ev, extra_w=dsts, rev=rev, base=4)

    def ln_and_transpose(src_tm, lntmp, dst_fm_q, tcq, rev=False):
        inv, ngm = ln_cols(src_tm)
        S.op("vector", lambda e: e.tensor_scalar(lntmp[:, 0:1024], src_tm, inv, ngm,
                                                 ALU.mult, ALU.add),
             r=[src_tm, inv, ngm], w=[lntmp[:, 0:1024]])
        dcq = (3 - tcq) if rev else tcq
        for half in range(2):
            srcs = [lntmp[:, 128 * (4 * half + j):128 * (4 * half + j) + 128] for j in range(4)]
            dsts = [dst_fm_q[:, 512 * (4 * half + j) + 128 * dcq:
                             512 * (4 * half + j) + 128 * dcq + 128] for j in range(4)]
            def ev(e, ba, dsts=dsts):
                for j in range(4):
                    e.tensor_copy(dsts[j], ba[:, 128 * j:128 * j + 128])
                return e.engine_nop()
            transpose_pack(srcs, "vector", ev, extra_w=dsts, rev=rev)

    esA = ExitStack()
    sbA = lambda n, shp, dt=F32: esA.enter_context(nc.sbuf_tensor(n, list(shp), dt))
    attn = sbA("attn", (128, NT * 1024), BF16)
    x1q = sbA("x1q", (128, 8 * QT), BF16)
    lntmp = sbA("lntmp", (128, 1024), BF16)
    ztm = sbA("ztm", (128, 4 * 2048), BF16)
    zscr = sbA("zscr", (128, 2 * 512), BF16)
    xbc = sbA("xbc", (128, 18 * 516), BF16)
    stash = sbA("stash", (128, 18 * 4), BF16)
    dt_fm = sbA("dt_fm", (32, QT))
    lns1 = sbA("lns1", (32, QT))
    aA = sbA("aA", (32, QT)); cumA = sbA("cumA", (32, QT))
    cumB = lns1
    losc = sbA("losc", (32, QT), FP16)
    stack66 = sbA("stack66", (66, QT), FP16)
    cumhl = sbA("cumhl", (64, QT), FP16)
    rhs66 = sbA("rhs66", (66, 4096), FP16)
    Xtm2 = sbA("Xtm2", (128, 2 * 2048), BF16)
    Btm2 = sbA("Btm2", (128, 256), BF16)
    Gp = sbA("Gp", (128, 128), BF16)
    Epg = sbA("Epg", (128, 2 * 1024), BF16)
    Mpg = sbA("Mpg", (128, 2 * 1024), BF16)
    decrow = sbA("decrow", (1, 2048), FP16)
    wcol = sbA("wcol", (128, 32), BF16)
    expc = sbA("expc", (128, 2048), BF16)
    rmsc4 = expc[:, :].bitcast(F32)[:, 0:4]
    ybuf = sbA("ybuf", (128, 2048), BF16)
    rmsc = sbA("rmsc", (128, 2))
    yfm = sbA("yfm", (128, 16 * 512), BF16)
    Sst = sbA("Sst", (128, 2048), BF16)
    Stmp = sbA("Stmp", (128, 2048), BF16)
    yps = Stmp
    drow = Stmp[0:1, :]
    yscr = Stmp[:, :].bitcast(F32)
    winr = sbA("winr", (128, 2 * 2048), BF16)
    wor = sbA("wor", (128, 2 * 1024), BF16)
    dbc = sbA("dbc", (128, 2048), BF16)
    cvacc = sbA("cvacc", (128, 1024), BF16)
    maskS = sbA("maskS", (128, 1024), BF16)
    e32pS = sbA("e32pS", (64, 2048), FP16)
    onesS = sbA("onesS", (1, 128), BF16)
    onesH = sbA("onesH", (1, 128), FP16)

    ccol = sbA("ccol", (128, 18))
    wcol4 = sbA("wcol4", (128, 4 * 18))
    nwcol = sbA("nwcol", (128, 16))
    scol = sbA("scol", (32, 4))

    S.dma("sync", ident[:, :], dr["ident"][:, :])
    S.dma("sync", identj[:, :], dr["identj"][:, :])
    S.dma("sync", maskS[:, :], dr["mask_rep"][:, :])
    S.dma("sync", rhs66[0:64, :], dr["e32x2"][:, :])
    S.dma("sync", e32pS[:, :], dr["e32p"][:, :])
    S.dma("sync", onesS[:, :], dr["ones_bf"][:, :])
    S.dma("sync", onesH[:, :], dr["ones_f16"][:, :])
    S.dma("gpsimd", xres[:, 0:NT * 1024].rearrange("p (j d) -> p j d", d=1024),
          bass.AP(dr["x"].tensor, 0, [[1024, 128], [128 * 1024, NT], [1, 1024]]))
    for d_ in ("f", "b"):
        S.dma("gpsimd", wbf[f"win_{d_}"][:], dr[f"win_{d_}"][:])
        S.dma("gpsimd", wbf[f"wout_{d_}"][:, :], dr[f"wout_{d_}"][:, :])
    S.op("vector", lambda e: e.memset(stack66[64:66, :], 1.0), w=[stack66[64:66, :]])
    S.op("vector", lambda e: e.memset(epscol[:, :], float(EPS)), w=[epscol[:, :]])
    S.op("vector", lambda e: e.memset(xbc[:, :], 0.0), w=[xbc[:, :]])

    def mamba_dir(d, rev):
        S.dma("gpsimd", drow[:, :], dr[f"drep_{d}"][:, :])
        S.dma("sync", wcol4[:, :].rearrange("p (c k) -> p c k", k=4),
              bass.AP(dr[f"convw_{d}"].tensor, 0, [[4, 128], [512, 18], [1, 4]]))
        S.dma("sync", ccol[:, :].unsqueeze(2),
              bass.AP(dr[f"convb_{d}"].tensor, 0, [[1, 128], [128, 18], [1, 1]]), slow=True)
        S.dma("sync", nwcol[:, :].unsqueeze(2),
              bass.AP(dr[f"normw_{d}"].tensor, 0, [[1, 128], [128, 16], [1, 1]]), slow=True)
        S.dma("sync", scol[0:32, 0:1], dr[f"dtb_{d}"][:, :])
        S.dma("sync", scol[0:32, 1:2], dr[f"alog_{d}"][:, :])
        S.op("vector", lambda e: e.tensor_scalar(scol[0:32, 2:3], scol[0:32, 0:1], -1.0,
                                                 None, ALU.mult),
             r=[scol[0:32, 0:1]], w=[scol[0:32, 2:3]])
        S.op("scalar", lambda e: e.activation(scol[0:32, 3:4], scol[0:32, 1:2], AF.Exp),
             r=[scol[0:32, 1:2]], w=[scol[0:32, 3:4]])
        for j in range(4):
            S.op("tensor", lambda e, j=j: e.matmul(bank(j), onesS[:, :],
                                                   drow[:, 512 * j:512 * j + 512],
                                                   start=True, stop=True),
                 r=[onesS[:, :], drow[:, 512 * j:512 * j + 512]], w=[bank(j)])
        S.op("vector", lambda e: e.tensor_copy(dbc[:, 0:1024], psum[:, 0:1024]),
             r=[psum[:, 0:1024]], w=[dbc[:, 0:1024]])
        S.op("vector", lambda e: e.tensor_copy(dbc[:, 1024:2048], psum[:, 1024:2048]),
             r=[psum[:, 1024:2048]], w=[dbc[:, 1024:2048]])
        S.op("vector", lambda e: e.memset(Sst[:, :], 0.0), w=[Sst[:, :]])
        S.op("vector", lambda e: e.memset(stash[:, :], 0.0), w=[stash[:, :]])

        x1cv = Stmp[:, :].bitcast(F32)[:, 0:8]
        def emit_x1_stats(qq, tcq):
            oqq = (NQ - 1 - qq) if rev else qq
            tc = 4 * oqq + tcq
            ln_cols(xres[:, 1024 * tc:1024 * tc + 1024],
                    cv=x1cv[:, 2 * tcq:2 * tcq + 2])
        def emit_x1_apply(qq, tcq):
            oqq = (NQ - 1 - qq) if rev else qq
            tc = 4 * oqq + tcq
            ln_apply_transpose(xres[:, 1024 * tc:1024 * tc + 1024],
                               x1cv[:, 2 * tcq:2 * tcq + 1],
                               x1cv[:, 2 * tcq + 1:2 * tcq + 2],
                               lntmp, x1q, tcq, rev=rev)
        def build_x1(qq):
            for tcq in range(4):
                ln_and_transpose(xres[:, 1024 * (4 * ((NQ - 1 - qq) if rev else qq) + tcq):
                                      1024 * (4 * ((NQ - 1 - qq) if rev else qq) + tcq) + 1024],
                                 lntmp, x1q, tcq, rev=rev)
        carry_steps = []
        build_x1(0)
        for q in range(NQ):
            oq = (NQ - 1 - q) if rev else q
            if q + 1 < NQ:
                for tcq in range(4):
                    carry_steps.append(lambda qq=q + 1, tcq=tcq: emit_x1_stats(qq, tcq))
            def win_dma(s2):
                slot = s2 % 2
                wsrc = wbf[f"win_{d}"]
                if s2 == 0:
                    src = bass.AP(wsrc.tensor, WIN_TILE_OFF[0],
                                  [[8 * 32, 128], [1, 8 * 32]])
                    S.dma("sync", winr[:, 0:256], src)
                else:
                    p0 = 2 * s2 - 1
                    src = bass.AP(wsrc.tensor, WIN_TILE_OFF[p0],
                                  [[8 * 128, 128], [WIN_TILE_OFF[p0 + 1] - WIN_TILE_OFF[p0], 2],
                                   [1, 8 * 128]])
                    S.dma("sync",
                          winr[:, 2048 * slot:2048 * slot + 2048].rearrange(
                              "p (m kw) -> p m kw", m=2), src)

            def emit_conv_tile(ct, pb):
                bk = bank(pb)
                wc = lambda k: wcol4[:, 4 * ct + k:4 * ct + k + 1]
                S.op("vector", lambda e, ct=ct, bk=bk: e.tensor_scalar(
                    cvacc[:, 3:512], bk[:, 0:509], wc(0), None, ALU.mult),
                    r=[bk, wc(0)], w=[cvacc[:, 3:512]])
                S.op("vector", lambda e, ct=ct: e.tensor_scalar(
                    cvacc[:, 0:3], stash[:, 4 * ct:4 * ct + 3], wc(0), None, ALU.mult),
                    r=[stash[:, 4 * ct:4 * ct + 3], wc(0)], w=[cvacc[:, 0:3]])
                for k in (1, 2):
                    d_, s_ = (k % 2, (k + 1) % 2)
                    hw = 3 - k
                    S.op("vector", lambda e, ct=ct, k=k, d_=d_, s_=s_, bk=bk, hw=hw:
                         e.scalar_tensor_tensor(
                             cvacc[:, 512 * d_ + hw:512 * d_ + 512], bk[:, 0:509 + k],
                             wc(k), cvacc[:, 512 * s_ + hw:512 * s_ + 512],
                             ALU.mult, ALU.add),
                         r=[bk, wc(k), cvacc[:, 512 * s_:512 * s_ + 512]],
                         w=[cvacc[:, 512 * d_:512 * d_ + 512]])
                    S.op("vector", lambda e, ct=ct, k=k, d_=d_, s_=s_, hw=hw:
                         e.scalar_tensor_tensor(
                             cvacc[:, 512 * d_:512 * d_ + hw],
                             stash[:, 4 * ct + k:4 * ct + 3],
                             wc(k), cvacc[:, 512 * s_:512 * s_ + hw],
                             ALU.mult, ALU.add),
                         r=[stash[:, 4 * ct + k:4 * ct + 3], wc(k),
                            cvacc[:, 512 * s_:512 * s_ + hw]],
                         w=[cvacc[:, 512 * d_:512 * d_ + hw]])
                S.op("vector", lambda e, ct=ct, bk=bk: e.scalar_tensor_tensor(
                    cvacc[:, 512:1024], bk[:, 0:512], wc(3), cvacc[:, 0:512],
                    ALU.mult, ALU.add),
                    r=[bk, wc(3), cvacc[:, 0:512]], w=[cvacc[:, 512:1024]])
                S.op("vector", lambda e, ct=ct, bk=bk: e.tensor_copy(
                    stash[:, 4 * ct:4 * ct + 3], bk[:, 509:512]),
                    r=[bk], w=[stash[:, 4 * ct:4 * ct + 3]])
                S.op("scalar", lambda e, ct=ct: e.activation(
                    xbc[:, 516 * ct + 3:516 * ct + 515], cvacc[:, 512:1024], AF.Silu,
                    bias=ccol[:, ct:ct + 1], scale=1.0),
                    r=[cvacc[:, 512:1024], ccol[:, ct:ct + 1]],
                    w=[xbc[:, 516 * ct + 3:516 * ct + 515]])
                S.op("vector", lambda e, ct=ct: e.tensor_copy(xbc[:, 516 * ct:516 * ct + 3],
                                                              stash[:, 4 * ct:4 * ct + 3]),
                     r=[stash[:, 4 * ct:4 * ct + 3]], w=[xbc[:, 516 * ct:516 * ct + 3]])

            def dt_steps():
                def s0():
                    S.op("scalar", lambda e: e.activation(lns1[0:32, :], dt_fm[0:32, :], AF.Sigmoid,
                                                          bias=scol[0:32, 2:3], scale=-1.0),
                         r=[dt_fm[0:32, :], scol[0:32, 2:3]], w=[lns1[0:32, :]])
                    S.op("scalar", lambda e: e.activation(lns1[0:32, :], lns1[0:32, :], AF.Ln),
                         r=[lns1[0:32, :]], w=[lns1[0:32, :]])
                    S.op("vector", lambda e: e.tensor_scalar(aA[0:32, :], lns1[0:32, :],
                                                             scol[0:32, 3:4], None, ALU.mult),
                         r=[lns1[0:32, :], scol[0:32, 3:4]], w=[aA[0:32, :]])
                    S.op("scalar", lambda e: e.activation(dt_fm[0:32, :], lns1[0:32, :], AF.Ln, scale=-1.0),
                         r=[lns1[0:32, :]], w=[dt_fm[0:32, :]])
                    S.op("vector", lambda e: e.tensor_scalar(dt_fm[0:32, :], dt_fm[0:32, :], -80.0,
                                                             None, ALU.max),
                         r=[dt_fm[0:32, :]], w=[dt_fm[0:32, :]])
                yield s0
                bufs = [aA, cumA, cumB, cumA, cumB, cumA, cumB, cumA]
                for i, dd in enumerate((1, 2, 4, 8, 16, 32, 64)):
                    def s1(srcb=bufs[i], dstb=bufs[i + 1], dd=dd):
                        def stg(e, s=srcb, t=dstb, dd=dd):
                            s3 = s[0:32, :].rearrange("p (c t) -> p c t", t=Q)
                            d3 = t[0:32, :].rearrange("p (c t) -> p c t", t=Q)
                            e.tensor_copy(d3[:, :, 0:dd], s3[:, :, 0:dd])
                            return e.tensor_tensor(d3[:, :, dd:Q], s3[:, :, dd:Q],
                                                   s3[:, :, 0:Q - dd], ALU.add)
                        S.op("vector", stg, r=[srcb[0:32, :]], w=[dstb[0:32, :]])
                    yield s1
                cum = bufs[7]
                scw0 = bass.AP(scr_cum.tensor, 0, [[128, 32], [4096, 4], [1, 128]])
                scw1 = bass.AP(scr_cum.tensor, 16384, [[128, 32], [4096, 4], [1, 128]])
                r3 = lambda t: t.rearrange("h (c u) -> h c u", u=128)
                def s2():
                    S.op("vector", lambda e: e.tensor_copy(cumhl[0:32, :], cum[0:32, :]),
                         r=[cum[0:32, :]], w=[cumhl[0:32, :]])
                    S.op("vector", lambda e: e.tensor_tensor(losc[0:32, :], cum[0:32, :],
                                                             cumhl[0:32, :], ALU.subtract),
                         r=[cum[0:32, :], cumhl[0:32, :]], w=[losc[0:32, :]])
                    S.dma("sync", scw0, r3(cumhl[0:32, :]))
                    S.dma("sync", scw1, r3(losc[0:32, :]))
                yield s2
                def s3():
                    S.dma("sync", r3(cumhl[32:64, :]), scw1)
                    S.op("vector", lambda e: e.tensor_tensor(lns1[0:32, :], dt_fm[0:32, :],
                                                             cum[0:32, :], ALU.subtract),
                         r=[dt_fm[0:32, :], cum[0:32, :]], w=[lns1[0:32, :]])
                    S.op("vector", lambda e: e.tensor_copy(stack66[0:32, :], lns1[0:32, :]),
                         r=[lns1[0:32, :]], w=[stack66[0:32, :]])
                yield s3
                def s4():
                    S.op("vector", lambda e: e.tensor_tensor(losc[0:32, :], lns1[0:32, :],
                                                             stack66[0:32, :], ALU.subtract),
                         r=[lns1[0:32, :], stack66[0:32, :]], w=[losc[0:32, :]])
                    snw = bass.AP(scr_ncl.tensor, 0, [[128, 32], [4096, 4], [1, 128]])
                    S.dma("sync", snw, r3(losc[0:32, :]))
                    S.dma("sync", r3(stack66[32:64, :]), snw)
                    if "dtq" in taps and d == "f" and q == NQ - 1:
                        tapcp("dtq", stack66[:, :])
                    if "cumhl" in taps and d == "f" and q == NQ - 1:
                        tapcp("cumhl", cumhl[:, :])
                yield s4

            win_dma(0); win_dma(1)
            dt_it = dt_steps()
            pend_conv = []
            pend_ztr = [None]
            def emit_ztr(zm):
                half = 512 * (zm % 2)
                srcs = [zscr[:, half + 128 * j:half + 128 * j + 128] for j in range(4)]
                zdst = ztm[:, :].rearrange("p (c f) -> p c f", f=2048)[:, :, 128 * zm:128 * zm + 128]
                def zev(e, ba, zdst=zdst):
                    return e.activation(zdst, ba.rearrange("p (c f) -> p c f", f=128),
                                        AF.Silu)
                transpose_pack(srcs, "scalar", zev, base=4,
                               extra_w=[ztm[:, :].rearrange("p (c f) -> p c f", f=2048)[:, :, 128 * zm:128 * zm + 128]])
            for pos in range(35):
                m = MORDER[pos]
                w = 128 if m < 34 else 32
                s2 = 0 if pos == 0 else (pos + 1) // 2
                slot = s2 % 2
                sub = 0 if pos == 0 else (pos - 1) % 2
                base = 2048 * slot + 1024 * sub
                pb = pos % 4
                for k in range(8):
                    wt = winr[:, base + w * k:base + w * k + w]
                    rhs = x1q[:, 512 * k:512 * k + 512]
                    S.op("tensor", lambda e, wt=wt, rhs=rhs, k=k, pb=pb, w=w: e.matmul(
                        bank(pb, 0, w), wt, rhs, start=(k == 0), stop=(k == 7)),
                        r=[wt, rhs], w=[bank(pb, 0, w)])
                if m < 16 and pend_ztr[0] is not None:
                    emit_ztr(pend_ztr[0]); pend_ztr[0] = None
                if m == 34:
                    S.op("vector", lambda e, pb=pb: e.tensor_copy(dt_fm[0:32, :], bank(pb, 0, 32)),
                         r=[bank(pb, 0, 32)], w=[dt_fm[0:32, :]])
                elif m >= 16:
                    pass
                else:
                    half = 512 * (m % 2)
                    S.op("vector", lambda e, pb=pb, half=half: e.tensor_copy(
                        zscr[:, half:half + 512], bank(pb)),
                        r=[bank(pb)], w=[zscr[:, half:half + 512]])
                if pend_conv:
                    emit_conv_tile(*pend_conv.pop(0))
                if m >= 16 and m < 34:
                    pend_conv.append((m - 16, pb))
                if m < 16:
                    pend_ztr[0] = m
                step = next(dt_it, None)
                if step is not None:
                    step()
                elif carry_steps:
                    carry_steps.pop(0)()
                if (pos == 0 or sub == 1) and s2 + 2 <= 17:
                    win_dma(s2 + 2)
            if pend_ztr[0] is not None:
                emit_ztr(pend_ztr[0]); pend_ztr[0] = None
            while carry_steps:
                carry_steps.pop(0)()
            if q + 1 < NQ:
                for tcq in range(4):
                    emit_x1_apply(q + 1, tcq)
                    if pend_conv:
                        emit_conv_tile(*pend_conv.pop(0))
            while pend_conv:
                emit_conv_tile(*pend_conv.pop(0))

            if "xbc" in taps and d == "f" and q == NQ - 1:
                tapcp("xbc", xbc[:, :])
            if "ztm" in taps and d == "f" and q == NQ - 1:
                tapcp("ztm", ztm[:, :])
            if "x1q" in taps and d == "f" and q == NQ - 1:
                tapcp("x1q", x1q[:, :])
            if stage < 2:
                continue

            def emit_xb_transp(cq):
                cofs = 128 * cq
                Xtm = Xtm2[:, 2048 * (cq % 2):2048 * (cq % 2) + 2048]
                Bfm = xbc[:, 516 * 16 + 3 + cofs:516 * 16 + 3 + cofs + 128]
                for pk in range(4):
                    srcs = [xbc[:, 516 * (4 * pk + j) + 3 + cofs:516 * (4 * pk + j) + 3 + cofs + 128]
                            for j in range(4)]
                    def xev(e, ba, pk=pk, Xtm=Xtm):
                        return e.activation(Xtm[:, 512 * pk:512 * pk + 512], ba, AF.Copy)
                    transpose_pack(srcs, "scalar", xev, base=0,
                                   extra_w=[Xtm[:, 512 * pk:512 * pk + 512]])
                def bev(e, ba):
                    return e.activation(Btm2[:, 128 * (cq % 2):128 * (cq % 2) + 128],
                                        ba[:, 0:128], AF.Copy)
                transpose_pack([Bfm], "scalar", bev, base=0,
                               extra_w=[Btm2[:, 128 * (cq % 2):128 * (cq % 2) + 128]])
            for cq in range(4):
                cofs = 128 * cq
                csl = slice(cofs, cofs + 128)
                Xtm = Xtm2[:, 2048 * (cq % 2):2048 * (cq % 2) + 2048]
                Btm = Btm2[:, 128 * (cq % 2):128 * (cq % 2) + 128]
                Bfm = xbc[:, 516 * 16 + 3 + cofs:516 * 16 + 3 + cofs + 128]
                Cfm = xbc[:, 516 * 17 + 3 + cofs:516 * 17 + 3 + cofs + 128]
                S.dma("sync", rhs66[64:66, :],
                      bass.AP(scr_cum.tensor, 4096 * cq, [[16384, 2], [1, 4096]]))
                if cq == 0:
                    emit_xb_transp(0)
                S.op("tensor", lambda e, Bfm=Bfm, Cfm=Cfm: e.matmul(
                    bank(6)[:, 0:128], Bfm, Cfm, start=True, stop=True),
                    r=[Bfm, Cfm], w=[bank(6)[:, 0:128]])
                S.op("vector", lambda e: e.tensor_copy(Gp[:, :], bank(6)[:, 0:128]),
                     r=[bank(6)[:, 0:128]], w=[Gp[:, :]])
                def emit_dpage_mm(pg, bs):
                    S.op("tensor", lambda e, bs=bs, hb=0: e.matmul(
                        bank(bs[0]), ident[:, :], maskS[:, 0:512],
                        start=True, stop=False),
                        r=[ident[:, :], maskS[:, 0:512]], w=[bank(bs[0])])
                    S.op("tensor", lambda e, bs=bs: e.matmul(
                        bank(bs[1]), ident[:, :], maskS[:, 512:1024],
                        start=True, stop=False),
                        r=[ident[:, :], maskS[:, 512:1024]], w=[bank(bs[1])])
                    st66 = stack66[:, csl]
                    for hb in range(2):
                        S.op("tensor", lambda e, bs=bs, hb=hb, pg=pg, st66=st66: e.matmul(
                            bank(bs[hb]), st66,
                            rhs66[:, 1024 * pg + 512 * hb:1024 * pg + 512 * hb + 512],
                            start=False, stop=True),
                            r=[st66, rhs66[:, 1024 * pg + 512 * hb:1024 * pg + 512 * hb + 512]],
                            w=[bank(bs[hb])])
                def emit_dpage_evac(pg, bs):
                    ps_ = pg % 2
                    S.op("scalar", lambda e, ps_=ps_, bs=bs: e.activation(
                        Epg[:, 1024 * ps_:1024 * ps_ + 1024],
                        psum[:, 512 * bs[0]:512 * bs[0] + 1024], AF.Exp),
                        r=[psum[:, 512 * bs[0]:512 * bs[0] + 1024]],
                        w=[Epg[:, 1024 * ps_:1024 * ps_ + 1024]])
                    S.op("vector", lambda e, ps_=ps_: e.tensor_tensor(
                        Mpg[:, 1024 * ps_:1024 * ps_ + 1024].rearrange("p (h t) -> p h t", t=128),
                        Epg[:, 1024 * ps_:1024 * ps_ + 1024].rearrange("p (h t) -> p h t", t=128),
                        Gp[:, :].unsqueeze(1).broadcast_to([128, 8, 128]), ALU.mult),
                        r=[Epg[:, 1024 * ps_:1024 * ps_ + 1024], Gp[:, :]],
                        w=[Mpg[:, 1024 * ps_:1024 * ps_ + 1024]])
                    S.op("vector", lambda e, ps_=ps_, pg=pg: e.tensor_copy(
                        wcol[:, 8 * pg:8 * pg + 8].unsqueeze(2),
                        Epg[:, 1024 * ps_:1024 * ps_ + 1024].rearrange("p (h t) -> p h t", t=128)[:, :, 127:128]),
                        r=[Epg[:, 1024 * ps_:1024 * ps_ + 1024]], w=[wcol[:, 8 * pg:8 * pg + 8]])
                def emit_dpage(pg, bs):
                    emit_dpage_mm(pg, bs)
                    emit_dpage_evac(pg, bs)
                def emit_intra(pg, ib):
                    ps_ = pg % 2
                    for hh in range(8):
                        h = 8 * pg + hh
                        mp = Mpg[:, 1024 * ps_ + 128 * hh:1024 * ps_ + 128 * hh + 128]
                        S.op("tensor", lambda e, mp=mp, h=h, hh=hh, ib=ib, Xtm=Xtm: e.matmul(
                            bank(ib)[:, 64 * hh:64 * hh + 64], mp, Xtm[:, 64 * h:64 * h + 64],
                            start=True, stop=True),
                            r=[mp, Xtm[:, 64 * h:64 * h + 64]],
                            w=[bank(ib)[:, 64 * hh:64 * hh + 64]])
                chl = cumhl[:, csl]
                def emit_expcum(hf):
                    for j_ in range(2):
                        S.op("tensor", lambda e, j_=j_, hf=hf, chl=chl: e.matmul(
                            bank(2 + j_), chl,
                            e32pS[:, 1024 * hf + 512 * j_:1024 * hf + 512 * j_ + 512],
                            start=True, stop=True),
                            r=[chl, e32pS[:, 1024 * hf + 512 * j_:1024 * hf + 512 * j_ + 512]],
                            w=[bank(2 + j_)])
                    S.op("scalar", lambda e, hf=hf: e.activation(
                        expc[:, 1024 * hf:1024 * hf + 1024], psum[:, 1024:2048], AF.Exp),
                        r=[psum[:, 1024:2048]], w=[expc[:, 1024 * hf:1024 * hf + 1024]])
                for hf in range(2):
                    pga, pgb = 2 * hf, 2 * hf + 1
                    if hf == 0:
                        emit_dpage(pga, [0, 1])
                        emit_dpage(pgb, [4, 5])
                        emit_expcum(0)
                        emit_dpage_mm(2, [0, 1])
                        emit_dpage_mm(3, [4, 5])
                    else:
                        emit_expcum(1)
                    emit_intra(pga, 6)
                    emit_intra(pgb, 7)
                    if hf == 0:
                        emit_dpage_evac(2, [0, 1])
                        emit_dpage_evac(3, [4, 5])
                    for js, sl in enumerate((pga, pgb)):
                        S.op("tensor", lambda e, js=js, sl=sl, Cfm=Cfm: e.matmul(
                            bank(2 + js), Cfm, Sst[:, 512 * sl:512 * sl + 512],
                            start=True, stop=True),
                            r=[Cfm, Sst[:, 512 * sl:512 * sl + 512]], w=[bank(2 + js)])
                    for js, sl in enumerate((pga, pgb)):
                        ib = 6 + js
                        S.op("vector", lambda e, sl=sl, js=js: e.tensor_tensor(
                            yscr[:, 512 * js:512 * js + 512],
                            expc[:, 512 * sl:512 * sl + 512], bank(2 + js), ALU.mult),
                            r=[expc[:, 512 * sl:512 * sl + 512], bank(2 + js)],
                            w=[yscr[:, 512 * js:512 * js + 512]])
                        S.op("vector", lambda e, js=js, ib=ib, sl=sl: e.tensor_tensor(
                            yps[:, 512 * sl:512 * sl + 512], yscr[:, 512 * js:512 * js + 512],
                            bank(ib), ALU.add),
                            r=[yscr[:, 512 * js:512 * js + 512], bank(ib)],
                            w=[yps[:, 512 * sl:512 * sl + 512]])
                        S.op("gpsimd", lambda e, sl=sl, Xtm=Xtm: e.tensor_tensor(
                            ybuf[:, 512 * sl:512 * sl + 512], Xtm[:, 512 * sl:512 * sl + 512],
                            dbc[:, 512 * sl:512 * sl + 512], ALU.mult),
                            r=[Xtm[:, 512 * sl:512 * sl + 512], dbc[:, 512 * sl:512 * sl + 512]],
                            w=[ybuf[:, 512 * sl:512 * sl + 512]])
                        S.op("vector", lambda e, sl=sl: e.tensor_tensor(
                            yscr[:, 512:1024].bitcast(BF16) if False else yps[:, 512 * sl:512 * sl + 512],
                            ybuf[:, 512 * sl:512 * sl + 512],
                            yps[:, 512 * sl:512 * sl + 512], ALU.add),
                            r=[ybuf[:, 512 * sl:512 * sl + 512], yps[:, 512 * sl:512 * sl + 512]],
                            w=[yps[:, 512 * sl:512 * sl + 512]])
                        S.op("vector", lambda e, sl=sl, cq=cq: e.tensor_tensor(
                            ybuf[:, 512 * sl:512 * sl + 512], yps[:, 512 * sl:512 * sl + 512],
                            ztm[:, 2048 * cq + 512 * sl:2048 * cq + 512 * sl + 512], ALU.mult),
                            r=[yps[:, 512 * sl:512 * sl + 512],
                               ztm[:, 2048 * cq + 512 * sl:2048 * cq + 512 * sl + 512]],
                            w=[ybuf[:, 512 * sl:512 * sl + 512]])
                        dbuf = Epg if sl < 2 else Mpg
                        dsl_ = 0 if sl == 3 else 512 * (sl % 2) + 1024 * (sl // 2)
                        S.op("vector", lambda e, sl=sl, dsl_=dsl_, dbuf=dbuf:
                             e.scalar_tensor_tensor(
                                 dbuf[:, dsl_:dsl_ + 512], ybuf[:, 512 * sl:512 * sl + 512],
                                 1.0, ybuf[:, 512 * sl:512 * sl + 512], ALU.bypass, ALU.mult,
                                 accum_out=rmsc4[:, sl:sl + 1]),
                             r=[ybuf[:, 512 * sl:512 * sl + 512]],
                             w=[dbuf[:, dsl_:dsl_ + 512], rmsc4[:, sl:sl + 1]])
                if cq + 1 < 4:
                    emit_xb_transp(cq + 1)
                lcol = cumhl[:, cofs + 127:cofs + 128]
                for j in range(2):
                    S.op("tensor", lambda e, j=j, lcol=lcol: e.matmul(
                        bank(j, 0, 1), lcol, e32pS[:, 512 * j:512 * j + 512],
                        start=True, stop=True),
                        r=[lcol, e32pS[:, 512 * j:512 * j + 512]], w=[bank(j, 0, 1)])
                for j in range(2):
                    S.op("tensor", lambda e, j=j, lcol=lcol: e.matmul(
                        bank(4 + j, 0, 1), lcol, e32pS[:, 1024 + 512 * j:1024 + 512 * j + 512],
                        start=True, stop=True),
                        r=[lcol, e32pS[:, 1024 + 512 * j:1024 + 512 * j + 512]],
                        w=[bank(4 + j, 0, 1)])
                S.op("scalar", lambda e: e.activation(decrow[:, 0:1024], psum[0:1, 0:1024],
                                                      AF.Exp),
                     r=[psum[0:1, 0:1024]], w=[decrow[:, 0:1024]])
                S.op("scalar", lambda e: e.activation(decrow[:, 1024:2048], psum[0:1, 2048:3072],
                                                      AF.Exp),
                     r=[psum[0:1, 2048:3072]], w=[decrow[:, 1024:2048]])
                for sh in range(2):
                    for j in range(2):
                        js = 2 * sh + j
                        S.op("tensor", lambda e, js=js, j=j, sh=sh: e.matmul(
                            bank(4 + 2 * sh + j), onesH[:, :], decrow[:, 512 * js:512 * js + 512],
                            start=True, stop=True),
                            r=[onesH[:, :], decrow[:, 512 * js:512 * js + 512]],
                            w=[bank(4 + 2 * sh + j)])
                S.op("scalar", lambda e: e.activation(rmsc[:, 1:2], epscol[:, :], AF.Sqrt),
                     r=[epscol[:, :]], w=[rmsc[:, 1:2]])
                S.op("vector", lambda e: e.tensor_reduce(
                    rmsc[:, 0:1], rmsc4[:, 0:4].unsqueeze(1), AXX, ALU.add),
                    r=[rmsc4[:, 0:4]], w=[rmsc[:, 0:1]])
                S.op("scalar", lambda e: e.activation(rmsc[:, 1:2], rmsc[:, 0:1], AF.Sqrt,
                                                      bias=epscol[:, :], scale=1.0 / DIN),
                     r=[rmsc[:, 0:1], epscol[:, :]], w=[rmsc[:, 1:2]])
                S.op("vector", lambda e: e.reciprocal(rmsc[:, 1:2], rmsc[:, 1:2]),
                     r=[rmsc[:, 1:2]], w=[rmsc[:, 1:2]])
                idn0 = identj if rev else ident
                S.op("vector", lambda e, idn0=idn0: e.tensor_scalar(
                    Gp[:, :], idn0[:, :], rmsc[:, 1:2], None, ALU.mult),
                    r=[idn0[:, :], rmsc[:, 1:2]], w=[Gp[:, :]])
                S.op("gpsimd", lambda e, Xtm=Xtm: e.tensor_tensor(
                    Stmp[:, :].rearrange("p (h w) -> p h w", w=64),
                    Xtm[:, :].rearrange("p (h w) -> p h w", w=64),
                    wcol[:, :].unsqueeze(2).broadcast_to([128, 32, 64]), ALU.mult),
                    r=[Xtm[:, :], wcol[:, :], rmsc[:, 0:1]], w=[Stmp[:, :]])
                ycofs = 128 * (3 - cq) if rev else cofs
                for pk in range(4):
                    b_ = (2, 3, 0, 1)[pk]
                    bk = bank(b_)
                    for j in range(4):
                        s = ybuf[:, 128 * (4 * pk + j):128 * (4 * pk + j) + 128]
                        S.op("tensor", lambda e, s=s, j=j, bk=bk: e.matmul(
                            bk[:, 128 * j:128 * j + 128], s, Gp[:, :],
                            start=True, stop=True),
                            r=[s, Gp[:, :]], w=[bk[:, 128 * j:128 * j + 128]])
                    def yev2(e, pk=pk, bk=bk, ycofs=ycofs):
                        d3 = yfm[:, :].rearrange("p (i t) -> p i t", t=512)[
                            :, 4 * pk:4 * pk + 4, ycofs:ycofs + 128]
                        s3 = bk.rearrange("p (j t) -> p j t", t=128)
                        nw3 = nwcol[:, 4 * pk:4 * pk + 4].unsqueeze(2).broadcast_to(
                            [128, 4, 128])
                        return e.tensor_tensor(d3, s3, nw3, ALU.mult)
                    dsts = [yfm[:, 512 * (4 * pk + j) + ycofs:512 * (4 * pk + j) + ycofs + 128]
                            for j in range(4)]
                    S.op("vector", yev2, r=[bk, nwcol[:, 4 * pk:4 * pk + 4]], w=dsts)
                for sh in range(2):
                    for j in range(2):
                        js = 2 * sh + j
                        S.op("tensor", lambda e, js=js, j=j, sh=sh, Btm=Btm: e.matmul(
                            bank(2 * sh + j), Btm[:, :], Stmp[:, 512 * js:512 * js + 512],
                            start=True, stop=True),
                            r=[Btm[:, :], Stmp[:, 512 * js:512 * js + 512]],
                            w=[bank(2 * sh + j)])
                    S.op("vector", lambda e, sh=sh: e.tensor_tensor(
                        Sst[:, 1024 * sh:1024 * sh + 1024], Sst[:, 1024 * sh:1024 * sh + 1024],
                        psum[:, 2048 + 1024 * sh:2048 + 1024 * sh + 1024], ALU.mult),
                        r=[Sst[:, 1024 * sh:1024 * sh + 1024],
                           psum[:, 2048 + 1024 * sh:2048 + 1024 * sh + 1024]],
                        w=[Sst[:, 1024 * sh:1024 * sh + 1024]])
                    S.op("vector", lambda e, sh=sh: e.tensor_tensor(
                        Sst[:, 1024 * sh:1024 * sh + 1024], Sst[:, 1024 * sh:1024 * sh + 1024],
                        psum[:, 1024 * sh:1024 * sh + 1024], ALU.add),
                        r=[Sst[:, 1024 * sh:1024 * sh + 1024],
                           psum[:, 1024 * sh:1024 * sh + 1024]],
                        w=[Sst[:, 1024 * sh:1024 * sh + 1024]])
                if "Xtm" in taps and d == "f" and q == 0 and cq == 0:
                    tapcp("Xtm", Xtm[:, :])
                if "Mpg" in taps and d == "f" and q == 0 and cq == 0:
                    tapcp("Mpg", Mpg[:, :])
                if "ybuf" in taps and d == "f" and q == 0 and cq == 0:
                    tapcp("ybuf", ybuf[:, :])
                if "Sst" in taps and d == "f" and q == 0 and cq == 0:
                    tapcp("Sst", Sst[:, :])
            if stage < 3:
                continue
            def wo_dma(k):
                S.dma("sync", wor[:, 1024 * (k % 2):1024 * (k % 2) + 1024],
                      wbf[f"wout_{d}"][128 * k:128 * k + 128, :])
            wo_dma(0); wo_dma(1)
            for k in range(16):
                slot = k % 2
                for tt in range(4):
                    lhs = yfm[:, 512 * k + 128 * tt:512 * k + 128 * tt + 128]
                    for nb in range(2):
                        S.op("tensor", lambda e, tt=tt, nb=nb, slot=slot, lhs=lhs, k=k: e.matmul(
                            bank(2 * tt + nb), lhs,
                            wor[:, 1024 * slot + 512 * nb:1024 * slot + 512 * nb + 512],
                            start=(k == 0), stop=(k == 15)),
                            r=[lhs, wor[:, 1024 * slot + 512 * nb:1024 * slot + 512 * nb + 512]],
                            w=[bank(2 * tt + nb)])
                if k + 2 < 16:
                    wo_dma(k + 2)
            for tt in range(4):
                otile = (NT - 4 * q - 4 + tt) if rev else (4 * q + tt)
                for nb in range(2):
                    dsl = attn[:, 1024 * otile + 512 * nb:1024 * otile + 512 * nb + 512]
                    if rev:
                        S.op("vector", lambda e, dsl=dsl, tt=tt, nb=nb: e.tensor_tensor(
                            dsl, dsl, bank(2 * tt + nb), ALU.add),
                            r=[dsl, bank(2 * tt + nb)], w=[dsl])
                    else:
                        S.op("vector", lambda e, dsl=dsl, tt=tt, nb=nb: e.tensor_copy(
                            dsl, bank(2 * tt + nb)),
                            r=[bank(2 * tt + nb)], w=[dsl])
            if rev and stage >= 4:
                def mk_fold(otile):
                    at = attn[:, 1024 * otile:1024 * otile + 1024]
                    xt = xres[:, 1024 * otile:1024 * otile + 1024]
                    st = {}
                    def c1():
                        st["ln"] = ln_cols(at)
                    def c2():
                        inv, ngm = st["ln"]
                        S.op("vector", lambda e, at=at, inv=inv, ngm=ngm: e.tensor_scalar(
                            lntmp[:, 0:1024], at, inv, ngm, ALU.mult, ALU.add),
                            r=[at, inv, ngm], w=[lntmp[:, 0:1024]])
                        S.op("vector", lambda e, xt=xt: e.tensor_tensor(
                            xt, xt, lntmp[:, 0:1024], ALU.add),
                            r=[xt, lntmp[:, 0:1024]], w=[xt])
                    return [c1, c2]
                for tt in range(4):
                    carry_steps.extend(mk_fold(NT - 4 * q - 4 + tt))
                if q == NQ - 1:
                    while carry_steps:
                        carry_steps.pop(0)()
        if "yfm" in taps and d == "f":
            tapcp("yfm", yfm[:, :])

    mamba_dir("f", False)
    tapcp("attn_f", attn[:, :])
    if "b" in dirs:
        mamba_dir("b", True)
        tapcp("attn", attn[:, :])

    if stage >= 4:
        tapcp("xmid", xres[:, :])
    S.barrier()

    esA.close()
    if stage >= 5:
        esB = ExitStack()
        sbB = lambda n, shp, dt=F32: esB.enter_context(nc.sbuf_tensor(n, list(shp), dt))
        x2q = sbB("x2q", (128, 8 * QT), BF16)
        lntmp2 = sbB("lntmp2", (128, 1024), BF16)
        wproj = sbB("wproj", (128, 8 * 16), BF16)
        wc1 = sbB("wc1", (16, 4 * 128), BF16)
        wc2 = sbB("wc2", (16, 4 * 128), BF16)
        wwin1 = sbB("wwin1", (128, 8 * 512), BF16)
        wb = sbB("wb", (128, 8))
        wwout1 = sbB("wwout1", (128, 4 * 4096), BF16)
        wwin2 = sbB("wwin2", (128, 32 * 512), BF16)
        wwout2 = sbB("wwout2", (128, 4 * 1024), BF16)
        acat = sbB("acat", (128, 64)); ascr = sbB("ascr", (128, 64))
        asrt = sbB("asrt", (128, 64)); aorig = sbB("aorig", (128, 64))
        zs_keep = sbB("zs_keep", (128, 64))
        astat = sbB("astat", (128, 24))
        invk16 = sbB("invk16", (128, 16))
        pos16 = sbB("pos16", (128, 16))
        afm = sbB("afm", (16, QT), BF16)
        g1b = sbB("g1b", (128, 4 * 512)); g2b = sbB("g2b", (128, 4 * 512))
        z1g = sbB("z1g", (128, 4 * 512), BF16); z2g = sbB("z2g", (128, 4 * 512), BF16)
        hfm = sbB("hfm", (128, 32 * 512), BF16)
        fout = sbB("fout", (128, 1024))

        r2 = lambda t, n: t.rearrange("p (a b) -> p a b", a=n)
        S.dma("gpsimd", r2(wproj[:, :], 8),
              bass.AP(dr["fproj"].tensor, 0, [[16, 128], [2048, 8], [1, 16]]))
        S.dma("gpsimd", r2(wc1[0:16, :], 4),
              bass.AP(dr["fc1"].tensor, 0, [[128, 16], [2048, 4], [1, 128]]))
        S.dma("gpsimd", r2(wc2[0:16, :], 4),
              bass.AP(dr["fc2"].tensor, 0, [[128, 16], [2048, 4], [1, 128]]))
        S.dma("gpsimd", r2(wwin1[:, :], 8),
              bass.AP(dr["fwin1"].tensor, 0, [[512, 128], [65536, 8], [1, 512]]))
        S.dma("sync", wb[:, 0:4].unsqueeze(2),
              bass.AP(dr["fb1"].tensor, 0, [[1, 128], [128, 4], [1, 1]]), slow=True)
        S.dma("gpsimd", r2(wwout1[:, :], 4),
              bass.AP(dr["fwout1t"].tensor, 0, [[4096, 128], [524288, 4], [1, 4096]]))
        S.dma("gpsimd", r2(wwin2[:, :], 32),
              bass.AP(dr["fwin2"].tensor, 0, [[512, 128], [65536, 32], [1, 512]]))
        S.dma("sync", wb[:, 4:8].unsqueeze(2),
              bass.AP(dr["fb2"].tensor, 0, [[1, 128], [128, 4], [1, 1]]), slow=True)
        S.dma("gpsimd", r2(wwout2[:, :], 4),
              bass.AP(dr["fwout2t"].tensor, 0, [[1024, 128], [131072, 4], [1, 1024]]))

        def bld_invk(e):
            for i in range(16):
                e.memset(invk16[:, i:i + 1], 1.0 / (i + 1))
                e.memset(pos16[:, i:i + 1], float(i + 1))
            return e.engine_nop()
        S.op("vector", bld_invk, w=[invk16[:, :], pos16[:, :]])

        a3 = lambda t: t.rearrange("p (c e) -> p c e", e=16)
        bc4 = lambda col: col.unsqueeze(2).broadcast_to([128, 4, 16])
        bck = lambda t: t.unsqueeze(1).broadcast_to([128, 4, 16])

        def emit_entmax():
            ra = lambda t: [t[:, :]]
            S.op("vector", lambda e: e.tensor_reduce(astat[:, 0:4], a3(acat[:, :]), AXX, ALU.add),
                 r=[acat[:, :]], w=[astat[:, 0:4]])
            S.op("vector", lambda e: e.tensor_scalar(astat[:, 0:4], astat[:, 0:4], 1.0 / 16,
                                                     None, ALU.mult),
                 r=[astat[:, 0:4]], w=[astat[:, 0:4]])
            S.op("vector", lambda e: e.tensor_tensor(a3(aorig[:, :]), a3(acat[:, :]),
                                                     bc4(astat[:, 0:4]), ALU.subtract),
                 r=[acat[:, :], astat[:, 0:4]], w=[aorig[:, :]])
            S.op("vector", lambda e: e.tensor_tensor(a3(ascr[:, :]), a3(aorig[:, :]),
                                                     a3(aorig[:, :]), ALU.mult),
                 r=[aorig[:, :]], w=[ascr[:, :]])
            S.op("vector", lambda e: e.tensor_reduce(astat[:, 4:8], a3(ascr[:, :]), AXX, ALU.add),
                 r=[ascr[:, :]], w=[astat[:, 4:8]])
            S.op("scalar", lambda e: e.activation(astat[:, 8:12], astat[:, 4:8], AF.Sqrt,
                                                  bias=epscol[:, :], scale=1.0 / 16),
                 r=[astat[:, 4:8], epscol[:, :]], w=[astat[:, 8:12]])
            S.op("vector", lambda e: e.reciprocal(astat[:, 8:12], astat[:, 8:12]),
                 r=[astat[:, 8:12]], w=[astat[:, 8:12]])
            S.op("vector", lambda e: e.tensor_tensor(a3(aorig[:, :]), a3(aorig[:, :]),
                                                     bc4(astat[:, 8:12]), ALU.mult),
                 r=[aorig[:, :], astat[:, 8:12]], w=[aorig[:, :]])
            S.op("vector", lambda e: e.tensor_scalar(aorig[:, :], aorig[:, :], 0.5, None, ALU.mult),
                 r=[aorig[:, :]], w=[aorig[:, :]])
            S.op("vector", lambda e: e.tensor_reduce(astat[:, 12:16], a3(aorig[:, :]), AXX, ALU.max),
                 r=[aorig[:, :]], w=[astat[:, 12:16]])
            S.op("vector", lambda e: e.tensor_tensor(a3(aorig[:, :]), a3(aorig[:, :]),
                                                     bc4(astat[:, 12:16]), ALU.subtract),
                 r=[aorig[:, :], astat[:, 12:16]], w=[aorig[:, :]])
            S.op("vector", lambda e: e.tensor_copy(asrt[:, :], aorig[:, :]),
                 r=[aorig[:, :]], w=[asrt[:, :]])
            def sap(buf, off, dims):
                pairs = [[64, 128]] + [[s, c] for (s, c) in dims if c > 1]
                return bass.AP(buf.tensor if hasattr(buf, "tensor") else buf, off, pairs)
            stages = []
            kk = 2
            while kk <= 16:
                jj = kk // 2
                while jj >= 1:
                    stages.append((kk, jj)); jj //= 2
                kk *= 2
            for (kk, jj) in stages:
                def mk(kk=kk, jj=jj):
                    def f(e):
                        u = 16 // kk
                        g = max(1, kk // (2 * jj))
                        last = None
                        if u == 1:
                            dims = [(16, 4), (2 * jj, g), (1, jj)]
                            lo = sap(asrt[:, :], 0, dims); hi = sap(asrt[:, :], jj, dims)
                            e.tensor_tensor(sap(ascr[:, :], 0, dims), lo, hi, ALU.max)
                            last = e.tensor_tensor(sap(ascr[:, :], jj, dims), lo, hi, ALU.min)
                        else:
                            for half in range(2):
                                o0 = kk * half
                                dims = [(16, 4), (2 * kk, u // 2), (2 * jj, g), (1, jj)]
                                lo = sap(asrt[:, :], o0, dims)
                                hi = sap(asrt[:, :], o0 + jj, dims)
                                omax = sap(ascr[:, :], o0 if half == 0 else o0 + jj, dims)
                                omin = sap(ascr[:, :], o0 + jj if half == 0 else o0, dims)
                                e.tensor_tensor(omax, lo, hi, ALU.max)
                                last = e.tensor_tensor(omin, lo, hi, ALU.min)
                        return last
                    return f
                S.op("vector", mk(), r=[asrt[:, :]], w=[ascr[:, :]])
                S.op("vector", lambda e: e.tensor_copy(asrt[:, :], ascr[:, :]),
                     r=[ascr[:, :]], w=[asrt[:, :]])
            S.op("vector", lambda e: e.tensor_copy(zs_keep[:, :], asrt[:, :]),
                 r=[asrt[:, :]], w=[zs_keep[:, :]])
            S.op("vector", lambda e: e.tensor_copy(ascr[:, :], asrt[:, :]),
                 r=[asrt[:, :]], w=[ascr[:, :]])
            S.op("vector", lambda e: e.tensor_tensor(acat[:, :], asrt[:, :], asrt[:, :], ALU.mult),
                 r=[asrt[:, :]], w=[acat[:, :]])
            for buf in (ascr, acat):
                for ddd in (1, 2, 4, 8):
                    def f2(e, buf=buf, ddd=ddd):
                        t = a3(buf[:, :]); s = a3(asrt[:, :])
                        e.tensor_copy(s[:, :, 0:ddd], t[:, :, 0:ddd])
                        return e.tensor_tensor(s[:, :, ddd:16], t[:, :, ddd:16],
                                               t[:, :, 0:16 - ddd], ALU.add)
                    S.op("vector", f2, r=[buf[:, :]], w=[asrt[:, :]])
                    S.op("vector", lambda e, buf=buf: e.tensor_copy(buf[:, :], asrt[:, :]),
                         r=[asrt[:, :]], w=[buf[:, :]])
            S.op("vector", lambda e: e.tensor_tensor(a3(ascr[:, :]), a3(ascr[:, :]),
                                                     bck(invk16[:, :]), ALU.mult),
                 r=[ascr[:, :], invk16[:, :]], w=[ascr[:, :]])
            S.op("vector", lambda e: e.tensor_tensor(a3(acat[:, :]), a3(acat[:, :]),
                                                     bck(invk16[:, :]), ALU.mult),
                 r=[acat[:, :], invk16[:, :]], w=[acat[:, :]])
            S.op("vector", lambda e: e.tensor_tensor(a3(asrt[:, :]), a3(ascr[:, :]),
                                                     a3(ascr[:, :]), ALU.mult),
                 r=[ascr[:, :]], w=[asrt[:, :]])
            S.op("vector", lambda e: e.tensor_tensor(a3(acat[:, :]), a3(acat[:, :]),
                                                     a3(asrt[:, :]), ALU.subtract),
                 r=[acat[:, :], asrt[:, :]], w=[acat[:, :]])
            S.op("vector", lambda e: e.tensor_tensor(a3(acat[:, :]), bck(invk16[:, :]),
                                                     a3(acat[:, :]), ALU.subtract),
                 r=[acat[:, :], invk16[:, :]], w=[acat[:, :]])
            S.op("vector", lambda e: e.tensor_scalar(acat[:, :], acat[:, :], 0.0, None, ALU.max),
                 r=[acat[:, :]], w=[acat[:, :]])
            S.op("scalar", lambda e: e.activation(acat[:, :], acat[:, :], AF.Sqrt),
                 r=[acat[:, :]], w=[acat[:, :]])
            S.op("vector", lambda e: e.tensor_tensor(ascr[:, :], ascr[:, :], acat[:, :],
                                                     ALU.subtract),
                 r=[ascr[:, :], acat[:, :]], w=[ascr[:, :]])
            S.op("vector", lambda e: e.tensor_tensor(acat[:, :], ascr[:, :], zs_keep[:, :],
                                                     ALU.is_le),
                 r=[ascr[:, :], zs_keep[:, :]], w=[acat[:, :]])
            S.op("vector", lambda e: e.tensor_reduce(astat[:, 16:20], a3(acat[:, :]), AXX, ALU.add),
                 r=[acat[:, :]], w=[astat[:, 16:20]])
            S.op("vector", lambda e: e.tensor_tensor(a3(acat[:, :]), bck(pos16[:, :]),
                                                     bc4(astat[:, 16:20]), ALU.is_le),
                 r=[pos16[:, :], astat[:, 16:20]], w=[acat[:, :]])
            S.op("vector", lambda e: e.tensor_copy(asrt[:, :], acat[:, :]),
                 r=[acat[:, :]], w=[asrt[:, :]])
            S.op("vector", lambda e: e.tensor_tensor(a3(acat[:, :])[:, :, 0:15],
                                                     a3(asrt[:, :])[:, :, 0:15],
                                                     a3(asrt[:, :])[:, :, 1:16], ALU.subtract),
                 r=[asrt[:, :]], w=[acat[:, :]])
            S.op("vector", lambda e: e.tensor_tensor(ascr[:, :], ascr[:, :], acat[:, :], ALU.mult),
                 r=[ascr[:, :], acat[:, :]], w=[ascr[:, :]])
            S.op("vector", lambda e: e.tensor_reduce(astat[:, 20:24], a3(ascr[:, :]), AXX, ALU.add),
                 r=[ascr[:, :]], w=[astat[:, 20:24]])
            S.op("vector", lambda e: e.tensor_tensor(a3(acat[:, :]), a3(aorig[:, :]),
                                                     bc4(astat[:, 20:24]), ALU.subtract),
                 r=[aorig[:, :], astat[:, 20:24]], w=[acat[:, :]])
            S.op("vector", lambda e: e.tensor_scalar(acat[:, :], acat[:, :], 0.0, None, ALU.max),
                 r=[acat[:, :]], w=[acat[:, :]])
            S.op("vector", lambda e: e.tensor_tensor(acat[:, :], acat[:, :], acat[:, :], ALU.mult),
                 r=[acat[:, :]], w=[acat[:, :]])

        def ffn_prelude_a(q):
            for tcq in range(4):
                tc = 4 * q + tcq
                ln_and_transpose(xres[:, 1024 * tc:1024 * tc + 1024], lntmp2, x2q, tcq)
            for tcq in range(4):
                for k in range(8):
                    xa = x2q[:, 512 * k + 128 * tcq:512 * k + 128 * tcq + 128]
                    S.op("tensor", lambda e, xa=xa, k=k, tcq=tcq: e.matmul(
                        bank(6, 0, 128, 64)[:, 16 * tcq:16 * tcq + 16], xa,
                        wproj[:, 16 * k:16 * k + 16], start=(k == 0), stop=(k == 7)),
                        r=[xa, wproj[:, 16 * k:16 * k + 16]],
                        w=[bank(6, 0, 128, 64)[:, 16 * tcq:16 * tcq + 16]])
            S.op("vector", lambda e: e.tensor_copy(acat[:, :], bank(6, 0, 128, 64)),
                 r=[bank(6, 0, 128, 64)], w=[acat[:, :]])
        def ffn_prelude_b():
            S.op("vector", lambda e: e.tensor_copy(lntmp2[:, 0:64], acat[:, :]),
                 r=[acat[:, :]], w=[lntmp2[:, 0:64]])
            b = 2 + tslot[0] % 2; tslot[0] += 1
            tb = tbank(b)
            for tcq in range(4):
                s_ = lntmp2[:, 16 * tcq:16 * tcq + 16]
                S.op("tensor", lambda e, s_=s_, tcq=tcq, tb=tb: e.transpose(
                    tb[0:16, 128 * tcq:128 * tcq + 128], s_, ident[:, :]),
                    r=[s_, ident[:, :]], w=[tb[:, 128 * tcq:128 * tcq + 128]])
            S.op("vector", lambda e, tb=tb: e.tensor_copy(afm[0:16, :], tb[0:16, :]),
                 r=[tb], w=[afm[0:16, :]])
        def emit_g(wc, gb, cg0):
            for r_ in range(4):
                cg = 6 + (r_ % 2)
                wca = wc[0:16, 128 * r_:128 * r_ + 128]
                S.op("tensor", lambda e, wca=wca, cg=cg: e.matmul(
                    bank(cg), wca, afm[0:16, :], start=True, stop=True),
                    r=[wca, afm[0:16, :]], w=[bank(cg)])
                S.op("vector", lambda e, gb=gb, r_=r_, cg=cg: e.tensor_copy(
                    gb[:, 512 * r_:512 * r_ + 512], bank(cg)),
                    r=[bank(cg)], w=[gb[:, 512 * r_:512 * r_ + 512]])
        z1r = hfm[:, 0:2048]
        ffn_prelude_a(0)
        emit_entmax()
        for q in range(NQ):
            if q == 0:
                for r_ in range(4):
                    for k in range(8):
                        wa = wwin1[:, 512 * k + 128 * r_:512 * k + 128 * r_ + 128]
                        S.op("tensor", lambda e, wa=wa, r_=r_, k=k: e.matmul(
                            bank(r_), wa, x2q[:, 512 * k:512 * k + 512],
                            start=(k == 0), stop=(k == 7)),
                            r=[wa, x2q[:, 512 * k:512 * k + 512]], w=[bank(r_)])
                    S.op("vector", lambda e, r_=r_: e.tensor_scalar(
                        z1r[:, 512 * r_:512 * r_ + 512], bank(r_), wb[:, r_:r_ + 1],
                        None, ALU.add),
                        r=[bank(r_), wb[:, r_:r_ + 1]],
                        w=[z1r[:, 512 * r_:512 * r_ + 512]])
                ffn_prelude_b()
                emit_g(wc1, g1b, 6)
                emit_g(wc2, g2b, 7)
                for r_ in range(4):
                    S.op("vector", lambda e, r_=r_: e.tensor_tensor(
                        z1g[:, 512 * r_:512 * r_ + 512], z1r[:, 512 * r_:512 * r_ + 512],
                        g1b[:, 512 * r_:512 * r_ + 512], ALU.mult),
                        r=[z1r[:, 512 * r_:512 * r_ + 512], g1b[:, 512 * r_:512 * r_ + 512]],
                        w=[z1g[:, 512 * r_:512 * r_ + 512]])
            else:
                emit_g(wc1, g1b, 6)
                emit_g(wc2, g2b, 7)
                for r_ in range(4):
                    for k in range(8):
                        wa = wwin1[:, 512 * k + 128 * r_:512 * k + 128 * r_ + 128]
                        S.op("tensor", lambda e, wa=wa, r_=r_, k=k: e.matmul(
                            bank(r_), wa, x2q[:, 512 * k:512 * k + 512],
                            start=(k == 0), stop=(k == 7)),
                            r=[wa, x2q[:, 512 * k:512 * k + 512]], w=[bank(r_)])
                    S.op("vector", lambda e, r_=r_: e.scalar_tensor_tensor(
                        z1g[:, 512 * r_:512 * r_ + 512], bank(r_), wb[:, r_:r_ + 1],
                        g1b[:, 512 * r_:512 * r_ + 512], ALU.add, ALU.mult),
                        r=[bank(r_), wb[:, r_:r_ + 1], g1b[:, 512 * r_:512 * r_ + 512]],
                        w=[z1g[:, 512 * r_:512 * r_ + 512]])
            if q + 1 < NQ:
                ffn_prelude_a(q + 1)
            for o in range(32):
                pb = 4 + o % 2
                for r_ in range(4):
                    wa = wwout1[:, 4096 * r_ + 128 * o:4096 * r_ + 128 * o + 128]
                    S.op("tensor", lambda e, wa=wa, r_=r_, pb=pb: e.matmul(
                        bank(pb), wa, z1g[:, 512 * r_:512 * r_ + 512],
                        start=(r_ == 0), stop=(r_ == 3)),
                        r=[wa, z1g[:, 512 * r_:512 * r_ + 512]], w=[bank(pb)])
                S.op("scalar", lambda e, o=o, pb=pb: e.activation(
                    hfm[:, 512 * o:512 * o + 512], bank(pb), AF.Gelu),
                    r=[bank(pb)], w=[hfm[:, 512 * o:512 * o + 512]])
            if "hfm" in taps and q == 0:
                tapcp("hfm", hfm[:, :])
            if q + 1 < NQ:
                emit_entmax()
            for r_ in range(4):
                for k in range(32):
                    wa = wwin2[:, 512 * k + 128 * r_:512 * k + 128 * r_ + 128]
                    S.op("tensor", lambda e, wa=wa, r_=r_, k=k: e.matmul(
                        bank(r_), wa, hfm[:, 512 * k:512 * k + 512],
                        start=(k == 0), stop=(k == 31)),
                        r=[wa, hfm[:, 512 * k:512 * k + 512]], w=[bank(r_)])
                S.op("vector", lambda e, r_=r_: e.scalar_tensor_tensor(
                    z2g[:, 512 * r_:512 * r_ + 512], bank(r_), wb[:, 4 + r_:5 + r_],
                    g2b[:, 512 * r_:512 * r_ + 512], ALU.add, ALU.mult),
                    r=[bank(r_), wb[:, 4 + r_:5 + r_], g2b[:, 512 * r_:512 * r_ + 512]],
                    w=[z2g[:, 512 * r_:512 * r_ + 512]])
            if q + 1 < NQ:
                ffn_prelude_b()
            for tt in range(4):
                pb = 4 + 2 * (tt % 2)
                for r_ in range(4):
                    za = z2g[:, 512 * r_ + 128 * tt:512 * r_ + 128 * tt + 128]
                    for nb in range(2):
                        S.op("tensor", lambda e, za=za, r_=r_, nb=nb, pb=pb: e.matmul(
                            bank(pb + nb), za,
                            wwout2[:, 1024 * r_ + 512 * nb:1024 * r_ + 512 * nb + 512],
                            start=(r_ == 0), stop=(r_ == 3)),
                            r=[za, wwout2[:, 1024 * r_ + 512 * nb:1024 * r_ + 512 * nb + 512]],
                            w=[bank(pb + nb)])
                tc = 4 * q + tt
                pspan = psum[:, 512 * pb:512 * pb + 1024]
                inv, ngm = ln_cols(pspan)
                S.op("vector", lambda e, pspan=pspan, inv=inv, ngm=ngm: e.tensor_scalar(
                    fout[:, :], pspan, inv, ngm, ALU.mult, ALU.add),
                     r=[pspan, inv, ngm], w=[fout[:, :]])
                xt = xres[:, 1024 * tc:1024 * tc + 1024]
                S.op("vector", lambda e, xt=xt: e.tensor_tensor(fout[:, :], fout[:, :], xt,
                                                                ALU.add),
                     r=[fout[:, :], xt], w=[fout[:, :]])
                S.dma("sync", out[128 * tc:128 * tc + 128, :], fout[:, :])
    S.finalize()
    alld = [(c, c.n) for name, c in S.chans.items() if name.startswith("dq_") and c.n > 0]
    tail = Op("sync", None, [], [], [], None)
    tail.waits = list(alld)
    S.ops.append(tail)
    tail2 = Op("tensor", None, [], [], [], None)
    tail2.waits = list(alld)
    S.ops.append(tail2)
    S.check()
    with nc.Block() as block:
        S.emit(nc, block, es)
    if stage >= 5:
        esB.close()
    es.close()
    return nc, tapd


def kernel(**inputs):
    x = np.asarray(inputs["L0_attn_hidden_states"])
    B, T, _ = x.shape
    nc, _ = build_program(T=T)
    in_maps = [host_inputs(inputs, b, T) for b in range(B)]
    res = run_bass_kernel_spmd(nc, in_maps, core_ids=list(range(8)))
    return np.stack([r["out"] for r in res.results]).astype(np.float32)
